# version 01
# baseline (speedup 1.0000x reference)
import numpy as np
import concourse.bass as bass
import concourse.mybir as mybir
from concourse.bass import IndirectOffsetOnAxis
from concourse.bass_utils import run_bass_kernel_spmd

F32 = mybir.dt.float32
F16 = mybir.dt.float16
BF16 = mybir.dt.bfloat16
U32 = mybir.dt.uint32
AF = mybir.ActivationFunctionType
ALU = mybir.AluOpType
AX = mybir.AxisListType.X

B, S, D = 2, 8192, 1024
TC = 2048
NT = 16
GT = 2
GTOK = GT * 128
NG = NT // GT
NKT = 64
NSLOT = 4096
DH = 129
QB = 256
NQB = TC // QB
EPS_LN = 1e-5
RSQ48 = float(1.0 / np.sqrt(48.0))

ENGS = ("sync", "tensor", "scalar", "vector", "gpsimd")


class Pass:
    def __init__(self, eng_name, eng, sems, c):
        self.e = eng_name
        self.eng = eng
        self.sems = sems
        self.c = c

    def bump(self, inst, sem, n):
        if sem is not None:
            self.c[sem] += n
            if inst is not None:
                inst.then_inc(self.sems[sem], n)
        return inst

    def wait(self, eng_name, sem, val=None):
        if self.e == eng_name:
            self.eng.wait_ge(self.sems[sem], self.c[sem] if val is None else val)

    def mm(self, out, lhsT, rhs, start=True, stop=True, sem=None, n=1):
        i = self.eng.matmul(out, lhsT, rhs, start=start, stop=stop) \
            if self.e == "tensor" else None
        return self.bump(i, sem, n)

    def tr(self, out, in_, ident, sem=None, n=1):
        i = self.eng.transpose(out, in_, ident) if self.e == "tensor" else None
        return self.bump(i, sem, n)

    def act(self, out, in_, func, bias=0.0, scale=1.0, accum_out=None,
            sem=None, n=1):
        i = None
        self.wait("scalar", "sc")
        if self.e == "scalar":
            if func == AF.Copy and not (isinstance(bias, float)
                                        and isinstance(scale, float)):
                func = AF.Identity
            kw = {"accum_out": accum_out} if accum_out is not None else {}
            i = self.eng.activation(out, in_, func, bias=bias, scale=scale, **kw)
        return self.bump(i, "sc", 1)

    def v(self, meth, *args, sem=None, n=1, **kw):
        self.wait("vector", "ve")
        i = getattr(self.eng, meth)(*args, **kw) if self.e == "vector" else None
        return self.bump(i, "ve", 1)

    def dma(self, queue, out, in_, sem=None, transpose=False, slow=False):
        i = None
        if self.e == queue:
            if transpose:
                i = self.eng.dma_start_transpose(out=out, in_=in_)
            elif slow:
                i = self.eng.dma_start(out=out, in_=in_,
                                       allow_slow_non_contiguous=True)
            else:
                i = self.eng.dma_start(out=out, in_=in_)
        return self.bump(i, sem, 16 if sem else 0)

    def gather(self, out, table_ap, idx_ap, sem=None):
        i = None
        if self.e == "gpsimd":
            i = self.eng.indirect_dma_start(
                out=out, out_offset=None, in_=table_ap,
                in_offset=IndirectOffsetOnAxis(idx_ap, 0))
        return self.bump(i, sem, 16 if sem else 0)

    def coll(self, kind, op, groups, ins, outs, sem=None):
        i = None
        if self.e == "gpsimd":
            i = self.eng.collective_compute(kind, op, replica_groups=groups,
                                            ins=ins, outs=outs)
        return self.bump(i, sem, 1 if sem else 0)


def build_nc(stage="full", core0_groups=None):
    nc = bass.Bass("TRN2", target_bir_lowering=False)

    def din(name, shape, dt=F32):
        return nc.dram_tensor(name, list(shape), dt, kind="ExternalInput").ap()

    x_in = din("x_in", (TC, D))
    wqT_in = din("wqT", (D, DH))
    wkT_in = din("wkT", (D, DH))
    wvT_in = din("wvT", (D, D))
    bqp_in = din("bqp", (128, 2))
    bkp_in = din("bkp", (128, 2))
    bvp_in = din("bvp", (128, 8))
    k0p_in = din("k0p", (128, 2))
    v0p_in = din("v0p", (128, 8))
    qproj_in = din("qproj", (128, D))
    routerT_in = din("routerT", (128, 48))
    auxkT_in = din("auxkT", (128, NSLOT))
    maskp_in = din("maskp", (128, 32))
    w1T_in = din("w1T", (D, 256))
    b1p_in = din("b1p", (128, 2))
    w2p_in = din("w2p", (128, 2))
    b2v_in = din("b2v_in", (128, 1))
    gw1v_in = din("gw1v_in", (128, 1))
    gbv_in = din("gbv_in", (128, 1))
    aux_in = din("aux_vals", (NSLOT, D))
    id16_in = din("id16", (128, 128), F16)

    x_out = nc.dram_tensor("x_out", [TC, D], F32, kind="ExternalOutput").ap()

    aux_bf = nc.dram_tensor("aux_bf", [NSLOT, D], BF16)
    kvv_in = nc.dram_tensor("kvv_in", [TC, D], F16)
    kvv_out = nc.dram_tensor("kvv_out", [S, D], F16)
    kvk_in = nc.dram_tensor("kvk_in", [TC, 2 * DH], F16)
    kvk_out = nc.dram_tensor("kvk_out", [S, 2 * DH], F16)
    ar_in = [nc.dram_tensor(f"ar_in{i}", [1, 1], F32) for i in range(2)]
    ar_out = [nc.dram_tensor(f"ar_out{i}", [1, 1], F32) for i in range(2)]

    A = {}

    def sbuf(name, shape, dt):
        A[name] = nc.alloc_sbuf_tensor(name, list(shape), dt).ap()
        return A[name]

    x_sb = sbuf("x_sb", (128, NT * D), F32)
    stage32 = sbuf("stage32", (128, 8192), F32)
    scrF = sbuf("scrF", (128, NSLOT), F32)
    scrG = sbuf("scrG", (128, 8192), BF16)
    arenaW = sbuf("arenaW", (128, 8192), F16)
    scrH_hi = sbuf("scrH_hi", (128, GT * D), F16)
    scrH_lo = sbuf("scrH_lo", (128, GT * D), F16)
    hfm_hi = sbuf("hfm_hi", (128, 8 * GTOK), F16)
    hfm_lo = sbuf("hfm_lo", (128, 8 * GTOK), F16)
    vbuf = sbuf("vbuf", (128, 2048), F16)
    qfm_hi = sbuf("qfm_hi", (128, TC), F16)
    qfm_lo = sbuf("qfm_lo", (128, TC), F16)
    qtm16 = sbuf("qtm16", (128, 2 * DH), F16)
    ktm16 = sbuf("ktm16", (128, 2 * DH), F16)
    pt16 = sbuf("pt16", (128, 2 * QB), F16)
    rkT_hi = sbuf("rkT_hi", (50, NSLOT), F16)
    rkT_lo = sbuf("rkT_lo", (48, NSLOT), F16)
    w1T16 = sbuf("w1T16", (128, 8 * 256), F16)
    wq16 = sbuf("wq16", (128, 8 * 2 * DH), F16)
    wk16 = sbuf("wk16", (128, 8 * 2 * DH), F16)
    wrqT16 = sbuf("wrqT16", (128, 8 * 96), F16)
    rt16 = sbuf("rt16", (128, 96), F16)
    w2T16 = sbuf("w2T16", (128, 2), F16)
    id16 = sbuf("id16s", (128, 128), F16)
    ones16 = sbuf("ones16", (128, 2), F16)
    ones16r = sbuf("ones16r", (2, 128), F16)
    bq16 = sbuf("bq16", (2, DH), F16)
    bk16 = sbuf("bk16", (2, DH), F16)
    k0hl = sbuf("k0hl", (128, 4), F16)
    k0aug = sbuf("k0aug", (4, 1), F16)
    b1c = sbuf("b1c", (128, 2), F32)
    b2v = sbuf("b2v", (128, 1), F32)
    gw1v = sbuf("gw1v", (128, 1), F32)
    gbv = sbuf("gbv", (128, 1), F32)
    sA = sbuf("sA", (128, 64), F32)
    sM = sbuf("sM", (128, 64), F32)
    one1 = sbuf("one1", (1, 16), F32)
    lv16 = sbuf("lv16", (128, 16), F16)
    invb = sbuf("invb", (128, 1), F32)
    top8 = sbuf("top8", (128, 8), F32)
    idx8 = sbuf("idx8", (128, 8), U32)
    w8 = sbuf("w8", (128, 8), F32)
    zsb = sbuf("zsb", (128, 8), F32)
    psk = sbuf("psk", (128, NT), F32)
    psk16 = sbuf("psk16", (128, NT), F16)
    ssink = sbuf("ssink", (128, NT), F32)
    mrow = sbuf("mrow", (128, NT), F32)

    h1pre = arenaW[:, 0:4096]
    rq_hi = arenaW[0:50, 4096:6144]
    rq_lo = arenaW[50:98, 6144:8192]
    rq_lo = arenaW[0:48, 6144:8192]
    qp16 = vbuf
    kfm_hi = stage32.bitcast(F16)[:, 0:8192]
    kfm_lo = stage32.bitcast(F16)[:, 8192:16384]
    fwd16 = scrF.bitcast(BF16)
    k129 = scrG.bitcast(F16)[0:4, 0:8192]
    wv16 = arenaW
    scrG16 = scrG.bitcast(F16)
    qaug = scrH_hi[0:4, 0:TC]
    v0r16 = scrG16[64:65, 6144:6144 + D]
    pskT = scrG16[64:65, 4096:6144]
    bv16 = scrG16[64:66, 2048:2048 + D]
    ones64 = scrG16[64:66, 3072:3200]
    stg32r = scrG16[32:33, 0:8192]
    stg64r = scrG16[64:65, 0:2048]
    tmp32 = scrF[:, 0:D]

    sem_names = ["ld", "te", "sc", "ve", "gp", "cc", "vs", "st", "aux", "kv"]
    sems = {s: nc.alloc_semaphore(s) for s in sem_names}

    x_in_t = x_in.rearrange("(t p) d -> t p d", p=128)
    x_out_t = x_out.rearrange("(t p) d -> t p d", p=128)

    def program(p: Pass):
        emit_prep(p)
        emit_ffn(p, 0)
        if stage in ("attn", "full"):
            emit_attn(p)
        if stage == "full":
            emit_ffn(p, 1)
        emit_out(p)

    def hilo_(p, dst_hi, dst_lo, src32):
        p.wait("scalar", "sc")
        p.act(dst_hi, src32, AF.Copy, sem="sc")
        p.wait("vector", "sc")
        p.v("tensor_tensor", dst_lo, src32, dst_hi, ALU.subtract, sem="ve")

    def emit_prep(p: Pass):
        for i in range(4):
            p.dma("gpsimd", aux_bf.ap()[i * 1024:(i + 1) * 1024, :],
                  aux_in[i * 1024:(i + 1) * 1024, :], sem="aux")
        for t in range(NT):
            p.dma("sync", x_sb[:, t * D:(t + 1) * D], x_in_t[t], sem="ld")
        p.dma("sync", id16[:, :], id16_in, sem="ld")
        p.dma("sync", b2v[:, :], b2v_in, sem="ld")
        p.dma("sync", gw1v[:, :], gw1v_in, sem="ld")
        p.dma("sync", gbv[:, :], gbv_in, sem="ld")
        p.dma("sync", b1c[:, :], b1p_in, sem="ld")
        p.dma("sync", stage32[:, 0:D], qproj_in, sem="ld")
        p.dma("sync", stage32[:, D:D + 48], routerT_in, sem="ld")
        p.dma("sync", stage32[:, D + 48:D + 48 + 2048]
              .rearrange("p (c d) -> p c d", c=8),
              w1T_in.rearrange("(c p) d -> p c d", p=128), sem="ld")
        p.dma("sync", stage32[:, 3120:3120 + NSLOT], auxkT_in, sem="ld")
        p.dma("sync", stage32[:, 7216:7218], w2p_in, sem="ld")
        p.dma("sync", stage32[:, 7218:7250], maskp_in, sem="ld")
        p.wait("scalar", "ld")
        p.wait("vector", "ld")
        p.v("memset", ones16[:, :], 1.0)
        p.v("memset", ones16r[:, :], 1.0)
        p.v("memset", rq_hi[0:50, :], 1.0, sem="ve")
        hilo_(p, qp16[:, 0:D], qp16[:, D:2 * D], stage32[:, 0:D])
        hilo_(p, rt16[:, 0:48], rt16[:, 48:96], stage32[:, D:D + 48])
        for ch in range(8):
            p.act(w1T16[:, ch * 256:(ch + 1) * 256],
                  stage32[:, D + 48 + ch * 256:D + 48 + (ch + 1) * 256],
                  AF.Copy, sem="sc")
        p.act(w2T16[:, 0:2], stage32[:, 7216:7218], AF.Copy, sem="sc")
        mh16 = scrF.bitcast(F16)[:, 0:32]
        ml16 = scrF.bitcast(F16)[:, 32:64]
        hilo_(p, mh16, ml16, stage32[:, 7218:7250])
        p.wait("sync", "ve")
        p.wait("sync", "sc")
        p.dma("sync", rkT_hi[48:49, :], mh16, sem="ld")
        p.dma("sync", rkT_hi[49:50, :], ml16, sem="ld")
        auxk_hi = scrG.bitcast(F16)[:, 0:NSLOT]
        auxk_lo = scrG.bitcast(F16)[:, NSLOT:2 * NSLOT]
        asrc = stage32[:, 3120:3120 + NSLOT]
        for q in range(4):
            sl = slice(q * 1024, (q + 1) * 1024)
            hilo_(p, auxk_hi[:, sl], auxk_lo[:, sl], asrc[:, sl])
        p.wait("tensor", "sc")
        p.wait("tensor", "ve")
        p.wait("tensor", "ld")
        p.wait("scalar", "ve")
        for n in range(8):
            ps = psA[0:48, (n % 4) * 512:(n % 4) * 512 + 512]
            rh = auxk_hi[:, n * 512:(n + 1) * 512]
            rl = auxk_lo[:, n * 512:(n + 1) * 512]
            p.mm(ps, rt16[:, 0:48], rh, start=True, stop=False)
            p.mm(ps, rt16[:, 0:48], rl, start=False, stop=False)
            p.mm(ps, rt16[:, 48:96], rh, start=False, stop=True, sem="te")
            p.wait("scalar", "te")
            p.act(stage32[0:48, n * 512:(n + 1) * 512], ps, AF.Copy,
                  scale=RSQ48, sem="sc")
            p.wait("tensor", "sc")
        for n in range(8):
            sl = slice(n * 512, (n + 1) * 512)
            hilo_(p, rkT_hi[0:48, sl], rkT_lo[0:48, sl], stage32[0:48, sl])
        for n in range(2):
            ps = psB[0:48, n * 512:(n + 1) * 512]
            qh = qp16[:, n * 512:(n + 1) * 512]
            ql = qp16[:, D + n * 512:D + (n + 1) * 512]
            p.mm(ps, rt16[:, 0:48], qh, start=True, stop=False)
            p.mm(ps, rt16[:, 0:48], ql, start=False, stop=False)
            p.mm(ps, rt16[:, 48:96], qh, start=False, stop=True, sem="te")
        p.wait("scalar", "te")
        p.wait("scalar", "ve")
        for n in range(2):
            p.act(stage32[0:48, 4096 + n * 512:4096 + (n + 1) * 512],
                  psB[0:48, n * 512:(n + 1) * 512], AF.Copy, sem="sc")
        wrq32 = stage32[0:48, 4096:4096 + 1024]
        wrq_h = scrH_hi[0:48, 0:1024]
        wrq_l = scrH_lo[0:48, 0:1024]
        hilo_(p, wrq_h, wrq_l, wrq32)
        p.wait("tensor", "sc")
        p.wait("tensor", "ve")
        psA16 = psA.bitcast(F16)
        for ch in range(8):
            p.tr(psA16[0:128, 0:48], wrq_h[:, ch * 128:(ch + 1) * 128],
                 id16[0:48, 0:48])
            p.tr(psA16[0:128, 1024:1072], wrq_l[:, ch * 128:(ch + 1) * 128],
                 id16[0:48, 0:48], sem="te")
            p.wait("scalar", "te")
            p.act(wrqT16[:, ch * 96:ch * 96 + 48], psA16[0:128, 0:48], AF.Copy,
                  sem="sc")
            p.act(wrqT16[:, ch * 96 + 48:ch * 96 + 96], psA16[0:128, 1024:1072],
                  AF.Copy, sem="sc")
            p.wait("tensor", "sc")
        p.wait("gpsimd", "aux")

    def emit_ffn(p: Pass, call):
        p.wait("scalar", "ve")
        p.wait("vector", "sc")
        for t in range(NT):
            xt = x_sb[:, t * D:(t + 1) * D]
            p.v("reduce_sum", out=sA[:, t:t + 1], in_=xt, axis=AX, sem="ve")
            p.wait("scalar", "sc")
            p.act(tmp32[:, :], xt, AF.Square, accum_out=sA[:, 16 + t:17 + t],
                  sem="sc")
        p.wait("vector", "sc")
        p.v("tensor_scalar", sA[:, 32:48], sA[:, 0:16], 1.0 / D, None, ALU.mult)
        p.v("tensor_scalar", sA[:, 48:64], sA[:, 16:32], 1.0 / D, None, ALU.mult)
        p.v("tensor_tensor", tmp32[:, 0:16], sA[:, 32:48], sA[:, 32:48], ALU.mult)
        p.v("tensor_sub", sA[:, 48:64], sA[:, 48:64], tmp32[:, 0:16])
        p.v("tensor_scalar", tmp32[:, 0:16], sA[:, 48:64], EPS_LN, None, ALU.add,
            sem="ve")
        p.wait("scalar", "ve")
        p.act(tmp32[:, 16:32], tmp32[:, 0:16], AF.Ln, sem="sc")
        p.wait("scalar", "sc")
        p.act(sM[:, 0:16], tmp32[:, 16:32], AF.Exp, scale=-0.5, sem="sc")
        p.wait("vector", "sc")
        p.v("tensor_tensor", tmp32[:, 48:64], sA[:, 32:48], sM[:, 0:16], ALU.mult)
        p.v("tensor_scalar", sM[:, 16:32], tmp32[:, 48:64], -1.0, None, ALU.mult)
        p.v("reciprocal", out=tmp32[:, 16:32], in_=tmp32[:, 0:16])
        p.v("tensor_tensor", tmp32[:, 32:48], sA[:, 48:64], tmp32[:, 16:32],
            ALU.mult, sem="ve")
        p.wait("scalar", "ve")
        p.act(sM[:, 32:48], tmp32[:, 32:48], AF.Ln, bias=1.0, sem="sc")
        p.wait("vector", "sc")
        p.v("tensor_copy", out=lv16[:, :], in_=sM[:, 32:48], sem="ve")
        p.wait("tensor", "ve")
        p.mm(psB[0:1, 0:16], ones16[:, 0:1], lv16[:, :], sem="te")
        p.wait("vector", "te")
        p.v("reduce_sum", out=one1[0:1, 1:2], in_=psB[0:1, 0:16], axis=AX,
            sem="ve")
        p.wait("sync", "ve")
        p.dma("sync", ar_in[call].ap(), one1[0:1, 1:2], sem="st")
        p.wait("gpsimd", "st")
        p.coll("AllReduce", ALU.add, [[0, 1, 2, 3, 4, 5, 6, 7]],
               [ar_in[call].ap().opt()], [ar_out[call].ap().opt()], sem="cc")
        p.wait("vector", "st")
        if call == 1:
            p.v("memset", rq_hi[0:50, :], 1.0, sem="ve")
        for g in range(NG):
            for ti in range(GT):
                t = g * GT + ti
                xt = x_sb[:, t * D:(t + 1) * D]
                p.wait("scalar", "ve")
                p.wait("scalar", "sc")
                p.act(tmp32[:, :], xt, AF.Copy, bias=sM[:, 16 + t:17 + t],
                      scale=sM[:, 0 + t:1 + t], sem="sc")
                p.wait("scalar", "sc")
                p.act(scrH_hi[:, ti * D:(ti + 1) * D], tmp32[:, :], AF.Copy,
                      sem="sc")
                p.wait("vector", "sc")
                p.v("tensor_tensor", scrH_lo[:, ti * D:(ti + 1) * D], tmp32[:, :],
                    scrH_hi[:, ti * D:(ti + 1) * D], ALU.subtract, sem="ve")
                p.v("scalar_tensor_tensor", xt, tmp32[:, :], 0.5, xt,
                    ALU.mult, ALU.add, sem="ve")
            p.wait("tensor", "ve")
            psA16_ = psA.bitcast(F16)
            for hl, (hsrc, dst) in enumerate(((scrH_hi, hfm_hi), (scrH_lo, hfm_lo))):
                for ch in range(8):
                    for ti in range(GT):
                        p.tr(psA16_[:, (2 * ti + hl) * 1024:(2 * ti + hl) * 1024 + 128],
                             hsrc[:, ti * D + ch * 128:ti * D + (ch + 1) * 128],
                             id16[:, :], sem="te")
                    p.wait("scalar", "te")
                    for ti in range(GT):
                        p.act(dst[:, ch * GTOK + ti * 128:ch * GTOK + (ti + 1) * 128],
                              psA16_[:, (2 * ti + hl) * 1024:(2 * ti + hl) * 1024 + 128],
                              AF.Copy, sem="sc")
                    p.wait("tensor", "sc")
            for m in range(2):
                ps = psB[:, m * 512:m * 512 + GTOK]
                for ch in range(8):
                    p.mm(ps, w1T16[:, ch * 256 + m * 128:ch * 256 + m * 128 + 128],
                         hfm_hi[:, ch * GTOK:(ch + 1) * GTOK],
                         start=(ch == 0), stop=(ch == 7), sem="te")
                p.wait("scalar", "te")
                p.act(h1pre[:, m * 2048 + g * GTOK:m * 2048 + (g + 1) * GTOK], ps,
                      AF.Gelu, bias=b1c[:, m:m + 1], sem="sc")
            p.wait("tensor", "sc")
            for ti in range(GT):
                t = g * GT + ti
                ps = psB[:, 1536 + t:1536 + t + 1]
                for m in range(2):
                    p.mm(ps, h1pre[:, m * 2048 + g * GTOK + ti * 128:
                                   m * 2048 + g * GTOK + (ti + 1) * 128],
                         w2T16[:, m:m + 1],
                         start=(t == 0 and m == 0), stop=(t == NT - 1 and m == 1),
                         sem="te")
            ps = psA[0:48, 0:GTOK]
            for ch in range(8):
                wh = wrqT16[:, ch * 96:ch * 96 + 48]
                wl = wrqT16[:, ch * 96 + 48:ch * 96 + 96]
                hh = hfm_hi[:, ch * GTOK:(ch + 1) * GTOK]
                hl_ = hfm_lo[:, ch * GTOK:(ch + 1) * GTOK]
                p.mm(ps, wh, hh, start=(ch == 0), stop=False)
                p.mm(ps, wh, hl_, start=False, stop=False)
                p.mm(ps, wl, hh, start=False, stop=(ch == 7), sem="te")
            p.wait("scalar", "te")
            p.act(rq_hi[0:48, g * GTOK:(g + 1) * GTOK], ps, AF.Copy, sem="sc")
            p.wait("vector", "sc")
            p.v("tensor_tensor", rq_lo[0:48, g * GTOK:(g + 1) * GTOK], ps,
                rq_hi[0:48, g * GTOK:(g + 1) * GTOK], ALU.subtract, sem="ve")
            p.wait("tensor", "ve")
        p.wait("scalar", "te")
        p.act(tmp32[:, 0:16], psB[:, 1536:1552], AF.Copy, bias=b2v[:, 0:1],
              sem="sc")
        p.wait("scalar", "sc")
        p.act(tmp32[:, 16:32], tmp32[:, 0:16], AF.Exp, scale=-1.0, sem="sc")
        p.wait("vector", "sc")
        p.v("tensor_scalar", tmp32[:, 16:32], tmp32[:, 16:32], 1.0, None, ALU.add)
        p.v("reciprocal", out=sM[:, 48:64], in_=tmp32[:, 16:32], sem="ve")
        p.wait("sync", "cc")
        p.dma("sync", one1[0:1, 2:3], ar_out[call].ap(), sem="ld")
        p.wait("sync", "ld")
        p.wait("vector", "ld")
        p.v("tensor_scalar", one1[0:1, 3:4], one1[0:1, 2:3], 1.0 / (B * S), 1e-6,
            ALU.mult, op1=ALU.add)
        p.v("reciprocal", out=one1[0:1, 4:5], in_=one1[0:1, 3:4])
        p.v("tensor_scalar", one1[0:1, 5:6], one1[0:1, 4:5], 0.5, None, ALU.mult)
        p.v("tensor_copy", out=one1.bitcast(F16)[0:1, 12:13], in_=one1[0:1, 5:6],
            sem="ve")
        p.wait("tensor", "ve")
        p.mm(psB[:, 1024:1025], ones16r[0:1, :], one1.bitcast(F16)[0:1, 12:13],
             sem="te")
        p.wait("scalar", "te")
        p.act(invb[:, :], psB[:, 1024:1025], AF.Copy, sem="sc")
        p.wait("vector", "sc")
        p.v("tensor_scalar", sM[:, 32:48], sM[:, 32:48], invb[:, 0:1], None,
            ALU.mult)
        p.v("scalar_tensor_tensor", sM[:, 48:64], sM[:, 48:64], 2.5,
            sM[:, 32:48], ALU.mult, ALU.add)
        p.v("tensor_scalar", sM[:, 48:64], sM[:, 48:64], 5.0, 0.0, ALU.min,
            op1=ALU.max, sem="ve")
        p.wait("scalar", "ve")
        p.act(tmp32[:, 16:32], sM[:, 48:64], AF.Copy, scale=gw1v[:, 0:1],
              bias=gbv[:, 0:1], sem="sc")
        p.wait("scalar", "sc")
        p.act(tmp32[:, 32:48], tmp32[:, 16:32], AF.Exp, scale=-1.0, sem="sc")
        p.wait("vector", "sc")
        p.v("tensor_scalar", tmp32[:, 32:48], tmp32[:, 32:48], 1.0, None, ALU.add)
        p.v("reciprocal", out=sM[:, 48:64], in_=tmp32[:, 32:48], sem="ve")
        for t in range(NT):
            g, ti = t // GT, t % GT
            half = psA if t % 2 == 0 else psB
            rq_s = slice(g * GTOK + ti * 128, g * GTOK + (ti + 1) * 128)
            for n in range(8):
                ps = half[:, (n % 4) * 512:(n % 4) * 512 + 512]
                p.mm(ps, rq_hi[0:50, rq_s], rkT_hi[0:50, n * 512:(n + 1) * 512],
                     start=True, stop=False)
                p.mm(ps, rq_hi[0:48, rq_s], rkT_lo[0:48, n * 512:(n + 1) * 512],
                     start=False, stop=False)
                p.mm(ps, rq_lo[0:48, rq_s], rkT_hi[0:48, n * 512:(n + 1) * 512],
                     start=False, stop=True, sem="te")
                if n % 4 == 3:
                    p.wait("scalar", "te")
                    p.wait("scalar", "ve")
                    p.act(scrF[:, (n // 4) * 2048:(n // 4) * 2048 + 2048],
                          half[:, 0:2048], AF.Copy, sem="sc")
                    p.wait("tensor", "sc")
            p.wait("vector", "sc")
            p.v("max", out=top8[:, :], in_=scrF[:, :])
            p.v("tensor_scalar", w8[:, :], top8[:, :], top8[:, 0:1], None,
                ALU.subtract)
            p.v("max_index", out=idx8[:, :], in_max=top8[:, :],
                in_values=scrF[:, :], sem="ve")
            p.wait("gpsimd", "ve")
            for k in range(8):
                p.gather(scrG[:, k * D:(k + 1) * D], aux_bf.ap(),
                         idx8[:, k:k + 1], sem="gp")
            p.wait("scalar", "ve")
            p.act(w8[:, :], w8[:, :], AF.Exp, sem="sc")
            p.wait("vector", "sc")
            p.v("reduce_sum", out=zsb[:, 0:1], in_=w8[:, :], axis=AX)
            p.v("reciprocal", out=zsb[:, 1:2], in_=zsb[:, 0:1])
            p.v("tensor_scalar", zsb[:, 2:3], zsb[:, 1:2],
                sM[:, 48 + t:49 + t], 0.5, ALU.mult, op1=ALU.mult)
            p.v("tensor_scalar", w8[:, :], w8[:, :], zsb[:, 2:3], None, ALU.mult)
            p.wait("vector", "gp")
            xt = x_sb[:, t * D:(t + 1) * D]
            for k in range(8):
                p.v("scalar_tensor_tensor", xt, scrG[:, k * D:(k + 1) * D],
                    w8[:, k:k + 1], xt, ALU.mult, ALU.add,
                    sem="ve" if k == 7 else None)

    def emit_attn(p: Pass):
        p.dma("sync", stage32.rearrange("p (c d) -> p c d", c=8),
              wvT_in.rearrange("(c p) d -> p c d", p=128), sem="ld")
        p.wait("scalar", "ld")
        p.wait("scalar", "te")
        p.wait("scalar", "ve")
        p.wait("vector", "sc")
        for ch in range(8):
            p.act(wv16[:, ch * D:(ch + 1) * D], stage32[:, ch * D:(ch + 1) * D],
                  AF.Copy)
        p.wait("sync", "sc")
        p.dma("sync", stage32[:, 0:8 * DH].rearrange("p (c d) -> p c d", c=8),
              wqT_in.rearrange("(c p) d -> p c d", p=128), sem="ld")
        p.dma("sync", stage32[:, 8 * DH:16 * DH].rearrange("p (c d) -> p c d", c=8),
              wkT_in.rearrange("(c p) d -> p c d", p=128), sem="ld")
        p.dma("sync", stage32[:, 4096:4098], bqp_in, sem="ld")
        p.dma("sync", stage32[:, 4098:4100], bkp_in, sem="ld")
        p.dma("sync", stage32[:, 4100:4102], k0p_in, sem="ld")
        p.dma("sync", stage32[:, 4102:4110], v0p_in, sem="ld")
        p.dma("sync", stage32[:, 4110:4118], bvp_in, sem="ld")
        p.wait("scalar", "ld")
        p.wait("vector", "ld")
        for ch in range(8):
            hilo_(p, wq16[:, ch * 258:ch * 258 + DH],
                  wq16[:, ch * 258 + DH:ch * 258 + 258],
                  stage32[:, ch * DH:(ch + 1) * DH])
            hilo_(p, wk16[:, ch * 258:ch * 258 + DH],
                  wk16[:, ch * 258 + DH:ch * 258 + 258],
                  stage32[:, 8 * DH + ch * DH:8 * DH + (ch + 1) * DH])
        bq_c = scrG16[:, 7680:7684]
        bk_c = scrG16[:, 7684:7688]
        v0_c = scrG16[:, 7688:7704]
        bv_c = scrG16[:, 7704:7720]
        hilo_(p, bq_c[:, 0:2], bq_c[:, 2:4], stage32[:, 4096:4098])
        hilo_(p, bk_c[:, 0:2], bk_c[:, 2:4], stage32[:, 4098:4100])
        hilo_(p, k0hl[:, 0:2], k0hl[:, 2:4], stage32[:, 4100:4102])
        hilo_(p, v0_c[:, 0:8], v0_c[:, 8:16], stage32[:, 4102:4110])
        hilo_(p, bv_c[:, 0:8], bv_c[:, 8:16], stage32[:, 4110:4118])
        p.wait("sync", "ve")
        p.wait("sync", "sc")
        for r, cc in ((0, 0), (1, 2)):
            p.dma("sync", bq16[r:r + 1, 0:128], bq_c[:, cc:cc + 1], sem="ld")
            p.dma("sync", bq16[r:r + 1, 128:129], bq_c[0:1, cc + 1:cc + 2], sem="ld")
            p.dma("sync", bk16[r:r + 1, 0:128], bk_c[:, cc:cc + 1], sem="ld")
            p.dma("sync", bk16[r:r + 1, 128:129], bk_c[0:1, cc + 1:cc + 2], sem="ld")
        p.dma("sync", v0r16[0:1, :], v0_c[:, 0:8], sem="ld")
        p.dma("sync", bv16[0:1, :], bv_c[:, 0:8], sem="ld")
        p.dma("sync", bv16[1:2, :], bv_c[:, 8:16], sem="ld")
        p.dma("sync", k0aug[0:1, :], k0hl[0:1, 1:2], sem="ld")
        p.dma("sync", k0aug[1:2, :], k0hl[0:1, 3:4], sem="ld")
        p.dma("sync", k0aug[2:3, :], k0hl[0:1, 1:2], sem="ld")
        p.wait("sync", "ld")
        p.wait("tensor", "ld")
        p.v("memset", ones64, 1.0)
        p.wait("scalar", "ve")
        for t in range(NT):
            xt = x_sb[:, t * D:(t + 1) * D]
            p.v("reduce_sum", out=sA[:, t:t + 1], in_=xt, axis=AX)
            p.act(tmp32[:, :], xt, AF.Square, accum_out=sA[:, 16 + t:17 + t])
        p.wait("vector", "sc")
        p.v("tensor_scalar", sA[:, 32:48], sA[:, 0:16], 1.0 / D, None, ALU.mult)
        p.v("tensor_scalar", sA[:, 48:64], sA[:, 16:32], 1.0 / D, None, ALU.mult)
        p.v("tensor_tensor", tmp32[:, 0:16], sA[:, 32:48], sA[:, 32:48], ALU.mult)
        p.v("tensor_sub", sA[:, 48:64], sA[:, 48:64], tmp32[:, 0:16])
        p.v("tensor_scalar", tmp32[:, 0:16], sA[:, 48:64], EPS_LN, None, ALU.add,
            sem="ve")
        p.wait("scalar", "ve")
        p.act(tmp32[:, 16:32], tmp32[:, 0:16], AF.Ln)
        p.act(sM[:, 0:16], tmp32[:, 16:32], AF.Exp, scale=-0.5, sem="sc")
        p.wait("vector", "sc")
        p.v("tensor_tensor", tmp32[:, 48:64], sA[:, 32:48], sM[:, 0:16], ALU.mult)
        p.v("tensor_scalar", sM[:, 16:32], tmp32[:, 48:64], -1.0, None, ALU.mult,
            sem="ve")
        for g in range(NG):
            for ti in range(GT):
                t = g * GT + ti
                xt = x_sb[:, t * D:(t + 1) * D]
                p.wait("scalar", "ve")
                p.act(tmp32[:, :], xt, AF.Copy, bias=sM[:, 16 + t:17 + t],
                      scale=sM[:, 0 + t:1 + t])
                p.wait("scalar", "sc")
                p.act(scrH_hi[:, ti * D:(ti + 1) * D], tmp32[:, :], AF.Copy,
                      sem="sc")
                p.wait("vector", "sc")
                p.v("tensor_tensor", scrH_lo[:, ti * D:(ti + 1) * D], tmp32[:, :],
                    scrH_hi[:, ti * D:(ti + 1) * D], ALU.subtract)
                p.v("tensor_add", xt, xt, tmp32[:, :], sem="ve")
            p.wait("tensor", "ve")
            psA16_ = psA.bitcast(F16)
            for hl, (hsrc, dst) in enumerate(((scrH_hi, hfm_hi), (scrH_lo, hfm_lo))):
                for ch in range(8):
                    for ti in range(GT):
                        p.tr(psA16_[:, (2 * ti + hl) * 1024:(2 * ti + hl) * 1024 + 128],
                             hsrc[:, ti * D + ch * 128:ti * D + (ch + 1) * 128],
                             id16[:, :], sem="te")
                    p.wait("scalar", "te")
                    for ti in range(GT):
                        p.act(dst[:, ch * GTOK + ti * 128:ch * GTOK + (ti + 1) * 128],
                              psA16_[:, (2 * ti + hl) * 1024:(2 * ti + hl) * 1024 + 128],
                              AF.Copy, sem="sc")
                    p.wait("tensor", "sc")
            for ti in range(GT):
                t = g * GT + ti
                tsl = slice(ti * 128, (ti + 1) * 128)
                psq = psB[:, 0:DH]
                for ch in range(8):
                    hh = hfm_hi[:, ch * GTOK + ti * 128:ch * GTOK + (ti + 1) * 128]
                    hl_ = hfm_lo[:, ch * GTOK + ti * 128:ch * GTOK + (ti + 1) * 128]
                    wh = wq16[:, ch * 258:ch * 258 + DH]
                    wl = wq16[:, ch * 258 + DH:ch * 258 + 258]
                    p.mm(psq, hh, wh, start=(ch == 0), stop=False)
                    p.mm(psq, hl_, wh, start=False, stop=False)
                    p.mm(psq, hh, wl, start=False, stop=False)
                p.mm(psq, ones16r[0:2, 0:128], bq16[0:2, :], start=False,
                     stop=True, sem="te")
                p.wait("scalar", "te")
                p.act(qtm16[:, 0:DH], psq, AF.Copy, sem="sc")
                p.wait("vector", "sc")
                p.v("tensor_tensor", qtm16[:, DH:2 * DH], psq, qtm16[:, 0:DH],
                    ALU.subtract, sem="ve")
                psk_ = psB[:, 512:512 + DH]
                for ch in range(8):
                    hh = hfm_hi[:, ch * GTOK + ti * 128:ch * GTOK + (ti + 1) * 128]
                    hl_ = hfm_lo[:, ch * GTOK + ti * 128:ch * GTOK + (ti + 1) * 128]
                    wh = wk16[:, ch * 258:ch * 258 + DH]
                    wl = wk16[:, ch * 258 + DH:ch * 258 + 258]
                    p.mm(psk_, hh, wh, start=(ch == 0), stop=False)
                    p.mm(psk_, hl_, wh, start=False, stop=False)
                    p.mm(psk_, hh, wl, start=False, stop=False)
                p.mm(psk_, ones16r[0:2, 0:128], bk16[0:2, :], start=False,
                     stop=True, sem="te")
                p.wait("scalar", "te")
                p.wait("scalar", "kv")
                p.act(ktm16[:, 0:DH], psk_, AF.Copy)
                p.wait("scalar", "sc")
                p.wait("vector", "sc")
                p.v("tensor_tensor", ktm16[:, DH:2 * DH], psk_, ktm16[:, 0:DH],
                    ALU.subtract, sem="ve")
                p.wait("sync", "ve")
                p.dma("sync", kvk_in.ap()[t * 128:(t + 1) * 128, :], ktm16[:, :],
                      sem="kv")
                for half in range(2):
                    psv = psA[:, half * 512:(half + 1) * 512]
                    for ch in range(8):
                        hh = hfm_hi[:, ch * GTOK + ti * 128:ch * GTOK + (ti + 1) * 128]
                        wv_ = wv16[:, ch * D + half * 512:ch * D + (half + 1) * 512]
                        p.mm(psv, hh, wv_, start=(ch == 0), stop=False)
                    p.mm(psv, ones64, bv16[0:2, half * 512:(half + 1) * 512],
                         start=False, stop=True, sem="te")
                p.wait("scalar", "te")
                p.wait("scalar", "kv")
                p.act(vbuf[:, 0:512], psA[:, 0:512], AF.Copy)
                p.wait("scalar", "sc")
                p.act(vbuf[:, 512:1024], psA[:, 512:1024], AF.Copy, sem="sc")
                p.wait("sync", "sc")
                p.dma("sync", kvv_in.ap()[t * 128:(t + 1) * 128, :],
                      vbuf[:, 0:1024], sem="kv")
                p.wait("tensor", "ve")
                p.tr(psA16_[:, 3072:3072 + 128], qtm16[:, 0:128], id16[:, :])
                p.tr(psA16_[:, 3072 + 128:3072 + 256], qtm16[:, DH:DH + 128],
                     id16[:, :])
                p.tr(psA16_[0:1, 3072 + 256:3072 + 384], qtm16[:, 128:129],
                     id16[:, :])
                p.tr(psA16_[0:1, 3072 + 384:3072 + 512], qtm16[:, 2 * DH - 1:2 * DH],
                     id16[:, :], sem="te")
                p.wait("scalar", "te")
                p.act(qfm_hi[:, t * 128:(t + 1) * 128], psA16_[:, 3072:3200],
                      AF.Copy)
                p.wait("scalar", "sc")
                p.act(qfm_lo[:, t * 128:(t + 1) * 128], psA16_[:, 3200:3328],
                      AF.Copy)
                p.wait("scalar", "sc")
                p.act(stg32r[0:1, t * 128:(t + 1) * 128],
                      psA16_[0:1, 3328:3456], AF.Copy)
                p.wait("scalar", "sc")
                p.act(stg64r[0:1, t * 128:(t + 1) * 128],
                      psA16_[0:1, 3456:3584], AF.Copy, sem="sc")
                p.wait("tensor", "sc")
        p.wait("sync", "sc")
        p.dma("sync", qaug[0:1, :], stg32r[0:1, 0:TC], sem="ld")
        p.dma("sync", qaug[1:2, :], stg32r[0:1, 0:TC], sem="ld")
        p.dma("sync", qaug[2:3, :], stg64r[0:1, 0:TC], sem="ld")
        p.wait("gpsimd", "kv")
        p.coll("AllGather", ALU.bypass, [[0, 1, 2, 3], [4, 5, 6, 7]],
               [kvk_in.ap().opt()], [kvk_out.ap().opt()], sem="cc")
        p.coll("AllGather", ALU.bypass, [[0, 1, 2, 3], [4, 5, 6, 7]],
               [kvv_in.ap().opt()], [kvv_out.ap().opt()], sem="cc")
        p.wait("sync", "cc", val=p.c["cc"] - 1)
        p.wait("scalar", "ld")
        psA16_ = psA.bitcast(F16)
        for kt in range(NKT):
            p.wait("sync", "te")
            p.dma("sync", ktm16[:, :], kvk_out.ap()[kt * 128:(kt + 1) * 128, :],
                  sem="vs")
            p.wait("tensor", "vs")
            p.tr(psA16_[:, 0:128], ktm16[:, 0:128], id16[:, :])
            p.tr(psA16_[:, 1024:1152], ktm16[:, DH:DH + 128], id16[:, :])
            p.tr(psA16_[0:1, 2048:2176], ktm16[:, 128:129], id16[:, :])
            p.tr(psA16_[0:1, 3072:3200], ktm16[:, 2 * DH - 1:2 * DH], id16[:, :],
                 sem="te")
            p.wait("scalar", "te")
            p.act(kfm_hi[:, kt * 128:(kt + 1) * 128], psA16_[:, 0:128], AF.Copy)
            p.wait("scalar", "sc")
            p.act(kfm_lo[:, kt * 128:(kt + 1) * 128], psA16_[:, 1024:1152],
                  AF.Copy)
            p.wait("scalar", "sc")
            p.act(scrG16[0:1, kt * 128:(kt + 1) * 128], psA16_[0:1, 2048:2176],
                  AF.Copy)
            p.wait("scalar", "sc")
            p.act(stg32r[0:1, kt * 128:(kt + 1) * 128], psA16_[0:1, 3072:3200],
                  AF.Copy, sem="sc")
            p.wait("tensor", "sc")
        p.wait("vector", "ld")
        p.wait("vector", "sc")
        p.v("memset", stg64r[0:1, 0:2048], 1.0, sem="ve")
        p.wait("sync", "sc")
        p.wait("sync", "ve")
        p.dma("sync", scrG16[1:2, 0:8192], stg32r[0:1, 0:8192], sem="ld")
        p.dma("sync", scrG16[2:3, 0:8192], scrG16[0:1, 0:8192], sem="ld")
        for i4 in range(4):
            p.dma("sync", scrG16[3:4, i4 * 2048:(i4 + 1) * 2048],
                  stg64r[0:1, 0:2048], sem="ld")
        p.wait("tensor", "ld")
        for t in range(NT):
            tsl = slice(t * 128, (t + 1) * 128)
            ps = psB[:, 1536 + t:1537 + t]
            p.mm(ps, qfm_hi[:, tsl], k0hl[:, 0:1], start=(t == 0), stop=False)
            p.mm(ps, qfm_hi[:, tsl], k0hl[:, 2:3], start=False, stop=False)
            p.mm(ps, qfm_lo[:, tsl], k0hl[:, 0:1], start=False, stop=False)
            p.mm(ps, qaug[0:3, tsl], k0aug[0:3, :], start=False,
                 stop=(t == NT - 1), sem="te")
        p.wait("scalar", "te")
        p.act(ssink[:, :], psB[:, 1536:1552], AF.Copy, sem="sc")
        for t in range(NT):
            tsl = slice(t * 128, (t + 1) * 128)
            for n in range(16):
                ps = psA[:, (n % 2) * 512:(n % 2) * 512 + 512]
                if n >= 2 or t > 0:
                    p.wait("tensor", "sc")
                p.mm(ps, qfm_hi[:, tsl], kfm_hi[:, n * 512:(n + 1) * 512],
                     start=True, stop=False)
                p.mm(ps, qaug[0:1, tsl], scrG16[0:1, n * 512:(n + 1) * 512],
                     start=False, stop=True, sem="te")
                p.wait("scalar", "te")
                if n == 0:
                    p.wait("scalar", "ve")
                p.act(fwd16[:, n * 512:(n + 1) * 512], ps, AF.Copy, sem="sc")
            p.wait("vector", "sc")
            p.v("reduce_max", out=mrow[:, t:t + 1], in_=fwd16[:, 0:8192], axis=AX,
                sem="ve")
        p.wait("vector", "sc")
        p.v("tensor_max", mrow[:, :], mrow[:, :], ssink[:, :])
        p.v("tensor_scalar", mrow[:, :], mrow[:, :], 0.5, None, ALU.add)
        p.v("tensor_sub", psk[:, :], ssink[:, :], mrow[:, :], sem="ve")
        p.wait("scalar", "ve")
        p.act(psk[:, :], psk[:, :], AF.Exp, sem="sc")
        p.wait("vector", "sc")
        p.v("tensor_copy", out=psk16[:, :], in_=psk[:, :])
        p.v("tensor_scalar", lv16[:, :], mrow[:, :], -1.0, None, ALU.mult,
            sem="ve")
        p.wait("tensor", "ve")
        for t in range(NT):
            p.tr(psA16_[0:1, 0:128], psk16[:, t:t + 1], id16[:, :])
            p.tr(psA16_[0:1, 1024:1152], lv16[:, t:t + 1], id16[:, :], sem="te")
            p.wait("scalar", "te")
            p.act(pskT[0:1, t * 128:(t + 1) * 128], psA16_[0:1, 0:128], AF.Copy)
            p.wait("scalar", "sc")
            p.act(stg32r[0:1, t * 128:(t + 1) * 128],
                  psA16_[0:1, 1024:1152], AF.Copy, sem="sc")
            p.wait("tensor", "sc")
        p.wait("sync", "sc")
        p.dma("sync", qaug[3:4, 0:TC], stg32r[0:1, 0:TC], sem="ld")
        p.wait("sync", "ld")
        p.wait("tensor", "ld")
        p.wait("sync", "cc")
        vs_base = p.c["vs"]
        NVT = NQB * NKT

        def issue_v(g):
            kt = g % NKT
            p.wait("sync", "te")
            p.dma("sync", vbuf[:, (g % 2) * D:(g % 2) * D + D],
                  kvv_out.ap()[kt * 128:(kt + 1) * 128, :], sem="vs")

        issue_v(0)
        issue_v(1)
        for qb in range(NQB):
            qsl = slice(qb * QB, (qb + 1) * QB)
            for kt in range(NKT):
                g = qb * NKT + kt
                par = kt % 2
                sbk = psB[:, par * 512:par * 512 + QB]
                if g >= 2:
                    p.wait("tensor", "sc")
                p.mm(sbk, kfm_hi[:, kt * 128:(kt + 1) * 128], qfm_hi[:, qsl],
                     start=True, stop=False)
                p.mm(sbk, kfm_hi[:, kt * 128:(kt + 1) * 128], qfm_lo[:, qsl],
                     start=False, stop=False)
                p.mm(sbk, kfm_lo[:, kt * 128:(kt + 1) * 128], qfm_hi[:, qsl],
                     start=False, stop=False)
                p.mm(sbk, scrG16[0:4, kt * 128:(kt + 1) * 128], qaug[0:4, qsl],
                     start=False, stop=True, sem="te")
                p.wait("scalar", "te")
                p.act(pt16[:, par * QB:(par + 1) * QB], sbk, AF.Exp, sem="sc")
                p.wait("tensor", "sc")
                p.wait("tensor", "vs")
                for qt in range(2):
                    ptsl = pt16[:, par * QB + qt * 128:par * QB + (qt + 1) * 128]
                    vsl = vbuf[:, (g % 2) * D:(g % 2) * D + D]
                    p.mm(psA[:, qt * 1024:qt * 1024 + 512], ptsl, vsl[:, 0:512],
                         start=(kt == 0), stop=False)
                    p.mm(psA[:, qt * 1024 + 512:qt * 1024 + 1024], ptsl,
                         vsl[:, 512:1024], start=(kt == 0), stop=False)
                    p.mm(psB[:, 1024 + qt:1025 + qt], ptsl, ones16[:, 0:1],
                         start=(kt == 0 and qt == 0),
                         stop=(kt == NKT - 1 and qt == 1), sem="te")
                if g + 2 < NVT:
                    issue_v(g + 2)
            for qt in range(2):
                t = qb * 2 + qt
                tsl2 = slice(t * 128, (t + 1) * 128)
                p.mm(psA[:, qt * 1024:qt * 1024 + 512], pskT[0:1, tsl2],
                     v0r16[0:1, 0:512], start=False, stop=True)
                p.mm(psA[:, qt * 1024 + 512:qt * 1024 + 1024],
                     pskT[0:1, tsl2], v0r16[0:1, 512:1024],
                     start=False, stop=True, sem="te")
            p.wait("vector", "te")
            p.v("tensor_copy", out=zsb[:, 0:2], in_=psB[:, 1024:1026])
            p.v("tensor_add", zsb[:, 2:4], zsb[:, 0:2], psk[:, 2 * qb:2 * qb + 2])
            p.v("reciprocal", out=zsb[:, 4:6], in_=zsb[:, 2:4])
            for qt in range(2):
                t = qb * 2 + qt
                xt = x_sb[:, t * D:(t + 1) * D]
                for half in range(2):
                    p.v("scalar_tensor_tensor", xt[:, half * 512:(half + 1) * 512],
                        psA[:, qt * 1024 + half * 512:qt * 1024 + (half + 1) * 512],
                        zsb[:, 4 + qt:5 + qt], xt[:, half * 512:(half + 1) * 512],
                        ALU.mult, ALU.add)
            p.wait("tensor", "ve")

    def emit_out(p: Pass):
        p.wait("sync", "ve")
        for t in range(NT):
            p.dma("sync", x_out_t[t], x_sb[:, t * D:(t + 1) * D], sem="st")
        p.wait("sync", "st")

    psA = nc.alloc_psum_tensor("psA", [128, 2048], F32).ap()
    psB = nc.alloc_psum_tensor("psB", [128, 2048], F32).ap()

    counts = []
    with nc.Block() as block:
        for eng_name in ENGS:
            c = {s: 0 for s in sem_names}
            counts.append(c)

            def make(f_name, f_c):
                def _(eng):
                    program(Pass(f_name, eng, sems, f_c))
                return _
            getattr(block, eng_name)(make(eng_name, c))
    for c in counts[1:]:
        assert c == counts[0], f"counter mismatch {c} vs {counts[0]}"
    return nc


def host_inputs(I, core):
    b, j = core // 4, core % 4
    t0 = j * TC
    f32 = np.float32

    def T(a):
        return np.ascontiguousarray(np.asarray(a, f32).T)

    def pad129(a):
        out = np.zeros((128, 2), f32)
        a = np.asarray(a, f32).reshape(-1)
        out[:, 0] = a[0:128]
        out[0, 1] = a[128]
        return out

    return {
        "x_in": np.ascontiguousarray(np.asarray(I["L0_attn_x"], f32)[b, t0:t0 + TC]),
        "wqT": T(I["L0_attn_w_q"]),
        "wkT": T(I["L0_attn_w_k"]),
        "wvT": T(I["L0_attn_w_v"]),
        "bqp": pad129(I["L0_attn_b_q"]),
        "bkp": pad129(I["L0_attn_b_k"]),
        "bvp": np.asarray(I["L0_attn_b_v"], f32).reshape(128, 8),
        "k0p": pad129(I["L0_attn_k_0"]),
        "v0p": np.asarray(I["L0_attn_v_0"], f32).reshape(128, 8),
        "qproj": np.asarray(I["L0_ffn_q_proj_w"], f32),
        "routerT": T(I["L0_ffn_router_w"]),
        "auxkT": T(I["L0_ffn_aux_keys"]),
        "maskp": np.asarray(I["L0_ffn_reliability_mask"], f32).reshape(128, 32),
        "w1T": T(I["L0_ffn_unc_w1"]),
        "b1p": np.ascontiguousarray(
            np.asarray(I["L0_ffn_unc_b1"], f32).reshape(2, 128).T),
        "w2p": np.ascontiguousarray(
            np.asarray(I["L0_ffn_unc_w2"], f32).reshape(2, 128).T),
        "b2v_in": np.broadcast_to(
            np.asarray(I["L0_ffn_unc_b2"], f32).reshape(1, 1), (128, 1)).copy(),
        "gw1v_in": np.full((128, 1), np.float32(I["L0_ffn_gate_w1"]), f32),
        "gbv_in": np.full((128, 1), np.float32(I["L0_ffn_gate_bias"]), f32),
        "aux_vals": np.asarray(I["L0_ffn_aux_values"], f32),
        "id16": np.eye(128, dtype=np.float16),
    }


def kernel(**inputs):
    nc = build_nc("full")
    in_maps = [host_inputs(inputs, c) for c in range(8)]
    res = run_bass_kernel_spmd(nc, in_maps, core_ids=list(range(8)))
    out = np.empty((B, S, D), np.float32)
    for c in range(8):
        b, j = c // 4, c % 4
        out[b, j * TC:(j + 1) * TC] = res.results[c]["x_out"]
    return out


# version 02
# speedup vs baseline: 1.0894x; 1.0894x over previous
import numpy as np
import concourse.bass as bass
import concourse.mybir as mybir
from concourse.bass import IndirectOffsetOnAxis
from concourse.bass_utils import run_bass_kernel_spmd

F32 = mybir.dt.float32
F16 = mybir.dt.float16
BF16 = mybir.dt.bfloat16
U32 = mybir.dt.uint32
AF = mybir.ActivationFunctionType
ALU = mybir.AluOpType
AX = mybir.AxisListType.X

B, S, D = 2, 8192, 1024
TC = 2048
NT = 16
GT = 2
GTOK = GT * 128
NG = NT // GT
NKT = 64
NSLOT = 4096
DH = 129
QB = 256
NQB = TC // QB
EPS_LN = 1e-5
RSQ48 = float(1.0 / np.sqrt(48.0))

ENGS = ("sync", "tensor", "scalar", "vector", "gpsimd")


class Pass:
    def __init__(self, eng_name, eng, sems, c):
        self.e = eng_name
        self.eng = eng
        self.sems = sems
        self.c = c

    def bump(self, inst, sem, n):
        if sem is not None:
            self.c[sem] += n
            if inst is not None:
                inst.then_inc(self.sems[sem], n)
        return inst

    def wait(self, eng_name, sem, val=None):
        if self.e == eng_name:
            self.eng.wait_ge(self.sems[sem], self.c[sem] if val is None else val)

    def mm(self, out, lhsT, rhs, start=True, stop=True, sem=None, n=1):
        i = self.eng.matmul(out, lhsT, rhs, start=start, stop=stop) \
            if self.e == "tensor" else None
        return self.bump(i, sem, n)

    def tr(self, out, in_, ident, sem=None, n=1):
        i = self.eng.transpose(out, in_, ident) if self.e == "tensor" else None
        return self.bump(i, sem, n)

    def act(self, out, in_, func, bias=0.0, scale=1.0, accum_out=None,
            sem=None, n=1):
        i = None
        self.wait("scalar", "sc")
        if self.e == "scalar":
            if func == AF.Copy and not (isinstance(bias, float)
                                        and isinstance(scale, float)):
                func = AF.Identity
            kw = {"accum_out": accum_out} if accum_out is not None else {}
            i = self.eng.activation(out, in_, func, bias=bias, scale=scale, **kw)
        return self.bump(i, "sc", 1)

    def v(self, meth, *args, sem=None, n=1, **kw):
        self.wait("vector", "ve")
        i = getattr(self.eng, meth)(*args, **kw) if self.e == "vector" else None
        return self.bump(i, "ve", 1)

    def dma(self, queue, out, in_, sem=None, transpose=False, slow=False):
        i = None
        if self.e == queue:
            if transpose:
                i = self.eng.dma_start_transpose(out=out, in_=in_)
            elif slow:
                i = self.eng.dma_start(out=out, in_=in_,
                                       allow_slow_non_contiguous=True)
            else:
                i = self.eng.dma_start(out=out, in_=in_)
        return self.bump(i, sem, 16 if sem else 0)

    def gather(self, out, table_ap, idx_ap, sem=None):
        i = None
        if self.e == "gpsimd":
            i = self.eng.indirect_dma_start(
                out=out, out_offset=None, in_=table_ap,
                in_offset=IndirectOffsetOnAxis(idx_ap, 0))
        return self.bump(i, sem, 16 if sem else 0)

    def coll(self, kind, op, groups, ins, outs, sem=None):
        i = None
        if self.e == "gpsimd":
            i = self.eng.collective_compute(kind, op, replica_groups=groups,
                                            ins=ins, outs=outs)
        return self.bump(i, sem, 1 if sem else 0)


def build_nc(stage="full", core0_groups=None):
    nc = bass.Bass("TRN2", target_bir_lowering=False)

    def din(name, shape, dt=F32):
        return nc.dram_tensor(name, list(shape), dt, kind="ExternalInput").ap()

    x_in = din("x_in", (TC, D))
    wqT_in = din("wqT", (D, DH))
    wkT_in = din("wkT", (D, DH))
    wvT_in = din("wvT", (D, D))
    bqp_in = din("bqp", (128, 2))
    bkp_in = din("bkp", (128, 2))
    bvp_in = din("bvp", (128, 8))
    k0p_in = din("k0p", (128, 2))
    v0p_in = din("v0p", (128, 8))
    qproj_in = din("qproj", (128, D))
    routerT_in = din("routerT", (128, 48))
    auxkT_in = din("auxkT", (128, NSLOT))
    maskp_in = din("maskp", (128, 32))
    w1T_in = din("w1T", (D, 256))
    b1p_in = din("b1p", (128, 2))
    w2p_in = din("w2p", (128, 2))
    b2v_in = din("b2v_in", (128, 1))
    gw1v_in = din("gw1v_in", (128, 1))
    gbv_in = din("gbv_in", (128, 1))
    aux_in = din("aux_vals", (NSLOT, D))
    id16_in = din("id16", (128, 128), F16)

    x_out = nc.dram_tensor("x_out", [TC, D], F32, kind="ExternalOutput").ap()

    aux_bf = nc.dram_tensor("aux_bf", [NSLOT, D], BF16)
    kvv_in = nc.dram_tensor("kvv_in", [TC, D], F16)
    kvv_out = nc.dram_tensor("kvv_out", [S, D], F16)
    kvk_in = nc.dram_tensor("kvk_in", [TC, 2 * DH], F16)
    kvk_out = nc.dram_tensor("kvk_out", [S, 2 * DH], F16)
    ar_in = [nc.dram_tensor(f"ar_in{i}", [1, 1], F32) for i in range(2)]
    ar_out = [nc.dram_tensor(f"ar_out{i}", [1, 1], F32) for i in range(2)]

    A = {}

    def sbuf(name, shape, dt):
        A[name] = nc.alloc_sbuf_tensor(name, list(shape), dt).ap()
        return A[name]

    x_sb = sbuf("x_sb", (128, NT * D), F32)
    stage32 = sbuf("stage32", (128, 8192), F32)
    scrF = sbuf("scrF", (128, NSLOT), F32)
    scrG = sbuf("scrG", (128, 8192), BF16)
    arenaW = sbuf("arenaW", (128, 8192), F16)
    scrH_hi = sbuf("scrH_hi", (128, GT * D), F16)
    scrH_lo = sbuf("scrH_lo", (128, GT * D), F16)
    hfm_hi = sbuf("hfm_hi", (128, 8 * GTOK), F16)
    hfm_lo = sbuf("hfm_lo", (128, 8 * GTOK), F16)
    vbuf = sbuf("vbuf", (128, 2048), F16)
    qfm_hi = sbuf("qfm_hi", (128, TC), F16)
    qfm_lo = sbuf("qfm_lo", (128, TC), F16)
    qtm16 = sbuf("qtm16", (128, 2 * DH), F16)
    ktm16 = sbuf("ktm16", (128, 4 * DH), F16)
    pt16 = sbuf("pt16", (128, 2 * QB), F16)
    rkT_hi = sbuf("rkT_hi", (50, NSLOT), F16)
    rkT_lo = sbuf("rkT_lo", (48, NSLOT), F16)
    w1T16 = sbuf("w1T16", (128, 8 * 256), F16)
    wq16 = sbuf("wq16", (128, 8 * 2 * DH), F16)
    wk16 = sbuf("wk16", (128, 8 * 2 * DH), F16)
    wrqT16 = sbuf("wrqT16", (128, 8 * 96), F16)
    rt16 = sbuf("rt16", (128, 96), F16)
    w2T16 = sbuf("w2T16", (128, 2), F16)
    id16 = sbuf("id16s", (128, 128), F16)
    ones16 = sbuf("ones16", (128, 2), F16)
    ones16r = sbuf("ones16r", (2, 128), F16)
    bq16 = sbuf("bq16", (2, DH), F16)
    bk16 = sbuf("bk16", (2, DH), F16)
    k0hl = sbuf("k0hl", (128, 4), F16)
    k0aug = sbuf("k0aug", (4, 1), F16)
    b1c = sbuf("b1c", (128, 2), F32)
    b2v = sbuf("b2v", (128, 1), F32)
    gw1v = sbuf("gw1v", (128, 1), F32)
    gbv = sbuf("gbv", (128, 1), F32)
    sA = sbuf("sA", (128, 64), F32)
    sM = sbuf("sM", (128, 64), F32)
    one1 = sbuf("one1", (1, 16), F32)
    lv16 = sbuf("lv16", (128, 16), F16)
    invb = sbuf("invb", (128, 1), F32)
    top8 = sbuf("top8", (128, 8), F32)
    idx8 = sbuf("idx8", (128, 8), U32)
    w8 = sbuf("w8", (128, 16), F32)
    zsb = sbuf("zsb", (128, 16), F32)
    psk = sbuf("psk", (128, NT), F32)
    psk16 = sbuf("psk16", (128, NT), F16)
    ssink = sbuf("ssink", (128, NT), F32)
    mrow = sbuf("mrow", (128, NT), F32)

    h1pre = arenaW[:, 0:4096]
    rq_hi = arenaW[0:50, 4096:6144]
    rq_lo = arenaW[50:98, 6144:8192]
    rq_lo = arenaW[0:48, 6144:8192]
    qp16 = vbuf
    kfm_hi = stage32.bitcast(F16)[:, 0:8192]
    kfm_lo = stage32.bitcast(F16)[:, 8192:16384]
    fwd16 = scrF.bitcast(BF16)
    k129 = scrG.bitcast(F16)[0:4, 0:8192]
    wv16 = arenaW
    scrG16 = scrG.bitcast(F16)
    qaug = scrH_hi[0:4, 0:TC]
    v0r16 = scrG16[64:65, 6144:6144 + D]
    pskT = scrG16[64:65, 4096:6144]
    bv16 = scrG16[64:66, 2048:2048 + D]
    ones64 = scrG16[64:66, 3072:3200]
    stg32r = scrG16[32:33, 0:8192]
    stg64r = scrG16[64:65, 0:2048]
    tmp32 = scrF[:, 0:D]

    sem_names = ["ld", "te", "sc", "ve", "gp", "cc", "vs", "st", "aux", "kv"]
    sems = {s: nc.alloc_semaphore(s) for s in sem_names}

    x_in_t = x_in.rearrange("(t p) d -> t p d", p=128)
    x_out_t = x_out.rearrange("(t p) d -> t p d", p=128)

    def program(p: Pass):
        emit_prep(p)
        emit_ffn(p, 0)
        if stage in ("attn", "full"):
            emit_attn(p)
        if stage == "full":
            emit_ffn(p, 1)
        emit_out(p)

    def hilo_(p, dst_hi, dst_lo, src32):
        p.wait("scalar", "sc")
        p.act(dst_hi, src32, AF.Copy, sem="sc")
        p.wait("vector", "sc")
        p.v("tensor_tensor", dst_lo, src32, dst_hi, ALU.subtract, sem="ve")

    def emit_prep(p: Pass):
        for i in range(4):
            p.dma("gpsimd", aux_bf.ap()[i * 1024:(i + 1) * 1024, :],
                  aux_in[i * 1024:(i + 1) * 1024, :], sem="aux")
        for t in range(NT):
            p.dma("sync", x_sb[:, t * D:(t + 1) * D], x_in_t[t], sem="ld")
        p.dma("sync", id16[:, :], id16_in, sem="ld")
        p.dma("sync", b2v[:, :], b2v_in, sem="ld")
        p.dma("sync", gw1v[:, :], gw1v_in, sem="ld")
        p.dma("sync", gbv[:, :], gbv_in, sem="ld")
        p.dma("sync", b1c[:, :], b1p_in, sem="ld")
        p.dma("sync", stage32[:, 0:D], qproj_in, sem="ld")
        p.dma("sync", stage32[:, D:D + 48], routerT_in, sem="ld")
        p.dma("sync", stage32[:, D + 48:D + 48 + 2048]
              .rearrange("p (c d) -> p c d", c=8),
              w1T_in.rearrange("(c p) d -> p c d", p=128), sem="ld")
        p.dma("sync", stage32[:, 3120:3120 + NSLOT], auxkT_in, sem="ld")
        p.dma("sync", stage32[:, 7216:7218], w2p_in, sem="ld")
        p.dma("sync", stage32[:, 7218:7250], maskp_in, sem="ld")
        p.wait("scalar", "ld")
        p.wait("vector", "ld")
        p.v("memset", ones16[:, :], 1.0)
        p.v("memset", ones16r[:, :], 1.0)
        p.v("memset", rq_hi[0:50, :], 1.0, sem="ve")
        hilo_(p, qp16[:, 0:D], qp16[:, D:2 * D], stage32[:, 0:D])
        hilo_(p, rt16[:, 0:48], rt16[:, 48:96], stage32[:, D:D + 48])
        for ch in range(8):
            p.act(w1T16[:, ch * 256:(ch + 1) * 256],
                  stage32[:, D + 48 + ch * 256:D + 48 + (ch + 1) * 256],
                  AF.Copy, sem="sc")
        p.act(w2T16[:, 0:2], stage32[:, 7216:7218], AF.Copy, sem="sc")
        mh16 = scrF.bitcast(F16)[:, 0:32]
        ml16 = scrF.bitcast(F16)[:, 32:64]
        hilo_(p, mh16, ml16, stage32[:, 7218:7250])
        p.wait("sync", "ve")
        p.wait("sync", "sc")
        p.dma("sync", rkT_hi[48:49, :], mh16, sem="ld")
        p.dma("sync", rkT_hi[49:50, :], ml16, sem="ld")
        auxk_hi = scrG.bitcast(F16)[:, 0:NSLOT]
        auxk_lo = scrG.bitcast(F16)[:, NSLOT:2 * NSLOT]
        asrc = stage32[:, 3120:3120 + NSLOT]
        for q in range(4):
            sl = slice(q * 1024, (q + 1) * 1024)
            hilo_(p, auxk_hi[:, sl], auxk_lo[:, sl], asrc[:, sl])
        p.wait("tensor", "sc")
        p.wait("tensor", "ve")
        p.wait("tensor", "ld")
        p.wait("scalar", "ve")
        for n in range(8):
            ps = psA[0:48, (n % 4) * 512:(n % 4) * 512 + 512]
            rh = auxk_hi[:, n * 512:(n + 1) * 512]
            rl = auxk_lo[:, n * 512:(n + 1) * 512]
            p.mm(ps, rt16[:, 0:48], rh, start=True, stop=False)
            p.mm(ps, rt16[:, 0:48], rl, start=False, stop=False)
            p.mm(ps, rt16[:, 48:96], rh, start=False, stop=True, sem="te")
            p.wait("scalar", "te")
            p.act(stage32[0:48, n * 512:(n + 1) * 512], ps, AF.Copy,
                  scale=RSQ48, sem="sc")
            p.wait("tensor", "sc")
        for n in range(8):
            sl = slice(n * 512, (n + 1) * 512)
            hilo_(p, rkT_hi[0:48, sl], rkT_lo[0:48, sl], stage32[0:48, sl])
        for n in range(2):
            ps = psB[0:48, n * 512:(n + 1) * 512]
            qh = qp16[:, n * 512:(n + 1) * 512]
            ql = qp16[:, D + n * 512:D + (n + 1) * 512]
            p.mm(ps, rt16[:, 0:48], qh, start=True, stop=False)
            p.mm(ps, rt16[:, 0:48], ql, start=False, stop=False)
            p.mm(ps, rt16[:, 48:96], qh, start=False, stop=True, sem="te")
        p.wait("scalar", "te")
        p.wait("scalar", "ve")
        for n in range(2):
            p.act(stage32[0:48, 4096 + n * 512:4096 + (n + 1) * 512],
                  psB[0:48, n * 512:(n + 1) * 512], AF.Copy, sem="sc")
        wrq32 = stage32[0:48, 4096:4096 + 1024]
        wrq_h = scrH_hi[0:48, 0:1024]
        wrq_l = scrH_lo[0:48, 0:1024]
        hilo_(p, wrq_h, wrq_l, wrq32)
        p.wait("tensor", "sc")
        p.wait("tensor", "ve")
        psA16 = psA.bitcast(F16)
        for ch in range(8):
            p.tr(psA16[0:128, 0:48], wrq_h[:, ch * 128:(ch + 1) * 128],
                 id16[0:48, 0:48])
            p.tr(psA16[0:128, 1024:1072], wrq_l[:, ch * 128:(ch + 1) * 128],
                 id16[0:48, 0:48], sem="te")
            p.wait("scalar", "te")
            p.act(wrqT16[:, ch * 96:ch * 96 + 48], psA16[0:128, 0:48], AF.Copy,
                  sem="sc")
            p.act(wrqT16[:, ch * 96 + 48:ch * 96 + 96], psA16[0:128, 1024:1072],
                  AF.Copy, sem="sc")
            p.wait("tensor", "sc")
        p.wait("gpsimd", "aux")

    def emit_ffn(p: Pass, call):
        p.wait("scalar", "ve")
        p.wait("vector", "sc")
        for t in range(NT):
            xt = x_sb[:, t * D:(t + 1) * D]
            p.v("reduce_sum", out=sA[:, t:t + 1], in_=xt, axis=AX, sem="ve")
            p.wait("scalar", "sc")
            p.act(tmp32[:, :], xt, AF.Square, accum_out=sA[:, 16 + t:17 + t],
                  sem="sc")
        p.wait("vector", "sc")
        p.v("tensor_scalar", sA[:, 32:48], sA[:, 0:16], 1.0 / D, None, ALU.mult)
        p.v("tensor_scalar", sA[:, 48:64], sA[:, 16:32], 1.0 / D, None, ALU.mult)
        p.v("tensor_tensor", tmp32[:, 0:16], sA[:, 32:48], sA[:, 32:48], ALU.mult)
        p.v("tensor_sub", sA[:, 48:64], sA[:, 48:64], tmp32[:, 0:16])
        p.v("tensor_scalar", tmp32[:, 0:16], sA[:, 48:64], EPS_LN, None, ALU.add,
            sem="ve")
        p.wait("scalar", "ve")
        p.act(tmp32[:, 16:32], tmp32[:, 0:16], AF.Ln, sem="sc")
        p.wait("scalar", "sc")
        p.act(sM[:, 0:16], tmp32[:, 16:32], AF.Exp, scale=-0.5, sem="sc")
        p.wait("vector", "sc")
        p.v("tensor_tensor", tmp32[:, 48:64], sA[:, 32:48], sM[:, 0:16], ALU.mult)
        p.v("tensor_scalar", sM[:, 16:32], tmp32[:, 48:64], -1.0, None, ALU.mult)
        p.v("reciprocal", out=tmp32[:, 16:32], in_=tmp32[:, 0:16])
        p.v("tensor_tensor", tmp32[:, 32:48], sA[:, 48:64], tmp32[:, 16:32],
            ALU.mult, sem="ve")
        p.wait("scalar", "ve")
        p.act(sM[:, 32:48], tmp32[:, 32:48], AF.Ln, bias=1.0, sem="sc")
        p.wait("vector", "sc")
        p.v("tensor_copy", out=lv16[:, :], in_=sM[:, 32:48], sem="ve")
        p.wait("tensor", "ve")
        p.mm(psB[0:1, 0:16], ones16[:, 0:1], lv16[:, :], sem="te")
        p.wait("vector", "te")
        p.v("reduce_sum", out=one1[0:1, 1:2], in_=psB[0:1, 0:16], axis=AX,
            sem="ve")
        p.wait("sync", "ve")
        p.dma("sync", ar_in[call].ap(), one1[0:1, 1:2], sem="st")
        p.wait("gpsimd", "st")
        p.coll("AllReduce", ALU.add, [[0, 1, 2, 3, 4, 5, 6, 7]],
               [ar_in[call].ap().opt()], [ar_out[call].ap().opt()], sem="cc")
        p.wait("vector", "st")
        if call == 1:
            p.v("memset", rq_hi[0:50, :], 1.0, sem="ve")
        for g in range(NG):
            for ti in range(GT):
                t = g * GT + ti
                xt = x_sb[:, t * D:(t + 1) * D]
                p.wait("scalar", "ve")
                p.wait("scalar", "sc")
                p.act(tmp32[:, :], xt, AF.Copy, bias=sM[:, 16 + t:17 + t],
                      scale=sM[:, 0 + t:1 + t], sem="sc")
                p.wait("scalar", "sc")
                p.act(scrH_hi[:, ti * D:(ti + 1) * D], tmp32[:, :], AF.Copy,
                      sem="sc")
                p.wait("vector", "sc")
                p.v("tensor_tensor", scrH_lo[:, ti * D:(ti + 1) * D], tmp32[:, :],
                    scrH_hi[:, ti * D:(ti + 1) * D], ALU.subtract, sem="ve")
                p.v("scalar_tensor_tensor", xt, tmp32[:, :], 0.5, xt,
                    ALU.mult, ALU.add, sem="ve")
            p.wait("tensor", "ve")
            psA16_ = psA.bitcast(F16)
            for hl, (hsrc, dst) in enumerate(((scrH_hi, hfm_hi), (scrH_lo, hfm_lo))):
                for ch in range(8):
                    for ti in range(GT):
                        p.tr(psA16_[:, (2 * ti + hl) * 1024:(2 * ti + hl) * 1024 + 128],
                             hsrc[:, ti * D + ch * 128:ti * D + (ch + 1) * 128],
                             id16[:, :], sem="te")
                    p.wait("scalar", "te")
                    for ti in range(GT):
                        p.act(dst[:, ch * GTOK + ti * 128:ch * GTOK + (ti + 1) * 128],
                              psA16_[:, (2 * ti + hl) * 1024:(2 * ti + hl) * 1024 + 128],
                              AF.Copy, sem="sc")
                    p.wait("tensor", "sc")
            for m in range(2):
                ps = psB[:, m * 512:m * 512 + GTOK]
                for ch in range(8):
                    p.mm(ps, w1T16[:, ch * 256 + m * 128:ch * 256 + m * 128 + 128],
                         hfm_hi[:, ch * GTOK:(ch + 1) * GTOK],
                         start=(ch == 0), stop=(ch == 7), sem="te")
                p.wait("scalar", "te")
                p.act(h1pre[:, m * 2048 + g * GTOK:m * 2048 + (g + 1) * GTOK], ps,
                      AF.Gelu, bias=b1c[:, m:m + 1], sem="sc")
            p.wait("tensor", "sc")
            for ti in range(GT):
                t = g * GT + ti
                ps = psB[:, 1536 + t:1536 + t + 1]
                for m in range(2):
                    p.mm(ps, h1pre[:, m * 2048 + g * GTOK + ti * 128:
                                   m * 2048 + g * GTOK + (ti + 1) * 128],
                         w2T16[:, m:m + 1],
                         start=(t == 0 and m == 0), stop=(t == NT - 1 and m == 1),
                         sem="te")
            ps = psA[0:48, 0:GTOK]
            for ch in range(8):
                wh = wrqT16[:, ch * 96:ch * 96 + 48]
                wl = wrqT16[:, ch * 96 + 48:ch * 96 + 96]
                hh = hfm_hi[:, ch * GTOK:(ch + 1) * GTOK]
                hl_ = hfm_lo[:, ch * GTOK:(ch + 1) * GTOK]
                p.mm(ps, wh, hh, start=(ch == 0), stop=False)
                p.mm(ps, wh, hl_, start=False, stop=False)
                p.mm(ps, wl, hh, start=False, stop=(ch == 7), sem="te")
            p.wait("scalar", "te")
            p.act(rq_hi[0:48, g * GTOK:(g + 1) * GTOK], ps, AF.Copy, sem="sc")
            p.wait("vector", "sc")
            p.v("tensor_tensor", rq_lo[0:48, g * GTOK:(g + 1) * GTOK], ps,
                rq_hi[0:48, g * GTOK:(g + 1) * GTOK], ALU.subtract, sem="ve")
            p.wait("tensor", "ve")
        p.wait("scalar", "te")
        p.act(tmp32[:, 0:16], psB[:, 1536:1552], AF.Copy, bias=b2v[:, 0:1],
              sem="sc")
        p.wait("scalar", "sc")
        p.act(tmp32[:, 16:32], tmp32[:, 0:16], AF.Exp, scale=-1.0, sem="sc")
        p.wait("vector", "sc")
        p.v("tensor_scalar", tmp32[:, 16:32], tmp32[:, 16:32], 1.0, None, ALU.add)
        p.v("reciprocal", out=sM[:, 48:64], in_=tmp32[:, 16:32], sem="ve")
        p.wait("sync", "cc")
        p.dma("sync", one1[0:1, 2:3], ar_out[call].ap(), sem="ld")
        p.wait("sync", "ld")
        p.wait("vector", "ld")
        p.v("tensor_scalar", one1[0:1, 3:4], one1[0:1, 2:3], 1.0 / (B * S), 1e-6,
            ALU.mult, op1=ALU.add)
        p.v("reciprocal", out=one1[0:1, 4:5], in_=one1[0:1, 3:4])
        p.v("tensor_scalar", one1[0:1, 5:6], one1[0:1, 4:5], 0.5, None, ALU.mult)
        p.v("tensor_copy", out=one1.bitcast(F16)[0:1, 12:13], in_=one1[0:1, 5:6],
            sem="ve")
        p.wait("tensor", "ve")
        p.mm(psB[:, 1024:1025], ones16r[0:1, :], one1.bitcast(F16)[0:1, 12:13],
             sem="te")
        p.wait("scalar", "te")
        p.act(invb[:, :], psB[:, 1024:1025], AF.Copy, sem="sc")
        p.wait("vector", "sc")
        p.v("tensor_scalar", sM[:, 32:48], sM[:, 32:48], invb[:, 0:1], None,
            ALU.mult)
        p.v("scalar_tensor_tensor", sM[:, 48:64], sM[:, 48:64], 2.5,
            sM[:, 32:48], ALU.mult, ALU.add)
        p.v("tensor_scalar", sM[:, 48:64], sM[:, 48:64], 5.0, 0.0, ALU.min,
            op1=ALU.max, sem="ve")
        p.wait("scalar", "ve")
        p.act(tmp32[:, 16:32], sM[:, 48:64], AF.Copy, scale=gw1v[:, 0:1],
              bias=gbv[:, 0:1], sem="sc")
        p.wait("scalar", "sc")
        p.act(tmp32[:, 32:48], tmp32[:, 16:32], AF.Exp, scale=-1.0, sem="sc")
        p.wait("vector", "sc")
        p.v("tensor_scalar", tmp32[:, 32:48], tmp32[:, 32:48], 1.0, None, ALU.add)
        p.v("reciprocal", out=sM[:, 48:64], in_=tmp32[:, 32:48], sem="ve")
        a4_ve = p.c["ve"]
        idx_ve = {}
        agg_ve = {}
        w8b = [w8[:, 0:8], w8[:, 8:16]]

        def gslot(t, k):
            if t % 2 == 0:
                return scrG[:, k * D:(k + 1) * D]
            half = [scrH_hi, scrH_lo, hfm_hi, hfm_lo][k // 2].bitcast(BF16)
            return half[:, (k % 2) * D:(k % 2) * D + D]

        def emit_scores_topk(t):
            g, ti = t // GT, t % GT
            half = psA if t % 2 == 0 else psB
            rq_s = slice(g * GTOK + ti * 128, g * GTOK + (ti + 1) * 128)
            for n in range(8):
                ps = half[:, (n % 4) * 512:(n % 4) * 512 + 512]
                p.mm(ps, rq_hi[0:50, rq_s], rkT_hi[0:50, n * 512:(n + 1) * 512],
                     start=True, stop=False)
                p.mm(ps, rq_hi[0:48, rq_s], rkT_lo[0:48, n * 512:(n + 1) * 512],
                     start=False, stop=False)
                p.mm(ps, rq_lo[0:48, rq_s], rkT_hi[0:48, n * 512:(n + 1) * 512],
                     start=False, stop=True, sem="te")
                if n % 4 == 3:
                    p.wait("scalar", "te")
                    p.wait("scalar", "ve", val=max(idx_ve.get(t - 1, 0), a4_ve))
                    p.act(scrF[:, (n // 4) * 2048:(n // 4) * 2048 + 2048],
                          half[:, 0:2048], AF.Copy, sem="sc")
                    p.wait("tensor", "sc")
            p.wait("vector", "sc")
            p.wait("vector", "gp", val=gp_done.get(t - 1, 0))
            p.v("max", out=top8[:, :], in_=scrF[:, :])
            p.v("tensor_scalar", w8b[t % 2], top8[:, :], top8[:, 0:1], None,
                ALU.subtract)
            p.v("max_index", out=idx8[:, :], in_max=top8[:, :],
                in_values=scrF[:, :], sem="ve")
            idx_ve[t] = p.c["ve"]
            p.wait("gpsimd", "ve", val=idx_ve[t])
            if t >= 2:
                p.wait("gpsimd", "ve", val=agg_ve.get(t - 2, 0))
            for k in range(8):
                p.gather(gslot(t, k), aux_bf.ap(), idx8[:, k:k + 1], sem="gp")
            p.wait("scalar", "ve")
            p.act(w8b[t % 2], w8b[t % 2], AF.Exp, sem="sc")
            p.wait("vector", "sc")
            p.v("reduce_sum", out=zsb[:, 8 + (t % 2):9 + (t % 2)], in_=w8b[t % 2],
                axis=AX)
            p.v("reciprocal", out=zsb[:, 10 + (t % 2):11 + (t % 2)],
                in_=zsb[:, 8 + (t % 2):9 + (t % 2)])
            p.v("tensor_scalar", zsb[:, 12 + (t % 2):13 + (t % 2)],
                zsb[:, 10 + (t % 2):11 + (t % 2)],
                sM[:, 48 + t:49 + t], 0.5, ALU.mult, op1=ALU.mult)
            p.v("tensor_scalar", w8b[t % 2], w8b[t % 2],
                zsb[:, 12 + (t % 2):13 + (t % 2)], None, ALU.mult, sem="ve")

        def emit_agg(t):
            p.wait("vector", "gp", val=gp_done[t])
            xt = x_sb[:, t * D:(t + 1) * D]
            for k in range(8):
                p.v("scalar_tensor_tensor", xt, gslot(t, k),
                    w8b[t % 2][:, k:k + 1], xt, ALU.mult, ALU.add, sem="ve")
            agg_ve[t] = p.c["ve"]

        gp_done = {}
        for t in range(NT):
            emit_scores_topk(t)
            gp_done[t] = p.c["gp"]
            if t >= 1:
                emit_agg(t - 1)
        emit_agg(NT - 1)

    def emit_attn(p: Pass):
        p.dma("sync", stage32.rearrange("p (c d) -> p c d", c=8),
              wvT_in.rearrange("(c p) d -> p c d", p=128), sem="ld")
        p.wait("scalar", "ld")
        p.wait("scalar", "te")
        p.wait("scalar", "ve")
        p.wait("vector", "sc")
        for ch in range(8):
            p.act(wv16[:, ch * D:(ch + 1) * D], stage32[:, ch * D:(ch + 1) * D],
                  AF.Copy)
        p.wait("sync", "sc")
        p.dma("sync", stage32[:, 0:8 * DH].rearrange("p (c d) -> p c d", c=8),
              wqT_in.rearrange("(c p) d -> p c d", p=128), sem="ld")
        p.dma("sync", stage32[:, 8 * DH:16 * DH].rearrange("p (c d) -> p c d", c=8),
              wkT_in.rearrange("(c p) d -> p c d", p=128), sem="ld")
        p.dma("sync", stage32[:, 4096:4098], bqp_in, sem="ld")
        p.dma("sync", stage32[:, 4098:4100], bkp_in, sem="ld")
        p.dma("sync", stage32[:, 4100:4102], k0p_in, sem="ld")
        p.dma("sync", stage32[:, 4102:4110], v0p_in, sem="ld")
        p.dma("sync", stage32[:, 4110:4118], bvp_in, sem="ld")
        p.wait("scalar", "ld")
        p.wait("vector", "ld")
        for ch in range(8):
            hilo_(p, wq16[:, ch * 258:ch * 258 + DH],
                  wq16[:, ch * 258 + DH:ch * 258 + 258],
                  stage32[:, ch * DH:(ch + 1) * DH])
            hilo_(p, wk16[:, ch * 258:ch * 258 + DH],
                  wk16[:, ch * 258 + DH:ch * 258 + 258],
                  stage32[:, 8 * DH + ch * DH:8 * DH + (ch + 1) * DH])
        bq_c = scrG16[:, 7680:7684]
        bk_c = scrG16[:, 7684:7688]
        v0_c = scrG16[:, 7688:7704]
        bv_c = scrG16[:, 7704:7720]
        hilo_(p, bq_c[:, 0:2], bq_c[:, 2:4], stage32[:, 4096:4098])
        hilo_(p, bk_c[:, 0:2], bk_c[:, 2:4], stage32[:, 4098:4100])
        hilo_(p, k0hl[:, 0:2], k0hl[:, 2:4], stage32[:, 4100:4102])
        hilo_(p, v0_c[:, 0:8], v0_c[:, 8:16], stage32[:, 4102:4110])
        hilo_(p, bv_c[:, 0:8], bv_c[:, 8:16], stage32[:, 4110:4118])
        p.wait("sync", "ve")
        p.wait("sync", "sc")
        for r, cc in ((0, 0), (1, 2)):
            p.dma("sync", bq16[r:r + 1, 0:128], bq_c[:, cc:cc + 1], sem="ld")
            p.dma("sync", bq16[r:r + 1, 128:129], bq_c[0:1, cc + 1:cc + 2], sem="ld")
            p.dma("sync", bk16[r:r + 1, 0:128], bk_c[:, cc:cc + 1], sem="ld")
            p.dma("sync", bk16[r:r + 1, 128:129], bk_c[0:1, cc + 1:cc + 2], sem="ld")
        p.dma("sync", v0r16[0:1, :], v0_c[:, 0:8], sem="ld")
        p.dma("sync", bv16[0:1, :], bv_c[:, 0:8], sem="ld")
        p.dma("sync", bv16[1:2, :], bv_c[:, 8:16], sem="ld")
        p.dma("sync", k0aug[0:1, :], k0hl[0:1, 1:2], sem="ld")
        p.dma("sync", k0aug[1:2, :], k0hl[0:1, 3:4], sem="ld")
        p.dma("sync", k0aug[2:3, :], k0hl[0:1, 1:2], sem="ld")
        p.wait("sync", "ld")
        p.wait("tensor", "ld")
        p.v("memset", ones64, 1.0)
        p.wait("scalar", "ve")
        for t in range(NT):
            xt = x_sb[:, t * D:(t + 1) * D]
            p.v("reduce_sum", out=sA[:, t:t + 1], in_=xt, axis=AX)
            p.act(tmp32[:, :], xt, AF.Square, accum_out=sA[:, 16 + t:17 + t])
        p.wait("vector", "sc")
        p.v("tensor_scalar", sA[:, 32:48], sA[:, 0:16], 1.0 / D, None, ALU.mult)
        p.v("tensor_scalar", sA[:, 48:64], sA[:, 16:32], 1.0 / D, None, ALU.mult)
        p.v("tensor_tensor", tmp32[:, 0:16], sA[:, 32:48], sA[:, 32:48], ALU.mult)
        p.v("tensor_sub", sA[:, 48:64], sA[:, 48:64], tmp32[:, 0:16])
        p.v("tensor_scalar", tmp32[:, 0:16], sA[:, 48:64], EPS_LN, None, ALU.add,
            sem="ve")
        p.wait("scalar", "ve")
        p.act(tmp32[:, 16:32], tmp32[:, 0:16], AF.Ln)
        p.act(sM[:, 0:16], tmp32[:, 16:32], AF.Exp, scale=-0.5, sem="sc")
        p.wait("vector", "sc")
        p.v("tensor_tensor", tmp32[:, 48:64], sA[:, 32:48], sM[:, 0:16], ALU.mult)
        p.v("tensor_scalar", sM[:, 16:32], tmp32[:, 48:64], -1.0, None, ALU.mult,
            sem="ve")
        for g in range(NG):
            for ti in range(GT):
                t = g * GT + ti
                xt = x_sb[:, t * D:(t + 1) * D]
                p.wait("scalar", "ve")
                p.act(tmp32[:, :], xt, AF.Copy, bias=sM[:, 16 + t:17 + t],
                      scale=sM[:, 0 + t:1 + t])
                p.wait("scalar", "sc")
                p.act(scrH_hi[:, ti * D:(ti + 1) * D], tmp32[:, :], AF.Copy,
                      sem="sc")
                p.wait("vector", "sc")
                p.v("tensor_tensor", scrH_lo[:, ti * D:(ti + 1) * D], tmp32[:, :],
                    scrH_hi[:, ti * D:(ti + 1) * D], ALU.subtract)
                p.v("tensor_add", xt, xt, tmp32[:, :], sem="ve")
            p.wait("tensor", "ve")
            psA16_ = psA.bitcast(F16)
            for hl, (hsrc, dst) in enumerate(((scrH_hi, hfm_hi), (scrH_lo, hfm_lo))):
                for ch in range(8):
                    for ti in range(GT):
                        p.tr(psA16_[:, (2 * ti + hl) * 1024:(2 * ti + hl) * 1024 + 128],
                             hsrc[:, ti * D + ch * 128:ti * D + (ch + 1) * 128],
                             id16[:, :], sem="te")
                    p.wait("scalar", "te")
                    for ti in range(GT):
                        p.act(dst[:, ch * GTOK + ti * 128:ch * GTOK + (ti + 1) * 128],
                              psA16_[:, (2 * ti + hl) * 1024:(2 * ti + hl) * 1024 + 128],
                              AF.Copy, sem="sc")
                    p.wait("tensor", "sc")
            for ti in range(GT):
                t = g * GT + ti
                tsl = slice(ti * 128, (ti + 1) * 128)
                psq = psB[:, 0:DH]
                for ch in range(8):
                    hh = hfm_hi[:, ch * GTOK + ti * 128:ch * GTOK + (ti + 1) * 128]
                    hl_ = hfm_lo[:, ch * GTOK + ti * 128:ch * GTOK + (ti + 1) * 128]
                    wh = wq16[:, ch * 258:ch * 258 + DH]
                    wl = wq16[:, ch * 258 + DH:ch * 258 + 258]
                    p.mm(psq, hh, wh, start=(ch == 0), stop=False)
                    p.mm(psq, hl_, wh, start=False, stop=False)
                    p.mm(psq, hh, wl, start=False, stop=False)
                p.mm(psq, ones16r[0:2, 0:128], bq16[0:2, :], start=False,
                     stop=True, sem="te")
                p.wait("scalar", "te")
                p.act(qtm16[:, 0:DH], psq, AF.Copy, sem="sc")
                p.wait("vector", "sc")
                p.v("tensor_tensor", qtm16[:, DH:2 * DH], psq, qtm16[:, 0:DH],
                    ALU.subtract, sem="ve")
                psk_ = psB[:, 512:512 + DH]
                for ch in range(8):
                    hh = hfm_hi[:, ch * GTOK + ti * 128:ch * GTOK + (ti + 1) * 128]
                    hl_ = hfm_lo[:, ch * GTOK + ti * 128:ch * GTOK + (ti + 1) * 128]
                    wh = wk16[:, ch * 258:ch * 258 + DH]
                    wl = wk16[:, ch * 258 + DH:ch * 258 + 258]
                    p.mm(psk_, hh, wh, start=(ch == 0), stop=False)
                    p.mm(psk_, hl_, wh, start=False, stop=False)
                    p.mm(psk_, hh, wl, start=False, stop=False)
                p.mm(psk_, ones16r[0:2, 0:128], bk16[0:2, :], start=False,
                     stop=True, sem="te")
                p.wait("scalar", "te")
                p.wait("scalar", "kv")
                p.act(ktm16[:, 0:DH], psk_, AF.Copy)
                p.wait("scalar", "sc")
                p.wait("vector", "sc")
                p.v("tensor_tensor", ktm16[:, DH:2 * DH], psk_, ktm16[:, 0:DH],
                    ALU.subtract, sem="ve")
                p.wait("sync", "ve")
                p.dma("sync", kvk_in.ap()[t * 128:(t + 1) * 128, :], ktm16[:, 0:2 * DH],
                      sem="kv")
                for half in range(2):
                    psv = psA[:, half * 512:(half + 1) * 512]
                    for ch in range(8):
                        hh = hfm_hi[:, ch * GTOK + ti * 128:ch * GTOK + (ti + 1) * 128]
                        wv_ = wv16[:, ch * D + half * 512:ch * D + (half + 1) * 512]
                        p.mm(psv, hh, wv_, start=(ch == 0), stop=False)
                    p.mm(psv, ones64, bv16[0:2, half * 512:(half + 1) * 512],
                         start=False, stop=True, sem="te")
                p.wait("scalar", "te")
                p.wait("scalar", "kv")
                p.act(vbuf[:, 0:512], psA[:, 0:512], AF.Copy)
                p.wait("scalar", "sc")
                p.act(vbuf[:, 512:1024], psA[:, 512:1024], AF.Copy, sem="sc")
                p.wait("sync", "sc")
                p.dma("sync", kvv_in.ap()[t * 128:(t + 1) * 128, :],
                      vbuf[:, 0:1024], sem="kv")
                p.wait("tensor", "ve")
                p.tr(psA16_[:, 3072:3072 + 128], qtm16[:, 0:128], id16[:, :])
                p.tr(psA16_[:, 3072 + 128:3072 + 256], qtm16[:, DH:DH + 128],
                     id16[:, :])
                p.tr(psA16_[0:1, 3072 + 256:3072 + 384], qtm16[:, 128:129],
                     id16[:, :])
                p.tr(psA16_[0:1, 3072 + 384:3072 + 512], qtm16[:, 2 * DH - 1:2 * DH],
                     id16[:, :], sem="te")
                p.wait("scalar", "te")
                p.act(qfm_hi[:, t * 128:(t + 1) * 128], psA16_[:, 3072:3200],
                      AF.Copy)
                p.wait("scalar", "sc")
                p.act(qfm_lo[:, t * 128:(t + 1) * 128], psA16_[:, 3200:3328],
                      AF.Copy)
                p.wait("scalar", "sc")
                p.act(stg32r[0:1, t * 128:(t + 1) * 128],
                      psA16_[0:1, 3328:3456], AF.Copy)
                p.wait("scalar", "sc")
                p.act(stg64r[0:1, t * 128:(t + 1) * 128],
                      psA16_[0:1, 3456:3584], AF.Copy, sem="sc")
                p.wait("tensor", "sc")
        p.wait("sync", "sc")
        p.dma("sync", qaug[0:1, :], stg32r[0:1, 0:TC], sem="ld")
        p.dma("sync", qaug[1:2, :], stg32r[0:1, 0:TC], sem="ld")
        p.dma("sync", qaug[2:3, :], stg64r[0:1, 0:TC], sem="ld")
        p.wait("gpsimd", "kv")
        p.coll("AllGather", ALU.bypass, [[0, 1, 2, 3], [4, 5, 6, 7]],
               [kvk_in.ap().opt()], [kvk_out.ap().opt()], sem="cc")
        p.coll("AllGather", ALU.bypass, [[0, 1, 2, 3], [4, 5, 6, 7]],
               [kvv_in.ap().opt()], [kvv_out.ap().opt()], sem="cc")
        p.wait("sync", "cc", val=p.c["cc"] - 1)
        p.wait("scalar", "ld")
        psA16_ = psA.bitcast(F16)
        tr_te = {}
        te0 = p.c["te"]

        def kdma(kt):
            p.wait("sync", "te", val=tr_te.get(kt - 2, te0))
            p.dma("sync",
                  ktm16[:, (kt % 2) * 2 * DH:(kt % 2) * 2 * DH + 2 * DH],
                  kvk_out.ap()[kt * 128:(kt + 1) * 128, :], sem="vs")

        kdma(0)
        kdma(1)
        for kt in range(NKT):
            sl0 = (kt % 2) * 2 * DH
            ktm = ktm16[:, sl0:sl0 + 2 * DH]
            par = (kt % 2) * 512
            p.wait("tensor", "vs")
            p.wait("tensor", "sc", val=tr_sc if kt >= 2 else 0)
            p.tr(psA16_[:, par * 2:par * 2 + 128], ktm[:, 0:128], id16[:, :])
            p.tr(psA16_[:, par * 2 + 128:par * 2 + 256], ktm[:, DH:DH + 128],
                 id16[:, :])
            p.tr(psA16_[0:1, par * 2 + 256:par * 2 + 384], ktm[:, 128:129],
                 id16[:, :])
            p.tr(psA16_[0:1, par * 2 + 384:par * 2 + 512],
                 ktm[:, 2 * DH - 1:2 * DH], id16[:, :], sem="te")
            tr_te[kt] = p.c["te"]
            p.wait("scalar", "te")
            p.act(kfm_hi[:, kt * 128:(kt + 1) * 128],
                  psA16_[:, par * 2:par * 2 + 128], AF.Copy)
            p.act(kfm_lo[:, kt * 128:(kt + 1) * 128],
                  psA16_[:, par * 2 + 128:par * 2 + 256], AF.Copy)
            p.act(scrG16[0:1, kt * 128:(kt + 1) * 128],
                  psA16_[0:1, par * 2 + 256:par * 2 + 384], AF.Copy)
            p.act(stg32r[0:1, kt * 128:(kt + 1) * 128],
                  psA16_[0:1, par * 2 + 384:par * 2 + 512], AF.Copy, sem="sc")
            tr_sc = p.c["sc"]
            if kt + 2 < NKT:
                kdma(kt + 2)
        p.wait("vector", "ld")
        p.wait("vector", "sc")
        p.v("memset", stg64r[0:1, 0:2048], 1.0, sem="ve")
        p.wait("sync", "sc")
        p.wait("sync", "ve")
        p.dma("sync", scrG16[1:2, 0:8192], stg32r[0:1, 0:8192], sem="ld")
        p.dma("sync", scrG16[2:3, 0:8192], scrG16[0:1, 0:8192], sem="ld")
        for i4 in range(4):
            p.dma("sync", scrG16[3:4, i4 * 2048:(i4 + 1) * 2048],
                  stg64r[0:1, 0:2048], sem="ld")
        p.wait("tensor", "ld")
        for t in range(NT):
            tsl = slice(t * 128, (t + 1) * 128)
            ps = psB[:, 1536 + t:1537 + t]
            p.mm(ps, qfm_hi[:, tsl], k0hl[:, 0:1], start=(t == 0), stop=False)
            p.mm(ps, qfm_hi[:, tsl], k0hl[:, 2:3], start=False, stop=False)
            p.mm(ps, qfm_lo[:, tsl], k0hl[:, 0:1], start=False, stop=False)
            p.mm(ps, qaug[0:3, tsl], k0aug[0:3, :], start=False,
                 stop=(t == NT - 1), sem="te")
        p.wait("scalar", "te")
        p.act(ssink[:, :], psB[:, 1536:1552], AF.Copy, sem="sc")
        for t in range(NT):
            tsl = slice(t * 128, (t + 1) * 128)
            for n in range(16):
                ps = psA[:, (n % 2) * 512:(n % 2) * 512 + 512]
                if n >= 2 or t > 0:
                    p.wait("tensor", "sc")
                p.mm(ps, qfm_hi[:, tsl], kfm_hi[:, n * 512:(n + 1) * 512],
                     start=True, stop=False)
                p.mm(ps, qaug[0:1, tsl], scrG16[0:1, n * 512:(n + 1) * 512],
                     start=False, stop=True, sem="te")
                p.wait("scalar", "te")
                if n == 0:
                    p.wait("scalar", "ve")
                p.act(fwd16[:, n * 512:(n + 1) * 512], ps, AF.Copy, sem="sc")
            p.wait("vector", "sc")
            p.v("reduce_max", out=mrow[:, t:t + 1], in_=fwd16[:, 0:8192], axis=AX,
                sem="ve")
        p.wait("vector", "sc")
        p.v("tensor_max", mrow[:, :], mrow[:, :], ssink[:, :])
        p.v("tensor_scalar", mrow[:, :], mrow[:, :], 0.5, None, ALU.add)
        p.v("tensor_sub", psk[:, :], ssink[:, :], mrow[:, :], sem="ve")
        p.wait("scalar", "ve")
        p.act(psk[:, :], psk[:, :], AF.Exp, sem="sc")
        p.wait("vector", "sc")
        p.v("tensor_copy", out=psk16[:, :], in_=psk[:, :])
        p.v("tensor_scalar", lv16[:, :], mrow[:, :], -1.0, None, ALU.mult,
            sem="ve")
        p.wait("tensor", "ve")
        for t in range(NT):
            p.tr(psA16_[0:1, 0:128], psk16[:, t:t + 1], id16[:, :])
            p.tr(psA16_[0:1, 1024:1152], lv16[:, t:t + 1], id16[:, :], sem="te")
            p.wait("scalar", "te")
            p.act(pskT[0:1, t * 128:(t + 1) * 128], psA16_[0:1, 0:128], AF.Copy)
            p.wait("scalar", "sc")
            p.act(stg32r[0:1, t * 128:(t + 1) * 128],
                  psA16_[0:1, 1024:1152], AF.Copy, sem="sc")
            p.wait("tensor", "sc")
        p.wait("sync", "sc")
        p.dma("sync", qaug[3:4, 0:TC], stg32r[0:1, 0:TC], sem="ld")
        p.wait("sync", "ld")
        p.wait("tensor", "ld")
        p.wait("sync", "cc")
        vs_base = p.c["vs"]
        NVT = NQB * NKT
        v_done = {}

        def issue_v(g):
            kt = g % NKT
            p.wait("sync", "te")
            p.dma("sync", vbuf[:, (g % 2) * D:(g % 2) * D + D],
                  kvv_out.ap()[kt * 128:(kt + 1) * 128, :], sem="vs")
            v_done[g] = p.c["vs"]

        exp_sc = {}
        issue_v(0)
        issue_v(1)

        def emit_scores(qb, kt):
            qsl_ = slice(qb * QB, (qb + 1) * QB)
            par = kt % 2
            sbk = psB[:, par * 512:par * 512 + QB]
            if kt >= 2:
                p.wait("tensor", "sc", val=exp_sc[kt - 2])
            p.mm(sbk, kfm_hi[:, kt * 128:(kt + 1) * 128], qfm_hi[:, qsl_],
                 start=True, stop=False)
            p.mm(sbk, kfm_hi[:, kt * 128:(kt + 1) * 128], qfm_lo[:, qsl_],
                 start=False, stop=False)
            p.mm(sbk, kfm_lo[:, kt * 128:(kt + 1) * 128], qfm_hi[:, qsl_],
                 start=False, stop=False)
            p.mm(sbk, scrG16[0:4, kt * 128:(kt + 1) * 128], qaug[0:4, qsl_],
                 start=False, stop=True, sem="te")
            p.wait("scalar", "te")
            p.act(pt16[:, par * QB:(par + 1) * QB], sbk, AF.Exp, sem="sc")
            exp_sc[kt] = p.c["sc"]

        def emit_pv(qb, kt):
            g = qb * NKT + kt
            par = kt % 2
            p.wait("tensor", "sc", val=exp_sc[kt])
            p.wait("tensor", "vs")
            for qt in range(2):
                ptsl = pt16[:, par * QB + qt * 128:par * QB + (qt + 1) * 128]
                vsl = vbuf[:, (g % 2) * D:(g % 2) * D + D]
                p.mm(psA[:, qt * 1024:qt * 1024 + 512], ptsl, vsl[:, 0:512],
                     start=(kt == 0), stop=False)
                p.mm(psA[:, qt * 1024 + 512:qt * 1024 + 1024], ptsl,
                     vsl[:, 512:1024], start=(kt == 0), stop=False)
                p.mm(psB[:, 1024 + qt:1025 + qt], ptsl, ones16[:, 0:1],
                     start=(kt == 0 and qt == 0),
                     stop=(kt == NKT - 1 and qt == 1), sem="te")
            if g + 2 < NVT:
                issue_v(g + 2)

        for qb in range(NQB):
            exp_sc.clear()
            emit_scores(qb, 0)
            for kt in range(1, NKT):
                emit_scores(qb, kt)
                emit_pv(qb, kt - 1)
            emit_pv(qb, NKT - 1)
            for qt in range(2):
                t = qb * 2 + qt
                tsl2 = slice(t * 128, (t + 1) * 128)
                p.mm(psA[:, qt * 1024:qt * 1024 + 512], pskT[0:1, tsl2],
                     v0r16[0:1, 0:512], start=False, stop=True)
                p.mm(psA[:, qt * 1024 + 512:qt * 1024 + 1024],
                     pskT[0:1, tsl2], v0r16[0:1, 512:1024],
                     start=False, stop=True, sem="te")
            p.wait("vector", "te")
            p.v("tensor_copy", out=zsb[:, 0:2], in_=psB[:, 1024:1026])
            p.v("tensor_add", zsb[:, 2:4], zsb[:, 0:2], psk[:, 2 * qb:2 * qb + 2])
            p.v("reciprocal", out=zsb[:, 4:6], in_=zsb[:, 2:4])
            for qt in range(2):
                t = qb * 2 + qt
                xt = x_sb[:, t * D:(t + 1) * D]
                for half in range(2):
                    p.v("scalar_tensor_tensor", xt[:, half * 512:(half + 1) * 512],
                        psA[:, qt * 1024 + half * 512:qt * 1024 + (half + 1) * 512],
                        zsb[:, 4 + qt:5 + qt], xt[:, half * 512:(half + 1) * 512],
                        ALU.mult, ALU.add)
            p.wait("tensor", "ve")

    def emit_out(p: Pass):
        p.wait("sync", "ve")
        for t in range(NT):
            p.dma("sync", x_out_t[t], x_sb[:, t * D:(t + 1) * D], sem="st")
        p.wait("sync", "st")

    psA = nc.alloc_psum_tensor("psA", [128, 2048], F32).ap()
    psB = nc.alloc_psum_tensor("psB", [128, 2048], F32).ap()

    counts = []
    with nc.Block() as block:
        for eng_name in ENGS:
            c = {s: 0 for s in sem_names}
            counts.append(c)

            def make(f_name, f_c):
                def _(eng):
                    program(Pass(f_name, eng, sems, f_c))
                return _
            getattr(block, eng_name)(make(eng_name, c))
    for c in counts[1:]:
        assert c == counts[0], f"counter mismatch {c} vs {counts[0]}"
    return nc


def host_inputs(I, core):
    b, j = core // 4, core % 4
    t0 = j * TC
    f32 = np.float32

    def T(a):
        return np.ascontiguousarray(np.asarray(a, f32).T)

    def pad129(a):
        out = np.zeros((128, 2), f32)
        a = np.asarray(a, f32).reshape(-1)
        out[:, 0] = a[0:128]
        out[0, 1] = a[128]
        return out

    return {
        "x_in": np.ascontiguousarray(np.asarray(I["L0_attn_x"], f32)[b, t0:t0 + TC]),
        "wqT": T(I["L0_attn_w_q"]),
        "wkT": T(I["L0_attn_w_k"]),
        "wvT": T(I["L0_attn_w_v"]),
        "bqp": pad129(I["L0_attn_b_q"]),
        "bkp": pad129(I["L0_attn_b_k"]),
        "bvp": np.asarray(I["L0_attn_b_v"], f32).reshape(128, 8),
        "k0p": pad129(I["L0_attn_k_0"]),
        "v0p": np.asarray(I["L0_attn_v_0"], f32).reshape(128, 8),
        "qproj": np.asarray(I["L0_ffn_q_proj_w"], f32),
        "routerT": T(I["L0_ffn_router_w"]),
        "auxkT": T(I["L0_ffn_aux_keys"]),
        "maskp": np.asarray(I["L0_ffn_reliability_mask"], f32).reshape(128, 32),
        "w1T": T(I["L0_ffn_unc_w1"]),
        "b1p": np.ascontiguousarray(
            np.asarray(I["L0_ffn_unc_b1"], f32).reshape(2, 128).T),
        "w2p": np.ascontiguousarray(
            np.asarray(I["L0_ffn_unc_w2"], f32).reshape(2, 128).T),
        "b2v_in": np.broadcast_to(
            np.asarray(I["L0_ffn_unc_b2"], f32).reshape(1, 1), (128, 1)).copy(),
        "gw1v_in": np.full((128, 1), np.float32(I["L0_ffn_gate_w1"]), f32),
        "gbv_in": np.full((128, 1), np.float32(I["L0_ffn_gate_bias"]), f32),
        "aux_vals": np.asarray(I["L0_ffn_aux_values"], f32),
        "id16": np.eye(128, dtype=np.float16),
    }


def kernel(**inputs):
    nc = build_nc("full")
    in_maps = [host_inputs(inputs, c) for c in range(8)]
    res = run_bass_kernel_spmd(nc, in_maps, core_ids=list(range(8)))
    out = np.empty((B, S, D), np.float32)
    for c in range(8):
        b, j = c // 4, c % 4
        out[b, j * TC:(j + 1) * TC] = res.results[c]["x_out"]
    return out


# version 03
# speedup vs baseline: 1.2070x; 1.1080x over previous
import numpy as np
import concourse.bass as bass
import concourse.mybir as mybir
from concourse.bass import IndirectOffsetOnAxis
from concourse.bass_utils import run_bass_kernel_spmd

F32 = mybir.dt.float32
F16 = mybir.dt.float16
BF16 = mybir.dt.bfloat16
U32 = mybir.dt.uint32
AF = mybir.ActivationFunctionType
ALU = mybir.AluOpType
AX = mybir.AxisListType.X

B, S, D = 2, 8192, 1024
TC = 2048
NT = 16
GT = 2
GTOK = GT * 128
NG = NT // GT
NKT = 64
NSLOT = 4096
DH = 129
QB = 256
NQB = TC // QB
EPS_LN = 1e-5
RSQ48 = float(1.0 / np.sqrt(48.0))

ENGS = ("sync", "tensor", "scalar", "vector", "gpsimd")


class Pass:
    def __init__(self, eng_name, eng, sems, c):
        self.e = eng_name
        self.eng = eng
        self.sems = sems
        self.c = c

    def bump(self, inst, sem, n):
        if sem is not None:
            self.c[sem] += n
            if inst is not None:
                inst.then_inc(self.sems[sem], n)
        return inst

    def wait(self, eng_name, sem, val=None):
        if self.e == eng_name:
            self.eng.wait_ge(self.sems[sem], self.c[sem] if val is None else val)

    def mm(self, out, lhsT, rhs, start=True, stop=True, sem=None, n=1):
        i = self.eng.matmul(out, lhsT, rhs, start=start, stop=stop) \
            if self.e == "tensor" else None
        return self.bump(i, sem, n)

    def tr(self, out, in_, ident, sem=None, n=1):
        i = self.eng.transpose(out, in_, ident) if self.e == "tensor" else None
        return self.bump(i, sem, n)

    def act(self, out, in_, func, bias=0.0, scale=1.0, accum_out=None,
            sem=None, n=1):
        i = None
        self.wait("scalar", "sc")
        if self.e == "scalar":
            if func == AF.Copy and not (isinstance(bias, float)
                                        and isinstance(scale, float)):
                func = AF.Identity
            kw = {"accum_out": accum_out} if accum_out is not None else {}
            i = self.eng.activation(out, in_, func, bias=bias, scale=scale, **kw)
        return self.bump(i, "sc", 1)

    def v(self, meth, *args, sem=None, n=1, **kw):
        self.wait("vector", "ve")
        i = getattr(self.eng, meth)(*args, **kw) if self.e == "vector" else None
        return self.bump(i, "ve", 1)

    def dma(self, queue, out, in_, sem=None, transpose=False, slow=False):
        i = None
        if self.e == queue:
            if transpose:
                i = self.eng.dma_start_transpose(out=out, in_=in_)
            elif slow:
                i = self.eng.dma_start(out=out, in_=in_,
                                       allow_slow_non_contiguous=True)
            else:
                i = self.eng.dma_start(out=out, in_=in_)
        return self.bump(i, sem, 16 if sem else 0)

    def gather(self, out, table_ap, idx_ap, sem=None):
        i = None
        if self.e == "gpsimd":
            i = self.eng.indirect_dma_start(
                out=out, out_offset=None, in_=table_ap,
                in_offset=IndirectOffsetOnAxis(idx_ap, 0))
        return self.bump(i, sem, 16 if sem else 0)

    def coll(self, kind, op, groups, ins, outs, sem=None):
        i = None
        if self.e == "gpsimd":
            i = self.eng.collective_compute(kind, op, replica_groups=groups,
                                            ins=ins, outs=outs)
        return self.bump(i, sem, 1 if sem else 0)


def build_nc(stage="full", core0_groups=None):
    nc = bass.Bass("TRN2", target_bir_lowering=False)

    def din(name, shape, dt=F32):
        return nc.dram_tensor(name, list(shape), dt, kind="ExternalInput").ap()

    x_in = din("x_in", (TC, D))
    wqT_in = din("wqT", (D, DH))
    wkT_in = din("wkT", (D, DH))
    wvT_in = din("wvT", (D, D))
    bqp_in = din("bqp", (128, 2))
    bkp_in = din("bkp", (128, 2))
    bvp_in = din("bvp", (128, 8))
    k0p_in = din("k0p", (128, 2))
    v0p_in = din("v0p", (128, 8))
    qproj_in = din("qproj", (128, D))
    routerT_in = din("routerT", (128, 48))
    auxkT_in = din("auxkT", (128, NSLOT))
    maskp_in = din("maskp", (128, 32))
    w1T_in = din("w1T", (D, 256))
    b1p_in = din("b1p", (128, 2))
    w2p_in = din("w2p", (128, 2))
    b2v_in = din("b2v_in", (128, 1))
    gw1v_in = din("gw1v_in", (128, 1))
    gbv_in = din("gbv_in", (128, 1))
    aux_in = din("aux_vals", (NSLOT, D))
    id16_in = din("id16", (128, 128), F16)

    x_out = nc.dram_tensor("x_out", [TC, D], F32, kind="ExternalOutput").ap()

    aux_bf = nc.dram_tensor("aux_bf", [NSLOT, D], BF16)
    kvv_in = nc.dram_tensor("kvv_in", [TC, D], F16)
    kvv_out = nc.dram_tensor("kvv_out", [S, D], F16)
    kvk_in = nc.dram_tensor("kvk_in", [2 * DH, TC], F16)
    kvk_out = nc.dram_tensor("kvk_out", [8 * DH, S // 4], F16)
    ar_in = [nc.dram_tensor(f"ar_in{i}", [1, 1], F32) for i in range(2)]
    ar_out = [nc.dram_tensor(f"ar_out{i}", [1, 1], F32) for i in range(2)]

    A = {}

    def sbuf(name, shape, dt):
        A[name] = nc.alloc_sbuf_tensor(name, list(shape), dt).ap()
        return A[name]

    x_sb = sbuf("x_sb", (128, NT * D), F32)
    stage32 = sbuf("stage32", (128, 8192), F32)
    scrF = sbuf("scrF", (128, NSLOT), F32)
    scrG = sbuf("scrG", (128, 8192), BF16)
    arenaW = sbuf("arenaW", (128, 8192), F16)
    scrH_hi = sbuf("scrH_hi", (128, GT * D), F16)
    scrH_lo = sbuf("scrH_lo", (128, GT * D), F16)
    hfm_hi = sbuf("hfm_hi", (128, 8 * GTOK), F16)
    hfm_lo = sbuf("hfm_lo", (128, 8 * GTOK), F16)
    vbuf = sbuf("vbuf", (128, 2048), F16)
    qfm_hi = sbuf("qfm_hi", (128, TC), F16)
    qfm_lo = sbuf("qfm_lo", (128, TC), F16)
    qtm16 = sbuf("qtm16", (128, 2 * DH), F16)
    ktm16 = sbuf("ktm16", (128, 4 * DH), F16)
    pt16 = sbuf("pt16", (128, 2 * QB), F16)
    rkT_hi = sbuf("rkT_hi", (50, NSLOT), F16)
    rkT_lo = sbuf("rkT_lo", (48, NSLOT), F16)
    w1T16 = sbuf("w1T16", (128, 8 * 256), F16)
    wq16 = sbuf("wq16", (128, 8 * 2 * DH), F16)
    wk16 = sbuf("wk16", (128, 8 * 2 * DH), F16)
    wrqT16 = sbuf("wrqT16", (128, 8 * 96), F16)
    rt16 = sbuf("rt16", (128, 96), F16)
    w2T16 = sbuf("w2T16", (128, 2), F16)
    id16 = sbuf("id16s", (128, 128), F16)
    ones16 = sbuf("ones16", (128, 2), F16)
    ones16r = sbuf("ones16r", (2, 128), F16)
    bq16 = sbuf("bq16", (2, DH), F16)
    bk16 = sbuf("bk16", (2, DH), F16)
    k0hl = sbuf("k0hl", (128, 4), F16)
    k0aug = sbuf("k0aug", (4, 1), F16)
    b1c = sbuf("b1c", (128, 2), F32)
    b2v = sbuf("b2v", (128, 1), F32)
    gw1v = sbuf("gw1v", (128, 1), F32)
    gbv = sbuf("gbv", (128, 1), F32)
    sA = sbuf("sA", (128, 64), F32)
    sM = sbuf("sM", (128, 64), F32)
    one1 = sbuf("one1", (1, 16), F32)
    lv16 = sbuf("lv16", (128, 16), F16)
    invb = sbuf("invb", (128, 1), F32)
    top8 = sbuf("top8", (128, 8), F32)
    idx8 = sbuf("idx8", (128, 8), U32)
    w8 = sbuf("w8", (128, 16), F32)
    zsb = sbuf("zsb", (128, 16), F32)
    psk = sbuf("psk", (128, NT), F32)
    psk16 = sbuf("psk16", (128, NT), F16)
    ssink = sbuf("ssink", (128, NT), F32)
    mrow = sbuf("mrow", (128, NT), F32)

    h1pre = arenaW[:, 0:4096]
    rq_hi = arenaW[0:50, 4096:6144]
    rq_lo = arenaW[50:98, 6144:8192]
    rq_lo = arenaW[0:48, 6144:8192]
    qp16 = vbuf
    kfm_hi = stage32.bitcast(F16)[:, 0:8192]
    kfm_lo = stage32.bitcast(F16)[:, 8192:16384]
    kfs_hi = stage32.bitcast(F16)[:, 8192:8192 + 2048]
    kfs_lo = stage32.bitcast(F16)[:, 10240:10240 + 2048]
    kfs_ra = stage32.bitcast(F16)[0:1, 12288:12288 + 2048]
    kfs_rb = stage32.bitcast(F16)[64:65, 12288:12288 + 2048]
    fwd16 = scrF.bitcast(BF16)
    k129 = scrG.bitcast(F16)[0:4, 0:8192]
    wv16 = arenaW
    scrG16 = scrG.bitcast(F16)
    qaug = scrH_hi[0:4, 0:TC]
    v0r16 = scrG16[64:65, 6144:6144 + D]
    pskT = scrG16[64:65, 4096:6144]
    bv16 = scrG16[64:66, 2048:2048 + D]
    ones64 = scrG16[64:66, 3072:3200]
    stg32r = scrG16[32:33, 0:8192]
    stg64r = scrG16[64:65, 0:2048]
    tmp32 = scrF[:, 0:D]

    sem_names = ["ld", "te", "sc", "ve", "gp", "cc", "vs", "st", "aux", "kv"]
    sems = {s: nc.alloc_semaphore(s) for s in sem_names}

    x_in_t = x_in.rearrange("(t p) d -> t p d", p=128)
    x_out_t = x_out.rearrange("(t p) d -> t p d", p=128)

    def program(p: Pass):
        emit_prep(p)
        emit_ffn(p, 0)
        if stage in ("attn", "full"):
            emit_attn(p)
        if stage == "full":
            emit_ffn(p, 1, skip_stats=True)
        emit_out(p)

    def hilo_(p, dst_hi, dst_lo, src32):
        p.wait("scalar", "sc")
        p.act(dst_hi, src32, AF.Copy, sem="sc")
        p.wait("vector", "sc")
        p.v("tensor_tensor", dst_lo, src32, dst_hi, ALU.subtract, sem="ve")

    def emit_prep(p: Pass):
        for i in range(4):
            p.dma("gpsimd", aux_bf.ap()[i * 1024:(i + 1) * 1024, :],
                  aux_in[i * 1024:(i + 1) * 1024, :], sem="aux")
        for t in range(NT):
            p.dma("sync", x_sb[:, t * D:(t + 1) * D], x_in_t[t], sem="ld")
        p.dma("sync", id16[:, :], id16_in, sem="ld")
        p.dma("sync", b2v[:, :], b2v_in, sem="ld")
        p.dma("sync", gw1v[:, :], gw1v_in, sem="ld")
        p.dma("sync", gbv[:, :], gbv_in, sem="ld")
        p.dma("sync", b1c[:, :], b1p_in, sem="ld")
        p.dma("sync", stage32[:, 0:D], qproj_in, sem="ld")
        p.dma("sync", stage32[:, D:D + 48], routerT_in, sem="ld")
        p.dma("sync", stage32[:, D + 48:D + 48 + 2048]
              .rearrange("p (c d) -> p c d", c=8),
              w1T_in.rearrange("(c p) d -> p c d", p=128), sem="ld")
        p.dma("sync", stage32[:, 3120:3120 + NSLOT], auxkT_in, sem="ld")
        p.dma("sync", stage32[:, 7216:7218], w2p_in, sem="ld")
        p.dma("sync", stage32[:, 7218:7250], maskp_in, sem="ld")
        p.wait("scalar", "ld")
        p.wait("vector", "ld")
        p.v("memset", ones16[:, :], 1.0)
        p.v("memset", ones16r[:, :], 1.0)
        p.v("memset", rq_hi[0:50, :], 1.0, sem="ve")
        hilo_(p, qp16[:, 0:D], qp16[:, D:2 * D], stage32[:, 0:D])
        hilo_(p, rt16[:, 0:48], rt16[:, 48:96], stage32[:, D:D + 48])
        for ch in range(8):
            p.act(w1T16[:, ch * 256:(ch + 1) * 256],
                  stage32[:, D + 48 + ch * 256:D + 48 + (ch + 1) * 256],
                  AF.Copy, sem="sc")
        p.act(w2T16[:, 0:2], stage32[:, 7216:7218], AF.Copy, sem="sc")
        mh16 = scrF.bitcast(F16)[:, 0:32]
        ml16 = scrF.bitcast(F16)[:, 32:64]
        hilo_(p, mh16, ml16, stage32[:, 7218:7250])
        p.wait("sync", "ve")
        p.wait("sync", "sc")
        p.dma("sync", rkT_hi[48:49, :], mh16, sem="ld")
        p.dma("sync", rkT_hi[49:50, :], ml16, sem="ld")
        auxk_hi = scrG.bitcast(F16)[:, 0:NSLOT]
        auxk_lo = scrG.bitcast(F16)[:, NSLOT:2 * NSLOT]
        asrc = stage32[:, 3120:3120 + NSLOT]
        for q in range(4):
            sl = slice(q * 1024, (q + 1) * 1024)
            hilo_(p, auxk_hi[:, sl], auxk_lo[:, sl], asrc[:, sl])
        p.wait("tensor", "sc")
        p.wait("tensor", "ve")
        p.wait("tensor", "ld")
        p.wait("scalar", "ve")
        for n in range(8):
            ps = psA[0:48, (n % 4) * 512:(n % 4) * 512 + 512]
            rh = auxk_hi[:, n * 512:(n + 1) * 512]
            rl = auxk_lo[:, n * 512:(n + 1) * 512]
            p.mm(ps, rt16[:, 0:48], rh, start=True, stop=False)
            p.mm(ps, rt16[:, 0:48], rl, start=False, stop=False)
            p.mm(ps, rt16[:, 48:96], rh, start=False, stop=True, sem="te")
            p.wait("scalar", "te")
            p.act(stage32[0:48, n * 512:(n + 1) * 512], ps, AF.Copy,
                  scale=RSQ48, sem="sc")
            p.wait("tensor", "sc")
        for n in range(8):
            sl = slice(n * 512, (n + 1) * 512)
            hilo_(p, rkT_hi[0:48, sl], rkT_lo[0:48, sl], stage32[0:48, sl])
        for n in range(2):
            ps = psB[0:48, n * 512:(n + 1) * 512]
            qh = qp16[:, n * 512:(n + 1) * 512]
            ql = qp16[:, D + n * 512:D + (n + 1) * 512]
            p.mm(ps, rt16[:, 0:48], qh, start=True, stop=False)
            p.mm(ps, rt16[:, 0:48], ql, start=False, stop=False)
            p.mm(ps, rt16[:, 48:96], qh, start=False, stop=True, sem="te")
        p.wait("scalar", "te")
        p.wait("scalar", "ve")
        for n in range(2):
            p.act(stage32[0:48, 4096 + n * 512:4096 + (n + 1) * 512],
                  psB[0:48, n * 512:(n + 1) * 512], AF.Copy, sem="sc")
        wrq32 = stage32[0:48, 4096:4096 + 1024]
        wrq_h = scrH_hi[0:48, 0:1024]
        wrq_l = scrH_lo[0:48, 0:1024]
        hilo_(p, wrq_h, wrq_l, wrq32)
        p.wait("tensor", "sc")
        p.wait("tensor", "ve")
        psA16 = psA.bitcast(F16)
        for ch in range(8):
            p.tr(psA16[0:128, 0:48], wrq_h[:, ch * 128:(ch + 1) * 128],
                 id16[0:48, 0:48])
            p.tr(psA16[0:128, 1024:1072], wrq_l[:, ch * 128:(ch + 1) * 128],
                 id16[0:48, 0:48], sem="te")
            p.wait("scalar", "te")
            p.act(wrqT16[:, ch * 96:ch * 96 + 48], psA16[0:128, 0:48], AF.Copy,
                  sem="sc")
            p.act(wrqT16[:, ch * 96 + 48:ch * 96 + 96], psA16[0:128, 1024:1072],
                  AF.Copy, sem="sc")
            p.wait("tensor", "sc")
        p.wait("gpsimd", "aux")

    def emit_ffn(p: Pass, call, skip_stats=False):
        if not skip_stats:
            p.wait("scalar", "ve")
            p.wait("vector", "sc")
            for t in range(NT):
                xt = x_sb[:, t * D:(t + 1) * D]
                p.v("reduce_sum", out=sA[:, t:t + 1], in_=xt, axis=AX, sem="ve")
                p.wait("scalar", "sc")
                p.act(tmp32[:, :], xt, AF.Square, accum_out=sA[:, 16 + t:17 + t],
                      sem="sc")
        p.wait("vector", "sc")
        p.wait("scalar", "ve")
        p.v("tensor_scalar", sA[:, 32:48], sA[:, 0:16], 1.0 / D, None, ALU.mult)
        p.v("tensor_scalar", sA[:, 48:64], sA[:, 16:32], 1.0 / D, None, ALU.mult)
        p.v("tensor_tensor", tmp32[:, 0:16], sA[:, 32:48], sA[:, 32:48], ALU.mult)
        p.v("tensor_sub", sA[:, 48:64], sA[:, 48:64], tmp32[:, 0:16])
        p.v("tensor_scalar", tmp32[:, 0:16], sA[:, 48:64], EPS_LN, None, ALU.add,
            sem="ve")
        p.wait("scalar", "ve")
        p.act(tmp32[:, 16:32], tmp32[:, 0:16], AF.Ln, sem="sc")
        p.wait("scalar", "sc")
        p.act(sM[:, 0:16], tmp32[:, 16:32], AF.Exp, scale=-0.5, sem="sc")
        p.wait("vector", "sc")
        p.v("tensor_tensor", tmp32[:, 48:64], sA[:, 32:48], sM[:, 0:16], ALU.mult)
        p.v("tensor_scalar", sM[:, 16:32], tmp32[:, 48:64], -1.0, None, ALU.mult)
        p.v("reciprocal", out=tmp32[:, 16:32], in_=tmp32[:, 0:16])
        p.v("tensor_tensor", tmp32[:, 32:48], sA[:, 48:64], tmp32[:, 16:32],
            ALU.mult, sem="ve")
        p.wait("scalar", "ve")
        p.act(sM[:, 32:48], tmp32[:, 32:48], AF.Ln, bias=1.0, sem="sc")
        p.wait("vector", "sc")
        p.v("tensor_copy", out=lv16[:, :], in_=sM[:, 32:48], sem="ve")
        p.wait("tensor", "ve")
        p.mm(psB[0:1, 0:16], ones16[:, 0:1], lv16[:, :], sem="te")
        p.wait("vector", "te")
        p.v("reduce_sum", out=one1[0:1, 1:2], in_=psB[0:1, 0:16], axis=AX,
            sem="ve")
        p.wait("sync", "ve")
        p.dma("sync", ar_in[call].ap(), one1[0:1, 1:2], sem="st")
        p.wait("gpsimd", "st")
        p.coll("AllReduce", ALU.add, [[0, 1, 2, 3, 4, 5, 6, 7]],
               [ar_in[call].ap().opt()], [ar_out[call].ap().opt()], sem="cc")
        p.wait("vector", "st")
        if call == 1:
            p.v("memset", rq_hi[0:50, :], 1.0, sem="ve")
        for g in range(NG):
            for ti in range(GT):
                t = g * GT + ti
                xt = x_sb[:, t * D:(t + 1) * D]
                p.wait("scalar", "ve")
                p.wait("scalar", "sc")
                p.act(tmp32[:, :], xt, AF.Copy, bias=sM[:, 16 + t:17 + t],
                      scale=sM[:, 0 + t:1 + t], sem="sc")
                p.wait("scalar", "sc")
                p.act(scrH_hi[:, ti * D:(ti + 1) * D], tmp32[:, :], AF.Copy,
                      sem="sc")
                p.wait("vector", "sc")
                p.v("tensor_tensor", scrH_lo[:, ti * D:(ti + 1) * D], tmp32[:, :],
                    scrH_hi[:, ti * D:(ti + 1) * D], ALU.subtract, sem="ve")
                p.v("scalar_tensor_tensor", xt, tmp32[:, :], 0.5, xt,
                    ALU.mult, ALU.add, sem="ve")
            p.wait("tensor", "ve")
            psA16_ = psA.bitcast(F16)
            for hl, (hsrc, dst) in enumerate(((scrH_hi, hfm_hi), (scrH_lo, hfm_lo))):
                for ch in range(8):
                    for ti in range(GT):
                        p.tr(psA16_[:, (2 * ti + hl) * 1024:(2 * ti + hl) * 1024 + 128],
                             hsrc[:, ti * D + ch * 128:ti * D + (ch + 1) * 128],
                             id16[:, :], sem="te")
                    p.wait("scalar", "te")
                    for ti in range(GT):
                        p.act(dst[:, ch * GTOK + ti * 128:ch * GTOK + (ti + 1) * 128],
                              psA16_[:, (2 * ti + hl) * 1024:(2 * ti + hl) * 1024 + 128],
                              AF.Copy, sem="sc")
                    p.wait("tensor", "sc")
            for m in range(2):
                ps = psB[:, m * 512:m * 512 + GTOK]
                for ch in range(8):
                    p.mm(ps, w1T16[:, ch * 256 + m * 128:ch * 256 + m * 128 + 128],
                         hfm_hi[:, ch * GTOK:(ch + 1) * GTOK],
                         start=(ch == 0), stop=(ch == 7), sem="te")
                p.wait("scalar", "te")
                p.act(h1pre[:, m * 2048 + g * GTOK:m * 2048 + (g + 1) * GTOK], ps,
                      AF.Gelu, bias=b1c[:, m:m + 1], sem="sc")
            p.wait("tensor", "sc")
            for ti in range(GT):
                t = g * GT + ti
                ps = psB[:, 1536 + t:1536 + t + 1]
                for m in range(2):
                    p.mm(ps, h1pre[:, m * 2048 + g * GTOK + ti * 128:
                                   m * 2048 + g * GTOK + (ti + 1) * 128],
                         w2T16[:, m:m + 1],
                         start=(t == 0 and m == 0), stop=(t == NT - 1 and m == 1),
                         sem="te")
            ps = psA[0:48, 0:GTOK]
            for ch in range(8):
                wh = wrqT16[:, ch * 96:ch * 96 + 48]
                wl = wrqT16[:, ch * 96 + 48:ch * 96 + 96]
                hh = hfm_hi[:, ch * GTOK:(ch + 1) * GTOK]
                hl_ = hfm_lo[:, ch * GTOK:(ch + 1) * GTOK]
                p.mm(ps, wh, hh, start=(ch == 0), stop=False)
                p.mm(ps, wh, hl_, start=False, stop=False)
                p.mm(ps, wl, hh, start=False, stop=(ch == 7), sem="te")
            p.wait("scalar", "te")
            p.act(rq_hi[0:48, g * GTOK:(g + 1) * GTOK], ps, AF.Copy, sem="sc")
            p.wait("vector", "sc")
            p.v("tensor_tensor", rq_lo[0:48, g * GTOK:(g + 1) * GTOK], ps,
                rq_hi[0:48, g * GTOK:(g + 1) * GTOK], ALU.subtract, sem="ve")
            p.wait("tensor", "ve")
        p.wait("scalar", "te")
        p.act(tmp32[:, 0:16], psB[:, 1536:1552], AF.Copy, bias=b2v[:, 0:1],
              sem="sc")
        p.wait("scalar", "sc")
        p.act(tmp32[:, 16:32], tmp32[:, 0:16], AF.Exp, scale=-1.0, sem="sc")
        p.wait("vector", "sc")
        p.v("tensor_scalar", tmp32[:, 16:32], tmp32[:, 16:32], 1.0, None, ALU.add)
        p.v("reciprocal", out=sM[:, 48:64], in_=tmp32[:, 16:32], sem="ve")
        p.wait("sync", "cc")
        p.dma("sync", one1[0:1, 2:3], ar_out[call].ap(), sem="ld")
        p.wait("sync", "ld")
        p.wait("vector", "ld")
        p.v("tensor_scalar", one1[0:1, 3:4], one1[0:1, 2:3], 1.0 / (B * S), 1e-6,
            ALU.mult, op1=ALU.add)
        p.v("reciprocal", out=one1[0:1, 4:5], in_=one1[0:1, 3:4])
        p.v("tensor_scalar", one1[0:1, 5:6], one1[0:1, 4:5], 0.5, None, ALU.mult)
        p.v("tensor_copy", out=one1.bitcast(F16)[0:1, 12:13], in_=one1[0:1, 5:6],
            sem="ve")
        p.wait("tensor", "ve")
        p.mm(psB[:, 1024:1025], ones16r[0:1, :], one1.bitcast(F16)[0:1, 12:13],
             sem="te")
        p.wait("scalar", "te")
        p.act(invb[:, :], psB[:, 1024:1025], AF.Copy, sem="sc")
        p.wait("vector", "sc")
        p.v("tensor_scalar", sM[:, 32:48], sM[:, 32:48], invb[:, 0:1], None,
            ALU.mult)
        p.v("scalar_tensor_tensor", sM[:, 48:64], sM[:, 48:64], 2.5,
            sM[:, 32:48], ALU.mult, ALU.add)
        p.v("tensor_scalar", sM[:, 48:64], sM[:, 48:64], 5.0, 0.0, ALU.min,
            op1=ALU.max, sem="ve")
        p.wait("scalar", "ve")
        p.act(tmp32[:, 16:32], sM[:, 48:64], AF.Copy, scale=gw1v[:, 0:1],
              bias=gbv[:, 0:1], sem="sc")
        p.wait("scalar", "sc")
        p.act(tmp32[:, 32:48], tmp32[:, 16:32], AF.Exp, scale=-1.0, sem="sc")
        p.wait("vector", "sc")
        p.v("tensor_scalar", tmp32[:, 32:48], tmp32[:, 32:48], 1.0, None, ALU.add)
        p.v("reciprocal", out=sM[:, 48:64], in_=tmp32[:, 32:48], sem="ve")
        a4_ve = p.c["ve"]
        idx_ve = {}
        agg_ve = {}
        w8b = [w8[:, 0:8], w8[:, 8:16]]

        def gslot(t, k):
            if t % 2 == 0:
                return scrG[:, k * D:(k + 1) * D]
            half = [scrH_hi, scrH_lo, hfm_hi, hfm_lo][k // 2].bitcast(BF16)
            return half[:, (k % 2) * D:(k % 2) * D + D]

        def emit_scores_topk(t):
            g, ti = t // GT, t % GT
            half = psA if t % 2 == 0 else psB
            rq_s = slice(g * GTOK + ti * 128, g * GTOK + (ti + 1) * 128)
            for n in range(8):
                ps = half[:, (n % 4) * 512:(n % 4) * 512 + 512]
                p.mm(ps, rq_hi[0:50, rq_s], rkT_hi[0:50, n * 512:(n + 1) * 512],
                     start=True, stop=False)
                p.mm(ps, rq_hi[0:48, rq_s], rkT_lo[0:48, n * 512:(n + 1) * 512],
                     start=False, stop=False)
                p.mm(ps, rq_lo[0:48, rq_s], rkT_hi[0:48, n * 512:(n + 1) * 512],
                     start=False, stop=True, sem="te")
                if n % 4 == 3:
                    p.wait("scalar", "te")
                    p.wait("scalar", "ve", val=max(idx_ve.get(t - 1, 0), a4_ve))
                    p.act(scrF[:, (n // 4) * 2048:(n // 4) * 2048 + 2048],
                          half[:, 0:2048], AF.Copy, sem="sc")
                    p.wait("tensor", "sc")
            p.wait("vector", "sc")
            p.wait("vector", "gp", val=gp_done.get(t - 1, 0))
            p.v("max", out=top8[:, :], in_=scrF[:, :])
            p.v("tensor_scalar", w8b[t % 2], top8[:, :], top8[:, 0:1], None,
                ALU.subtract)
            p.v("max_index", out=idx8[:, :], in_max=top8[:, :],
                in_values=scrF[:, :], sem="ve")
            idx_ve[t] = p.c["ve"]
            p.wait("gpsimd", "ve", val=idx_ve[t])
            if t >= 2:
                p.wait("gpsimd", "ve", val=agg_ve.get(t - 2, 0))
            for k in range(8):
                p.gather(gslot(t, k), aux_bf.ap(), idx8[:, k:k + 1], sem="gp")
            p.wait("scalar", "ve")
            p.act(w8b[t % 2], w8b[t % 2], AF.Exp, sem="sc")
            p.wait("vector", "sc")
            p.v("reduce_sum", out=zsb[:, 8 + (t % 2):9 + (t % 2)], in_=w8b[t % 2],
                axis=AX)
            p.v("reciprocal", out=zsb[:, 10 + (t % 2):11 + (t % 2)],
                in_=zsb[:, 8 + (t % 2):9 + (t % 2)])
            p.v("tensor_scalar", zsb[:, 12 + (t % 2):13 + (t % 2)],
                zsb[:, 10 + (t % 2):11 + (t % 2)],
                sM[:, 48 + t:49 + t], 0.5, ALU.mult, op1=ALU.mult)
            p.v("tensor_scalar", w8b[t % 2], w8b[t % 2],
                zsb[:, 12 + (t % 2):13 + (t % 2)], None, ALU.mult, sem="ve")

        def emit_agg(t):
            p.wait("vector", "gp", val=gp_done[t])
            xt = x_sb[:, t * D:(t + 1) * D]
            for k in range(8):
                p.v("scalar_tensor_tensor", xt, gslot(t, k),
                    w8b[t % 2][:, k:k + 1], xt, ALU.mult, ALU.add, sem="ve")
            agg_ve[t] = p.c["ve"]
            if call == 0:
                p.v("reduce_sum", out=sA[:, t:t + 1], in_=xt, axis=AX, sem="ve")
                p.wait("scalar", "ve", val=p.c["ve"])
                p.act(vbuf.bitcast(F32)[:, 0:1024], xt, AF.Square,
                      accum_out=sA[:, 16 + t:17 + t], sem="sc")

        gp_done = {}
        for t in range(NT):
            emit_scores_topk(t)
            gp_done[t] = p.c["gp"]
            if t >= 1:
                emit_agg(t - 1)
        emit_agg(NT - 1)

    def emit_attn(p: Pass):
        p.dma("sync", stage32.rearrange("p (c d) -> p c d", c=8),
              wvT_in.rearrange("(c p) d -> p c d", p=128), sem="ld")
        p.wait("scalar", "ld")
        p.wait("scalar", "te")
        p.wait("scalar", "ve")
        p.wait("vector", "sc")
        for ch in range(8):
            p.act(wv16[:, ch * D:(ch + 1) * D], stage32[:, ch * D:(ch + 1) * D],
                  AF.Copy)
        p.wait("sync", "sc")
        p.dma("sync", stage32[:, 0:8 * DH].rearrange("p (c d) -> p c d", c=8),
              wqT_in.rearrange("(c p) d -> p c d", p=128), sem="ld")
        p.dma("sync", stage32[:, 8 * DH:16 * DH].rearrange("p (c d) -> p c d", c=8),
              wkT_in.rearrange("(c p) d -> p c d", p=128), sem="ld")
        p.dma("sync", stage32[:, 4096:4098], bqp_in, sem="ld")
        p.dma("sync", stage32[:, 4098:4100], bkp_in, sem="ld")
        p.dma("sync", stage32[:, 4100:4102], k0p_in, sem="ld")
        p.dma("sync", stage32[:, 4102:4110], v0p_in, sem="ld")
        p.dma("sync", stage32[:, 4110:4118], bvp_in, sem="ld")
        p.wait("scalar", "ld")
        p.wait("vector", "ld")
        for ch in range(8):
            hilo_(p, wq16[:, ch * 258:ch * 258 + DH],
                  wq16[:, ch * 258 + DH:ch * 258 + 258],
                  stage32[:, ch * DH:(ch + 1) * DH])
            hilo_(p, wk16[:, ch * 258:ch * 258 + DH],
                  wk16[:, ch * 258 + DH:ch * 258 + 258],
                  stage32[:, 8 * DH + ch * DH:8 * DH + (ch + 1) * DH])
        bq_c = scrG16[:, 7680:7684]
        bk_c = scrG16[:, 7684:7688]
        v0_c = scrG16[:, 7688:7704]
        bv_c = scrG16[:, 7704:7720]
        hilo_(p, bq_c[:, 0:2], bq_c[:, 2:4], stage32[:, 4096:4098])
        hilo_(p, bk_c[:, 0:2], bk_c[:, 2:4], stage32[:, 4098:4100])
        hilo_(p, k0hl[:, 0:2], k0hl[:, 2:4], stage32[:, 4100:4102])
        hilo_(p, v0_c[:, 0:8], v0_c[:, 8:16], stage32[:, 4102:4110])
        hilo_(p, bv_c[:, 0:8], bv_c[:, 8:16], stage32[:, 4110:4118])
        p.wait("sync", "ve")
        p.wait("sync", "sc")
        for r, cc in ((0, 0), (1, 2)):
            p.dma("sync", bq16[r:r + 1, 0:128], bq_c[:, cc:cc + 1], sem="ld")
            p.dma("sync", bq16[r:r + 1, 128:129], bq_c[0:1, cc + 1:cc + 2], sem="ld")
            p.dma("sync", bk16[r:r + 1, 0:128], bk_c[:, cc:cc + 1], sem="ld")
            p.dma("sync", bk16[r:r + 1, 128:129], bk_c[0:1, cc + 1:cc + 2], sem="ld")
        p.dma("sync", v0r16[0:1, :], v0_c[:, 0:8], sem="ld")
        p.dma("sync", bv16[0:1, :], bv_c[:, 0:8], sem="ld")
        p.dma("sync", bv16[1:2, :], bv_c[:, 8:16], sem="ld")
        p.dma("sync", k0aug[0:1, :], k0hl[0:1, 1:2], sem="ld")
        p.dma("sync", k0aug[1:2, :], k0hl[0:1, 3:4], sem="ld")
        p.dma("sync", k0aug[2:3, :], k0hl[0:1, 1:2], sem="ld")
        p.wait("sync", "ld")
        p.wait("tensor", "ld")
        p.v("memset", ones64, 1.0)
        p.wait("vector", "sc")
        p.v("tensor_scalar", sA[:, 32:48], sA[:, 0:16], 1.0 / D, None, ALU.mult)
        p.v("tensor_scalar", sA[:, 48:64], sA[:, 16:32], 1.0 / D, None, ALU.mult)
        p.v("tensor_tensor", tmp32[:, 0:16], sA[:, 32:48], sA[:, 32:48], ALU.mult)
        p.v("tensor_sub", sA[:, 48:64], sA[:, 48:64], tmp32[:, 0:16])
        p.v("tensor_scalar", tmp32[:, 0:16], sA[:, 48:64], EPS_LN, None, ALU.add,
            sem="ve")
        p.wait("scalar", "ve")
        p.act(tmp32[:, 16:32], tmp32[:, 0:16], AF.Ln)
        p.act(sM[:, 0:16], tmp32[:, 16:32], AF.Exp, scale=-0.5, sem="sc")
        p.wait("vector", "sc")
        p.v("tensor_tensor", tmp32[:, 48:64], sA[:, 32:48], sM[:, 0:16], ALU.mult)
        p.v("tensor_scalar", sM[:, 16:32], tmp32[:, 48:64], -1.0, None, ALU.mult,
            sem="ve")
        for g in range(NG):
            for ti in range(GT):
                t = g * GT + ti
                xt = x_sb[:, t * D:(t + 1) * D]
                p.wait("scalar", "ve")
                p.act(tmp32[:, :], xt, AF.Copy, bias=sM[:, 16 + t:17 + t],
                      scale=sM[:, 0 + t:1 + t])
                p.wait("scalar", "sc")
                p.act(scrH_hi[:, ti * D:(ti + 1) * D], tmp32[:, :], AF.Copy,
                      sem="sc")
                p.wait("vector", "sc")
                p.v("tensor_tensor", scrH_lo[:, ti * D:(ti + 1) * D], tmp32[:, :],
                    scrH_hi[:, ti * D:(ti + 1) * D], ALU.subtract)
                p.v("tensor_add", xt, xt, tmp32[:, :], sem="ve")
            p.wait("tensor", "ve")
            psA16_ = psA.bitcast(F16)
            for hl, (hsrc, dst) in enumerate(((scrH_hi, hfm_hi), (scrH_lo, hfm_lo))):
                for ch in range(8):
                    for ti in range(GT):
                        p.tr(psA16_[:, (2 * ti + hl) * 1024:(2 * ti + hl) * 1024 + 128],
                             hsrc[:, ti * D + ch * 128:ti * D + (ch + 1) * 128],
                             id16[:, :], sem="te")
                    p.wait("scalar", "te")
                    for ti in range(GT):
                        p.act(dst[:, ch * GTOK + ti * 128:ch * GTOK + (ti + 1) * 128],
                              psA16_[:, (2 * ti + hl) * 1024:(2 * ti + hl) * 1024 + 128],
                              AF.Copy, sem="sc")
                    p.wait("tensor", "sc")
            for ti in range(GT):
                t = g * GT + ti
                tsl = slice(ti * 128, (ti + 1) * 128)
                psq = psB[:, 0:DH]
                for ch in range(8):
                    hh = hfm_hi[:, ch * GTOK + ti * 128:ch * GTOK + (ti + 1) * 128]
                    hl_ = hfm_lo[:, ch * GTOK + ti * 128:ch * GTOK + (ti + 1) * 128]
                    wh = wq16[:, ch * 258:ch * 258 + DH]
                    wl = wq16[:, ch * 258 + DH:ch * 258 + 258]
                    p.mm(psq, hh, wh, start=(ch == 0), stop=False)
                    p.mm(psq, hl_, wh, start=False, stop=False)
                    p.mm(psq, hh, wl, start=False, stop=False)
                p.mm(psq, ones16r[0:2, 0:128], bq16[0:2, :], start=False,
                     stop=True, sem="te")
                p.wait("scalar", "te")
                p.act(qtm16[:, 0:DH], psq, AF.Copy, sem="sc")
                p.wait("vector", "sc")
                p.v("tensor_tensor", qtm16[:, DH:2 * DH], psq, qtm16[:, 0:DH],
                    ALU.subtract, sem="ve")
                psk_ = psB[:, 512:512 + DH]
                for ch in range(8):
                    hh = hfm_hi[:, ch * GTOK + ti * 128:ch * GTOK + (ti + 1) * 128]
                    hl_ = hfm_lo[:, ch * GTOK + ti * 128:ch * GTOK + (ti + 1) * 128]
                    wh = wk16[:, ch * 258:ch * 258 + DH]
                    wl = wk16[:, ch * 258 + DH:ch * 258 + 258]
                    p.mm(psk_, hh, wh, start=(ch == 0), stop=False)
                    p.mm(psk_, hl_, wh, start=False, stop=False)
                    p.mm(psk_, hh, wl, start=False, stop=False)
                p.mm(psk_, ones16r[0:2, 0:128], bk16[0:2, :], start=False,
                     stop=True, sem="te")
                p.wait("scalar", "te")
                p.wait("scalar", "kv")
                p.act(ktm16[:, 0:DH], psk_, AF.Copy)
                p.wait("scalar", "sc")
                p.wait("vector", "sc")
                p.v("tensor_tensor", ktm16[:, DH:2 * DH], psk_, ktm16[:, 0:DH],
                    ALU.subtract, sem="ve")
                for half in range(2):
                    psv = psA[:, half * 512:(half + 1) * 512]
                    for ch in range(8):
                        hh = hfm_hi[:, ch * GTOK + ti * 128:ch * GTOK + (ti + 1) * 128]
                        wv_ = wv16[:, ch * D + half * 512:ch * D + (half + 1) * 512]
                        p.mm(psv, hh, wv_, start=(ch == 0), stop=False)
                    p.mm(psv, ones64, bv16[0:2, half * 512:(half + 1) * 512],
                         start=False, stop=True, sem="te")
                p.wait("scalar", "te")
                p.wait("scalar", "kv")
                p.act(vbuf[:, 0:512], psA[:, 0:512], AF.Copy)
                p.wait("scalar", "sc")
                p.act(vbuf[:, 512:1024], psA[:, 512:1024], AF.Copy, sem="sc")
                p.wait("sync", "sc")
                p.dma("sync", kvv_in.ap()[t * 128:(t + 1) * 128, :],
                      vbuf[:, 0:1024], sem="kv")
                p.wait("tensor", "ve")
                p.tr(psA16_[:, 3072:3072 + 128], qtm16[:, 0:128], id16[:, :])
                p.tr(psA16_[:, 3072 + 128:3072 + 256], qtm16[:, DH:DH + 128],
                     id16[:, :])
                p.tr(psA16_[0:1, 3072 + 256:3072 + 384], qtm16[:, 128:129],
                     id16[:, :])
                p.tr(psA16_[0:1, 3072 + 384:3072 + 512], qtm16[:, 2 * DH - 1:2 * DH],
                     id16[:, :], sem="te")
                p.tr(psA16_[:, 3584:3584 + 128], ktm16[:, 0:128], id16[:, :])
                p.tr(psA16_[:, 3584 + 128:3584 + 256], ktm16[:, DH:DH + 128],
                     id16[:, :])
                p.tr(psA16_[0:1, 3584 + 256:3584 + 384], ktm16[:, 128:129],
                     id16[:, :])
                p.tr(psA16_[0:1, 3584 + 384:3584 + 512], ktm16[:, 2 * DH - 1:2 * DH],
                     id16[:, :], sem="te")
                p.wait("scalar", "te")
                p.act(qfm_hi[:, t * 128:(t + 1) * 128], psA16_[:, 3072:3200],
                      AF.Copy)
                p.act(qfm_lo[:, t * 128:(t + 1) * 128], psA16_[:, 3200:3328],
                      AF.Copy)
                p.act(stg32r[0:1, t * 128:(t + 1) * 128],
                      psA16_[0:1, 3328:3456], AF.Copy)
                p.act(stg64r[0:1, t * 128:(t + 1) * 128],
                      psA16_[0:1, 3456:3584], AF.Copy)
                p.act(kfs_hi[:, t * 128:(t + 1) * 128], psA16_[:, 3584:3712],
                      AF.Copy)
                p.act(kfs_lo[:, t * 128:(t + 1) * 128], psA16_[:, 3712:3840],
                      AF.Copy)
                p.act(kfs_ra[0:1, t * 128:(t + 1) * 128],
                      psA16_[0:1, 3840:3968], AF.Copy)
                p.act(kfs_rb[0:1, t * 128:(t + 1) * 128],
                      psA16_[0:1, 3968:4096], AF.Copy, sem="sc")
                p.wait("tensor", "sc")
        p.wait("sync", "sc")
        p.dma("sync", kvk_in.ap()[0:128, :], kfs_hi[:, 0:TC], sem="kv")
        p.dma("sync", kvk_in.ap()[128:129, :], kfs_ra[0:1, 0:TC], sem="kv")
        p.dma("sync", kvk_in.ap()[129:257, :], kfs_lo[:, 0:TC], sem="kv")
        p.dma("sync", kvk_in.ap()[257:258, :], kfs_rb[0:1, 0:TC], sem="kv")
        p.dma("sync", qaug[0:1, :], stg32r[0:1, 0:TC], sem="ld")
        p.dma("sync", qaug[1:2, :], stg32r[0:1, 0:TC], sem="ld")
        p.dma("sync", qaug[2:3, :], stg64r[0:1, 0:TC], sem="ld")
        p.wait("sync", "ld")
        p.wait("gpsimd", "kv")
        p.coll("AllGather", ALU.bypass, [[0, 1, 2, 3], [4, 5, 6, 7]],
               [kvk_in.ap().opt()], [kvk_out.ap().opt()], sem="cc")
        p.coll("AllGather", ALU.bypass, [[0, 1, 2, 3], [4, 5, 6, 7]],
               [kvv_in.ap().opt()], [kvv_out.ap().opt()], sem="cc")
        p.wait("sync", "cc", val=p.c["cc"] - 1)
        p.wait("sync", "sc")
        p.wait("sync", "kv")
        for r in range(4):
            p.dma("sync", kfm_hi[:, r * 2048:(r + 1) * 2048],
                  kvk_out.ap()[r * 2 * DH:r * 2 * DH + 128, :], sem="ld")
            p.dma("sync", kfm_lo[:, r * 2048:(r + 1) * 2048],
                  kvk_out.ap()[r * 2 * DH + 129:r * 2 * DH + 257, :], sem="ld")
            p.dma("sync", scrG16[0:1, r * 2048:(r + 1) * 2048],
                  kvk_out.ap()[r * 2 * DH + 128:r * 2 * DH + 129, :], sem="ld")
            p.dma("sync", stg32r[0:1, r * 2048:(r + 1) * 2048],
                  kvk_out.ap()[r * 2 * DH + 257:r * 2 * DH + 258, :], sem="ld")
        p.wait("sync", "ld")
        p.wait("vector", "ld")
        p.wait("vector", "sc")
        p.v("memset", stg64r[0:1, 0:2048], 1.0, sem="ve")
        p.wait("sync", "sc")
        p.wait("sync", "ve")
        p.dma("sync", scrG16[1:2, 0:8192], stg32r[0:1, 0:8192], sem="ld")
        p.dma("sync", scrG16[2:3, 0:8192], scrG16[0:1, 0:8192], sem="ld")
        for i4 in range(4):
            p.dma("sync", scrG16[3:4, i4 * 2048:(i4 + 1) * 2048],
                  stg64r[0:1, 0:2048], sem="ld")
        p.wait("tensor", "ld")
        for t in range(NT):
            tsl = slice(t * 128, (t + 1) * 128)
            ps = psB[:, 1536 + t:1537 + t]
            p.mm(ps, qfm_hi[:, tsl], k0hl[:, 0:1], start=(t == 0), stop=False)
            p.mm(ps, qfm_hi[:, tsl], k0hl[:, 2:3], start=False, stop=False)
            p.mm(ps, qfm_lo[:, tsl], k0hl[:, 0:1], start=False, stop=False)
            p.mm(ps, qaug[0:3, tsl], k0aug[0:3, :], start=False,
                 stop=(t == NT - 1), sem="te")
        p.wait("scalar", "te")
        p.act(ssink[:, :], psB[:, 1536:1552], AF.Copy, sem="sc")
        p.wait("tensor", "sc")
        fwd_sc = {}
        red_ve = {}
        for t in range(NT):
            tsl = slice(t * 128, (t + 1) * 128)
            for n in range(16):
                ps = psA[:, (n % 4) * 512:(n % 4) * 512 + 512]
                key = (t * 16 + n) - 4
                if key >= 0:
                    p.wait("tensor", "sc", val=fwd_sc[key])
                p.mm(ps, qfm_hi[:, tsl], kfm_hi[:, n * 512:(n + 1) * 512],
                     start=True, stop=False)
                p.mm(ps, qaug[0:1, tsl], scrG16[0:1, n * 512:(n + 1) * 512],
                     start=False, stop=True, sem="te")
                p.wait("scalar", "te")
                p.wait("scalar", "ve", val=red_ve.get((t - 1, n // 4), 0))
                p.act(fwd16[:, n * 512:(n + 1) * 512], ps, AF.Copy, sem="sc")
                fwd_sc[t * 16 + n] = p.c["sc"]
                if n % 4 == 3:
                    p.wait("vector", "sc")
                    p.v("reduce_max", out=zsb[:, 12 + (n // 4):13 + (n // 4)],
                        in_=fwd16[:, (n // 4) * 2048:(n // 4) * 2048 + 2048],
                        axis=AX, sem="ve")
                    red_ve[(t, n // 4)] = p.c["ve"]
            p.v("reduce_max", out=mrow[:, t:t + 1], in_=zsb[:, 12:16], axis=AX,
                sem="ve")
        p.wait("vector", "sc")
        p.v("tensor_max", mrow[:, :], mrow[:, :], ssink[:, :])
        p.v("tensor_scalar", mrow[:, :], mrow[:, :], 0.5, None, ALU.add)
        p.v("tensor_sub", psk[:, :], ssink[:, :], mrow[:, :], sem="ve")
        p.wait("scalar", "ve")
        p.act(psk[:, :], psk[:, :], AF.Exp, sem="sc")
        p.wait("vector", "sc")
        p.v("tensor_copy", out=psk16[:, :], in_=psk[:, :])
        p.v("tensor_scalar", lv16[:, :], mrow[:, :], -1.0, None, ALU.mult,
            sem="ve")
        p.wait("tensor", "ve")
        for t in range(NT):
            p.tr(psA16_[0:1, 0:128], psk16[:, t:t + 1], id16[:, :])
            p.tr(psA16_[0:1, 1024:1152], lv16[:, t:t + 1], id16[:, :], sem="te")
            p.wait("scalar", "te")
            p.act(pskT[0:1, t * 128:(t + 1) * 128], psA16_[0:1, 0:128], AF.Copy)
            p.wait("scalar", "sc")
            p.act(stg32r[0:1, t * 128:(t + 1) * 128],
                  psA16_[0:1, 1024:1152], AF.Copy, sem="sc")
            p.wait("tensor", "sc")
        p.wait("sync", "sc")
        p.dma("sync", qaug[3:4, 0:TC], stg32r[0:1, 0:TC], sem="ld")
        p.wait("sync", "ld")
        p.wait("tensor", "ld")
        p.wait("sync", "cc")
        vs_base = p.c["vs"]
        NVT = NQB * NKT
        v_done = {}

        def issue_v(g):
            kt = g % NKT
            p.wait("sync", "te")
            p.dma("sync", vbuf[:, (g % 2) * D:(g % 2) * D + D],
                  kvv_out.ap()[kt * 128:(kt + 1) * 128, :], sem="vs")
            v_done[g] = p.c["vs"]

        exp_sc = {}
        issue_v(0)
        issue_v(1)

        def emit_scores(qb, kt):
            qsl_ = slice(qb * QB, (qb + 1) * QB)
            par = kt % 2
            sbk = psB[:, par * 512:par * 512 + QB]
            if kt >= 2:
                p.wait("tensor", "sc", val=exp_sc[kt - 2])
            p.mm(sbk, kfm_hi[:, kt * 128:(kt + 1) * 128], qfm_hi[:, qsl_],
                 start=True, stop=False)
            p.mm(sbk, kfm_hi[:, kt * 128:(kt + 1) * 128], qfm_lo[:, qsl_],
                 start=False, stop=False)
            p.mm(sbk, kfm_lo[:, kt * 128:(kt + 1) * 128], qfm_hi[:, qsl_],
                 start=False, stop=False)
            p.mm(sbk, scrG16[0:4, kt * 128:(kt + 1) * 128], qaug[0:4, qsl_],
                 start=False, stop=True, sem="te")
            p.wait("scalar", "te")
            p.act(pt16[:, par * QB:(par + 1) * QB], sbk, AF.Exp, sem="sc")
            exp_sc[kt] = p.c["sc"]

        def emit_pv(qb, kt):
            g = qb * NKT + kt
            par = kt % 2
            p.wait("tensor", "sc", val=exp_sc[kt])
            p.wait("tensor", "vs")
            for qt in range(2):
                ptsl = pt16[:, par * QB + qt * 128:par * QB + (qt + 1) * 128]
                vsl = vbuf[:, (g % 2) * D:(g % 2) * D + D]
                p.mm(psA[:, qt * 1024:qt * 1024 + 512], ptsl, vsl[:, 0:512],
                     start=(kt == 0), stop=False)
                p.mm(psA[:, qt * 1024 + 512:qt * 1024 + 1024], ptsl,
                     vsl[:, 512:1024], start=(kt == 0), stop=False)
                p.mm(psB[:, 1024 + qt:1025 + qt], ptsl, ones16[:, 0:1],
                     start=(kt == 0 and qt == 0),
                     stop=(kt == NKT - 1 and qt == 1), sem="te")
            if g + 2 < NVT:
                issue_v(g + 2)

        for qb in range(NQB):
            exp_sc.clear()
            emit_scores(qb, 0)
            for kt in range(1, NKT):
                emit_scores(qb, kt)
                emit_pv(qb, kt - 1)
            emit_pv(qb, NKT - 1)
            for qt in range(2):
                t = qb * 2 + qt
                tsl2 = slice(t * 128, (t + 1) * 128)
                p.mm(psA[:, qt * 1024:qt * 1024 + 512], pskT[0:1, tsl2],
                     v0r16[0:1, 0:512], start=False, stop=True)
                p.mm(psA[:, qt * 1024 + 512:qt * 1024 + 1024],
                     pskT[0:1, tsl2], v0r16[0:1, 512:1024],
                     start=False, stop=True, sem="te")
            p.wait("vector", "te")
            p.v("tensor_copy", out=zsb[:, 0:2], in_=psB[:, 1024:1026])
            p.v("tensor_add", zsb[:, 2:4], zsb[:, 0:2], psk[:, 2 * qb:2 * qb + 2])
            p.v("reciprocal", out=zsb[:, 4:6], in_=zsb[:, 2:4])
            for qt in range(2):
                t = qb * 2 + qt
                xt = x_sb[:, t * D:(t + 1) * D]
                for half in range(2):
                    p.v("scalar_tensor_tensor", xt[:, half * 512:(half + 1) * 512],
                        psA[:, qt * 1024 + half * 512:qt * 1024 + (half + 1) * 512],
                        zsb[:, 4 + qt:5 + qt], xt[:, half * 512:(half + 1) * 512],
                        ALU.mult, ALU.add, sem="ve")
                p.v("reduce_sum", out=sA[:, t:t + 1], in_=xt, axis=AX, sem="ve")
                p.wait("scalar", "ve")
                p.act(scrH_lo.bitcast(F32)[:, 0:1024], xt, AF.Square,
                      accum_out=sA[:, 16 + t:17 + t], sem="sc")
            p.wait("tensor", "ve")

    def emit_out(p: Pass):
        p.wait("sync", "ve")
        for t in range(NT):
            p.dma("sync", x_out_t[t], x_sb[:, t * D:(t + 1) * D], sem="st")
        p.wait("sync", "st")

    psA = nc.alloc_psum_tensor("psA", [128, 2048], F32).ap()
    psB = nc.alloc_psum_tensor("psB", [128, 2048], F32).ap()

    counts = []
    with nc.Block() as block:
        for eng_name in ENGS:
            c = {s: 0 for s in sem_names}
            counts.append(c)

            def make(f_name, f_c):
                def _(eng):
                    program(Pass(f_name, eng, sems, f_c))
                return _
            getattr(block, eng_name)(make(eng_name, c))
    for c in counts[1:]:
        assert c == counts[0], f"counter mismatch {c} vs {counts[0]}"
    return nc


def host_inputs(I, core):
    b, j = core // 4, core % 4
    t0 = j * TC
    f32 = np.float32

    def T(a):
        return np.ascontiguousarray(np.asarray(a, f32).T)

    def pad129(a):
        out = np.zeros((128, 2), f32)
        a = np.asarray(a, f32).reshape(-1)
        out[:, 0] = a[0:128]
        out[0, 1] = a[128]
        return out

    return {
        "x_in": np.ascontiguousarray(np.asarray(I["L0_attn_x"], f32)[b, t0:t0 + TC]),
        "wqT": T(I["L0_attn_w_q"]),
        "wkT": T(I["L0_attn_w_k"]),
        "wvT": T(I["L0_attn_w_v"]),
        "bqp": pad129(I["L0_attn_b_q"]),
        "bkp": pad129(I["L0_attn_b_k"]),
        "bvp": np.asarray(I["L0_attn_b_v"], f32).reshape(128, 8),
        "k0p": pad129(I["L0_attn_k_0"]),
        "v0p": np.asarray(I["L0_attn_v_0"], f32).reshape(128, 8),
        "qproj": np.asarray(I["L0_ffn_q_proj_w"], f32),
        "routerT": T(I["L0_ffn_router_w"]),
        "auxkT": T(I["L0_ffn_aux_keys"]),
        "maskp": np.asarray(I["L0_ffn_reliability_mask"], f32).reshape(128, 32),
        "w1T": T(I["L0_ffn_unc_w1"]),
        "b1p": np.ascontiguousarray(
            np.asarray(I["L0_ffn_unc_b1"], f32).reshape(2, 128).T),
        "w2p": np.ascontiguousarray(
            np.asarray(I["L0_ffn_unc_w2"], f32).reshape(2, 128).T),
        "b2v_in": np.broadcast_to(
            np.asarray(I["L0_ffn_unc_b2"], f32).reshape(1, 1), (128, 1)).copy(),
        "gw1v_in": np.full((128, 1), np.float32(I["L0_ffn_gate_w1"]), f32),
        "gbv_in": np.full((128, 1), np.float32(I["L0_ffn_gate_bias"]), f32),
        "aux_vals": np.asarray(I["L0_ffn_aux_values"], f32),
        "id16": np.eye(128, dtype=np.float16),
    }


def kernel(**inputs):
    nc = build_nc("full")
    in_maps = [host_inputs(inputs, c) for c in range(8)]
    res = run_bass_kernel_spmd(nc, in_maps, core_ids=list(range(8)))
    out = np.empty((B, S, D), np.float32)
    for c in range(8):
        b, j = c // 4, c % 4
        out[b, j * TC:(j + 1) * TC] = res.results[c]["x_out"]
    return out


# version 04
# speedup vs baseline: 1.2170x; 1.0083x over previous
import numpy as np
import concourse.bass as bass
import concourse.mybir as mybir
from concourse.bass import IndirectOffsetOnAxis
from concourse.bass_utils import run_bass_kernel_spmd

F32 = mybir.dt.float32
F16 = mybir.dt.float16
BF16 = mybir.dt.bfloat16
U32 = mybir.dt.uint32
AF = mybir.ActivationFunctionType
ALU = mybir.AluOpType
AX = mybir.AxisListType.X

B, S, D = 2, 8192, 1024
TC = 2048
NT = 16
GT = 2
GTOK = GT * 128
NG = NT // GT
NKT = 64
NSLOT = 4096
DH = 129
QB = 256
NQB = TC // QB
EPS_LN = 1e-5
RSQ48 = float(1.0 / np.sqrt(48.0))

ENGS = ("sync", "tensor", "scalar", "vector", "gpsimd")


class Pass:
    def __init__(self, eng_name, eng, sems, c):
        self.e = eng_name
        self.eng = eng
        self.sems = sems
        self.c = c

    def bump(self, inst, sem, n):
        if sem is not None:
            self.c[sem] += n
            if inst is not None:
                inst.then_inc(self.sems[sem], n)
        return inst

    def wait(self, eng_name, sem, val=None):
        if self.e == eng_name:
            self.eng.wait_ge(self.sems[sem], self.c[sem] if val is None else val)

    def mm(self, out, lhsT, rhs, start=True, stop=True, sem=None, n=1):
        i = self.eng.matmul(out, lhsT, rhs, start=start, stop=stop) \
            if self.e == "tensor" else None
        return self.bump(i, sem, n)

    def tr(self, out, in_, ident, sem=None, n=1):
        i = self.eng.transpose(out, in_, ident) if self.e == "tensor" else None
        return self.bump(i, sem, n)

    def act(self, out, in_, func, bias=0.0, scale=1.0, accum_out=None,
            sem=None, n=1):
        i = None
        self.wait("scalar", "sc")
        if self.e == "scalar":
            if func == AF.Copy and not (isinstance(bias, float)
                                        and isinstance(scale, float)):
                func = AF.Identity
            kw = {"accum_out": accum_out} if accum_out is not None else {}
            i = self.eng.activation(out, in_, func, bias=bias, scale=scale, **kw)
        return self.bump(i, "sc", 1)

    def v(self, meth, *args, sem=None, n=1, **kw):
        self.wait("vector", "ve")
        i = getattr(self.eng, meth)(*args, **kw) if self.e == "vector" else None
        return self.bump(i, "ve", 1)

    def dma(self, queue, out, in_, sem=None, transpose=False, slow=False):
        i = None
        if self.e == queue:
            if transpose:
                i = self.eng.dma_start_transpose(out=out, in_=in_)
            elif slow:
                i = self.eng.dma_start(out=out, in_=in_,
                                       allow_slow_non_contiguous=True)
            else:
                i = self.eng.dma_start(out=out, in_=in_)
        return self.bump(i, sem, 16 if sem else 0)

    def gather(self, out, table_ap, idx_ap, sem=None):
        i = None
        if self.e == "gpsimd":
            i = self.eng.indirect_dma_start(
                out=out, out_offset=None, in_=table_ap,
                in_offset=IndirectOffsetOnAxis(idx_ap, 0))
        return self.bump(i, sem, 16 if sem else 0)

    def coll(self, kind, op, groups, ins, outs, sem=None):
        i = None
        if self.e == "gpsimd":
            i = self.eng.collective_compute(kind, op, replica_groups=groups,
                                            ins=ins, outs=outs)
        return self.bump(i, sem, 1 if sem else 0)


def build_nc(stage="full", core0_groups=None):
    nc = bass.Bass("TRN2", target_bir_lowering=False)

    def din(name, shape, dt=F32):
        return nc.dram_tensor(name, list(shape), dt, kind="ExternalInput").ap()

    x_in = din("x_in", (TC, D))
    wqT_in = din("wqT", (D, DH))
    wkT_in = din("wkT", (D, DH))
    wvT_in = din("wvT", (D, D))
    bqp_in = din("bqp", (128, 2))
    bkp_in = din("bkp", (128, 2))
    bvp_in = din("bvp", (128, 8))
    k0p_in = din("k0p", (128, 2))
    v0p_in = din("v0p", (128, 8))
    qproj_in = din("qproj", (128, D))
    routerT_in = din("routerT", (128, 48))
    auxkT_in = din("auxkT", (128, NSLOT))
    maskp_in = din("maskp", (128, 32))
    w1T_in = din("w1T", (D, 256))
    b1p_in = din("b1p", (128, 2))
    w2p_in = din("w2p", (128, 2))
    b2v_in = din("b2v_in", (128, 1))
    gw1v_in = din("gw1v_in", (128, 1))
    gbv_in = din("gbv_in", (128, 1))
    aux_in = din("aux_vals", (NSLOT, D))
    id16_in = din("id16", (128, 128), F16)

    x_out = nc.dram_tensor("x_out", [TC, D], F32, kind="ExternalOutput").ap()

    aux_bf = nc.dram_tensor("aux_bf", [NSLOT, D], BF16)
    kvv_inh = [nc.dram_tensor(f"kvv_in{i}", [TC // 2, D], F16) for i in range(2)]
    kvv_outh = [nc.dram_tensor(f"kvv_out{i}", [S // 2, D], F16) for i in range(2)]
    kvk_inh = [nc.dram_tensor(f"kvk_in{i}", [2 * DH, TC // 2], F16)
               for i in range(2)]
    kvk_outh = [nc.dram_tensor(f"kvk_out{i}", [8 * DH, S // 8], F16)
                for i in range(2)]
    ar_in = [nc.dram_tensor(f"ar_in{i}", [1, 1], F32) for i in range(2)]
    ar_out = [nc.dram_tensor(f"ar_out{i}", [1, 1], F32) for i in range(2)]

    A = {}

    def sbuf(name, shape, dt):
        A[name] = nc.alloc_sbuf_tensor(name, list(shape), dt).ap()
        return A[name]

    x_sb = sbuf("x_sb", (128, NT * D), F32)
    stage32 = sbuf("stage32", (128, 8192), F32)
    scrF = sbuf("scrF", (128, NSLOT), F32)
    scrG = sbuf("scrG", (128, 8192), BF16)
    arenaW = sbuf("arenaW", (128, 8192), F16)
    scrH_hi = sbuf("scrH_hi", (128, GT * D), F16)
    scrH_lo = sbuf("scrH_lo", (128, GT * D), F16)
    hfm_hi = sbuf("hfm_hi", (128, 8 * GTOK), F16)
    hfm_lo = sbuf("hfm_lo", (128, 8 * GTOK), F16)
    vbuf = sbuf("vbuf", (128, 2048), F16)
    qfm_hi = sbuf("qfm_hi", (128, TC), F16)
    qfm_lo = sbuf("qfm_lo", (128, TC), F16)
    qtm16 = sbuf("qtm16", (128, 2 * DH), F16)
    ktm16 = sbuf("ktm16", (128, 4 * DH), F16)
    pt16 = sbuf("pt16", (128, 2 * QB), F16)
    rkT_hi = sbuf("rkT_hi", (50, NSLOT), F16)
    rkT_lo = sbuf("rkT_lo", (48, NSLOT), F16)
    w1T16 = sbuf("w1T16", (128, 8 * 256), F16)
    wq16 = sbuf("wq16", (128, 8 * 2 * DH), F16)
    wk16 = sbuf("wk16", (128, 8 * 2 * DH), F16)
    wrqT16 = sbuf("wrqT16", (128, 8 * 96), F16)
    rt16 = sbuf("rt16", (128, 96), F16)
    w2T16 = sbuf("w2T16", (128, 2), F16)
    id16 = sbuf("id16s", (128, 128), F16)
    ones16 = sbuf("ones16", (128, 2), F16)
    ones16r = sbuf("ones16r", (2, 128), F16)
    bq16 = sbuf("bq16", (2, DH), F16)
    bk16 = sbuf("bk16", (2, DH), F16)
    k0hl = sbuf("k0hl", (128, 4), F16)
    k0aug = sbuf("k0aug", (4, 1), F16)
    b1c = sbuf("b1c", (128, 2), F32)
    b2v = sbuf("b2v", (128, 1), F32)
    gw1v = sbuf("gw1v", (128, 1), F32)
    gbv = sbuf("gbv", (128, 1), F32)
    sA = sbuf("sA", (128, 64), F32)
    sM = sbuf("sM", (128, 64), F32)
    one1 = sbuf("one1", (1, 16), F32)
    lv16 = sbuf("lv16", (128, 16), F16)
    invb = sbuf("invb", (128, 1), F32)
    top8 = sbuf("top8", (128, 8), F32)
    idx8 = sbuf("idx8", (128, 8), U32)
    w8 = sbuf("w8", (128, 16), F32)
    zsb = sbuf("zsb", (128, 16), F32)
    psk = sbuf("psk", (128, NT), F32)
    psk16 = sbuf("psk16", (128, NT), F16)
    ssink = sbuf("ssink", (128, NT), F32)
    mrow = sbuf("mrow", (128, NT), F32)

    h1pre = arenaW[:, 0:4096]
    rq_hi = arenaW[0:50, 4096:6144]
    rq_lo = arenaW[50:98, 6144:8192]
    rq_lo = arenaW[0:48, 6144:8192]
    qp16 = vbuf
    kfm_hi = stage32.bitcast(F16)[:, 0:8192]
    kfm_lo = stage32.bitcast(F16)[:, 8192:16384]
    kfs_hi = stage32.bitcast(F16)[:, 8192:8192 + 2048]
    kfs_lo = stage32.bitcast(F16)[:, 10240:10240 + 2048]
    kfs_ra = stage32.bitcast(F16)[0:1, 12288:12288 + 2048]
    kfs_rb = stage32.bitcast(F16)[64:65, 12288:12288 + 2048]
    fwd16 = scrF.bitcast(BF16)
    k129 = scrG.bitcast(F16)[0:4, 0:8192]
    wv16 = arenaW
    scrG16 = scrG.bitcast(F16)
    qaug = scrH_hi[0:4, 0:TC]
    v0r16 = scrG16[64:65, 6144:6144 + D]
    pskT = scrG16[64:65, 4096:6144]
    bv16 = scrG16[64:66, 2048:2048 + D]
    ones64 = scrG16[64:66, 3072:3200]
    stg32r = scrG16[32:33, 0:8192]
    stg64r = scrG16[64:65, 0:2048]
    tmp32 = scrF[:, 0:D]

    sem_names = ["ld", "te", "sc", "ve", "gp", "cc", "vs", "st", "aux", "kv"]
    sems = {s: nc.alloc_semaphore(s) for s in sem_names}

    x_in_t = x_in.rearrange("(t p) d -> t p d", p=128)
    x_out_t = x_out.rearrange("(t p) d -> t p d", p=128)

    def program(p: Pass):
        emit_prep(p)
        emit_ffn(p, 0)
        if stage in ("attn", "full"):
            emit_attn(p)
        if stage == "full":
            emit_ffn(p, 1, skip_stats=True)
        emit_out(p)

    def hilo_(p, dst_hi, dst_lo, src32):
        p.wait("scalar", "sc")
        p.act(dst_hi, src32, AF.Copy, sem="sc")
        p.wait("vector", "sc")
        p.v("tensor_tensor", dst_lo, src32, dst_hi, ALU.subtract, sem="ve")

    def emit_prep(p: Pass):
        for i in range(4):
            p.dma("gpsimd", aux_bf.ap()[i * 1024:(i + 1) * 1024, :],
                  aux_in[i * 1024:(i + 1) * 1024, :], sem="aux")
        for t in range(NT):
            p.dma("sync", x_sb[:, t * D:(t + 1) * D], x_in_t[t], sem="ld")
        p.dma("sync", id16[:, :], id16_in, sem="ld")
        p.dma("sync", b2v[:, :], b2v_in, sem="ld")
        p.dma("sync", gw1v[:, :], gw1v_in, sem="ld")
        p.dma("sync", gbv[:, :], gbv_in, sem="ld")
        p.dma("sync", b1c[:, :], b1p_in, sem="ld")
        p.dma("sync", stage32[:, 0:D], qproj_in, sem="ld")
        p.dma("sync", stage32[:, D:D + 48], routerT_in, sem="ld")
        p.dma("sync", stage32[:, D + 48:D + 48 + 2048]
              .rearrange("p (c d) -> p c d", c=8),
              w1T_in.rearrange("(c p) d -> p c d", p=128), sem="ld")
        p.dma("sync", stage32[:, 3120:3120 + NSLOT], auxkT_in, sem="ld")
        p.dma("sync", stage32[:, 7216:7218], w2p_in, sem="ld")
        p.dma("sync", stage32[:, 7218:7250], maskp_in, sem="ld")
        p.wait("scalar", "ld")
        p.wait("vector", "ld")
        p.v("memset", ones16[:, :], 1.0)
        p.v("memset", ones16r[:, :], 1.0)
        p.v("memset", rq_hi[0:50, :], 1.0, sem="ve")
        hilo_(p, qp16[:, 0:D], qp16[:, D:2 * D], stage32[:, 0:D])
        hilo_(p, rt16[:, 0:48], rt16[:, 48:96], stage32[:, D:D + 48])
        for ch in range(8):
            p.act(w1T16[:, ch * 256:(ch + 1) * 256],
                  stage32[:, D + 48 + ch * 256:D + 48 + (ch + 1) * 256],
                  AF.Copy, sem="sc")
        p.act(w2T16[:, 0:2], stage32[:, 7216:7218], AF.Copy, sem="sc")
        mh16 = scrF.bitcast(F16)[:, 0:32]
        ml16 = scrF.bitcast(F16)[:, 32:64]
        hilo_(p, mh16, ml16, stage32[:, 7218:7250])
        p.wait("sync", "ve")
        p.wait("sync", "sc")
        p.dma("sync", rkT_hi[48:49, :], mh16, sem="ld")
        p.dma("sync", rkT_hi[49:50, :], ml16, sem="ld")
        auxk_hi = scrG.bitcast(F16)[:, 0:NSLOT]
        auxk_lo = scrG.bitcast(F16)[:, NSLOT:2 * NSLOT]
        asrc = stage32[:, 3120:3120 + NSLOT]
        for q in range(4):
            sl = slice(q * 1024, (q + 1) * 1024)
            hilo_(p, auxk_hi[:, sl], auxk_lo[:, sl], asrc[:, sl])
        p.wait("tensor", "sc")
        p.wait("tensor", "ve")
        p.wait("tensor", "ld")
        p.wait("scalar", "ve")
        for n in range(8):
            ps = psA[0:48, (n % 4) * 512:(n % 4) * 512 + 512]
            rh = auxk_hi[:, n * 512:(n + 1) * 512]
            rl = auxk_lo[:, n * 512:(n + 1) * 512]
            p.mm(ps, rt16[:, 0:48], rh, start=True, stop=False)
            p.mm(ps, rt16[:, 0:48], rl, start=False, stop=False)
            p.mm(ps, rt16[:, 48:96], rh, start=False, stop=True, sem="te")
            p.wait("scalar", "te")
            p.act(stage32[0:48, n * 512:(n + 1) * 512], ps, AF.Copy,
                  scale=RSQ48, sem="sc")
            p.wait("tensor", "sc")
        for n in range(8):
            sl = slice(n * 512, (n + 1) * 512)
            hilo_(p, rkT_hi[0:48, sl], rkT_lo[0:48, sl], stage32[0:48, sl])
        for n in range(2):
            ps = psB[0:48, n * 512:(n + 1) * 512]
            qh = qp16[:, n * 512:(n + 1) * 512]
            ql = qp16[:, D + n * 512:D + (n + 1) * 512]
            p.mm(ps, rt16[:, 0:48], qh, start=True, stop=False)
            p.mm(ps, rt16[:, 0:48], ql, start=False, stop=False)
            p.mm(ps, rt16[:, 48:96], qh, start=False, stop=True, sem="te")
        p.wait("scalar", "te")
        p.wait("scalar", "ve")
        for n in range(2):
            p.act(stage32[0:48, 4096 + n * 512:4096 + (n + 1) * 512],
                  psB[0:48, n * 512:(n + 1) * 512], AF.Copy, sem="sc")
        wrq32 = stage32[0:48, 4096:4096 + 1024]
        wrq_h = scrH_hi[0:48, 0:1024]
        wrq_l = scrH_lo[0:48, 0:1024]
        hilo_(p, wrq_h, wrq_l, wrq32)
        p.wait("tensor", "sc")
        p.wait("tensor", "ve")
        psA16 = psA.bitcast(F16)
        for ch in range(8):
            p.tr(psA16[0:128, 0:48], wrq_h[:, ch * 128:(ch + 1) * 128],
                 id16[0:48, 0:48])
            p.tr(psA16[0:128, 1024:1072], wrq_l[:, ch * 128:(ch + 1) * 128],
                 id16[0:48, 0:48], sem="te")
            p.wait("scalar", "te")
            p.act(wrqT16[:, ch * 96:ch * 96 + 48], psA16[0:128, 0:48], AF.Copy,
                  sem="sc")
            p.act(wrqT16[:, ch * 96 + 48:ch * 96 + 96], psA16[0:128, 1024:1072],
                  AF.Copy, sem="sc")
            p.wait("tensor", "sc")
        p.wait("gpsimd", "aux")

    def emit_ffn(p: Pass, call, skip_stats=False):
        if not skip_stats:
            p.wait("scalar", "ve")
            p.wait("vector", "sc")
            for t in range(NT):
                xt = x_sb[:, t * D:(t + 1) * D]
                p.v("reduce_sum", out=sA[:, t:t + 1], in_=xt, axis=AX, sem="ve")
                p.wait("scalar", "sc")
                p.act(tmp32[:, :], xt, AF.Square, accum_out=sA[:, 16 + t:17 + t],
                      sem="sc")
        p.wait("vector", "sc")
        p.wait("scalar", "ve")
        p.v("tensor_scalar", sA[:, 32:48], sA[:, 0:16], 1.0 / D, None, ALU.mult)
        p.v("tensor_scalar", sA[:, 48:64], sA[:, 16:32], 1.0 / D, None, ALU.mult)
        p.v("tensor_tensor", tmp32[:, 0:16], sA[:, 32:48], sA[:, 32:48], ALU.mult)
        p.v("tensor_sub", sA[:, 48:64], sA[:, 48:64], tmp32[:, 0:16])
        p.v("tensor_scalar", tmp32[:, 0:16], sA[:, 48:64], EPS_LN, None, ALU.add,
            sem="ve")
        p.wait("scalar", "ve")
        p.act(tmp32[:, 16:32], tmp32[:, 0:16], AF.Ln, sem="sc")
        p.wait("scalar", "sc")
        p.act(sM[:, 0:16], tmp32[:, 16:32], AF.Exp, scale=-0.5, sem="sc")
        p.wait("vector", "sc")
        p.v("tensor_tensor", tmp32[:, 48:64], sA[:, 32:48], sM[:, 0:16], ALU.mult)
        p.v("tensor_scalar", sM[:, 16:32], tmp32[:, 48:64], -1.0, None, ALU.mult)
        p.v("reciprocal", out=tmp32[:, 16:32], in_=tmp32[:, 0:16])
        p.v("tensor_tensor", tmp32[:, 32:48], sA[:, 48:64], tmp32[:, 16:32],
            ALU.mult, sem="ve")
        p.wait("scalar", "ve")
        p.act(sM[:, 32:48], tmp32[:, 32:48], AF.Ln, bias=1.0, sem="sc")
        p.wait("vector", "sc")
        p.v("tensor_copy", out=lv16[:, :], in_=sM[:, 32:48], sem="ve")
        p.wait("tensor", "ve")
        p.mm(psB[0:1, 0:16], ones16[:, 0:1], lv16[:, :], sem="te")
        p.wait("vector", "te")
        p.v("reduce_sum", out=one1[0:1, 1:2], in_=psB[0:1, 0:16], axis=AX,
            sem="ve")
        p.wait("sync", "ve")
        p.dma("sync", ar_in[call].ap(), one1[0:1, 1:2], sem="st")
        p.wait("gpsimd", "st")
        p.coll("AllReduce", ALU.add, [[0, 1, 2, 3, 4, 5, 6, 7]],
               [ar_in[call].ap().opt()], [ar_out[call].ap().opt()], sem="cc")
        p.wait("vector", "st")
        if call == 1:
            p.v("memset", rq_hi[0:50, :], 1.0, sem="ve")
        for g in range(NG):
            for ti in range(GT):
                t = g * GT + ti
                xt = x_sb[:, t * D:(t + 1) * D]
                p.wait("scalar", "ve")
                p.wait("scalar", "sc")
                p.act(tmp32[:, :], xt, AF.Copy, bias=sM[:, 16 + t:17 + t],
                      scale=sM[:, 0 + t:1 + t], sem="sc")
                p.wait("scalar", "sc")
                p.act(scrH_hi[:, ti * D:(ti + 1) * D], tmp32[:, :], AF.Copy,
                      sem="sc")
                p.wait("vector", "sc")
                p.v("tensor_tensor", scrH_lo[:, ti * D:(ti + 1) * D], tmp32[:, :],
                    scrH_hi[:, ti * D:(ti + 1) * D], ALU.subtract, sem="ve")
                p.v("scalar_tensor_tensor", xt, tmp32[:, :], 0.5, xt,
                    ALU.mult, ALU.add, sem="ve")
            p.wait("tensor", "ve")
            psA16_ = psA.bitcast(F16)
            for hl, (hsrc, dst) in enumerate(((scrH_hi, hfm_hi), (scrH_lo, hfm_lo))):
                for ch in range(8):
                    for ti in range(GT):
                        p.tr(psA16_[:, (2 * ti + hl) * 1024:(2 * ti + hl) * 1024 + 128],
                             hsrc[:, ti * D + ch * 128:ti * D + (ch + 1) * 128],
                             id16[:, :], sem="te")
                    p.wait("scalar", "te")
                    for ti in range(GT):
                        p.act(dst[:, ch * GTOK + ti * 128:ch * GTOK + (ti + 1) * 128],
                              psA16_[:, (2 * ti + hl) * 1024:(2 * ti + hl) * 1024 + 128],
                              AF.Copy, sem="sc")
                    p.wait("tensor", "sc")
            for m in range(2):
                ps = psB[:, m * 512:m * 512 + GTOK]
                for ch in range(8):
                    p.mm(ps, w1T16[:, ch * 256 + m * 128:ch * 256 + m * 128 + 128],
                         hfm_hi[:, ch * GTOK:(ch + 1) * GTOK],
                         start=(ch == 0), stop=(ch == 7), sem="te")
                p.wait("scalar", "te")
                p.act(h1pre[:, m * 2048 + g * GTOK:m * 2048 + (g + 1) * GTOK], ps,
                      AF.Gelu, bias=b1c[:, m:m + 1], sem="sc")
            p.wait("tensor", "sc")
            for ti in range(GT):
                t = g * GT + ti
                ps = psB[:, 1536 + t:1536 + t + 1]
                for m in range(2):
                    p.mm(ps, h1pre[:, m * 2048 + g * GTOK + ti * 128:
                                   m * 2048 + g * GTOK + (ti + 1) * 128],
                         w2T16[:, m:m + 1],
                         start=(t == 0 and m == 0), stop=(t == NT - 1 and m == 1),
                         sem="te")
            ps = psA[0:48, 0:GTOK]
            for ch in range(8):
                wh = wrqT16[:, ch * 96:ch * 96 + 48]
                wl = wrqT16[:, ch * 96 + 48:ch * 96 + 96]
                hh = hfm_hi[:, ch * GTOK:(ch + 1) * GTOK]
                hl_ = hfm_lo[:, ch * GTOK:(ch + 1) * GTOK]
                p.mm(ps, wh, hh, start=(ch == 0), stop=False)
                p.mm(ps, wh, hl_, start=False, stop=False)
                p.mm(ps, wl, hh, start=False, stop=(ch == 7), sem="te")
            p.wait("scalar", "te")
            p.act(rq_hi[0:48, g * GTOK:(g + 1) * GTOK], ps, AF.Copy, sem="sc")
            p.wait("vector", "sc")
            p.v("tensor_tensor", rq_lo[0:48, g * GTOK:(g + 1) * GTOK], ps,
                rq_hi[0:48, g * GTOK:(g + 1) * GTOK], ALU.subtract, sem="ve")
            p.wait("tensor", "ve")
        p.wait("scalar", "te")
        p.act(tmp32[:, 0:16], psB[:, 1536:1552], AF.Copy, bias=b2v[:, 0:1],
              sem="sc")
        p.wait("scalar", "sc")
        p.act(tmp32[:, 16:32], tmp32[:, 0:16], AF.Exp, scale=-1.0, sem="sc")
        p.wait("vector", "sc")
        p.v("tensor_scalar", tmp32[:, 16:32], tmp32[:, 16:32], 1.0, None, ALU.add)
        p.v("reciprocal", out=sM[:, 48:64], in_=tmp32[:, 16:32], sem="ve")
        p.wait("sync", "cc")
        p.dma("sync", one1[0:1, 2:3], ar_out[call].ap(), sem="ld")
        p.wait("sync", "ld")
        p.wait("vector", "ld")
        p.v("tensor_scalar", one1[0:1, 3:4], one1[0:1, 2:3], 1.0 / (B * S), 1e-6,
            ALU.mult, op1=ALU.add)
        p.v("reciprocal", out=one1[0:1, 4:5], in_=one1[0:1, 3:4])
        p.v("tensor_scalar", one1[0:1, 5:6], one1[0:1, 4:5], 0.5, None, ALU.mult)
        p.v("tensor_copy", out=one1.bitcast(F16)[0:1, 12:13], in_=one1[0:1, 5:6],
            sem="ve")
        p.wait("tensor", "ve")
        p.mm(psB[:, 1024:1025], ones16r[0:1, :], one1.bitcast(F16)[0:1, 12:13],
             sem="te")
        p.wait("scalar", "te")
        p.act(invb[:, :], psB[:, 1024:1025], AF.Copy, sem="sc")
        p.wait("vector", "sc")
        p.v("tensor_scalar", sM[:, 32:48], sM[:, 32:48], invb[:, 0:1], None,
            ALU.mult)
        p.v("scalar_tensor_tensor", sM[:, 48:64], sM[:, 48:64], 2.5,
            sM[:, 32:48], ALU.mult, ALU.add)
        p.v("tensor_scalar", sM[:, 48:64], sM[:, 48:64], 5.0, 0.0, ALU.min,
            op1=ALU.max, sem="ve")
        p.wait("scalar", "ve")
        p.act(tmp32[:, 16:32], sM[:, 48:64], AF.Copy, scale=gw1v[:, 0:1],
              bias=gbv[:, 0:1], sem="sc")
        p.wait("scalar", "sc")
        p.act(tmp32[:, 32:48], tmp32[:, 16:32], AF.Exp, scale=-1.0, sem="sc")
        p.wait("vector", "sc")
        p.v("tensor_scalar", tmp32[:, 32:48], tmp32[:, 32:48], 1.0, None, ALU.add)
        p.v("reciprocal", out=sM[:, 48:64], in_=tmp32[:, 32:48], sem="ve")
        a4_ve = p.c["ve"]
        idx_ve = {}
        agg_ve = {}
        w8b = [w8[:, 0:8], w8[:, 8:16]]

        def gslot(t, k):
            if t % 2 == 0:
                return scrG[:, k * D:(k + 1) * D]
            half = [scrH_hi, scrH_lo, hfm_hi, hfm_lo][k // 2].bitcast(BF16)
            return half[:, (k % 2) * D:(k % 2) * D + D]

        def emit_scores_topk(t):
            g, ti = t // GT, t % GT
            half = psA if t % 2 == 0 else psB
            rq_s = slice(g * GTOK + ti * 128, g * GTOK + (ti + 1) * 128)
            for n in range(8):
                ps = half[:, (n % 4) * 512:(n % 4) * 512 + 512]
                p.mm(ps, rq_hi[0:50, rq_s], rkT_hi[0:50, n * 512:(n + 1) * 512],
                     start=True, stop=False)
                p.mm(ps, rq_hi[0:48, rq_s], rkT_lo[0:48, n * 512:(n + 1) * 512],
                     start=False, stop=False)
                p.mm(ps, rq_lo[0:48, rq_s], rkT_hi[0:48, n * 512:(n + 1) * 512],
                     start=False, stop=True, sem="te")
                if n % 4 == 3:
                    p.wait("scalar", "te")
                    p.wait("scalar", "ve", val=max(idx_ve.get(t - 1, 0), a4_ve))
                    p.act(scrF[:, (n // 4) * 2048:(n // 4) * 2048 + 2048],
                          half[:, 0:2048], AF.Copy, sem="sc")
                    p.wait("tensor", "sc")
            p.wait("vector", "sc")
            p.wait("vector", "gp", val=gp_done.get(t - 1, 0))
            p.v("max", out=top8[:, :], in_=scrF[:, :])
            p.v("tensor_scalar", w8b[t % 2], top8[:, :], top8[:, 0:1], None,
                ALU.subtract)
            p.v("max_index", out=idx8[:, :], in_max=top8[:, :],
                in_values=scrF[:, :], sem="ve")
            idx_ve[t] = p.c["ve"]
            p.wait("gpsimd", "ve", val=idx_ve[t])
            if t >= 2:
                p.wait("gpsimd", "ve", val=agg_ve.get(t - 2, 0))
            for k in range(8):
                p.gather(gslot(t, k), aux_bf.ap(), idx8[:, k:k + 1], sem="gp")
            p.wait("scalar", "ve")
            p.act(w8b[t % 2], w8b[t % 2], AF.Exp, sem="sc")
            p.wait("vector", "sc")
            p.v("reduce_sum", out=zsb[:, 8 + (t % 2):9 + (t % 2)], in_=w8b[t % 2],
                axis=AX)
            p.v("reciprocal", out=zsb[:, 10 + (t % 2):11 + (t % 2)],
                in_=zsb[:, 8 + (t % 2):9 + (t % 2)])
            p.v("tensor_scalar", zsb[:, 12 + (t % 2):13 + (t % 2)],
                zsb[:, 10 + (t % 2):11 + (t % 2)],
                sM[:, 48 + t:49 + t], 0.5, ALU.mult, op1=ALU.mult)
            p.v("tensor_scalar", w8b[t % 2], w8b[t % 2],
                zsb[:, 12 + (t % 2):13 + (t % 2)], None, ALU.mult, sem="ve")

        acc16 = vbuf[:, 0:1024]
        def emit_agg(t):
            p.wait("vector", "gp", val=gp_done[t])
            xt = x_sb[:, t * D:(t + 1) * D]
            p.v("tensor_scalar", acc16, gslot(t, 0), w8b[t % 2][:, 0:1], None,
                ALU.mult)
            for k in range(1, 8):
                p.v("scalar_tensor_tensor", acc16, gslot(t, k),
                    w8b[t % 2][:, k:k + 1], acc16, ALU.mult, ALU.add)
            p.v("scalar_tensor_tensor", xt, acc16, 1.0, xt, ALU.mult, ALU.add,
                sem="ve")
            agg_ve[t] = p.c["ve"]
            if call == 0:
                p.v("reduce_sum", out=sA[:, t:t + 1], in_=xt, axis=AX, sem="ve")
                p.wait("scalar", "ve", val=p.c["ve"])
                p.act(qfm_hi.bitcast(F32)[:, 0:1024], xt, AF.Square,
                      accum_out=sA[:, 16 + t:17 + t], sem="sc")

        gp_done = {}
        for t in range(NT):
            emit_scores_topk(t)
            gp_done[t] = p.c["gp"]
            if t >= 1:
                emit_agg(t - 1)
        emit_agg(NT - 1)

    def emit_attn(p: Pass):
        p.dma("sync", stage32.rearrange("p (c d) -> p c d", c=8),
              wvT_in.rearrange("(c p) d -> p c d", p=128), sem="ld")
        p.wait("scalar", "ld")
        p.wait("scalar", "te")
        p.wait("scalar", "ve")
        p.wait("vector", "sc")
        for ch in range(8):
            p.act(wv16[:, ch * D:(ch + 1) * D], stage32[:, ch * D:(ch + 1) * D],
                  AF.Copy)
        p.wait("sync", "sc")
        p.dma("sync", stage32[:, 0:8 * DH].rearrange("p (c d) -> p c d", c=8),
              wqT_in.rearrange("(c p) d -> p c d", p=128), sem="ld")
        p.dma("sync", stage32[:, 8 * DH:16 * DH].rearrange("p (c d) -> p c d", c=8),
              wkT_in.rearrange("(c p) d -> p c d", p=128), sem="ld")
        p.dma("sync", stage32[:, 4096:4098], bqp_in, sem="ld")
        p.dma("sync", stage32[:, 4098:4100], bkp_in, sem="ld")
        p.dma("sync", stage32[:, 4100:4102], k0p_in, sem="ld")
        p.dma("sync", stage32[:, 4102:4110], v0p_in, sem="ld")
        p.dma("sync", stage32[:, 4110:4118], bvp_in, sem="ld")
        p.wait("scalar", "ld")
        p.wait("vector", "ld")
        for ch in range(8):
            hilo_(p, wq16[:, ch * 258:ch * 258 + DH],
                  wq16[:, ch * 258 + DH:ch * 258 + 258],
                  stage32[:, ch * DH:(ch + 1) * DH])
            hilo_(p, wk16[:, ch * 258:ch * 258 + DH],
                  wk16[:, ch * 258 + DH:ch * 258 + 258],
                  stage32[:, 8 * DH + ch * DH:8 * DH + (ch + 1) * DH])
        bq_c = scrG16[:, 7680:7684]
        bk_c = scrG16[:, 7684:7688]
        v0_c = scrG16[:, 7688:7704]
        bv_c = scrG16[:, 7704:7720]
        hilo_(p, bq_c[:, 0:2], bq_c[:, 2:4], stage32[:, 4096:4098])
        hilo_(p, bk_c[:, 0:2], bk_c[:, 2:4], stage32[:, 4098:4100])
        hilo_(p, k0hl[:, 0:2], k0hl[:, 2:4], stage32[:, 4100:4102])
        hilo_(p, v0_c[:, 0:8], v0_c[:, 8:16], stage32[:, 4102:4110])
        hilo_(p, bv_c[:, 0:8], bv_c[:, 8:16], stage32[:, 4110:4118])
        p.wait("sync", "ve")
        p.wait("sync", "sc")
        for r, cc in ((0, 0), (1, 2)):
            p.dma("sync", bq16[r:r + 1, 0:128], bq_c[:, cc:cc + 1], sem="ld")
            p.dma("sync", bq16[r:r + 1, 128:129], bq_c[0:1, cc + 1:cc + 2], sem="ld")
            p.dma("sync", bk16[r:r + 1, 0:128], bk_c[:, cc:cc + 1], sem="ld")
            p.dma("sync", bk16[r:r + 1, 128:129], bk_c[0:1, cc + 1:cc + 2], sem="ld")
        p.dma("sync", v0r16[0:1, :], v0_c[:, 0:8], sem="ld")
        p.dma("sync", bv16[0:1, :], bv_c[:, 0:8], sem="ld")
        p.dma("sync", bv16[1:2, :], bv_c[:, 8:16], sem="ld")
        p.dma("sync", k0aug[0:1, :], k0hl[0:1, 1:2], sem="ld")
        p.dma("sync", k0aug[1:2, :], k0hl[0:1, 3:4], sem="ld")
        p.dma("sync", k0aug[2:3, :], k0hl[0:1, 1:2], sem="ld")
        p.wait("sync", "ld")
        p.wait("tensor", "ld")
        p.v("memset", ones64, 1.0)
        p.wait("vector", "sc")
        p.v("tensor_scalar", sA[:, 32:48], sA[:, 0:16], 1.0 / D, None, ALU.mult)
        p.v("tensor_scalar", sA[:, 48:64], sA[:, 16:32], 1.0 / D, None, ALU.mult)
        p.v("tensor_tensor", tmp32[:, 0:16], sA[:, 32:48], sA[:, 32:48], ALU.mult)
        p.v("tensor_sub", sA[:, 48:64], sA[:, 48:64], tmp32[:, 0:16])
        p.v("tensor_scalar", tmp32[:, 0:16], sA[:, 48:64], EPS_LN, None, ALU.add,
            sem="ve")
        p.wait("scalar", "ve")
        p.act(tmp32[:, 16:32], tmp32[:, 0:16], AF.Ln)
        p.act(sM[:, 0:16], tmp32[:, 16:32], AF.Exp, scale=-0.5, sem="sc")
        p.wait("vector", "sc")
        p.v("tensor_tensor", tmp32[:, 48:64], sA[:, 32:48], sM[:, 0:16], ALU.mult)
        p.v("tensor_scalar", sM[:, 16:32], tmp32[:, 48:64], -1.0, None, ALU.mult,
            sem="ve")
        cc_kb = None
        for g in range(NG):
            for ti in range(GT):
                t = g * GT + ti
                xt = x_sb[:, t * D:(t + 1) * D]
                p.wait("scalar", "ve")
                p.act(tmp32[:, :], xt, AF.Copy, bias=sM[:, 16 + t:17 + t],
                      scale=sM[:, 0 + t:1 + t])
                p.wait("scalar", "sc")
                p.act(scrH_hi[:, ti * D:(ti + 1) * D], tmp32[:, :], AF.Copy,
                      sem="sc")
                p.wait("vector", "sc")
                p.v("tensor_tensor", scrH_lo[:, ti * D:(ti + 1) * D], tmp32[:, :],
                    scrH_hi[:, ti * D:(ti + 1) * D], ALU.subtract)
                p.v("tensor_add", xt, xt, tmp32[:, :], sem="ve")
            p.wait("tensor", "ve")
            psA16_ = psA.bitcast(F16)
            for hl, (hsrc, dst) in enumerate(((scrH_hi, hfm_hi), (scrH_lo, hfm_lo))):
                for ch in range(8):
                    for ti in range(GT):
                        p.tr(psA16_[:, (2 * ti + hl) * 1024:(2 * ti + hl) * 1024 + 128],
                             hsrc[:, ti * D + ch * 128:ti * D + (ch + 1) * 128],
                             id16[:, :], sem="te")
                    p.wait("scalar", "te")
                    for ti in range(GT):
                        p.act(dst[:, ch * GTOK + ti * 128:ch * GTOK + (ti + 1) * 128],
                              psA16_[:, (2 * ti + hl) * 1024:(2 * ti + hl) * 1024 + 128],
                              AF.Copy, sem="sc")
                    p.wait("tensor", "sc")
            for ti in range(GT):
                t = g * GT + ti
                tsl = slice(ti * 128, (ti + 1) * 128)
                psq = psB[:, 0:DH]
                for ch in range(8):
                    hh = hfm_hi[:, ch * GTOK + ti * 128:ch * GTOK + (ti + 1) * 128]
                    hl_ = hfm_lo[:, ch * GTOK + ti * 128:ch * GTOK + (ti + 1) * 128]
                    wh = wq16[:, ch * 258:ch * 258 + DH]
                    wl = wq16[:, ch * 258 + DH:ch * 258 + 258]
                    p.mm(psq, hh, wh, start=(ch == 0), stop=False)
                    p.mm(psq, hl_, wh, start=False, stop=False)
                    p.mm(psq, hh, wl, start=False, stop=False)
                p.mm(psq, ones16r[0:2, 0:128], bq16[0:2, :], start=False,
                     stop=True, sem="te")
                p.wait("scalar", "te")
                p.act(qtm16[:, 0:DH], psq, AF.Copy, sem="sc")
                p.wait("vector", "sc")
                p.v("tensor_tensor", qtm16[:, DH:2 * DH], psq, qtm16[:, 0:DH],
                    ALU.subtract, sem="ve")
                psk_ = psB[:, 512:512 + DH]
                for ch in range(8):
                    hh = hfm_hi[:, ch * GTOK + ti * 128:ch * GTOK + (ti + 1) * 128]
                    hl_ = hfm_lo[:, ch * GTOK + ti * 128:ch * GTOK + (ti + 1) * 128]
                    wh = wk16[:, ch * 258:ch * 258 + DH]
                    wl = wk16[:, ch * 258 + DH:ch * 258 + 258]
                    p.mm(psk_, hh, wh, start=(ch == 0), stop=False)
                    p.mm(psk_, hl_, wh, start=False, stop=False)
                    p.mm(psk_, hh, wl, start=False, stop=False)
                p.mm(psk_, ones16r[0:2, 0:128], bk16[0:2, :], start=False,
                     stop=True, sem="te")
                p.wait("scalar", "te")
                p.wait("scalar", "kv")
                p.act(ktm16[:, 0:DH], psk_, AF.Copy)
                p.wait("scalar", "sc")
                p.wait("vector", "sc")
                p.v("tensor_tensor", ktm16[:, DH:2 * DH], psk_, ktm16[:, 0:DH],
                    ALU.subtract, sem="ve")
                for half in range(2):
                    psv = psA[:, half * 512:(half + 1) * 512]
                    for ch in range(8):
                        hh = hfm_hi[:, ch * GTOK + ti * 128:ch * GTOK + (ti + 1) * 128]
                        wv_ = wv16[:, ch * D + half * 512:ch * D + (half + 1) * 512]
                        p.mm(psv, hh, wv_, start=(ch == 0), stop=False)
                    p.mm(psv, ones64, bv16[0:2, half * 512:(half + 1) * 512],
                         start=False, stop=True, sem="te")
                p.wait("scalar", "te")
                p.wait("scalar", "kv")
                p.act(vbuf[:, 0:512], psA[:, 0:512], AF.Copy)
                p.wait("scalar", "sc")
                p.act(vbuf[:, 512:1024], psA[:, 512:1024], AF.Copy, sem="sc")
                p.wait("sync", "sc")
                p.dma("sync", kvv_inh[t // 8].ap()[(t % 8) * 128:(t % 8 + 1) * 128, :],
                      vbuf[:, 0:1024], sem="kv")
                p.wait("tensor", "ve")
                p.tr(psA16_[:, 3072:3072 + 128], qtm16[:, 0:128], id16[:, :])
                p.tr(psA16_[:, 3072 + 128:3072 + 256], qtm16[:, DH:DH + 128],
                     id16[:, :])
                p.tr(psA16_[0:1, 3072 + 256:3072 + 384], qtm16[:, 128:129],
                     id16[:, :])
                p.tr(psA16_[0:1, 3072 + 384:3072 + 512], qtm16[:, 2 * DH - 1:2 * DH],
                     id16[:, :], sem="te")
                p.tr(psA16_[:, 3584:3584 + 128], ktm16[:, 0:128], id16[:, :])
                p.tr(psA16_[:, 3584 + 128:3584 + 256], ktm16[:, DH:DH + 128],
                     id16[:, :])
                p.tr(psA16_[0:1, 3584 + 256:3584 + 384], ktm16[:, 128:129],
                     id16[:, :])
                p.tr(psA16_[0:1, 3584 + 384:3584 + 512], ktm16[:, 2 * DH - 1:2 * DH],
                     id16[:, :], sem="te")
                p.wait("scalar", "te")
                p.act(qfm_hi[:, t * 128:(t + 1) * 128], psA16_[:, 3072:3200],
                      AF.Copy)
                p.act(qfm_lo[:, t * 128:(t + 1) * 128], psA16_[:, 3200:3328],
                      AF.Copy)
                p.act(stg32r[0:1, t * 128:(t + 1) * 128],
                      psA16_[0:1, 3328:3456], AF.Copy)
                p.act(stg64r[0:1, t * 128:(t + 1) * 128],
                      psA16_[0:1, 3456:3584], AF.Copy)
                p.act(kfs_hi[:, t * 128:(t + 1) * 128], psA16_[:, 3584:3712],
                      AF.Copy)
                p.act(kfs_lo[:, t * 128:(t + 1) * 128], psA16_[:, 3712:3840],
                      AF.Copy)
                p.act(kfs_ra[0:1, t * 128:(t + 1) * 128],
                      psA16_[0:1, 3840:3968], AF.Copy)
                p.act(kfs_rb[0:1, t * 128:(t + 1) * 128],
                      psA16_[0:1, 3968:4096], AF.Copy, sem="sc")
                p.wait("tensor", "sc")
                if t in (7, 15):
                    hh = t // 8
                    cs = slice(hh * 1024, hh * 1024 + 1024)
                    p.wait("sync", "sc")
                    p.dma("sync", kvk_inh[hh].ap()[0:128, :], kfs_hi[:, cs],
                          sem="kv")
                    p.dma("sync", kvk_inh[hh].ap()[128:129, :], kfs_ra[0:1, cs],
                          sem="kv")
                    p.dma("sync", kvk_inh[hh].ap()[129:257, :], kfs_lo[:, cs],
                          sem="kv")
                    p.dma("sync", kvk_inh[hh].ap()[257:258, :], kfs_rb[0:1, cs],
                          sem="kv")
                    p.wait("gpsimd", "kv")
                    p.coll("AllGather", ALU.bypass, [[0, 1, 2, 3], [4, 5, 6, 7]],
                           [kvk_inh[hh].ap().opt()], [kvk_outh[hh].ap().opt()],
                           sem="cc")
                    p.coll("AllGather", ALU.bypass, [[0, 1, 2, 3], [4, 5, 6, 7]],
                           [kvv_inh[hh].ap().opt()], [kvv_outh[hh].ap().opt()],
                           sem="cc")
                    if hh == 1:
                        cc_kb = p.c["cc"] - 1
        p.wait("sync", "sc")
        p.dma("sync", qaug[0:1, :], stg32r[0:1, 0:TC], sem="ld")
        p.dma("sync", qaug[1:2, :], stg32r[0:1, 0:TC], sem="ld")
        p.dma("sync", qaug[2:3, :], stg64r[0:1, 0:TC], sem="ld")
        p.wait("sync", "ld")
        p.wait("sync", "cc", val=cc_kb)
        for hh in range(2):
            for r in range(4):
                ko = kvk_outh[hh].ap()
                c0 = r * 2048 + hh * 1024
                p.dma("sync", kfm_hi[:, c0:c0 + 1024],
                      ko[r * 2 * DH:r * 2 * DH + 128, :], sem="ld")
                p.dma("sync", kfm_lo[:, c0:c0 + 1024],
                      ko[r * 2 * DH + 129:r * 2 * DH + 257, :], sem="ld")
                p.dma("sync", scrG16[0:1, c0:c0 + 1024],
                      ko[r * 2 * DH + 128:r * 2 * DH + 129, :], sem="ld")
                p.dma("sync", stg32r[0:1, c0:c0 + 1024],
                      ko[r * 2 * DH + 257:r * 2 * DH + 258, :], sem="ld")
        p.wait("sync", "ld")
        p.wait("vector", "ld")
        p.wait("vector", "sc")
        p.v("memset", stg64r[0:1, 0:2048], 1.0, sem="ve")
        p.wait("sync", "sc")
        p.wait("sync", "ve")
        p.dma("sync", scrG16[1:2, 0:8192], stg32r[0:1, 0:8192], sem="ld")
        p.dma("sync", scrG16[2:3, 0:8192], scrG16[0:1, 0:8192], sem="ld")
        for i4 in range(4):
            p.dma("sync", scrG16[3:4, i4 * 2048:(i4 + 1) * 2048],
                  stg64r[0:1, 0:2048], sem="ld")
        p.wait("tensor", "ld")
        for t in range(NT):
            tsl = slice(t * 128, (t + 1) * 128)
            ps = psB[:, 1536 + t:1537 + t]
            p.mm(ps, qfm_hi[:, tsl], k0hl[:, 0:1], start=(t == 0), stop=False)
            p.mm(ps, qfm_hi[:, tsl], k0hl[:, 2:3], start=False, stop=False)
            p.mm(ps, qfm_lo[:, tsl], k0hl[:, 0:1], start=False, stop=False)
            p.mm(ps, qaug[0:3, tsl], k0aug[0:3, :], start=False,
                 stop=(t == NT - 1), sem="te")
        p.wait("scalar", "te")
        p.act(ssink[:, :], psB[:, 1536:1552], AF.Copy, sem="sc")
        p.wait("tensor", "sc")
        fwd_sc = {}
        red_ve = {}
        for t in range(NT):
            tsl = slice(t * 128, (t + 1) * 128)
            for n in range(16):
                ps = psA[:, (n % 4) * 512:(n % 4) * 512 + 512]
                key = (t * 16 + n) - 4
                if key >= 0:
                    p.wait("tensor", "sc", val=fwd_sc[key])
                p.mm(ps, qfm_hi[:, tsl], kfm_hi[:, n * 512:(n + 1) * 512],
                     start=True, stop=False)
                p.mm(ps, qaug[0:1, tsl], scrG16[0:1, n * 512:(n + 1) * 512],
                     start=False, stop=True, sem="te")
                p.wait("scalar", "te")
                p.wait("scalar", "ve", val=red_ve.get((t - 1, n // 4), 0))
                p.act(fwd16[:, n * 512:(n + 1) * 512], ps, AF.Copy, sem="sc")
                fwd_sc[t * 16 + n] = p.c["sc"]
                if n % 4 == 3:
                    p.wait("vector", "sc")
                    p.v("reduce_max", out=zsb[:, 12 + (n // 4):13 + (n // 4)],
                        in_=fwd16[:, (n // 4) * 2048:(n // 4) * 2048 + 2048],
                        axis=AX, sem="ve")
                    red_ve[(t, n // 4)] = p.c["ve"]
            p.v("reduce_max", out=mrow[:, t:t + 1], in_=zsb[:, 12:16], axis=AX,
                sem="ve")
        p.wait("vector", "sc")
        p.v("tensor_max", mrow[:, :], mrow[:, :], ssink[:, :])
        p.v("tensor_scalar", mrow[:, :], mrow[:, :], 0.5, None, ALU.add)
        p.v("tensor_sub", psk[:, :], ssink[:, :], mrow[:, :], sem="ve")
        p.wait("scalar", "ve")
        p.act(psk[:, :], psk[:, :], AF.Exp, sem="sc")
        p.wait("vector", "sc")
        p.v("tensor_copy", out=psk16[:, :], in_=psk[:, :])
        p.v("tensor_scalar", lv16[:, :], mrow[:, :], -1.0, None, ALU.mult,
            sem="ve")
        p.wait("tensor", "ve")
        for t in range(NT):
            p.tr(psA16_[0:1, 0:128], psk16[:, t:t + 1], id16[:, :])
            p.tr(psA16_[0:1, 1024:1152], lv16[:, t:t + 1], id16[:, :], sem="te")
            p.wait("scalar", "te")
            p.act(pskT[0:1, t * 128:(t + 1) * 128], psA16_[0:1, 0:128], AF.Copy)
            p.wait("scalar", "sc")
            p.act(stg32r[0:1, t * 128:(t + 1) * 128],
                  psA16_[0:1, 1024:1152], AF.Copy, sem="sc")
            p.wait("tensor", "sc")
        p.wait("sync", "sc")
        p.dma("sync", qaug[3:4, 0:TC], stg32r[0:1, 0:TC], sem="ld")
        p.wait("sync", "ld")
        p.wait("tensor", "ld")
        p.wait("sync", "cc")
        vs_base = p.c["vs"]
        NVT = NQB * NKT
        v_done = {}

        def issue_v(g):
            kt = g % NKT
            r, o = kt // 16, kt % 16
            hh = 0 if o < 8 else 1
            row = r * 1024 + (o - hh * 8) * 128
            p.wait("sync", "te")
            p.dma("sync", vbuf[:, (g % 2) * D:(g % 2) * D + D],
                  kvv_outh[hh].ap()[row:row + 128, :], sem="vs")
            v_done[g] = p.c["vs"]

        exp_sc = {}
        issue_v(0)
        issue_v(1)

        def emit_scores(qb, kt):
            qsl_ = slice(qb * QB, (qb + 1) * QB)
            par = kt % 2
            sbk = psB[:, par * 512:par * 512 + QB]
            if kt >= 2:
                p.wait("tensor", "sc", val=exp_sc[kt - 2])
            p.mm(sbk, kfm_hi[:, kt * 128:(kt + 1) * 128], qfm_hi[:, qsl_],
                 start=True, stop=False)
            p.mm(sbk, kfm_hi[:, kt * 128:(kt + 1) * 128], qfm_lo[:, qsl_],
                 start=False, stop=False)
            p.mm(sbk, kfm_lo[:, kt * 128:(kt + 1) * 128], qfm_hi[:, qsl_],
                 start=False, stop=False)
            p.mm(sbk, scrG16[0:4, kt * 128:(kt + 1) * 128], qaug[0:4, qsl_],
                 start=False, stop=True, sem="te")
            p.wait("scalar", "te")
            p.act(pt16[:, par * QB:(par + 1) * QB], sbk, AF.Exp, sem="sc")
            exp_sc[kt] = p.c["sc"]

        def emit_pv(qb, kt):
            g = qb * NKT + kt
            par = kt % 2
            p.wait("tensor", "sc", val=exp_sc[kt])
            p.wait("tensor", "vs")
            for qt in range(2):
                ptsl = pt16[:, par * QB + qt * 128:par * QB + (qt + 1) * 128]
                vsl = vbuf[:, (g % 2) * D:(g % 2) * D + D]
                p.mm(psA[:, qt * 1024:qt * 1024 + 512], ptsl, vsl[:, 0:512],
                     start=(kt == 0), stop=False)
                p.mm(psA[:, qt * 1024 + 512:qt * 1024 + 1024], ptsl,
                     vsl[:, 512:1024], start=(kt == 0), stop=False)
                p.mm(psB[:, 1024 + qt:1025 + qt], ptsl, ones16[:, 0:1],
                     start=(kt == 0 and qt == 0),
                     stop=(kt == NKT - 1 and qt == 1), sem="te")
            if g + 2 < NVT:
                issue_v(g + 2)

        for qb in range(NQB):
            exp_sc.clear()
            emit_scores(qb, 0)
            for kt in range(1, NKT):
                emit_scores(qb, kt)
                emit_pv(qb, kt - 1)
            emit_pv(qb, NKT - 1)
            for qt in range(2):
                t = qb * 2 + qt
                tsl2 = slice(t * 128, (t + 1) * 128)
                p.mm(psA[:, qt * 1024:qt * 1024 + 512], pskT[0:1, tsl2],
                     v0r16[0:1, 0:512], start=False, stop=True)
                p.mm(psA[:, qt * 1024 + 512:qt * 1024 + 1024],
                     pskT[0:1, tsl2], v0r16[0:1, 512:1024],
                     start=False, stop=True, sem="te")
            p.wait("vector", "te")
            p.v("tensor_copy", out=zsb[:, 0:2], in_=psB[:, 1024:1026])
            p.v("tensor_add", zsb[:, 2:4], zsb[:, 0:2], psk[:, 2 * qb:2 * qb + 2])
            p.v("reciprocal", out=zsb[:, 4:6], in_=zsb[:, 2:4])
            for qt in range(2):
                t = qb * 2 + qt
                xt = x_sb[:, t * D:(t + 1) * D]
                for half in range(2):
                    p.v("scalar_tensor_tensor", xt[:, half * 512:(half + 1) * 512],
                        psA[:, qt * 1024 + half * 512:qt * 1024 + (half + 1) * 512],
                        zsb[:, 4 + qt:5 + qt], xt[:, half * 512:(half + 1) * 512],
                        ALU.mult, ALU.add, sem="ve")
                p.v("reduce_sum", out=sA[:, t:t + 1], in_=xt, axis=AX, sem="ve")
                p.wait("scalar", "ve")
                p.act(scrH_lo.bitcast(F32)[:, 0:1024], xt, AF.Square,
                      accum_out=sA[:, 16 + t:17 + t], sem="sc")
            p.wait("tensor", "ve")

    def emit_out(p: Pass):
        p.wait("sync", "ve")
        for t in range(NT):
            p.dma("sync", x_out_t[t], x_sb[:, t * D:(t + 1) * D], sem="st")
        p.wait("sync", "st")

    psA = nc.alloc_psum_tensor("psA", [128, 2048], F32).ap()
    psB = nc.alloc_psum_tensor("psB", [128, 2048], F32).ap()

    counts = []
    with nc.Block() as block:
        for eng_name in ENGS:
            c = {s: 0 for s in sem_names}
            counts.append(c)

            def make(f_name, f_c):
                def _(eng):
                    program(Pass(f_name, eng, sems, f_c))
                return _
            getattr(block, eng_name)(make(eng_name, c))
    for c in counts[1:]:
        assert c == counts[0], f"counter mismatch {c} vs {counts[0]}"
    return nc


def host_inputs(I, core):
    b, j = core // 4, core % 4
    t0 = j * TC
    f32 = np.float32

    def T(a):
        return np.ascontiguousarray(np.asarray(a, f32).T)

    def pad129(a):
        out = np.zeros((128, 2), f32)
        a = np.asarray(a, f32).reshape(-1)
        out[:, 0] = a[0:128]
        out[0, 1] = a[128]
        return out

    return {
        "x_in": np.ascontiguousarray(np.asarray(I["L0_attn_x"], f32)[b, t0:t0 + TC]),
        "wqT": T(I["L0_attn_w_q"]),
        "wkT": T(I["L0_attn_w_k"]),
        "wvT": T(I["L0_attn_w_v"]),
        "bqp": pad129(I["L0_attn_b_q"]),
        "bkp": pad129(I["L0_attn_b_k"]),
        "bvp": np.asarray(I["L0_attn_b_v"], f32).reshape(128, 8),
        "k0p": pad129(I["L0_attn_k_0"]),
        "v0p": np.asarray(I["L0_attn_v_0"], f32).reshape(128, 8),
        "qproj": np.asarray(I["L0_ffn_q_proj_w"], f32),
        "routerT": T(I["L0_ffn_router_w"]),
        "auxkT": T(I["L0_ffn_aux_keys"]),
        "maskp": np.asarray(I["L0_ffn_reliability_mask"], f32).reshape(128, 32),
        "w1T": T(I["L0_ffn_unc_w1"]),
        "b1p": np.ascontiguousarray(
            np.asarray(I["L0_ffn_unc_b1"], f32).reshape(2, 128).T),
        "w2p": np.ascontiguousarray(
            np.asarray(I["L0_ffn_unc_w2"], f32).reshape(2, 128).T),
        "b2v_in": np.broadcast_to(
            np.asarray(I["L0_ffn_unc_b2"], f32).reshape(1, 1), (128, 1)).copy(),
        "gw1v_in": np.full((128, 1), np.float32(I["L0_ffn_gate_w1"]), f32),
        "gbv_in": np.full((128, 1), np.float32(I["L0_ffn_gate_bias"]), f32),
        "aux_vals": np.asarray(I["L0_ffn_aux_values"], f32),
        "id16": np.eye(128, dtype=np.float16),
    }


def kernel(**inputs):
    nc = build_nc("full")
    in_maps = [host_inputs(inputs, c) for c in range(8)]
    res = run_bass_kernel_spmd(nc, in_maps, core_ids=list(range(8)))
    out = np.empty((B, S, D), np.float32)
    for c in range(8):
        b, j = c // 4, c % 4
        out[b, j * TC:(j + 1) * TC] = res.results[c]["x_out"]
    return out


# version 06
# speedup vs baseline: 1.4941x; 1.2276x over previous
import numpy as np
import concourse.bass as bass
import concourse.mybir as mybir
from concourse.bass import IndirectOffsetOnAxis
from concourse.bass_utils import run_bass_kernel_spmd

F32 = mybir.dt.float32
F16 = mybir.dt.float16
BF16 = mybir.dt.bfloat16
U32 = mybir.dt.uint32
AF = mybir.ActivationFunctionType
ALU = mybir.AluOpType
AX = mybir.AxisListType.X

B, S, D = 2, 8192, 1024
TC = 2048
NT = 16
GT = 2
GTOK = GT * 128
NG = NT // GT
NKT = 64
NSLOT = 4096
DH = 129
QB = 256
NQB = TC // QB
EPS_LN = 1e-5
RSQ48 = float(1.0 / np.sqrt(48.0))

ENGS = ("sync", "tensor", "scalar", "vector", "gpsimd")


class Pass:
    def __init__(self, eng_name, eng, sems, c):
        self.e = eng_name
        self.eng = eng
        self.sems = sems
        self.c = c

    def bump(self, inst, sem, n):
        if sem is not None:
            self.c[sem] += n
            if inst is not None:
                inst.then_inc(self.sems[sem], n)
        return inst

    def wait(self, eng_name, sem, val=None):
        if self.e == eng_name:
            self.eng.wait_ge(self.sems[sem], self.c[sem] if val is None else val)

    def mm(self, out, lhsT, rhs, start=True, stop=True, sem=None, n=1):
        i = self.eng.matmul(out, lhsT, rhs, start=start, stop=stop) \
            if self.e == "tensor" else None
        return self.bump(i, sem, n)

    def tr(self, out, in_, ident, sem=None, n=1):
        i = self.eng.transpose(out, in_, ident) if self.e == "tensor" else None
        return self.bump(i, sem, n)

    def act(self, out, in_, func, bias=0.0, scale=1.0, accum_out=None,
            sem=None, n=1):
        i = None
        self.wait("scalar", "sc")
        if self.e == "scalar":
            if func == AF.Copy and not (isinstance(bias, float)
                                        and isinstance(scale, float)):
                func = AF.Identity
            kw = {"accum_out": accum_out} if accum_out is not None else {}
            i = self.eng.activation(out, in_, func, bias=bias, scale=scale, **kw)
        return self.bump(i, "sc", 1)

    def v(self, meth, *args, sem=None, n=1, **kw):
        self.wait("vector", "ve")
        i = getattr(self.eng, meth)(*args, **kw) if self.e == "vector" else None
        return self.bump(i, "ve", 1)

    def dma(self, queue, out, in_, sem=None, transpose=False, slow=False):
        i = None
        if self.e == queue:
            if transpose:
                i = self.eng.dma_start_transpose(out=out, in_=in_)
            elif slow:
                i = self.eng.dma_start(out=out, in_=in_,
                                       allow_slow_non_contiguous=True)
            else:
                i = self.eng.dma_start(out=out, in_=in_)
        return self.bump(i, sem, 16 if sem else 0)

    def gather(self, out, table_ap, idx_ap, sem=None):
        i = None
        if self.e == "gpsimd":
            i = self.eng.indirect_dma_start(
                out=out, out_offset=None, in_=table_ap,
                in_offset=IndirectOffsetOnAxis(idx_ap, 0))
        return self.bump(i, sem, 16 if sem else 0)

    def coll(self, kind, op, groups, ins, outs, sem=None):
        i = None
        if self.e == "gpsimd":
            i = self.eng.collective_compute(kind, op, replica_groups=groups,
                                            ins=ins, outs=outs)
        return self.bump(i, sem, 1 if sem else 0)


def build_nc(stage="full", core0_groups=None):
    nc = bass.Bass("TRN2", target_bir_lowering=False)

    def din(name, shape, dt=F32):
        return nc.dram_tensor(name, list(shape), dt, kind="ExternalInput").ap()

    x_in = din("x_in", (TC, D))
    wqT_in = din("wqT", (D, DH))
    wkT_in = din("wkT", (D, DH))
    wvT_in = din("wvT", (D, D))
    bqp_in = din("bqp", (128, 2))
    bkp_in = din("bkp", (128, 2))
    bvp_in = din("bvp", (128, 8))
    k0p_in = din("k0p", (128, 2))
    v0p_in = din("v0p", (128, 8))
    qproj_in = din("qproj", (128, D))
    routerT_in = din("routerT", (128, 48))
    auxkT_in = din("auxkT", (128, NSLOT))
    maskp_in = din("maskp", (128, 32))
    w1T_in = din("w1T", (D, 256))
    b1p_in = din("b1p", (128, 2))
    w2p_in = din("w2p", (128, 2))
    b2v_in = din("b2v_in", (128, 1))
    gw1v_in = din("gw1v_in", (128, 1))
    gbv_in = din("gbv_in", (128, 1))
    aux_in = din("aux_vals", (NSLOT, D))
    id16_in = din("id16", (128, 128), F16)

    x_out = nc.dram_tensor("x_out", [TC, D], F32, kind="ExternalOutput").ap()

    aux_bf = nc.dram_tensor("aux_bf", [NSLOT, D], F16)
    kvv_inh = [nc.dram_tensor(f"kvv_in{i}", [TC // 2, D], F16) for i in range(2)]
    kvv_outh = [nc.dram_tensor(f"kvv_out{i}", [S // 2, D], F16) for i in range(2)]
    kvk_inh = [nc.dram_tensor(f"kvk_in{i}", [2 * DH, TC // 2], F16)
               for i in range(2)]
    kvk_outh = [nc.dram_tensor(f"kvk_out{i}", [8 * DH, S // 8], F16)
                for i in range(2)]
    ar_in = [nc.dram_tensor(f"ar_in{i}", [1, 1], F32) for i in range(2)]
    ar_out = [nc.dram_tensor(f"ar_out{i}", [1, 1], F32) for i in range(2)]

    A = {}

    def sbuf(name, shape, dt):
        A[name] = nc.alloc_sbuf_tensor(name, list(shape), dt).ap()
        return A[name]

    x_sb = sbuf("x_sb", (128, NT * D), F32)
    stage32 = sbuf("stage32", (128, 8192), F32)
    scrF = sbuf("scrF", (128, NSLOT), F32)
    scrG = sbuf("scrG", (128, 8192), BF16)
    arenaW = sbuf("arenaW", (128, 8192), F16)
    scrH_hi = sbuf("scrH_hi", (128, GT * D), F16)
    scrH_lo = sbuf("scrH_lo", (128, GT * D), F16)
    hfm_hi = sbuf("hfm_hi", (128, 8 * GTOK), F16)
    hfm_lo = sbuf("hfm_lo", (128, 8 * GTOK), F16)
    vbuf = sbuf("vbuf", (128, 3072), F16)
    qfm_hi = sbuf("qfm_hi", (128, TC), F16)
    qfm_lo = sbuf("qfm_lo", (128, TC), F16)
    qtm16 = sbuf("qtm16", (128, DH), F16)
    ktm16 = sbuf("ktm16", (128, DH), F16)
    pt16 = sbuf("pt16", (128, 3 * QB), F16)
    qtm_lo = None
    rkT_hi = sbuf("rkT_hi", (50, NSLOT), F16)
    rkT_lo = sbuf("rkT_lo", (48, NSLOT), F16)
    w1T16 = sbuf("w1T16", (128, 8 * 256), F16)
    wq16 = sbuf("wq16", (128, 8 * 2 * DH), F16)
    wk16 = sbuf("wk16", (128, 8 * 2 * DH), F16)
    wrqT16 = sbuf("wrqT16", (128, 8 * 96), F16)

    w2T16 = sbuf("w2T16", (128, 2), F16)
    id16 = sbuf("id16s", (128, 128), F16)
    ones16 = sbuf("ones16", (128, 2), F16)
    ones16r = sbuf("ones16r", (2, 128), F16)
    bk16 = sbuf("bk16", (2, DH), F16)
    k0hl = sbuf("k0hl", (128, 4), F16)
    k0aug = sbuf("k0aug", (4, 1), F16)
    b1c = sbuf("b1c", (128, 2), F32)
    b2v = sbuf("b2v", (128, 1), F32)
    gw1v = sbuf("gw1v", (128, 1), F32)
    gbv = sbuf("gbv", (128, 1), F32)
    sA = sbuf("sA", (128, 64), F32)
    sM = sbuf("sM", (128, 64), F32)
    one1 = sbuf("one1", (1, 16), F32)
    lv16 = sbuf("lv16", (128, 16), F16)
    invb = sbuf("invb", (128, 1), F32)
    top8 = sbuf("top8", (128, 8), F32)
    idx8 = sbuf("idx8", (128, 8), U32)
    w8 = sbuf("w8", (128, 16), F32)
    zsb = sbuf("zsb", (128, 16), F32)
    psk = sbuf("psk", (128, NT), F32)
    psk16 = sbuf("psk16", (128, NT), F16)
    ssink = sbuf("ssink", (128, NT), F32)
    mrow = sbuf("mrow", (128, NT), F32)

    h1pre = arenaW[:, 0:4096]
    rq_hi = arenaW[0:50, 4096:6144]
    rq_lo = arenaW[50:98, 6144:8192]
    rq_lo = arenaW[0:48, 6144:8192]
    qp16 = vbuf
    kfm_hi = stage32.bitcast(F16)[:, 0:8192]
    kfm_lo = stage32.bitcast(F16)[:, 8192:16384]
    kfs_hi = stage32.bitcast(F16)[:, 8192:8192 + 2048]
    kfs_lo = stage32.bitcast(F16)[:, 10240:10240 + 2048]
    kfs_ra = stage32.bitcast(F16)[0:1, 12288:12288 + 2048]
    kfs_rb = stage32.bitcast(F16)[64:65, 12288:12288 + 2048]
    fwd16 = scrF.bitcast(BF16)
    k129 = scrG.bitcast(F16)[0:4, 0:8192]
    wv16 = arenaW
    scrG16 = scrG.bitcast(F16)
    qaug = scrH_hi[0:4, 0:TC]
    v0r16 = scrG16[64:65, 6144:6144 + D]
    pskT = scrG16[64:65, 4096:6144]
    bv16 = scrG16[64:66, 2048:2048 + D]
    ones64 = scrG16[64:66, 3072:3200]
    stg32r = scrG16[32:33, 0:8192]
    stg64r = scrG16[64:65, 0:2048]
    tmp32 = scrF[:, 0:D]
    qtm_lo = pt16[:, 0:DH]
    ktm_lo = pt16[:, 256:256 + DH]
    rt16 = pt16[:, 512:608]
    bq16 = pt16[0:2, 608:608 + DH]

    sem_names = ["ld", "te", "sc", "ve", "gp", "cc", "vs", "st", "aux", "kv"]
    sems = {s: nc.alloc_semaphore(s) for s in sem_names}

    x_in_t = x_in.rearrange("(t p) d -> t p d", p=128)
    x_out_t = x_out.rearrange("(t p) d -> t p d", p=128)

    def program(p: Pass):
        emit_prep(p)
        emit_ffn(p, 0)
        if stage in ("attn", "full"):
            emit_attn(p)
        if stage == "full":
            emit_ffn(p, 1, skip_stats=True)
        emit_out(p)

    def hilo_(p, dst_hi, dst_lo, src32):
        p.wait("scalar", "sc")
        p.act(dst_hi, src32, AF.Copy, sem="sc")
        p.wait("vector", "sc")
        p.v("tensor_tensor", dst_lo, src32, dst_hi, ALU.subtract, sem="ve")

    def emit_prep(p: Pass):
        for i in range(4):
            p.dma("gpsimd", aux_bf.ap()[i * 1024:(i + 1) * 1024, :],
                  aux_in[i * 1024:(i + 1) * 1024, :], sem="aux")
        for t in range(NT):
            p.dma("sync", x_sb[:, t * D:(t + 1) * D], x_in_t[t], sem="ld")
        p.dma("sync", id16[:, :], id16_in, sem="ld")
        p.dma("sync", b2v[:, :], b2v_in, sem="ld")
        p.dma("sync", gw1v[:, :], gw1v_in, sem="ld")
        p.dma("sync", gbv[:, :], gbv_in, sem="ld")
        p.dma("sync", b1c[:, :], b1p_in, sem="ld")
        p.dma("sync", stage32[:, 0:D], qproj_in, sem="ld")
        p.dma("sync", stage32[:, D:D + 48], routerT_in, sem="ld")
        p.dma("sync", stage32[:, D + 48:D + 48 + 2048]
              .rearrange("p (c d) -> p c d", c=8),
              w1T_in.rearrange("(c p) d -> p c d", p=128), sem="ld")
        p.dma("sync", stage32[:, 3120:3120 + NSLOT], auxkT_in, sem="ld")
        p.dma("sync", stage32[:, 7216:7218], w2p_in, sem="ld")
        p.dma("sync", stage32[:, 7218:7250], maskp_in, sem="ld")
        p.wait("scalar", "ld")
        p.wait("vector", "ld")
        p.v("memset", ones16[:, :], 1.0)
        p.v("memset", ones16r[:, :], 1.0)
        p.v("memset", rq_hi[0:50, :], 1.0, sem="ve")
        hilo_(p, qp16[:, 0:D], qp16[:, D:2 * D], stage32[:, 0:D])
        hilo_(p, rt16[:, 0:48], rt16[:, 48:96], stage32[:, D:D + 48])
        for ch in range(8):
            p.act(w1T16[:, ch * 256:(ch + 1) * 256],
                  stage32[:, D + 48 + ch * 256:D + 48 + (ch + 1) * 256],
                  AF.Copy, sem="sc")
        p.act(w2T16[:, 0:2], stage32[:, 7216:7218], AF.Copy, sem="sc")
        mh16 = scrF.bitcast(F16)[:, 0:32]
        ml16 = scrF.bitcast(F16)[:, 32:64]
        hilo_(p, mh16, ml16, stage32[:, 7218:7250])
        p.wait("sync", "ve")
        p.wait("sync", "sc")
        p.dma("sync", rkT_hi[48:49, :], mh16, sem="ld")
        p.dma("sync", rkT_hi[49:50, :], ml16, sem="ld")
        auxk_hi = scrG.bitcast(F16)[:, 0:NSLOT]
        auxk_lo = scrG.bitcast(F16)[:, NSLOT:2 * NSLOT]
        asrc = stage32[:, 3120:3120 + NSLOT]
        for q in range(4):
            sl = slice(q * 1024, (q + 1) * 1024)
            hilo_(p, auxk_hi[:, sl], auxk_lo[:, sl], asrc[:, sl])
        p.wait("tensor", "sc")
        p.wait("tensor", "ve")
        p.wait("tensor", "ld")
        p.wait("scalar", "ve")
        for n in range(8):
            ps = psA[0:48, (n % 4) * 512:(n % 4) * 512 + 512]
            rh = auxk_hi[:, n * 512:(n + 1) * 512]
            rl = auxk_lo[:, n * 512:(n + 1) * 512]
            p.mm(ps, rt16[:, 0:48], rh, start=True, stop=False)
            p.mm(ps, rt16[:, 0:48], rl, start=False, stop=False)
            p.mm(ps, rt16[:, 48:96], rh, start=False, stop=True, sem="te")
            p.wait("scalar", "te")
            p.act(stage32[0:48, n * 512:(n + 1) * 512], ps, AF.Copy,
                  scale=RSQ48, sem="sc")
            p.wait("tensor", "sc")
        for n in range(8):
            sl = slice(n * 512, (n + 1) * 512)
            hilo_(p, rkT_hi[0:48, sl], rkT_lo[0:48, sl], stage32[0:48, sl])
        for n in range(2):
            ps = psB[0:48, n * 512:(n + 1) * 512]
            qh = qp16[:, n * 512:(n + 1) * 512]
            ql = qp16[:, D + n * 512:D + (n + 1) * 512]
            p.mm(ps, rt16[:, 0:48], qh, start=True, stop=False)
            p.mm(ps, rt16[:, 0:48], ql, start=False, stop=False)
            p.mm(ps, rt16[:, 48:96], qh, start=False, stop=True, sem="te")
        p.wait("scalar", "te")
        p.wait("scalar", "ve")
        for n in range(2):
            p.act(stage32[0:48, 4096 + n * 512:4096 + (n + 1) * 512],
                  psB[0:48, n * 512:(n + 1) * 512], AF.Copy, sem="sc")
        wrq32 = stage32[0:48, 4096:4096 + 1024]
        wrq_h = scrH_hi[0:48, 0:1024]
        wrq_l = scrH_lo[0:48, 0:1024]
        hilo_(p, wrq_h, wrq_l, wrq32)
        p.wait("tensor", "sc")
        p.wait("tensor", "ve")
        psA16 = psA.bitcast(F16)
        for ch in range(8):
            p.tr(psA16[0:128, 0:48], wrq_h[:, ch * 128:(ch + 1) * 128],
                 id16[0:48, 0:48])
            p.tr(psA16[0:128, 1024:1072], wrq_l[:, ch * 128:(ch + 1) * 128],
                 id16[0:48, 0:48], sem="te")
            p.wait("scalar", "te")
            p.act(wrqT16[:, ch * 96:ch * 96 + 48], psA16[0:128, 0:48], AF.Copy,
                  sem="sc")
            p.act(wrqT16[:, ch * 96 + 48:ch * 96 + 96], psA16[0:128, 1024:1072],
                  AF.Copy, sem="sc")
            p.wait("tensor", "sc")
        p.wait("gpsimd", "aux")

    def emit_ffn(p: Pass, call, skip_stats=False):
        if not skip_stats:
            p.wait("scalar", "ve")
            p.wait("vector", "sc")
            for t in range(NT):
                xt = x_sb[:, t * D:(t + 1) * D]
                p.v("reduce_sum", out=sA[:, t:t + 1], in_=xt, axis=AX, sem="ve")
                p.wait("scalar", "sc")
                p.act(tmp32[:, :], xt, AF.Square, accum_out=sA[:, 16 + t:17 + t],
                      sem="sc")
        p.wait("vector", "sc")
        p.wait("scalar", "ve")
        p.v("tensor_scalar", sA[:, 32:48], sA[:, 0:16], 1.0 / D, None, ALU.mult)
        p.v("tensor_scalar", sA[:, 48:64], sA[:, 16:32], 1.0 / D, None, ALU.mult)
        p.v("tensor_tensor", tmp32[:, 0:16], sA[:, 32:48], sA[:, 32:48], ALU.mult)
        p.v("tensor_sub", sA[:, 48:64], sA[:, 48:64], tmp32[:, 0:16])
        p.v("tensor_scalar", tmp32[:, 0:16], sA[:, 48:64], EPS_LN, None, ALU.add,
            sem="ve")
        p.wait("scalar", "ve")
        p.act(tmp32[:, 16:32], tmp32[:, 0:16], AF.Ln, sem="sc")
        p.wait("scalar", "sc")
        p.act(sM[:, 0:16], tmp32[:, 16:32], AF.Exp, scale=-0.5, sem="sc")
        p.wait("vector", "sc")
        p.v("tensor_tensor", tmp32[:, 48:64], sA[:, 32:48], sM[:, 0:16], ALU.mult)
        p.v("tensor_scalar", sM[:, 16:32], tmp32[:, 48:64], -1.0, None, ALU.mult)
        p.v("reciprocal", out=tmp32[:, 16:32], in_=tmp32[:, 0:16])
        p.v("tensor_tensor", tmp32[:, 32:48], sA[:, 48:64], tmp32[:, 16:32],
            ALU.mult, sem="ve")
        p.wait("scalar", "ve")
        p.act(sM[:, 32:48], tmp32[:, 32:48], AF.Ln, bias=1.0, sem="sc")
        p.wait("vector", "sc")
        p.v("tensor_copy", out=lv16[:, :], in_=sM[:, 32:48], sem="ve")
        p.wait("tensor", "ve")
        p.mm(psB[0:1, 0:16], ones16[:, 0:1], lv16[:, :], sem="te")
        p.wait("vector", "te")
        p.v("reduce_sum", out=one1[0:1, 1:2], in_=psB[0:1, 0:16], axis=AX,
            sem="ve")
        p.wait("sync", "ve")
        p.dma("sync", ar_in[call].ap(), one1[0:1, 1:2], sem="st")
        p.wait("gpsimd", "st")
        p.coll("AllReduce", ALU.add, [[0, 1, 2, 3, 4, 5, 6, 7]],
               [ar_in[call].ap().opt()], [ar_out[call].ap().opt()], sem="cc")
        p.wait("vector", "st")
        if call == 1:
            p.v("memset", rq_hi[0:50, :], 1.0, sem="ve")
        for g in range(NG):
            for ti in range(GT):
                t = g * GT + ti
                xt = x_sb[:, t * D:(t + 1) * D]
                p.wait("scalar", "ve")
                p.wait("scalar", "sc")
                p.act(tmp32[:, :], xt, AF.Copy, bias=sM[:, 16 + t:17 + t],
                      scale=sM[:, 0 + t:1 + t], sem="sc")
                p.wait("scalar", "sc")
                p.act(scrH_hi[:, ti * D:(ti + 1) * D], tmp32[:, :], AF.Copy,
                      sem="sc")
                p.wait("vector", "sc")
                p.v("tensor_tensor", scrH_lo[:, ti * D:(ti + 1) * D], tmp32[:, :],
                    scrH_hi[:, ti * D:(ti + 1) * D], ALU.subtract, sem="ve")
                p.v("scalar_tensor_tensor", xt, tmp32[:, :], 0.5, xt,
                    ALU.mult, ALU.add, sem="ve")
            p.wait("tensor", "ve")
            psA16_ = psA.bitcast(F16)
            for hl, (hsrc, dst) in enumerate(((scrH_hi, hfm_hi), (scrH_lo, hfm_lo))):
                for ch in range(8):
                    for ti in range(GT):
                        p.tr(psA16_[:, (2 * ti + hl) * 1024:(2 * ti + hl) * 1024 + 128],
                             hsrc[:, ti * D + ch * 128:ti * D + (ch + 1) * 128],
                             id16[:, :], sem="te")
                    p.wait("scalar", "te")
                    for ti in range(GT):
                        p.act(dst[:, ch * GTOK + ti * 128:ch * GTOK + (ti + 1) * 128],
                              psA16_[:, (2 * ti + hl) * 1024:(2 * ti + hl) * 1024 + 128],
                              AF.Copy, sem="sc")
                    p.wait("tensor", "sc")
            for m in range(2):
                ps = psB[:, m * 512:m * 512 + GTOK]
                for ch in range(8):
                    p.mm(ps, w1T16[:, ch * 256 + m * 128:ch * 256 + m * 128 + 128],
                         hfm_hi[:, ch * GTOK:(ch + 1) * GTOK],
                         start=(ch == 0), stop=(ch == 7), sem="te")
                p.wait("scalar", "te")
                p.act(h1pre[:, m * 2048 + g * GTOK:m * 2048 + (g + 1) * GTOK], ps,
                      AF.Gelu, bias=b1c[:, m:m + 1], sem="sc")
            p.wait("tensor", "sc")
            for ti in range(GT):
                t = g * GT + ti
                ps = psB[:, 1536 + t:1536 + t + 1]
                for m in range(2):
                    p.mm(ps, h1pre[:, m * 2048 + g * GTOK + ti * 128:
                                   m * 2048 + g * GTOK + (ti + 1) * 128],
                         w2T16[:, m:m + 1],
                         start=(t == 0 and m == 0), stop=(t == NT - 1 and m == 1),
                         sem="te")
            ps = psA[0:48, 0:GTOK]
            for ch in range(8):
                wh = wrqT16[:, ch * 96:ch * 96 + 48]
                wl = wrqT16[:, ch * 96 + 48:ch * 96 + 96]
                hh = hfm_hi[:, ch * GTOK:(ch + 1) * GTOK]
                hl_ = hfm_lo[:, ch * GTOK:(ch + 1) * GTOK]
                p.mm(ps, wh, hh, start=(ch == 0), stop=False)
                p.mm(ps, wh, hl_, start=False, stop=False)
                p.mm(ps, wl, hh, start=False, stop=(ch == 7), sem="te")
            p.wait("scalar", "te")
            p.act(rq_hi[0:48, g * GTOK:(g + 1) * GTOK], ps, AF.Copy, sem="sc")
            p.wait("vector", "sc")
            p.v("tensor_tensor", rq_lo[0:48, g * GTOK:(g + 1) * GTOK], ps,
                rq_hi[0:48, g * GTOK:(g + 1) * GTOK], ALU.subtract, sem="ve")
            p.wait("tensor", "ve")
        p.wait("scalar", "te")
        p.act(tmp32[:, 0:16], psB[:, 1536:1552], AF.Copy, bias=b2v[:, 0:1],
              sem="sc")
        p.wait("scalar", "sc")
        p.act(tmp32[:, 16:32], tmp32[:, 0:16], AF.Exp, scale=-1.0, sem="sc")
        p.wait("vector", "sc")
        p.v("tensor_scalar", tmp32[:, 16:32], tmp32[:, 16:32], 1.0, None, ALU.add)
        p.v("reciprocal", out=sM[:, 48:64], in_=tmp32[:, 16:32], sem="ve")
        p.wait("sync", "cc")
        p.dma("sync", one1[0:1, 2:3], ar_out[call].ap(), sem="ld")
        p.wait("sync", "ld")
        p.wait("vector", "ld")
        p.v("tensor_scalar", one1[0:1, 3:4], one1[0:1, 2:3], 1.0 / (B * S), 1e-6,
            ALU.mult, op1=ALU.add)
        p.v("reciprocal", out=one1[0:1, 4:5], in_=one1[0:1, 3:4])
        p.v("tensor_scalar", one1[0:1, 5:6], one1[0:1, 4:5], 0.5, None, ALU.mult)
        p.v("tensor_copy", out=one1.bitcast(F16)[0:1, 12:13], in_=one1[0:1, 5:6],
            sem="ve")
        p.wait("tensor", "ve")
        p.mm(psB[:, 1024:1025], ones16r[0:1, :], one1.bitcast(F16)[0:1, 12:13],
             sem="te")
        p.wait("scalar", "te")
        p.act(invb[:, :], psB[:, 1024:1025], AF.Copy, sem="sc")
        p.wait("vector", "sc")
        p.v("tensor_scalar", sM[:, 32:48], sM[:, 32:48], invb[:, 0:1], None,
            ALU.mult)
        p.v("scalar_tensor_tensor", sM[:, 48:64], sM[:, 48:64], 2.5,
            sM[:, 32:48], ALU.mult, ALU.add)
        p.v("tensor_scalar", sM[:, 48:64], sM[:, 48:64], 5.0, 0.0, ALU.min,
            op1=ALU.max, sem="ve")
        p.wait("scalar", "ve")
        p.act(tmp32[:, 16:32], sM[:, 48:64], AF.Copy, scale=gw1v[:, 0:1],
              bias=gbv[:, 0:1], sem="sc")
        p.wait("scalar", "sc")
        p.act(tmp32[:, 32:48], tmp32[:, 16:32], AF.Exp, scale=-1.0, sem="sc")
        p.wait("vector", "sc")
        p.v("tensor_scalar", tmp32[:, 32:48], tmp32[:, 32:48], 1.0, None, ALU.add)
        p.v("reciprocal", out=sM[:, 48:64], in_=tmp32[:, 32:48], sem="ve")
        a4_ve = p.c["ve"]
        idx_ve = {}
        agg_ve = {}
        w8b = [w8[:, 0:8], w8[:, 8:16]]

        def gslot(t, k):
            if t % 2 == 0:
                return scrG16[:, k * D:(k + 1) * D]
            half = [scrH_hi, scrH_lo, hfm_hi, hfm_lo][k // 2]
            return half[:, (k % 2) * D:(k % 2) * D + D]

        def emit_scores_topk(t):
            g, ti = t // GT, t % GT
            half = psA if t % 2 == 0 else psB
            rq_s = slice(g * GTOK + ti * 128, g * GTOK + (ti + 1) * 128)
            for n in range(8):
                ps = half[:, (n % 4) * 512:(n % 4) * 512 + 512]
                p.mm(ps, rq_hi[0:50, rq_s], rkT_hi[0:50, n * 512:(n + 1) * 512],
                     start=True, stop=False)
                p.mm(ps, rq_hi[0:48, rq_s], rkT_lo[0:48, n * 512:(n + 1) * 512],
                     start=False, stop=False)
                p.mm(ps, rq_lo[0:48, rq_s], rkT_hi[0:48, n * 512:(n + 1) * 512],
                     start=False, stop=True, sem="te")
                if n % 4 == 3:
                    p.wait("scalar", "te")
                    p.wait("scalar", "ve", val=max(idx_ve.get(t - 1, 0), a4_ve))
                    p.act(scrF[:, (n // 4) * 2048:(n // 4) * 2048 + 2048],
                          half[:, 0:2048], AF.Copy, sem="sc")
                    p.wait("tensor", "sc")
            p.wait("vector", "sc")
            p.wait("vector", "gp", val=gp_done.get(t - 1, 0))
            p.v("max", out=top8[:, :], in_=scrF[:, :])
            p.v("tensor_scalar", w8b[t % 2], top8[:, :], top8[:, 0:1], None,
                ALU.subtract)
            p.v("max_index", out=idx8[:, :], in_max=top8[:, :],
                in_values=scrF[:, :], sem="ve")
            idx_ve[t] = p.c["ve"]
            p.wait("gpsimd", "ve", val=idx_ve[t])
            if t >= 2:
                p.wait("gpsimd", "ve", val=agg_ve.get(t - 2, 0))
            for k in range(8):
                p.gather(gslot(t, k), aux_bf.ap(), idx8[:, k:k + 1], sem="gp")
            p.wait("scalar", "ve")
            p.act(w8b[t % 2], w8b[t % 2], AF.Exp, sem="sc")
            p.wait("vector", "sc")
            p.v("reduce_sum", out=zsb[:, 8 + (t % 2):9 + (t % 2)], in_=w8b[t % 2],
                axis=AX)
            p.v("reciprocal", out=zsb[:, 10 + (t % 2):11 + (t % 2)],
                in_=zsb[:, 8 + (t % 2):9 + (t % 2)])
            p.v("tensor_scalar", zsb[:, 12 + (t % 2):13 + (t % 2)],
                zsb[:, 10 + (t % 2):11 + (t % 2)],
                sM[:, 48 + t:49 + t], 0.5, ALU.mult, op1=ALU.mult)
            p.v("tensor_scalar", w8b[t % 2], w8b[t % 2],
                zsb[:, 12 + (t % 2):13 + (t % 2)], None, ALU.mult, sem="ve")

        acc16 = vbuf[:, 0:1024]
        def emit_agg(t):
            p.wait("vector", "gp", val=gp_done[t])
            xt = x_sb[:, t * D:(t + 1) * D]
            p.v("tensor_scalar", acc16, gslot(t, 0), w8b[t % 2][:, 0:1], None,
                ALU.mult)
            for k in range(1, 8):
                p.v("scalar_tensor_tensor", acc16, gslot(t, k),
                    w8b[t % 2][:, k:k + 1], acc16, ALU.mult, ALU.add)
            p.v("scalar_tensor_tensor", xt, acc16, 1.0, xt, ALU.mult, ALU.add,
                sem="ve")
            agg_ve[t] = p.c["ve"]
            if call == 0:
                p.v("reduce_sum", out=sA[:, t:t + 1], in_=xt, axis=AX, sem="ve")

        gp_done = {}
        for t in range(NT):
            emit_scores_topk(t)
            gp_done[t] = p.c["gp"]
            if t >= 1:
                emit_agg(t - 1)
        emit_agg(NT - 1)
        if call == 0:
            p.wait("scalar", "ve")
            for t in range(NT):
                p.act(qfm_hi.bitcast(F32)[:, 0:1024],
                      x_sb[:, t * D:(t + 1) * D], AF.Square,
                      accum_out=sA[:, 16 + t:17 + t], sem="sc")

    def emit_attn(p: Pass):
        p.dma("sync", stage32.rearrange("p (c d) -> p c d", c=8),
              wvT_in.rearrange("(c p) d -> p c d", p=128), sem="ld")
        p.wait("scalar", "ld")
        p.wait("scalar", "te")
        p.wait("scalar", "ve")
        p.wait("vector", "sc")
        for ch in range(8):
            p.act(wv16[:, ch * D:(ch + 1) * D], stage32[:, ch * D:(ch + 1) * D],
                  AF.Copy)
        p.wait("sync", "sc")
        p.dma("sync", stage32[:, 0:8 * DH].rearrange("p (c d) -> p c d", c=8),
              wqT_in.rearrange("(c p) d -> p c d", p=128), sem="ld")
        p.dma("sync", stage32[:, 8 * DH:16 * DH].rearrange("p (c d) -> p c d", c=8),
              wkT_in.rearrange("(c p) d -> p c d", p=128), sem="ld")
        p.dma("sync", stage32[:, 4096:4098], bqp_in, sem="ld")
        p.dma("sync", stage32[:, 4098:4100], bkp_in, sem="ld")
        p.dma("sync", stage32[:, 4100:4102], k0p_in, sem="ld")
        p.dma("sync", stage32[:, 4102:4110], v0p_in, sem="ld")
        p.dma("sync", stage32[:, 4110:4118], bvp_in, sem="ld")
        p.wait("scalar", "ld")
        p.wait("vector", "ld")
        for ch in range(8):
            hilo_(p, wq16[:, ch * 258:ch * 258 + DH],
                  wq16[:, ch * 258 + DH:ch * 258 + 258],
                  stage32[:, ch * DH:(ch + 1) * DH])
            hilo_(p, wk16[:, ch * 258:ch * 258 + DH],
                  wk16[:, ch * 258 + DH:ch * 258 + 258],
                  stage32[:, 8 * DH + ch * DH:8 * DH + (ch + 1) * DH])
        bq_c = scrG16[:, 7680:7684]
        bk_c = scrG16[:, 7684:7688]
        v0_c = scrG16[:, 7688:7704]
        bv_c = scrG16[:, 7704:7720]
        hilo_(p, bq_c[:, 0:2], bq_c[:, 2:4], stage32[:, 4096:4098])
        hilo_(p, bk_c[:, 0:2], bk_c[:, 2:4], stage32[:, 4098:4100])
        hilo_(p, k0hl[:, 0:2], k0hl[:, 2:4], stage32[:, 4100:4102])
        hilo_(p, v0_c[:, 0:8], v0_c[:, 8:16], stage32[:, 4102:4110])
        hilo_(p, bv_c[:, 0:8], bv_c[:, 8:16], stage32[:, 4110:4118])
        p.wait("sync", "ve")
        p.wait("sync", "sc")
        for r, cc in ((0, 0), (1, 2)):
            p.dma("sync", bq16[r:r + 1, 0:128], bq_c[:, cc:cc + 1], sem="ld")
            p.dma("sync", bq16[r:r + 1, 128:129], bq_c[0:1, cc + 1:cc + 2], sem="ld")
            p.dma("sync", bk16[r:r + 1, 0:128], bk_c[:, cc:cc + 1], sem="ld")
            p.dma("sync", bk16[r:r + 1, 128:129], bk_c[0:1, cc + 1:cc + 2], sem="ld")
        p.dma("sync", v0r16[0:1, :], v0_c[:, 0:8], sem="ld")
        p.dma("sync", bv16[0:1, :], bv_c[:, 0:8], sem="ld")
        p.dma("sync", bv16[1:2, :], bv_c[:, 8:16], sem="ld")
        p.dma("sync", k0aug[0:1, :], k0hl[0:1, 1:2], sem="ld")
        p.dma("sync", k0aug[1:2, :], k0hl[0:1, 3:4], sem="ld")
        p.dma("sync", k0aug[2:3, :], k0hl[0:1, 1:2], sem="ld")
        p.wait("sync", "ld")
        p.wait("tensor", "ld")
        p.v("memset", ones64, 1.0)
        p.wait("vector", "sc")
        p.v("tensor_scalar", sA[:, 32:48], sA[:, 0:16], 1.0 / D, None, ALU.mult)
        p.v("tensor_scalar", sA[:, 48:64], sA[:, 16:32], 1.0 / D, None, ALU.mult)
        p.v("tensor_tensor", tmp32[:, 0:16], sA[:, 32:48], sA[:, 32:48], ALU.mult)
        p.v("tensor_sub", sA[:, 48:64], sA[:, 48:64], tmp32[:, 0:16])
        p.v("tensor_scalar", tmp32[:, 0:16], sA[:, 48:64], EPS_LN, None, ALU.add,
            sem="ve")
        p.wait("scalar", "ve")
        p.act(tmp32[:, 16:32], tmp32[:, 0:16], AF.Ln)
        p.act(sM[:, 0:16], tmp32[:, 16:32], AF.Exp, scale=-0.5, sem="sc")
        p.wait("vector", "sc")
        p.v("tensor_tensor", tmp32[:, 48:64], sA[:, 32:48], sM[:, 0:16], ALU.mult)
        p.v("tensor_scalar", sM[:, 16:32], tmp32[:, 48:64], -1.0, None, ALU.mult,
            sem="ve")
        cc_kb = None
        for g in range(NG):
            for ti in range(GT):
                t = g * GT + ti
                xt = x_sb[:, t * D:(t + 1) * D]
                p.wait("scalar", "ve")
                p.act(tmp32[:, :], xt, AF.Copy, bias=sM[:, 16 + t:17 + t],
                      scale=sM[:, 0 + t:1 + t])
                p.wait("scalar", "sc")
                p.act(scrH_hi[:, ti * D:(ti + 1) * D], tmp32[:, :], AF.Copy,
                      sem="sc")
                p.wait("vector", "sc")
                p.v("tensor_tensor", scrH_lo[:, ti * D:(ti + 1) * D], tmp32[:, :],
                    scrH_hi[:, ti * D:(ti + 1) * D], ALU.subtract)
                p.v("tensor_add", xt, xt, tmp32[:, :], sem="ve")
            p.wait("tensor", "ve")
            psA16_ = psA.bitcast(F16)
            for hl, (hsrc, dst) in enumerate(((scrH_hi, hfm_hi), (scrH_lo, hfm_lo))):
                for ch in range(8):
                    for ti in range(GT):
                        p.tr(psA16_[:, (2 * ti + hl) * 1024:(2 * ti + hl) * 1024 + 128],
                             hsrc[:, ti * D + ch * 128:ti * D + (ch + 1) * 128],
                             id16[:, :], sem="te")
                    p.wait("scalar", "te")
                    for ti in range(GT):
                        p.act(dst[:, ch * GTOK + ti * 128:ch * GTOK + (ti + 1) * 128],
                              psA16_[:, (2 * ti + hl) * 1024:(2 * ti + hl) * 1024 + 128],
                              AF.Copy, sem="sc")
                    p.wait("tensor", "sc")
            for ti in range(GT):
                t = g * GT + ti
                tsl = slice(ti * 128, (ti + 1) * 128)
                psq = psB[:, 0:DH]
                for ch in range(8):
                    hh = hfm_hi[:, ch * GTOK + ti * 128:ch * GTOK + (ti + 1) * 128]
                    hl_ = hfm_lo[:, ch * GTOK + ti * 128:ch * GTOK + (ti + 1) * 128]
                    wh = wq16[:, ch * 258:ch * 258 + DH]
                    wl = wq16[:, ch * 258 + DH:ch * 258 + 258]
                    p.mm(psq, hh, wh, start=(ch == 0), stop=False)
                    p.mm(psq, hl_, wh, start=False, stop=False)
                    p.mm(psq, hh, wl, start=False, stop=False)
                p.mm(psq, ones16r[0:2, 0:128], bq16[0:2, :], start=False,
                     stop=True, sem="te")
                p.wait("scalar", "te")
                p.act(qtm16[:, 0:DH], psq, AF.Copy, sem="sc")
                p.wait("vector", "sc")
                p.v("tensor_tensor", qtm_lo, psq, qtm16[:, 0:DH],
                    ALU.subtract, sem="ve")
                psk_ = psB[:, 512:512 + DH]
                for ch in range(8):
                    hh = hfm_hi[:, ch * GTOK + ti * 128:ch * GTOK + (ti + 1) * 128]
                    hl_ = hfm_lo[:, ch * GTOK + ti * 128:ch * GTOK + (ti + 1) * 128]
                    wh = wk16[:, ch * 258:ch * 258 + DH]
                    wl = wk16[:, ch * 258 + DH:ch * 258 + 258]
                    p.mm(psk_, hh, wh, start=(ch == 0), stop=False)
                    p.mm(psk_, hl_, wh, start=False, stop=False)
                    p.mm(psk_, hh, wl, start=False, stop=False)
                p.mm(psk_, ones16r[0:2, 0:128], bk16[0:2, :], start=False,
                     stop=True, sem="te")
                p.wait("scalar", "te")
                p.wait("scalar", "kv")
                p.act(ktm16[:, 0:DH], psk_, AF.Copy)
                p.wait("scalar", "sc")
                p.wait("vector", "sc")
                p.v("tensor_tensor", ktm_lo, psk_, ktm16[:, 0:DH],
                    ALU.subtract, sem="ve")
                for half in range(2):
                    psv = psA[:, half * 512:(half + 1) * 512]
                    for ch in range(8):
                        hh = hfm_hi[:, ch * GTOK + ti * 128:ch * GTOK + (ti + 1) * 128]
                        wv_ = wv16[:, ch * D + half * 512:ch * D + (half + 1) * 512]
                        p.mm(psv, hh, wv_, start=(ch == 0), stop=False)
                    p.mm(psv, ones64, bv16[0:2, half * 512:(half + 1) * 512],
                         start=False, stop=True, sem="te")
                p.wait("scalar", "te")
                p.wait("scalar", "kv")
                p.act(vbuf[:, 0:512], psA[:, 0:512], AF.Copy)
                p.wait("scalar", "sc")
                p.act(vbuf[:, 512:1024], psA[:, 512:1024], AF.Copy, sem="sc")
                p.wait("sync", "sc")
                p.dma("sync", kvv_inh[t // 8].ap()[(t % 8) * 128:(t % 8 + 1) * 128, :],
                      vbuf[:, 0:1024], sem="kv")
                p.wait("tensor", "ve")
                p.tr(psA16_[:, 3072:3072 + 128], qtm16[:, 0:128], id16[:, :])
                p.tr(psA16_[:, 3072 + 128:3072 + 256], qtm_lo[:, 0:128],
                     id16[:, :])
                p.tr(psA16_[0:1, 3072 + 256:3072 + 384], qtm16[:, 128:129],
                     id16[:, :])
                p.tr(psA16_[0:1, 3072 + 384:3072 + 512], qtm_lo[:, 128:129],
                     id16[:, :], sem="te")
                p.tr(psA16_[:, 3584:3584 + 128], ktm16[:, 0:128], id16[:, :])
                p.tr(psA16_[:, 3584 + 128:3584 + 256], ktm_lo[:, 0:128],
                     id16[:, :])
                p.tr(psA16_[0:1, 3584 + 256:3584 + 384], ktm16[:, 128:129],
                     id16[:, :])
                p.tr(psA16_[0:1, 3584 + 384:3584 + 512], ktm_lo[:, 128:129],
                     id16[:, :], sem="te")
                p.wait("scalar", "te")
                p.act(qfm_hi[:, t * 128:(t + 1) * 128], psA16_[:, 3072:3200],
                      AF.Copy)
                p.act(qfm_lo[:, t * 128:(t + 1) * 128], psA16_[:, 3200:3328],
                      AF.Copy)
                p.act(stg32r[0:1, t * 128:(t + 1) * 128],
                      psA16_[0:1, 3328:3456], AF.Copy)
                p.act(stg64r[0:1, t * 128:(t + 1) * 128],
                      psA16_[0:1, 3456:3584], AF.Copy)
                p.act(kfs_hi[:, t * 128:(t + 1) * 128], psA16_[:, 3584:3712],
                      AF.Copy)
                p.act(kfs_lo[:, t * 128:(t + 1) * 128], psA16_[:, 3712:3840],
                      AF.Copy)
                p.act(kfs_ra[0:1, t * 128:(t + 1) * 128],
                      psA16_[0:1, 3840:3968], AF.Copy)
                p.act(kfs_rb[0:1, t * 128:(t + 1) * 128],
                      psA16_[0:1, 3968:4096], AF.Copy, sem="sc")
                p.wait("tensor", "sc")
                if t in (7, 15):
                    hh = t // 8
                    cs = slice(hh * 1024, hh * 1024 + 1024)
                    p.wait("sync", "sc")
                    p.dma("sync", kvk_inh[hh].ap()[0:128, :], kfs_hi[:, cs],
                          sem="kv")
                    p.dma("sync", kvk_inh[hh].ap()[128:129, :], kfs_ra[0:1, cs],
                          sem="kv")
                    p.dma("sync", kvk_inh[hh].ap()[129:257, :], kfs_lo[:, cs],
                          sem="kv")
                    p.dma("sync", kvk_inh[hh].ap()[257:258, :], kfs_rb[0:1, cs],
                          sem="kv")
                    p.wait("gpsimd", "kv")
                    p.coll("AllGather", ALU.bypass, [[0, 1, 2, 3], [4, 5, 6, 7]],
                           [kvk_inh[hh].ap().opt()], [kvk_outh[hh].ap().opt()],
                           sem="cc")
                    p.coll("AllGather", ALU.bypass, [[0, 1, 2, 3], [4, 5, 6, 7]],
                           [kvv_inh[hh].ap().opt()], [kvv_outh[hh].ap().opt()],
                           sem="cc")
                    if hh == 1:
                        cc_kb = p.c["cc"] - 1
        p.wait("sync", "sc")
        p.dma("sync", qaug[0:1, :], stg32r[0:1, 0:TC], sem="ld")
        p.dma("sync", qaug[1:2, :], stg32r[0:1, 0:TC], sem="ld")
        p.dma("sync", qaug[2:3, :], stg64r[0:1, 0:TC], sem="ld")
        p.wait("sync", "ld")
        p.wait("sync", "cc", val=cc_kb)
        for hh in range(2):
            for r in range(4):
                ko = kvk_outh[hh].ap()
                c0 = r * 2048 + hh * 1024
                p.dma("sync", kfm_hi[:, c0:c0 + 1024],
                      ko[r * 2 * DH:r * 2 * DH + 128, :], sem="ld")
                p.dma("sync", kfm_lo[:, c0:c0 + 1024],
                      ko[r * 2 * DH + 129:r * 2 * DH + 257, :], sem="ld")
                p.dma("sync", scrG16[0:1, c0:c0 + 1024],
                      ko[r * 2 * DH + 128:r * 2 * DH + 129, :], sem="ld")
                p.dma("sync", stg32r[0:1, c0:c0 + 1024],
                      ko[r * 2 * DH + 257:r * 2 * DH + 258, :], sem="ld")
        p.wait("sync", "ld")
        p.wait("vector", "ld")
        p.wait("vector", "sc")
        p.v("memset", stg64r[0:1, 0:2048], 1.0, sem="ve")
        p.wait("sync", "sc")
        p.wait("sync", "ve")
        p.dma("sync", scrG16[1:2, 0:8192], stg32r[0:1, 0:8192], sem="ld")
        p.dma("sync", scrG16[2:3, 0:8192], scrG16[0:1, 0:8192], sem="ld")
        for i4 in range(4):
            p.dma("sync", scrG16[3:4, i4 * 2048:(i4 + 1) * 2048],
                  stg64r[0:1, 0:2048], sem="ld")
        p.wait("tensor", "ld")
        for t in range(NT):
            tsl = slice(t * 128, (t + 1) * 128)
            ps = psB[:, 1536 + t:1537 + t]
            p.mm(ps, qfm_hi[:, tsl], k0hl[:, 0:1], start=(t == 0), stop=False)
            p.mm(ps, qfm_hi[:, tsl], k0hl[:, 2:3], start=False, stop=False)
            p.mm(ps, qfm_lo[:, tsl], k0hl[:, 0:1], start=False, stop=False)
            p.mm(ps, qaug[0:3, tsl], k0aug[0:3, :], start=False,
                 stop=(t == NT - 1), sem="te")
        p.wait("scalar", "te")
        p.act(ssink[:, :], psB[:, 1536:1552], AF.Copy, sem="sc")
        p.wait("tensor", "sc")
        fwd_sc = {}
        red_ve = {}
        for t in range(NT):
            tsl = slice(t * 128, (t + 1) * 128)
            for n in range(16):
                ps = psA[:, (n % 4) * 512:(n % 4) * 512 + 512]
                key = (t * 16 + n) - 4
                if key >= 0:
                    p.wait("tensor", "sc", val=fwd_sc[key])
                p.mm(ps, qfm_hi[:, tsl], kfm_hi[:, n * 512:(n + 1) * 512],
                     start=True, stop=False)
                p.mm(ps, qaug[0:1, tsl], scrG16[0:1, n * 512:(n + 1) * 512],
                     start=False, stop=True, sem="te")
                p.wait("scalar", "te")
                p.wait("scalar", "ve", val=red_ve.get((t - 1, n // 4), 0))
                p.act(fwd16[:, n * 512:(n + 1) * 512], ps, AF.Copy, sem="sc")
                fwd_sc[t * 16 + n] = p.c["sc"]
                if n % 4 == 3:
                    p.wait("vector", "sc")
                    p.v("reduce_max", out=zsb[:, 12 + (n // 4):13 + (n // 4)],
                        in_=fwd16[:, (n // 4) * 2048:(n // 4) * 2048 + 2048],
                        axis=AX, sem="ve")
                    red_ve[(t, n // 4)] = p.c["ve"]
            p.v("reduce_max", out=mrow[:, t:t + 1], in_=zsb[:, 12:16], axis=AX,
                sem="ve")
        p.wait("vector", "sc")
        p.v("tensor_max", mrow[:, :], mrow[:, :], ssink[:, :])
        p.v("tensor_scalar", mrow[:, :], mrow[:, :], 0.5, None, ALU.add)
        p.v("tensor_sub", psk[:, :], ssink[:, :], mrow[:, :], sem="ve")
        p.wait("scalar", "ve")
        p.act(psk[:, :], psk[:, :], AF.Exp, sem="sc")
        p.wait("vector", "sc")
        p.v("tensor_copy", out=psk16[:, :], in_=psk[:, :])
        p.v("tensor_scalar", lv16[:, :], mrow[:, :], -1.0, None, ALU.mult,
            sem="ve")
        p.wait("tensor", "ve")
        for t in range(NT):
            p.tr(psA16_[0:1, 0:128], psk16[:, t:t + 1], id16[:, :])
            p.tr(psA16_[0:1, 1024:1152], lv16[:, t:t + 1], id16[:, :], sem="te")
            p.wait("scalar", "te")
            p.act(pskT[0:1, t * 128:(t + 1) * 128], psA16_[0:1, 0:128], AF.Copy)
            p.wait("scalar", "sc")
            p.act(stg32r[0:1, t * 128:(t + 1) * 128],
                  psA16_[0:1, 1024:1152], AF.Copy, sem="sc")
            p.wait("tensor", "sc")
        p.wait("sync", "sc")
        p.dma("sync", qaug[3:4, 0:TC], stg32r[0:1, 0:TC], sem="ld")
        p.wait("sync", "ld")
        p.wait("tensor", "ld")
        p.wait("sync", "cc")
        vs_base = p.c["vs"]
        NVT = NQB * NKT
        v_done = {}

        def issue_v(g):
            kt = g % NKT
            r, o = kt // 16, kt % 16
            hh = 0 if o < 8 else 1
            row = r * 1024 + (o - hh * 8) * 128
            if g >= 1:
                p.wait("sync", "vs", val=v_done[g - 1])
            if g >= 3:
                p.wait("sync", "te", val=pv_te[g - 3])
            p.dma("sync", vbuf[:, (g % 3) * D:(g % 3) * D + D],
                  kvv_outh[hh].ap()[row:row + 128, :], sem="vs")
            v_done[g] = p.c["vs"]

        exp_sc = {}
        issue_v(0)
        issue_v(1)

        def emit_scores(qb, kt):
            qsl_ = slice(qb * QB, (qb + 1) * QB)
            par = kt % 2
            sbk = psB[:, par * 512:par * 512 + QB]
            if kt >= 2:
                p.wait("tensor", "sc", val=exp_sc[kt - 2])
            p.mm(sbk, kfm_hi[:, kt * 128:(kt + 1) * 128], qfm_hi[:, qsl_],
                 start=True, stop=False)
            p.mm(sbk, kfm_hi[:, kt * 128:(kt + 1) * 128], qfm_lo[:, qsl_],
                 start=False, stop=False)
            p.mm(sbk, kfm_lo[:, kt * 128:(kt + 1) * 128], qfm_hi[:, qsl_],
                 start=False, stop=False)
            p.mm(sbk, scrG16[0:4, kt * 128:(kt + 1) * 128], qaug[0:4, qsl_],
                 start=False, stop=True, sem="te")
            p.wait("scalar", "te")
            p.act(pt16[:, (kt % 3) * QB:(kt % 3 + 1) * QB], sbk, AF.Exp,
                  sem="sc")
            exp_sc[kt] = p.c["sc"]

        def emit_pv(qb, kt):
            g = qb * NKT + kt
            par = kt % 3
            p.wait("tensor", "sc", val=exp_sc[kt])
            p.wait("tensor", "vs", val=v_done[g])
            for qt in range(2):
                ptsl = pt16[:, par * QB + qt * 128:par * QB + (qt + 1) * 128]
                vsl = vbuf[:, (g % 3) * D:(g % 3) * D + D]
                p.mm(psA[:, qt * 1024:qt * 1024 + 512], ptsl, vsl[:, 0:512],
                     start=(kt == 0), stop=False)
                p.mm(psA[:, qt * 1024 + 512:qt * 1024 + 1024], ptsl,
                     vsl[:, 512:1024], start=(kt == 0), stop=False)
                p.mm(psB[:, 1024 + qt:1025 + qt], ptsl, ones16[:, 0:1],
                     start=(kt == 0 and qt == 0),
                     stop=(kt == NKT - 1 and qt == 1), sem="te")
            pv_te[g] = p.c["te"]
            if g + 3 < NVT:
                issue_v(g + 3)

        pv_te = {}
        issue_v(2)
        for qb in range(NQB):
            exp_sc.clear()
            emit_scores(qb, 0)
            emit_scores(qb, 1)
            for kt in range(NKT):
                if kt + 2 < NKT:
                    emit_scores(qb, kt + 2)
                emit_pv(qb, kt)
            for qt in range(2):
                t = qb * 2 + qt
                tsl2 = slice(t * 128, (t + 1) * 128)
                p.mm(psA[:, qt * 1024:qt * 1024 + 512], pskT[0:1, tsl2],
                     v0r16[0:1, 0:512], start=False, stop=True)
                p.mm(psA[:, qt * 1024 + 512:qt * 1024 + 1024],
                     pskT[0:1, tsl2], v0r16[0:1, 512:1024],
                     start=False, stop=True, sem="te")
            p.wait("vector", "te")
            p.v("tensor_copy", out=zsb[:, 0:2], in_=psB[:, 1024:1026])
            p.v("tensor_add", zsb[:, 2:4], zsb[:, 0:2], psk[:, 2 * qb:2 * qb + 2])
            p.v("reciprocal", out=zsb[:, 4:6], in_=zsb[:, 2:4])
            for qt in range(2):
                t = qb * 2 + qt
                xt = x_sb[:, t * D:(t + 1) * D]
                for half in range(2):
                    p.v("scalar_tensor_tensor", xt[:, half * 512:(half + 1) * 512],
                        psA[:, qt * 1024 + half * 512:qt * 1024 + (half + 1) * 512],
                        zsb[:, 4 + qt:5 + qt], xt[:, half * 512:(half + 1) * 512],
                        ALU.mult, ALU.add, sem="ve")
                p.v("reduce_sum", out=sA[:, t:t + 1], in_=xt, axis=AX, sem="ve")
            p.wait("tensor", "ve")
        p.wait("scalar", "ve")
        for t in range(NT):
            p.act(scrH_lo.bitcast(F32)[:, 0:1024], x_sb[:, t * D:(t + 1) * D],
                  AF.Square, accum_out=sA[:, 16 + t:17 + t], sem="sc")

    def emit_out(p: Pass):
        p.wait("sync", "ve")
        for t in range(NT):
            p.dma("sync", x_out_t[t], x_sb[:, t * D:(t + 1) * D], sem="st")
        p.wait("sync", "st")

    psA = nc.alloc_psum_tensor("psA", [128, 2048], F32).ap()
    psB = nc.alloc_psum_tensor("psB", [128, 2048], F32).ap()

    counts = []
    with nc.Block() as block:
        for eng_name in ENGS:
            c = {s: 0 for s in sem_names}
            counts.append(c)

            def make(f_name, f_c):
                def _(eng):
                    program(Pass(f_name, eng, sems, f_c))
                return _
            getattr(block, eng_name)(make(eng_name, c))
    for c in counts[1:]:
        assert c == counts[0], f"counter mismatch {c} vs {counts[0]}"
    return nc


def host_inputs(I, core):
    b, j = core // 4, core % 4
    t0 = j * TC
    f32 = np.float32

    def T(a):
        return np.ascontiguousarray(np.asarray(a, f32).T)

    def pad129(a):
        out = np.zeros((128, 2), f32)
        a = np.asarray(a, f32).reshape(-1)
        out[:, 0] = a[0:128]
        out[0, 1] = a[128]
        return out

    return {
        "x_in": np.ascontiguousarray(np.asarray(I["L0_attn_x"], f32)[b, t0:t0 + TC]),
        "wqT": T(I["L0_attn_w_q"]),
        "wkT": T(I["L0_attn_w_k"]),
        "wvT": T(I["L0_attn_w_v"]),
        "bqp": pad129(I["L0_attn_b_q"]),
        "bkp": pad129(I["L0_attn_b_k"]),
        "bvp": np.asarray(I["L0_attn_b_v"], f32).reshape(128, 8),
        "k0p": pad129(I["L0_attn_k_0"]),
        "v0p": np.asarray(I["L0_attn_v_0"], f32).reshape(128, 8),
        "qproj": np.asarray(I["L0_ffn_q_proj_w"], f32),
        "routerT": T(I["L0_ffn_router_w"]),
        "auxkT": T(I["L0_ffn_aux_keys"]),
        "maskp": np.asarray(I["L0_ffn_reliability_mask"], f32).reshape(128, 32),
        "w1T": T(I["L0_ffn_unc_w1"]),
        "b1p": np.ascontiguousarray(
            np.asarray(I["L0_ffn_unc_b1"], f32).reshape(2, 128).T),
        "w2p": np.ascontiguousarray(
            np.asarray(I["L0_ffn_unc_w2"], f32).reshape(2, 128).T),
        "b2v_in": np.broadcast_to(
            np.asarray(I["L0_ffn_unc_b2"], f32).reshape(1, 1), (128, 1)).copy(),
        "gw1v_in": np.full((128, 1), np.float32(I["L0_ffn_gate_w1"]), f32),
        "gbv_in": np.full((128, 1), np.float32(I["L0_ffn_gate_bias"]), f32),
        "aux_vals": np.asarray(I["L0_ffn_aux_values"], f32),
        "id16": np.eye(128, dtype=np.float16),
    }


def kernel(**inputs):
    nc = build_nc("full")
    in_maps = [host_inputs(inputs, c) for c in range(8)]
    res = run_bass_kernel_spmd(nc, in_maps, core_ids=list(range(8)))
    out = np.empty((B, S, D), np.float32)
    for c in range(8):
        b, j = c // 4, c % 4
        out[b, j * TC:(j + 1) * TC] = res.results[c]["x_out"]
    return out


# version 07
# speedup vs baseline: 1.8090x; 1.2108x over previous
import numpy as np
import concourse.bass as bass
import concourse.mybir as mybir
from concourse.bass import IndirectOffsetOnAxis
from concourse.bass_utils import run_bass_kernel_spmd

F32 = mybir.dt.float32
F16 = mybir.dt.float16
BF16 = mybir.dt.bfloat16
U32 = mybir.dt.uint32
AF = mybir.ActivationFunctionType
ALU = mybir.AluOpType
AX = mybir.AxisListType.X

B, S, D = 2, 8192, 1024
TC = 2048
NT = 16
GT = 2
GTOK = GT * 128
NG = NT // GT
NKT = 64
NSLOT = 4096
DH = 129
QB = 256
NQB = TC // QB
EPS_LN = 1e-5
RSQ48 = float(1.0 / np.sqrt(48.0))

ENGS = ("sync", "tensor", "scalar", "vector", "gpsimd")


class Pass:
    def __init__(self, eng_name, eng, sems, c):
        self.e = eng_name
        self.eng = eng
        self.sems = sems
        self.c = c

    def bump(self, inst, sem, n):
        if sem is not None:
            self.c[sem] += n
            if inst is not None:
                inst.then_inc(self.sems[sem], n)
        return inst

    def wait(self, eng_name, sem, val=None):
        if self.e == eng_name:
            self.eng.wait_ge(self.sems[sem], self.c[sem] if val is None else val)

    def mm(self, out, lhsT, rhs, start=True, stop=True, sem=None, n=1):
        i = self.eng.matmul(out, lhsT, rhs, start=start, stop=stop) \
            if self.e == "tensor" else None
        return self.bump(i, sem, n)

    def tr(self, out, in_, ident, sem=None, n=1):
        i = self.eng.transpose(out, in_, ident) if self.e == "tensor" else None
        return self.bump(i, sem, n)

    def act(self, out, in_, func, bias=0.0, scale=1.0, accum_out=None,
            sem=None, n=1):
        i = None
        self.wait("scalar", "sc")
        if self.e == "scalar":
            if func == AF.Copy and not (isinstance(bias, float)
                                        and isinstance(scale, float)):
                func = AF.Identity
            kw = {"accum_out": accum_out} if accum_out is not None else {}
            i = self.eng.activation(out, in_, func, bias=bias, scale=scale, **kw)
        return self.bump(i, "sc", 1)

    def v(self, meth, *args, sem=None, n=1, **kw):
        self.wait("vector", "ve")
        i = getattr(self.eng, meth)(*args, **kw) if self.e == "vector" else None
        return self.bump(i, "ve", 1)

    def dma(self, queue, out, in_, sem=None, transpose=False, slow=False):
        i = None
        if self.e == queue:
            if transpose:
                i = self.eng.dma_start_transpose(out=out, in_=in_)
            elif slow:
                i = self.eng.dma_start(out=out, in_=in_,
                                       allow_slow_non_contiguous=True)
            else:
                i = self.eng.dma_start(out=out, in_=in_)
        return self.bump(i, sem, 16 if sem else 0)

    def gather(self, out, table_ap, idx_ap, sem=None):
        i = None
        if self.e == "gpsimd":
            i = self.eng.indirect_dma_start(
                out=out, out_offset=None, in_=table_ap,
                in_offset=IndirectOffsetOnAxis(idx_ap, 0))
        return self.bump(i, sem, 16 if sem else 0)

    def coll(self, kind, op, groups, ins, outs, sem=None):
        i = None
        if self.e == "gpsimd":
            i = self.eng.collective_compute(kind, op, replica_groups=groups,
                                            ins=ins, outs=outs)
        return self.bump(i, sem, 1 if sem else 0)


def build_nc(stage="full", core0_groups=None):
    nc = bass.Bass("TRN2", target_bir_lowering=False)

    def din(name, shape, dt=F32):
        return nc.dram_tensor(name, list(shape), dt, kind="ExternalInput").ap()

    x_in = din("x_in", (TC, D))
    wqT_in = din("wqT", (D, DH))
    wkT_in = din("wkT", (D, DH))
    wvT_in = din("wvT", (D, D))
    bqp_in = din("bqp", (128, 2))
    bkp_in = din("bkp", (128, 2))
    bvp_in = din("bvp", (128, 8))
    k0p_in = din("k0p", (128, 2))
    v0p_in = din("v0p", (128, 8))
    qproj_in = din("qproj", (128, D))
    routerT_in = din("routerT", (128, 48))
    auxkT_in = din("auxkT", (128, NSLOT))
    maskp_in = din("maskp", (128, 32))
    w1T_in = din("w1T", (D, 256))
    b1p_in = din("b1p", (128, 2))
    w2p_in = din("w2p", (128, 2))
    b2v_in = din("b2v_in", (128, 1))
    gw1v_in = din("gw1v_in", (128, 1))
    gbv_in = din("gbv_in", (128, 1))
    aux_in = din("aux_vals", (NSLOT, D))
    id16_in = din("id16", (128, 128), F16)

    x_out = nc.dram_tensor("x_out", [TC, D], F32, kind="ExternalOutput").ap()

    aux_bf = nc.dram_tensor("aux_bf", [NSLOT, D], F16)
    kvv_inh = [nc.dram_tensor(f"kvv_in{i}", [TC // 2, D], F16) for i in range(2)]
    kvv_outh = [nc.dram_tensor(f"kvv_out{i}", [S // 2, D], F16) for i in range(2)]
    kvk_inh = [nc.dram_tensor(f"kvk_in{i}", [2 * DH, TC // 2], F16)
               for i in range(2)]
    kvk_outh = [nc.dram_tensor(f"kvk_out{i}", [8 * DH, S // 8], F16)
                for i in range(2)]
    ar_in = [nc.dram_tensor(f"ar_in{i}", [1, 1], F32) for i in range(2)]
    ar_out = [nc.dram_tensor(f"ar_out{i}", [1, 1], F32) for i in range(2)]

    A = {}

    def sbuf(name, shape, dt):
        A[name] = nc.alloc_sbuf_tensor(name, list(shape), dt).ap()
        return A[name]

    x_sb = sbuf("x_sb", (128, NT * D), F32)
    stage32 = sbuf("stage32", (128, 8192), F32)
    scrF = sbuf("scrF", (128, NSLOT), F32)
    scrG = sbuf("scrG", (128, 8192), BF16)
    arenaW = sbuf("arenaW", (128, 8192), F16)
    scrH_hi = sbuf("scrH_hi", (128, GT * D), F16)
    scrH_lo = sbuf("scrH_lo", (128, GT * D), F16)
    hfm_hi = sbuf("hfm_hi", (128, 8 * GTOK), F16)
    hfm_lo = sbuf("hfm_lo", (128, 8 * GTOK), F16)
    vbuf = sbuf("vbuf", (128, 3072), F16)
    qfm_hi = sbuf("qfm_hi", (128, TC), F16)
    qfm_lo = sbuf("qfm_lo", (128, TC), F16)
    qtm16 = sbuf("qtm16", (128, DH), F16)
    ktm16 = sbuf("ktm16", (128, DH), F16)
    pt16 = sbuf("pt16", (128, 3 * QB), F16)
    qtm_lo = None
    rkT_hi = sbuf("rkT_hi", (50, NSLOT), F16)
    rkT_lo = sbuf("rkT_lo", (48, NSLOT), F16)
    w1T16 = sbuf("w1T16", (128, 8 * 256), F16)
    wq16 = sbuf("wq16", (128, 8 * 2 * DH), F16)
    wk16 = sbuf("wk16", (128, 8 * 2 * DH), F16)
    wrqT16 = sbuf("wrqT16", (128, 8 * 96), F16)

    w2T16 = sbuf("w2T16", (128, 2), F16)
    id16 = sbuf("id16s", (128, 128), F16)
    ones16 = sbuf("ones16", (128, 2), F16)
    ones16r = sbuf("ones16r", (2, 128), F16)
    bk16 = sbuf("bk16", (2, DH), F16)
    k0hl = sbuf("k0hl", (128, 4), F16)
    k0aug = sbuf("k0aug", (4, 1), F16)
    b1c = sbuf("b1c", (128, 2), F32)
    b2v = sbuf("b2v", (128, 1), F32)
    gw1v = sbuf("gw1v", (128, 1), F32)
    gbv = sbuf("gbv", (128, 1), F32)
    sA = sbuf("sA", (128, 64), F32)
    sM = sbuf("sM", (128, 64), F32)
    one1 = sbuf("one1", (1, 16), F32)
    lv16 = sbuf("lv16", (128, 16), F16)
    invb = sbuf("invb", (128, 1), F32)
    top8 = sbuf("top8", (128, 8), F32)
    idx8 = sbuf("idx8", (128, 8), U32)
    w8 = sbuf("w8", (128, 16), F32)
    zsb = sbuf("zsb", (128, 16), F32)
    psk = sbuf("psk", (128, NT), F32)
    psk16 = sbuf("psk16", (128, NT), F16)
    ssink = sbuf("ssink", (128, NT), F32)
    mrow = sbuf("mrow", (128, NT), F32)

    h1pre = arenaW[:, 0:4096]
    rq_hi = arenaW[0:50, 4096:6144]
    rq_lo = arenaW[50:98, 6144:8192]
    rq_lo = arenaW[0:48, 6144:8192]
    qp16 = vbuf
    kfm_hi = stage32.bitcast(F16)[:, 0:8192]
    kfm_lo = stage32.bitcast(F16)[:, 8192:16384]
    kfs_hi = stage32.bitcast(F16)[:, 8192:8192 + 2048]
    kfs_lo = stage32.bitcast(F16)[:, 10240:10240 + 2048]
    kfs_ra = stage32.bitcast(F16)[0:1, 12288:12288 + 2048]
    kfs_rb = stage32.bitcast(F16)[64:65, 12288:12288 + 2048]
    fwd16 = scrF.bitcast(BF16)
    k129 = scrG.bitcast(F16)[0:4, 0:8192]
    wv16 = arenaW
    scrG16 = scrG.bitcast(F16)
    qaug = scrH_hi[0:4, 0:TC]
    v0r16 = scrG16[64:65, 6144:6144 + D]
    pskT = scrG16[64:65, 4096:6144]
    bv16 = scrG16[64:66, 2048:2048 + D]
    ones64 = scrG16[64:66, 3072:3200]
    stg32r = scrG16[32:33, 0:8192]
    stg64r = scrG16[64:65, 0:2048]
    tmp32 = scrF[:, 0:D]
    qtm_lo = pt16[:, 0:DH]
    ktm_lo = pt16[:, 256:256 + DH]
    rt16 = pt16[:, 512:608]
    bq16 = pt16[0:2, 608:608 + DH]

    sem_names = ["ld", "te", "sc", "ve", "gp", "cc", "vs", "st", "aux", "kv",
                 "vs1", "vs2"]
    sems = {s: nc.alloc_semaphore(s) for s in sem_names}

    x_in_t = x_in.rearrange("(t p) d -> t p d", p=128)
    x_out_t = x_out.rearrange("(t p) d -> t p d", p=128)

    def program(p: Pass):
        emit_prep(p)
        emit_ffn(p, 0)
        if stage in ("attn", "full"):
            emit_attn(p)
        if stage == "full":
            emit_ffn(p, 1, skip_stats=True)
        emit_out(p)

    def hilo_(p, dst_hi, dst_lo, src32):
        p.wait("scalar", "sc")
        p.act(dst_hi, src32, AF.Copy, sem="sc")
        p.wait("vector", "sc")
        p.v("tensor_tensor", dst_lo, src32, dst_hi, ALU.subtract, sem="ve")

    def emit_prep(p: Pass):
        for i in range(4):
            p.dma("gpsimd", aux_bf.ap()[i * 1024:(i + 1) * 1024, :],
                  aux_in[i * 1024:(i + 1) * 1024, :], sem="aux")
        for t in range(NT):
            p.dma("sync", x_sb[:, t * D:(t + 1) * D], x_in_t[t], sem="ld")
        p.dma("sync", id16[:, :], id16_in, sem="ld")
        p.dma("sync", b2v[:, :], b2v_in, sem="ld")
        p.dma("sync", gw1v[:, :], gw1v_in, sem="ld")
        p.dma("sync", gbv[:, :], gbv_in, sem="ld")
        p.dma("sync", b1c[:, :], b1p_in, sem="ld")
        p.dma("sync", stage32[:, 0:D], qproj_in, sem="ld")
        p.dma("sync", stage32[:, D:D + 48], routerT_in, sem="ld")
        p.dma("sync", stage32[:, D + 48:D + 48 + 2048]
              .rearrange("p (c d) -> p c d", c=8),
              w1T_in.rearrange("(c p) d -> p c d", p=128), sem="ld")
        p.dma("sync", stage32[:, 3120:3120 + NSLOT], auxkT_in, sem="ld")
        p.dma("sync", stage32[:, 7216:7218], w2p_in, sem="ld")
        p.dma("sync", stage32[:, 7218:7250], maskp_in, sem="ld")
        p.wait("scalar", "ld")
        p.wait("vector", "ld")
        p.v("memset", ones16[:, :], 1.0)
        p.v("memset", ones16r[:, :], 1.0)
        p.v("memset", rq_hi[0:50, :], 1.0, sem="ve")
        hilo_(p, qp16[:, 0:D], qp16[:, D:2 * D], stage32[:, 0:D])
        hilo_(p, rt16[:, 0:48], rt16[:, 48:96], stage32[:, D:D + 48])
        for ch in range(8):
            p.act(w1T16[:, ch * 256:(ch + 1) * 256],
                  stage32[:, D + 48 + ch * 256:D + 48 + (ch + 1) * 256],
                  AF.Copy, sem="sc")
        p.act(w2T16[:, 0:2], stage32[:, 7216:7218], AF.Copy, sem="sc")
        mh16 = scrF.bitcast(F16)[:, 0:32]
        ml16 = scrF.bitcast(F16)[:, 32:64]
        hilo_(p, mh16, ml16, stage32[:, 7218:7250])
        p.wait("sync", "ve")
        p.wait("sync", "sc")
        p.dma("sync", rkT_hi[48:49, :], mh16, sem="ld")
        p.dma("sync", rkT_hi[49:50, :], ml16, sem="ld")
        auxk_hi = scrG.bitcast(F16)[:, 0:NSLOT]
        auxk_lo = scrG.bitcast(F16)[:, NSLOT:2 * NSLOT]
        asrc = stage32[:, 3120:3120 + NSLOT]
        for q in range(4):
            sl = slice(q * 1024, (q + 1) * 1024)
            hilo_(p, auxk_hi[:, sl], auxk_lo[:, sl], asrc[:, sl])
        p.wait("tensor", "sc")
        p.wait("tensor", "ve")
        p.wait("tensor", "ld")
        p.wait("scalar", "ve")
        for n in range(8):
            ps = psA[0:48, (n % 4) * 512:(n % 4) * 512 + 512]
            rh = auxk_hi[:, n * 512:(n + 1) * 512]
            rl = auxk_lo[:, n * 512:(n + 1) * 512]
            p.mm(ps, rt16[:, 0:48], rh, start=True, stop=False)
            p.mm(ps, rt16[:, 0:48], rl, start=False, stop=False)
            p.mm(ps, rt16[:, 48:96], rh, start=False, stop=True, sem="te")
            p.wait("scalar", "te")
            p.act(stage32[0:48, n * 512:(n + 1) * 512], ps, AF.Copy,
                  scale=RSQ48, sem="sc")
            p.wait("tensor", "sc")
        for n in range(8):
            sl = slice(n * 512, (n + 1) * 512)
            hilo_(p, rkT_hi[0:48, sl], rkT_lo[0:48, sl], stage32[0:48, sl])
        for n in range(2):
            ps = psB[0:48, n * 512:(n + 1) * 512]
            qh = qp16[:, n * 512:(n + 1) * 512]
            ql = qp16[:, D + n * 512:D + (n + 1) * 512]
            p.mm(ps, rt16[:, 0:48], qh, start=True, stop=False)
            p.mm(ps, rt16[:, 0:48], ql, start=False, stop=False)
            p.mm(ps, rt16[:, 48:96], qh, start=False, stop=True, sem="te")
        p.wait("scalar", "te")
        p.wait("scalar", "ve")
        for n in range(2):
            p.act(stage32[0:48, 4096 + n * 512:4096 + (n + 1) * 512],
                  psB[0:48, n * 512:(n + 1) * 512], AF.Copy, sem="sc")
        wrq32 = stage32[0:48, 4096:4096 + 1024]
        wrq_h = scrH_hi[0:48, 0:1024]
        wrq_l = scrH_lo[0:48, 0:1024]
        hilo_(p, wrq_h, wrq_l, wrq32)
        p.wait("tensor", "sc")
        p.wait("tensor", "ve")
        psA16 = psA.bitcast(F16)
        for ch in range(8):
            p.tr(psA16[0:128, 0:48], wrq_h[:, ch * 128:(ch + 1) * 128],
                 id16[0:48, 0:48])
            p.tr(psA16[0:128, 1024:1072], wrq_l[:, ch * 128:(ch + 1) * 128],
                 id16[0:48, 0:48], sem="te")
            p.wait("scalar", "te")
            p.act(wrqT16[:, ch * 96:ch * 96 + 48], psA16[0:128, 0:48], AF.Copy,
                  sem="sc")
            p.act(wrqT16[:, ch * 96 + 48:ch * 96 + 96], psA16[0:128, 1024:1072],
                  AF.Copy, sem="sc")
            p.wait("tensor", "sc")
        p.wait("gpsimd", "aux")

    def emit_ffn(p: Pass, call, skip_stats=False):
        if not skip_stats:
            p.wait("scalar", "ve")
            p.wait("vector", "sc")
            for t in range(NT):
                xt = x_sb[:, t * D:(t + 1) * D]
                p.v("reduce_sum", out=sA[:, t:t + 1], in_=xt, axis=AX, sem="ve")
                p.wait("scalar", "sc")
                p.act(tmp32[:, :], xt, AF.Square, accum_out=sA[:, 16 + t:17 + t],
                      sem="sc")
        p.wait("vector", "sc")
        p.wait("scalar", "ve")
        p.v("tensor_scalar", sA[:, 32:48], sA[:, 0:16], 1.0 / D, None, ALU.mult)
        p.v("tensor_scalar", sA[:, 48:64], sA[:, 16:32], 1.0 / D, None, ALU.mult)
        p.v("tensor_tensor", tmp32[:, 0:16], sA[:, 32:48], sA[:, 32:48], ALU.mult)
        p.v("tensor_sub", sA[:, 48:64], sA[:, 48:64], tmp32[:, 0:16])
        p.v("tensor_scalar", tmp32[:, 0:16], sA[:, 48:64], EPS_LN, None, ALU.add,
            sem="ve")
        p.wait("scalar", "ve")
        p.act(tmp32[:, 16:32], tmp32[:, 0:16], AF.Ln, sem="sc")
        p.wait("scalar", "sc")
        p.act(sM[:, 0:16], tmp32[:, 16:32], AF.Exp, scale=-0.5, sem="sc")
        p.wait("vector", "sc")
        p.v("tensor_tensor", tmp32[:, 48:64], sA[:, 32:48], sM[:, 0:16], ALU.mult)
        p.v("tensor_scalar", sM[:, 16:32], tmp32[:, 48:64], -1.0, None, ALU.mult)
        p.v("reciprocal", out=tmp32[:, 16:32], in_=tmp32[:, 0:16])
        p.v("tensor_tensor", tmp32[:, 32:48], sA[:, 48:64], tmp32[:, 16:32],
            ALU.mult, sem="ve")
        p.wait("scalar", "ve")
        p.act(sM[:, 32:48], tmp32[:, 32:48], AF.Ln, bias=1.0, sem="sc")
        p.wait("vector", "sc")
        p.v("tensor_copy", out=lv16[:, :], in_=sM[:, 32:48], sem="ve")
        p.wait("tensor", "ve")
        p.mm(psB[0:1, 0:16], ones16[:, 0:1], lv16[:, :], sem="te")
        p.wait("vector", "te")
        p.v("reduce_sum", out=one1[0:1, 1:2], in_=psB[0:1, 0:16], axis=AX,
            sem="ve")
        p.wait("sync", "ve")
        p.dma("sync", ar_in[call].ap(), one1[0:1, 1:2], sem="st")
        p.wait("gpsimd", "st")
        p.coll("AllReduce", ALU.add, [[0, 1, 2, 3, 4, 5, 6, 7]],
               [ar_in[call].ap().opt()], [ar_out[call].ap().opt()], sem="cc")
        p.wait("vector", "st")
        if call == 1:
            p.v("memset", rq_hi[0:50, :], 1.0, sem="ve")
        for g in range(NG):
            for ti in range(GT):
                t = g * GT + ti
                xt = x_sb[:, t * D:(t + 1) * D]
                p.wait("scalar", "ve")
                p.wait("scalar", "sc")
                p.act(tmp32[:, :], xt, AF.Copy, bias=sM[:, 16 + t:17 + t],
                      scale=sM[:, 0 + t:1 + t], sem="sc")
                p.wait("scalar", "sc")
                p.act(scrH_hi[:, ti * D:(ti + 1) * D], tmp32[:, :], AF.Copy,
                      sem="sc")
                p.wait("vector", "sc")
                p.v("tensor_tensor", scrH_lo[:, ti * D:(ti + 1) * D], tmp32[:, :],
                    scrH_hi[:, ti * D:(ti + 1) * D], ALU.subtract, sem="ve")
                p.v("scalar_tensor_tensor", xt, tmp32[:, :], 0.5, xt,
                    ALU.mult, ALU.add, sem="ve")
            p.wait("tensor", "ve")
            psA16_ = psA.bitcast(F16)
            for hl, (hsrc, dst) in enumerate(((scrH_hi, hfm_hi), (scrH_lo, hfm_lo))):
                for ch in range(8):
                    for ti in range(GT):
                        p.tr(psA16_[:, (2 * ti + hl) * 1024:(2 * ti + hl) * 1024 + 128],
                             hsrc[:, ti * D + ch * 128:ti * D + (ch + 1) * 128],
                             id16[:, :], sem="te")
                    p.wait("scalar", "te")
                    for ti in range(GT):
                        p.act(dst[:, ch * GTOK + ti * 128:ch * GTOK + (ti + 1) * 128],
                              psA16_[:, (2 * ti + hl) * 1024:(2 * ti + hl) * 1024 + 128],
                              AF.Copy, sem="sc")
                    p.wait("tensor", "sc")
            for m in range(2):
                ps = psB[:, m * 512:m * 512 + GTOK]
                for ch in range(8):
                    p.mm(ps, w1T16[:, ch * 256 + m * 128:ch * 256 + m * 128 + 128],
                         hfm_hi[:, ch * GTOK:(ch + 1) * GTOK],
                         start=(ch == 0), stop=(ch == 7), sem="te")
                p.wait("scalar", "te")
                p.act(h1pre[:, m * 2048 + g * GTOK:m * 2048 + (g + 1) * GTOK], ps,
                      AF.Gelu, bias=b1c[:, m:m + 1], sem="sc")
            p.wait("tensor", "sc")
            for ti in range(GT):
                t = g * GT + ti
                ps = psB[:, 1536 + t:1536 + t + 1]
                for m in range(2):
                    p.mm(ps, h1pre[:, m * 2048 + g * GTOK + ti * 128:
                                   m * 2048 + g * GTOK + (ti + 1) * 128],
                         w2T16[:, m:m + 1],
                         start=(t == 0 and m == 0), stop=(t == NT - 1 and m == 1),
                         sem="te")
            ps = psA[0:48, 0:GTOK]
            for ch in range(8):
                wh = wrqT16[:, ch * 96:ch * 96 + 48]
                wl = wrqT16[:, ch * 96 + 48:ch * 96 + 96]
                hh = hfm_hi[:, ch * GTOK:(ch + 1) * GTOK]
                hl_ = hfm_lo[:, ch * GTOK:(ch + 1) * GTOK]
                p.mm(ps, wh, hh, start=(ch == 0), stop=False)
                p.mm(ps, wh, hl_, start=False, stop=False)
                p.mm(ps, wl, hh, start=False, stop=(ch == 7), sem="te")
            p.wait("scalar", "te")
            p.act(rq_hi[0:48, g * GTOK:(g + 1) * GTOK], ps, AF.Copy, sem="sc")
            p.wait("vector", "sc")
            p.v("tensor_tensor", rq_lo[0:48, g * GTOK:(g + 1) * GTOK], ps,
                rq_hi[0:48, g * GTOK:(g + 1) * GTOK], ALU.subtract, sem="ve")
            p.wait("tensor", "ve")
        p.wait("scalar", "te")
        p.act(tmp32[:, 0:16], psB[:, 1536:1552], AF.Copy, bias=b2v[:, 0:1],
              sem="sc")
        p.wait("scalar", "sc")
        p.act(tmp32[:, 16:32], tmp32[:, 0:16], AF.Exp, scale=-1.0, sem="sc")
        p.wait("vector", "sc")
        p.v("tensor_scalar", tmp32[:, 16:32], tmp32[:, 16:32], 1.0, None, ALU.add)
        p.v("reciprocal", out=sM[:, 48:64], in_=tmp32[:, 16:32], sem="ve")
        p.wait("sync", "cc")
        p.dma("sync", one1[0:1, 2:3], ar_out[call].ap(), sem="ld")
        p.wait("sync", "ld")
        p.wait("vector", "ld")
        p.v("tensor_scalar", one1[0:1, 3:4], one1[0:1, 2:3], 1.0 / (B * S), 1e-6,
            ALU.mult, op1=ALU.add)
        p.v("reciprocal", out=one1[0:1, 4:5], in_=one1[0:1, 3:4])
        p.v("tensor_scalar", one1[0:1, 5:6], one1[0:1, 4:5], 0.5, None, ALU.mult)
        p.v("tensor_copy", out=one1.bitcast(F16)[0:1, 12:13], in_=one1[0:1, 5:6],
            sem="ve")
        p.wait("tensor", "ve")
        p.mm(psB[:, 1024:1025], ones16r[0:1, :], one1.bitcast(F16)[0:1, 12:13],
             sem="te")
        p.wait("scalar", "te")
        p.act(invb[:, :], psB[:, 1024:1025], AF.Copy, sem="sc")
        p.wait("vector", "sc")
        p.v("tensor_scalar", sM[:, 32:48], sM[:, 32:48], invb[:, 0:1], None,
            ALU.mult)
        p.v("scalar_tensor_tensor", sM[:, 48:64], sM[:, 48:64], 2.5,
            sM[:, 32:48], ALU.mult, ALU.add)
        p.v("tensor_scalar", sM[:, 48:64], sM[:, 48:64], 5.0, 0.0, ALU.min,
            op1=ALU.max, sem="ve")
        p.wait("scalar", "ve")
        p.act(tmp32[:, 16:32], sM[:, 48:64], AF.Copy, scale=gw1v[:, 0:1],
              bias=gbv[:, 0:1], sem="sc")
        p.wait("scalar", "sc")
        p.act(tmp32[:, 32:48], tmp32[:, 16:32], AF.Exp, scale=-1.0, sem="sc")
        p.wait("vector", "sc")
        p.v("tensor_scalar", tmp32[:, 32:48], tmp32[:, 32:48], 1.0, None, ALU.add)
        p.v("reciprocal", out=sM[:, 48:64], in_=tmp32[:, 32:48], sem="ve")
        a4_ve = p.c["ve"]
        idx_ve = {}
        agg_ve = {}
        w8b = [w8[:, 0:8], w8[:, 8:16]]

        def gslot(t, k):
            if t % 2 == 0:
                return scrG16[:, k * D:(k + 1) * D]
            half = [scrH_hi, scrH_lo, hfm_hi, hfm_lo][k // 2]
            return half[:, (k % 2) * D:(k % 2) * D + D]

        def emit_scores_topk(t):
            g, ti = t // GT, t % GT
            half = psA if t % 2 == 0 else psB
            rq_s = slice(g * GTOK + ti * 128, g * GTOK + (ti + 1) * 128)
            for n in range(8):
                ps = half[:, (n % 4) * 512:(n % 4) * 512 + 512]
                p.mm(ps, rq_hi[0:50, rq_s], rkT_hi[0:50, n * 512:(n + 1) * 512],
                     start=True, stop=False)
                p.mm(ps, rq_hi[0:48, rq_s], rkT_lo[0:48, n * 512:(n + 1) * 512],
                     start=False, stop=False)
                p.mm(ps, rq_lo[0:48, rq_s], rkT_hi[0:48, n * 512:(n + 1) * 512],
                     start=False, stop=True, sem="te")
                if n % 4 == 3:
                    p.wait("scalar", "te")
                    p.wait("scalar", "ve", val=max(idx_ve.get(t - 1, 0), a4_ve))
                    p.act(scrF[:, (n // 4) * 2048:(n // 4) * 2048 + 2048],
                          half[:, 0:2048], AF.Copy, sem="sc")
                    p.wait("tensor", "sc")
            p.wait("vector", "sc")
            p.wait("vector", "gp", val=gp_done.get(t - 1, 0))
            p.v("max", out=top8[:, :], in_=scrF[:, :])
            p.v("tensor_scalar", w8b[t % 2], top8[:, :], top8[:, 0:1], None,
                ALU.subtract)
            p.v("max_index", out=idx8[:, :], in_max=top8[:, :],
                in_values=scrF[:, :], sem="ve")
            idx_ve[t] = p.c["ve"]
            p.wait("gpsimd", "ve", val=idx_ve[t])
            if t >= 2:
                p.wait("gpsimd", "ve", val=agg_ve.get(t - 2, 0))
            for k in range(8):
                p.gather(gslot(t, k), aux_bf.ap(), idx8[:, k:k + 1], sem="gp")
            p.wait("scalar", "ve")
            p.act(w8b[t % 2], w8b[t % 2], AF.Exp, sem="sc")
            p.wait("vector", "sc")
            p.v("reduce_sum", out=zsb[:, 8 + (t % 2):9 + (t % 2)], in_=w8b[t % 2],
                axis=AX)
            p.v("reciprocal", out=zsb[:, 10 + (t % 2):11 + (t % 2)],
                in_=zsb[:, 8 + (t % 2):9 + (t % 2)])
            p.v("tensor_scalar", zsb[:, 12 + (t % 2):13 + (t % 2)],
                zsb[:, 10 + (t % 2):11 + (t % 2)],
                sM[:, 48 + t:49 + t], 0.5, ALU.mult, op1=ALU.mult)
            p.v("tensor_scalar", w8b[t % 2], w8b[t % 2],
                zsb[:, 12 + (t % 2):13 + (t % 2)], None, ALU.mult, sem="ve")

        acc16 = vbuf[:, 0:1024]
        def emit_agg(t):
            p.wait("vector", "gp", val=gp_done[t])
            xt = x_sb[:, t * D:(t + 1) * D]
            p.v("tensor_scalar", acc16, gslot(t, 0), w8b[t % 2][:, 0:1], None,
                ALU.mult)
            for k in range(1, 8):
                p.v("scalar_tensor_tensor", acc16, gslot(t, k),
                    w8b[t % 2][:, k:k + 1], acc16, ALU.mult, ALU.add)
            p.v("scalar_tensor_tensor", xt, acc16, 1.0, xt, ALU.mult, ALU.add,
                sem="ve")
            agg_ve[t] = p.c["ve"]
            if call == 0:
                p.v("reduce_sum", out=sA[:, t:t + 1], in_=xt, axis=AX, sem="ve")

        gp_done = {}
        for t in range(NT):
            emit_scores_topk(t)
            gp_done[t] = p.c["gp"]
            if t >= 1:
                emit_agg(t - 1)
        emit_agg(NT - 1)
        if call == 0:
            p.wait("scalar", "ve")
            for t in range(NT):
                p.act(qfm_hi.bitcast(F32)[:, 0:1024],
                      x_sb[:, t * D:(t + 1) * D], AF.Square,
                      accum_out=sA[:, 16 + t:17 + t], sem="sc")

    def emit_attn(p: Pass):
        p.dma("sync", stage32.rearrange("p (c d) -> p c d", c=8),
              wvT_in.rearrange("(c p) d -> p c d", p=128), sem="ld")
        p.wait("scalar", "ld")
        p.wait("scalar", "te")
        p.wait("scalar", "ve")
        p.wait("vector", "sc")
        for ch in range(8):
            p.act(wv16[:, ch * D:(ch + 1) * D], stage32[:, ch * D:(ch + 1) * D],
                  AF.Copy)
        p.wait("sync", "sc")
        p.dma("sync", stage32[:, 0:8 * DH].rearrange("p (c d) -> p c d", c=8),
              wqT_in.rearrange("(c p) d -> p c d", p=128), sem="ld")
        p.dma("sync", stage32[:, 8 * DH:16 * DH].rearrange("p (c d) -> p c d", c=8),
              wkT_in.rearrange("(c p) d -> p c d", p=128), sem="ld")
        p.dma("sync", stage32[:, 4096:4098], bqp_in, sem="ld")
        p.dma("sync", stage32[:, 4098:4100], bkp_in, sem="ld")
        p.dma("sync", stage32[:, 4100:4102], k0p_in, sem="ld")
        p.dma("sync", stage32[:, 4102:4110], v0p_in, sem="ld")
        p.dma("sync", stage32[:, 4110:4118], bvp_in, sem="ld")
        p.wait("scalar", "ld")
        p.wait("vector", "ld")
        for ch in range(8):
            hilo_(p, wq16[:, ch * 258:ch * 258 + DH],
                  wq16[:, ch * 258 + DH:ch * 258 + 258],
                  stage32[:, ch * DH:(ch + 1) * DH])
            hilo_(p, wk16[:, ch * 258:ch * 258 + DH],
                  wk16[:, ch * 258 + DH:ch * 258 + 258],
                  stage32[:, 8 * DH + ch * DH:8 * DH + (ch + 1) * DH])
        bq_c = scrG16[:, 7680:7684]
        bk_c = scrG16[:, 7684:7688]
        v0_c = scrG16[:, 7688:7704]
        bv_c = scrG16[:, 7704:7720]
        hilo_(p, bq_c[:, 0:2], bq_c[:, 2:4], stage32[:, 4096:4098])
        hilo_(p, bk_c[:, 0:2], bk_c[:, 2:4], stage32[:, 4098:4100])
        hilo_(p, k0hl[:, 0:2], k0hl[:, 2:4], stage32[:, 4100:4102])
        hilo_(p, v0_c[:, 0:8], v0_c[:, 8:16], stage32[:, 4102:4110])
        hilo_(p, bv_c[:, 0:8], bv_c[:, 8:16], stage32[:, 4110:4118])
        p.wait("sync", "ve")
        p.wait("sync", "sc")
        for r, cc in ((0, 0), (1, 2)):
            p.dma("sync", bq16[r:r + 1, 0:128], bq_c[:, cc:cc + 1], sem="ld")
            p.dma("sync", bq16[r:r + 1, 128:129], bq_c[0:1, cc + 1:cc + 2], sem="ld")
            p.dma("sync", bk16[r:r + 1, 0:128], bk_c[:, cc:cc + 1], sem="ld")
            p.dma("sync", bk16[r:r + 1, 128:129], bk_c[0:1, cc + 1:cc + 2], sem="ld")
        p.dma("sync", v0r16[0:1, :], v0_c[:, 0:8], sem="ld")
        p.dma("sync", bv16[0:1, :], bv_c[:, 0:8], sem="ld")
        p.dma("sync", bv16[1:2, :], bv_c[:, 8:16], sem="ld")
        p.dma("sync", k0aug[0:1, :], k0hl[0:1, 1:2], sem="ld")
        p.dma("sync", k0aug[1:2, :], k0hl[0:1, 3:4], sem="ld")
        p.dma("sync", k0aug[2:3, :], k0hl[0:1, 1:2], sem="ld")
        p.wait("sync", "ld")
        p.wait("tensor", "ld")
        p.v("memset", ones64, 1.0)
        p.wait("vector", "sc")
        p.v("tensor_scalar", sA[:, 32:48], sA[:, 0:16], 1.0 / D, None, ALU.mult)
        p.v("tensor_scalar", sA[:, 48:64], sA[:, 16:32], 1.0 / D, None, ALU.mult)
        p.v("tensor_tensor", tmp32[:, 0:16], sA[:, 32:48], sA[:, 32:48], ALU.mult)
        p.v("tensor_sub", sA[:, 48:64], sA[:, 48:64], tmp32[:, 0:16])
        p.v("tensor_scalar", tmp32[:, 0:16], sA[:, 48:64], EPS_LN, None, ALU.add,
            sem="ve")
        p.wait("scalar", "ve")
        p.act(tmp32[:, 16:32], tmp32[:, 0:16], AF.Ln)
        p.act(sM[:, 0:16], tmp32[:, 16:32], AF.Exp, scale=-0.5, sem="sc")
        p.wait("vector", "sc")
        p.v("tensor_tensor", tmp32[:, 48:64], sA[:, 32:48], sM[:, 0:16], ALU.mult)
        p.v("tensor_scalar", sM[:, 16:32], tmp32[:, 48:64], -1.0, None, ALU.mult,
            sem="ve")
        cc_kb = None
        for g in range(NG):
            for ti in range(GT):
                t = g * GT + ti
                xt = x_sb[:, t * D:(t + 1) * D]
                p.wait("scalar", "ve")
                p.act(tmp32[:, :], xt, AF.Copy, bias=sM[:, 16 + t:17 + t],
                      scale=sM[:, 0 + t:1 + t])
                p.wait("scalar", "sc")
                p.act(scrH_hi[:, ti * D:(ti + 1) * D], tmp32[:, :], AF.Copy,
                      sem="sc")
                p.wait("vector", "sc")
                p.v("tensor_tensor", scrH_lo[:, ti * D:(ti + 1) * D], tmp32[:, :],
                    scrH_hi[:, ti * D:(ti + 1) * D], ALU.subtract)
                p.v("tensor_add", xt, xt, tmp32[:, :], sem="ve")
            p.wait("tensor", "ve")
            psA16_ = psA.bitcast(F16)
            for hl, (hsrc, dst) in enumerate(((scrH_hi, hfm_hi), (scrH_lo, hfm_lo))):
                for ch in range(8):
                    for ti in range(GT):
                        p.tr(psA16_[:, (2 * ti + hl) * 1024:(2 * ti + hl) * 1024 + 128],
                             hsrc[:, ti * D + ch * 128:ti * D + (ch + 1) * 128],
                             id16[:, :], sem="te")
                    p.wait("scalar", "te")
                    for ti in range(GT):
                        p.act(dst[:, ch * GTOK + ti * 128:ch * GTOK + (ti + 1) * 128],
                              psA16_[:, (2 * ti + hl) * 1024:(2 * ti + hl) * 1024 + 128],
                              AF.Copy, sem="sc")
                    p.wait("tensor", "sc")
            for ti in range(GT):
                t = g * GT + ti
                tsl = slice(ti * 128, (ti + 1) * 128)
                psq = psB[:, 0:DH]
                for ch in range(8):
                    hh = hfm_hi[:, ch * GTOK + ti * 128:ch * GTOK + (ti + 1) * 128]
                    hl_ = hfm_lo[:, ch * GTOK + ti * 128:ch * GTOK + (ti + 1) * 128]
                    wh = wq16[:, ch * 258:ch * 258 + DH]
                    wl = wq16[:, ch * 258 + DH:ch * 258 + 258]
                    p.mm(psq, hh, wh, start=(ch == 0), stop=False)
                    p.mm(psq, hl_, wh, start=False, stop=False)
                    p.mm(psq, hh, wl, start=False, stop=False)
                p.mm(psq, ones16r[0:2, 0:128], bq16[0:2, :], start=False,
                     stop=True, sem="te")
                p.wait("scalar", "te")
                p.act(qtm16[:, 0:DH], psq, AF.Copy, sem="sc")
                p.wait("vector", "sc")
                p.v("tensor_tensor", qtm_lo, psq, qtm16[:, 0:DH],
                    ALU.subtract, sem="ve")
                psk_ = psB[:, 512:512 + DH]
                for ch in range(8):
                    hh = hfm_hi[:, ch * GTOK + ti * 128:ch * GTOK + (ti + 1) * 128]
                    hl_ = hfm_lo[:, ch * GTOK + ti * 128:ch * GTOK + (ti + 1) * 128]
                    wh = wk16[:, ch * 258:ch * 258 + DH]
                    wl = wk16[:, ch * 258 + DH:ch * 258 + 258]
                    p.mm(psk_, hh, wh, start=(ch == 0), stop=False)
                    p.mm(psk_, hl_, wh, start=False, stop=False)
                    p.mm(psk_, hh, wl, start=False, stop=False)
                p.mm(psk_, ones16r[0:2, 0:128], bk16[0:2, :], start=False,
                     stop=True, sem="te")
                p.wait("scalar", "te")
                p.wait("scalar", "kv")
                p.act(ktm16[:, 0:DH], psk_, AF.Copy)
                p.wait("scalar", "sc")
                p.wait("vector", "sc")
                p.v("tensor_tensor", ktm_lo, psk_, ktm16[:, 0:DH],
                    ALU.subtract, sem="ve")
                for half in range(2):
                    psv = psA[:, half * 512:(half + 1) * 512]
                    for ch in range(8):
                        hh = hfm_hi[:, ch * GTOK + ti * 128:ch * GTOK + (ti + 1) * 128]
                        wv_ = wv16[:, ch * D + half * 512:ch * D + (half + 1) * 512]
                        p.mm(psv, hh, wv_, start=(ch == 0), stop=False)
                    p.mm(psv, ones64, bv16[0:2, half * 512:(half + 1) * 512],
                         start=False, stop=True, sem="te")
                p.wait("scalar", "te")
                p.wait("scalar", "kv")
                p.act(vbuf[:, 0:512], psA[:, 0:512], AF.Copy)
                p.wait("scalar", "sc")
                p.act(vbuf[:, 512:1024], psA[:, 512:1024], AF.Copy, sem="sc")
                p.wait("sync", "sc")
                p.dma("sync", kvv_inh[t // 8].ap()[(t % 8) * 128:(t % 8 + 1) * 128, :],
                      vbuf[:, 0:1024], sem="kv")
                p.wait("tensor", "ve")
                p.tr(psA16_[:, 3072:3072 + 128], qtm16[:, 0:128], id16[:, :])
                p.tr(psA16_[:, 3072 + 128:3072 + 256], qtm_lo[:, 0:128],
                     id16[:, :])
                p.tr(psA16_[0:1, 3072 + 256:3072 + 384], qtm16[:, 128:129],
                     id16[:, :])
                p.tr(psA16_[0:1, 3072 + 384:3072 + 512], qtm_lo[:, 128:129],
                     id16[:, :], sem="te")
                p.tr(psA16_[:, 3584:3584 + 128], ktm16[:, 0:128], id16[:, :])
                p.tr(psA16_[:, 3584 + 128:3584 + 256], ktm_lo[:, 0:128],
                     id16[:, :])
                p.tr(psA16_[0:1, 3584 + 256:3584 + 384], ktm16[:, 128:129],
                     id16[:, :])
                p.tr(psA16_[0:1, 3584 + 384:3584 + 512], ktm_lo[:, 128:129],
                     id16[:, :], sem="te")
                p.wait("scalar", "te")
                p.act(qfm_hi[:, t * 128:(t + 1) * 128], psA16_[:, 3072:3200],
                      AF.Copy)
                p.act(qfm_lo[:, t * 128:(t + 1) * 128], psA16_[:, 3200:3328],
                      AF.Copy)
                p.act(stg32r[0:1, t * 128:(t + 1) * 128],
                      psA16_[0:1, 3328:3456], AF.Copy)
                p.act(stg64r[0:1, t * 128:(t + 1) * 128],
                      psA16_[0:1, 3456:3584], AF.Copy)
                p.act(kfs_hi[:, t * 128:(t + 1) * 128], psA16_[:, 3584:3712],
                      AF.Copy)
                p.act(kfs_lo[:, t * 128:(t + 1) * 128], psA16_[:, 3712:3840],
                      AF.Copy)
                p.act(kfs_ra[0:1, t * 128:(t + 1) * 128],
                      psA16_[0:1, 3840:3968], AF.Copy)
                p.act(kfs_rb[0:1, t * 128:(t + 1) * 128],
                      psA16_[0:1, 3968:4096], AF.Copy, sem="sc")
                p.wait("tensor", "sc")
                if t in (7, 15):
                    hh = t // 8
                    cs = slice(hh * 1024, hh * 1024 + 1024)
                    p.wait("sync", "sc")
                    p.dma("sync", kvk_inh[hh].ap()[0:128, :], kfs_hi[:, cs],
                          sem="kv")
                    p.dma("sync", kvk_inh[hh].ap()[128:129, :], kfs_ra[0:1, cs],
                          sem="kv")
                    p.dma("sync", kvk_inh[hh].ap()[129:257, :], kfs_lo[:, cs],
                          sem="kv")
                    p.dma("sync", kvk_inh[hh].ap()[257:258, :], kfs_rb[0:1, cs],
                          sem="kv")
                    p.wait("gpsimd", "kv")
                    p.coll("AllGather", ALU.bypass, [[0, 1, 2, 3], [4, 5, 6, 7]],
                           [kvk_inh[hh].ap().opt()], [kvk_outh[hh].ap().opt()],
                           sem="cc")
                    p.coll("AllGather", ALU.bypass, [[0, 1, 2, 3], [4, 5, 6, 7]],
                           [kvv_inh[hh].ap().opt()], [kvv_outh[hh].ap().opt()],
                           sem="cc")
                    if hh == 1:
                        cc_kb = p.c["cc"] - 1
        p.wait("sync", "sc")
        p.dma("sync", qaug[0:1, :], stg32r[0:1, 0:TC], sem="ld")
        p.dma("sync", qaug[1:2, :], stg32r[0:1, 0:TC], sem="ld")
        p.dma("sync", qaug[2:3, :], stg64r[0:1, 0:TC], sem="ld")
        p.wait("sync", "ld")
        p.wait("sync", "cc", val=cc_kb)
        for hh in range(2):
            for r in range(4):
                ko = kvk_outh[hh].ap()
                c0 = r * 2048 + hh * 1024
                p.dma("sync", kfm_hi[:, c0:c0 + 1024],
                      ko[r * 2 * DH:r * 2 * DH + 128, :], sem="ld")
                p.dma("sync", kfm_lo[:, c0:c0 + 1024],
                      ko[r * 2 * DH + 129:r * 2 * DH + 257, :], sem="ld")
                p.dma("sync", scrG16[0:1, c0:c0 + 1024],
                      ko[r * 2 * DH + 128:r * 2 * DH + 129, :], sem="ld")
                p.dma("sync", stg32r[0:1, c0:c0 + 1024],
                      ko[r * 2 * DH + 257:r * 2 * DH + 258, :], sem="ld")
        p.wait("sync", "ld")
        p.wait("vector", "ld")
        p.wait("vector", "sc")
        p.v("memset", stg64r[0:1, 0:2048], 1.0, sem="ve")
        p.wait("sync", "sc")
        p.wait("sync", "ve")
        p.dma("sync", scrG16[1:2, 0:8192], stg32r[0:1, 0:8192], sem="ld")
        p.dma("sync", scrG16[2:3, 0:8192], scrG16[0:1, 0:8192], sem="ld")
        for i4 in range(4):
            p.dma("sync", scrG16[3:4, i4 * 2048:(i4 + 1) * 2048],
                  stg64r[0:1, 0:2048], sem="ld")
        p.wait("tensor", "ld")
        for t in range(NT):
            tsl = slice(t * 128, (t + 1) * 128)
            ps = psB[:, 1536 + t:1537 + t]
            p.mm(ps, qfm_hi[:, tsl], k0hl[:, 0:1], start=(t == 0), stop=False)
            p.mm(ps, qfm_hi[:, tsl], k0hl[:, 2:3], start=False, stop=False)
            p.mm(ps, qfm_lo[:, tsl], k0hl[:, 0:1], start=False, stop=False)
            p.mm(ps, qaug[0:3, tsl], k0aug[0:3, :], start=False,
                 stop=(t == NT - 1), sem="te")
        p.wait("scalar", "te")
        p.act(ssink[:, :], psB[:, 1536:1552], AF.Copy, sem="sc")
        p.wait("tensor", "sc")
        fwd_sc = {}
        red_ve = {}
        for t in range(NT):
            tsl = slice(t * 128, (t + 1) * 128)
            for n in range(16):
                ps = psA[:, (n % 4) * 512:(n % 4) * 512 + 512]
                key = (t * 16 + n) - 4
                if key >= 0:
                    p.wait("tensor", "sc", val=fwd_sc[key])
                p.mm(ps, qfm_hi[:, tsl], kfm_hi[:, n * 512:(n + 1) * 512],
                     start=True, stop=False)
                p.mm(ps, qaug[0:1, tsl], scrG16[0:1, n * 512:(n + 1) * 512],
                     start=False, stop=True, sem="te")
                p.wait("scalar", "te")
                p.wait("scalar", "ve", val=red_ve.get((t - 1, n // 4), 0))
                p.act(fwd16[:, n * 512:(n + 1) * 512], ps, AF.Copy, sem="sc")
                fwd_sc[t * 16 + n] = p.c["sc"]
                if n % 4 == 3:
                    p.wait("vector", "sc")
                    p.v("reduce_max", out=zsb[:, 12 + (n // 4):13 + (n // 4)],
                        in_=fwd16[:, (n // 4) * 2048:(n // 4) * 2048 + 2048],
                        axis=AX, sem="ve")
                    red_ve[(t, n // 4)] = p.c["ve"]
            p.v("reduce_max", out=mrow[:, t:t + 1], in_=zsb[:, 12:16], axis=AX,
                sem="ve")
        p.wait("vector", "sc")
        p.v("tensor_max", mrow[:, :], mrow[:, :], ssink[:, :])
        p.v("tensor_scalar", mrow[:, :], mrow[:, :], 0.5, None, ALU.add)
        p.v("tensor_sub", psk[:, :], ssink[:, :], mrow[:, :], sem="ve")
        p.wait("scalar", "ve")
        p.act(psk[:, :], psk[:, :], AF.Exp, sem="sc")
        p.wait("vector", "sc")
        p.v("tensor_copy", out=psk16[:, :], in_=psk[:, :])
        p.v("tensor_scalar", lv16[:, :], mrow[:, :], -1.0, None, ALU.mult,
            sem="ve")
        p.wait("tensor", "ve")
        for t in range(NT):
            p.tr(psA16_[0:1, 0:128], psk16[:, t:t + 1], id16[:, :])
            p.tr(psA16_[0:1, 1024:1152], lv16[:, t:t + 1], id16[:, :], sem="te")
            p.wait("scalar", "te")
            p.act(pskT[0:1, t * 128:(t + 1) * 128], psA16_[0:1, 0:128], AF.Copy)
            p.wait("scalar", "sc")
            p.act(stg32r[0:1, t * 128:(t + 1) * 128],
                  psA16_[0:1, 1024:1152], AF.Copy, sem="sc")
            p.wait("tensor", "sc")
        p.wait("sync", "sc")
        p.dma("sync", qaug[3:4, 0:TC], stg32r[0:1, 0:TC], sem="ld")
        p.wait("sync", "ld")
        p.wait("tensor", "ld")
        p.wait("sync", "cc")
        NVT = NQB * NKT
        v_done = {}
        vsems = ["vs", "vs1", "vs2"]

        def issue_v(g):
            kt = g % NKT
            r, o = kt // 16, kt % 16
            hh = 0 if o < 8 else 1
            row = r * 1024 + (o - hh * 8) * 128
            if g >= 3:
                p.wait("sync", "te", val=pv_te[g - 3])
            p.dma("sync", vbuf[:, (g % 3) * D:(g % 3) * D + D],
                  kvv_outh[hh].ap()[row:row + 128, :], sem=vsems[g % 3])
            v_done[g] = p.c[vsems[g % 3]]

        exp_sc = {}
        issue_v(0)
        issue_v(1)

        def emit_scores(qb, kt):
            qsl_ = slice(qb * QB, (qb + 1) * QB)
            par = kt % 2
            sbk = psB[:, par * 512:par * 512 + QB]
            if kt >= 2:
                p.wait("tensor", "sc", val=exp_sc[kt - 2])
            p.mm(sbk, kfm_hi[:, kt * 128:(kt + 1) * 128], qfm_hi[:, qsl_],
                 start=True, stop=False)
            p.mm(sbk, kfm_hi[:, kt * 128:(kt + 1) * 128], qfm_lo[:, qsl_],
                 start=False, stop=False)
            p.mm(sbk, kfm_lo[:, kt * 128:(kt + 1) * 128], qfm_hi[:, qsl_],
                 start=False, stop=False)
            p.mm(sbk, scrG16[0:4, kt * 128:(kt + 1) * 128], qaug[0:4, qsl_],
                 start=False, stop=True, sem="te")
            p.wait("scalar", "te")
            p.act(pt16[:, (kt % 3) * QB:(kt % 3 + 1) * QB], sbk, AF.Exp,
                  sem="sc")
            exp_sc[kt] = p.c["sc"]

        def emit_pv(qb, kt):
            g = qb * NKT + kt
            par = kt % 3
            p.wait("tensor", "sc", val=exp_sc[kt])
            p.wait("tensor", vsems[g % 3], val=v_done[g])
            for qt in range(2):
                ptsl = pt16[:, par * QB + qt * 128:par * QB + (qt + 1) * 128]
                vsl = vbuf[:, (g % 3) * D:(g % 3) * D + D]
                p.mm(psA[:, qt * 1024:qt * 1024 + 512], ptsl, vsl[:, 0:512],
                     start=(kt == 0), stop=False)
                p.mm(psA[:, qt * 1024 + 512:qt * 1024 + 1024], ptsl,
                     vsl[:, 512:1024], start=(kt == 0), stop=False)
                p.mm(psB[:, 1024 + qt:1025 + qt], ptsl, ones16[:, 0:1],
                     start=(kt == 0 and qt == 0),
                     stop=(kt == NKT - 1 and qt == 1), sem="te")
            pv_te[g] = p.c["te"]
            if g + 3 < NVT:
                issue_v(g + 3)

        pv_te = {}
        issue_v(2)
        for qb in range(NQB):
            exp_sc.clear()
            emit_scores(qb, 0)
            emit_scores(qb, 1)
            for kt in range(NKT):
                if kt + 2 < NKT:
                    emit_scores(qb, kt + 2)
                emit_pv(qb, kt)
            for qt in range(2):
                t = qb * 2 + qt
                tsl2 = slice(t * 128, (t + 1) * 128)
                p.mm(psA[:, qt * 1024:qt * 1024 + 512], pskT[0:1, tsl2],
                     v0r16[0:1, 0:512], start=False, stop=True)
                p.mm(psA[:, qt * 1024 + 512:qt * 1024 + 1024],
                     pskT[0:1, tsl2], v0r16[0:1, 512:1024],
                     start=False, stop=True, sem="te")
            p.wait("vector", "te")
            p.v("tensor_copy", out=zsb[:, 0:2], in_=psB[:, 1024:1026])
            p.v("tensor_add", zsb[:, 2:4], zsb[:, 0:2], psk[:, 2 * qb:2 * qb + 2])
            p.v("reciprocal", out=zsb[:, 4:6], in_=zsb[:, 2:4])
            for qt in range(2):
                t = qb * 2 + qt
                xt = x_sb[:, t * D:(t + 1) * D]
                for half in range(2):
                    p.v("scalar_tensor_tensor", xt[:, half * 512:(half + 1) * 512],
                        psA[:, qt * 1024 + half * 512:qt * 1024 + (half + 1) * 512],
                        zsb[:, 4 + qt:5 + qt], xt[:, half * 512:(half + 1) * 512],
                        ALU.mult, ALU.add, sem="ve")
                p.v("reduce_sum", out=sA[:, t:t + 1], in_=xt, axis=AX, sem="ve")
            p.wait("tensor", "ve")
        p.wait("scalar", "ve")
        for t in range(NT):
            p.act(scrH_lo.bitcast(F32)[:, 0:1024], x_sb[:, t * D:(t + 1) * D],
                  AF.Square, accum_out=sA[:, 16 + t:17 + t], sem="sc")

    def emit_out(p: Pass):
        p.wait("sync", "ve")
        for t in range(NT):
            p.dma("sync", x_out_t[t], x_sb[:, t * D:(t + 1) * D], sem="st")
        p.wait("sync", "st")

    psA = nc.alloc_psum_tensor("psA", [128, 2048], F32).ap()
    psB = nc.alloc_psum_tensor("psB", [128, 2048], F32).ap()

    counts = []
    with nc.Block() as block:
        for eng_name in ENGS:
            c = {s: 0 for s in sem_names}
            counts.append(c)

            def make(f_name, f_c):
                def _(eng):
                    program(Pass(f_name, eng, sems, f_c))
                return _
            getattr(block, eng_name)(make(eng_name, c))
    for c in counts[1:]:
        assert c == counts[0], f"counter mismatch {c} vs {counts[0]}"
    return nc


def host_inputs(I, core):
    b, j = core // 4, core % 4
    t0 = j * TC
    f32 = np.float32

    def T(a):
        return np.ascontiguousarray(np.asarray(a, f32).T)

    def pad129(a):
        out = np.zeros((128, 2), f32)
        a = np.asarray(a, f32).reshape(-1)
        out[:, 0] = a[0:128]
        out[0, 1] = a[128]
        return out

    return {
        "x_in": np.ascontiguousarray(np.asarray(I["L0_attn_x"], f32)[b, t0:t0 + TC]),
        "wqT": T(I["L0_attn_w_q"]),
        "wkT": T(I["L0_attn_w_k"]),
        "wvT": T(I["L0_attn_w_v"]),
        "bqp": pad129(I["L0_attn_b_q"]),
        "bkp": pad129(I["L0_attn_b_k"]),
        "bvp": np.asarray(I["L0_attn_b_v"], f32).reshape(128, 8),
        "k0p": pad129(I["L0_attn_k_0"]),
        "v0p": np.asarray(I["L0_attn_v_0"], f32).reshape(128, 8),
        "qproj": np.asarray(I["L0_ffn_q_proj_w"], f32),
        "routerT": T(I["L0_ffn_router_w"]),
        "auxkT": T(I["L0_ffn_aux_keys"]),
        "maskp": np.asarray(I["L0_ffn_reliability_mask"], f32).reshape(128, 32),
        "w1T": T(I["L0_ffn_unc_w1"]),
        "b1p": np.ascontiguousarray(
            np.asarray(I["L0_ffn_unc_b1"], f32).reshape(2, 128).T),
        "w2p": np.ascontiguousarray(
            np.asarray(I["L0_ffn_unc_w2"], f32).reshape(2, 128).T),
        "b2v_in": np.broadcast_to(
            np.asarray(I["L0_ffn_unc_b2"], f32).reshape(1, 1), (128, 1)).copy(),
        "gw1v_in": np.full((128, 1), np.float32(I["L0_ffn_gate_w1"]), f32),
        "gbv_in": np.full((128, 1), np.float32(I["L0_ffn_gate_bias"]), f32),
        "aux_vals": np.asarray(I["L0_ffn_aux_values"], f32),
        "id16": np.eye(128, dtype=np.float16),
    }


def kernel(**inputs):
    nc = build_nc("full")
    in_maps = [host_inputs(inputs, c) for c in range(8)]
    res = run_bass_kernel_spmd(nc, in_maps, core_ids=list(range(8)))
    out = np.empty((B, S, D), np.float32)
    for c in range(8):
        b, j = c // 4, c % 4
        out[b, j * TC:(j + 1) * TC] = res.results[c]["x_out"]
    return out


# version 08
# speedup vs baseline: 2.1198x; 1.1718x over previous
import numpy as np
import concourse.bass as bass
import concourse.mybir as mybir
from concourse.bass import IndirectOffsetOnAxis
from concourse.bass_utils import run_bass_kernel_spmd

F32 = mybir.dt.float32
F16 = mybir.dt.float16
BF16 = mybir.dt.bfloat16
U32 = mybir.dt.uint32
AF = mybir.ActivationFunctionType
ALU = mybir.AluOpType
AX = mybir.AxisListType.X

B, S, D = 2, 8192, 1024
TC = 2048
NT = 16
GT = 2
GTOK = GT * 128
NG = NT // GT
NKT = 64
NSLOT = 4096
DH = 129
QB = 256
NQB = TC // QB
EPS_LN = 1e-5
RSQ48 = float(1.0 / np.sqrt(48.0))

ENGS = ("sync", "tensor", "scalar", "vector", "gpsimd")


class Pass:
    def __init__(self, eng_name, eng, sems, c):
        self.e = eng_name
        self.eng = eng
        self.sems = sems
        self.c = c

    def bump(self, inst, sem, n):
        if sem is not None:
            self.c[sem] += n
            if inst is not None:
                inst.then_inc(self.sems[sem], n)
        return inst

    def wait(self, eng_name, sem, val=None):
        if self.e == eng_name:
            self.eng.wait_ge(self.sems[sem], self.c[sem] if val is None else val)

    def mm(self, out, lhsT, rhs, start=True, stop=True, sem=None, n=1):
        i = self.eng.matmul(out, lhsT, rhs, start=start, stop=stop) \
            if self.e == "tensor" else None
        return self.bump(i, sem, n)

    def tr(self, out, in_, ident, sem=None, n=1):
        i = self.eng.transpose(out, in_, ident) if self.e == "tensor" else None
        return self.bump(i, sem, n)

    def act(self, out, in_, func, bias=0.0, scale=1.0, accum_out=None,
            sem=None, n=1):
        i = None
        self.wait("scalar", "sc")
        if self.e == "scalar":
            if func == AF.Copy and not (isinstance(bias, float)
                                        and isinstance(scale, float)):
                func = AF.Identity
            kw = {"accum_out": accum_out} if accum_out is not None else {}
            i = self.eng.activation(out, in_, func, bias=bias, scale=scale, **kw)
        return self.bump(i, "sc", 1)

    def v(self, meth, *args, sem=None, n=1, **kw):
        self.wait("vector", "ve")
        i = getattr(self.eng, meth)(*args, **kw) if self.e == "vector" else None
        return self.bump(i, "ve", 1)

    def dma(self, queue, out, in_, sem=None, transpose=False, slow=False):
        i = None
        if self.e == queue:
            if transpose:
                i = self.eng.dma_start_transpose(out=out, in_=in_)
            elif slow:
                i = self.eng.dma_start(out=out, in_=in_,
                                       allow_slow_non_contiguous=True)
            else:
                i = self.eng.dma_start(out=out, in_=in_)
        return self.bump(i, sem, 16 if sem else 0)

    def gather(self, out, table_ap, idx_ap, sem=None):
        i = None
        if self.e == "gpsimd":
            i = self.eng.indirect_dma_start(
                out=out, out_offset=None, in_=table_ap,
                in_offset=IndirectOffsetOnAxis(idx_ap, 0))
        return self.bump(i, sem, 16 if sem else 0)

    def coll(self, kind, op, groups, ins, outs, sem=None):
        i = None
        if self.e == "gpsimd":
            i = self.eng.collective_compute(kind, op, replica_groups=groups,
                                            ins=ins, outs=outs)
        return self.bump(i, sem, 1 if sem else 0)


def build_nc(stage="full", core0_groups=None):
    nc = bass.Bass("TRN2", target_bir_lowering=False)

    def din(name, shape, dt=F32):
        return nc.dram_tensor(name, list(shape), dt, kind="ExternalInput").ap()

    x_in = din("x_in", (TC, D))
    wqT_in = din("wqT", (D, DH))
    wkT_in = din("wkT", (D, DH))
    wvT_in = din("wvT", (D, D))
    bqp_in = din("bqp", (128, 2))
    bkp_in = din("bkp", (128, 2))
    bvp_in = din("bvp", (128, 8))
    k0p_in = din("k0p", (128, 2))
    v0p_in = din("v0p", (128, 8))
    qproj_in = din("qproj", (128, D))
    routerT_in = din("routerT", (128, 48))
    auxkT_in = din("auxkT", (128, NSLOT))
    maskp_in = din("maskp", (128, 32))
    w1T_in = din("w1T", (D, 256))
    b1p_in = din("b1p", (128, 2))
    w2p_in = din("w2p", (128, 2))
    b2v_in = din("b2v_in", (128, 1))
    gw1v_in = din("gw1v_in", (128, 1))
    gbv_in = din("gbv_in", (128, 1))
    aux_in = din("aux_vals", (NSLOT, D))
    id16_in = din("id16", (128, 128), F16)

    x_out = nc.dram_tensor("x_out", [TC, D], F32, kind="ExternalOutput").ap()

    aux_bf = nc.dram_tensor("aux_bf", [NSLOT, D], F16)
    kvv_inh = [nc.dram_tensor(f"kvv_in{i}", [TC // 2, D], F16) for i in range(2)]
    kvv_outh = [nc.dram_tensor(f"kvv_out{i}", [S // 2, D], F16) for i in range(2)]
    kvk_inh = [nc.dram_tensor(f"kvk_in{i}", [2 * DH, TC // 2], F16)
               for i in range(2)]
    kvk_outh = [nc.dram_tensor(f"kvk_out{i}", [8 * DH, S // 8], F16)
                for i in range(2)]
    ar_in = [nc.dram_tensor(f"ar_in{i}", [1, 1], F32) for i in range(2)]
    ar_out = [nc.dram_tensor(f"ar_out{i}", [1, 1], F32) for i in range(2)]

    A = {}

    def sbuf(name, shape, dt):
        A[name] = nc.alloc_sbuf_tensor(name, list(shape), dt).ap()
        return A[name]

    x_sb = sbuf("x_sb", (128, NT * D), F32)
    stage32 = sbuf("stage32", (128, 8192), F32)
    scrF = sbuf("scrF", (128, NSLOT), F32)
    scrG = sbuf("scrG", (128, 8192), BF16)
    arenaW = sbuf("arenaW", (128, 8192), F16)
    scrH_hi = sbuf("scrH_hi", (128, GT * D), F16)
    scrH_lo = sbuf("scrH_lo", (128, GT * D), F16)
    hfm_hi = sbuf("hfm_hi", (128, 8 * GTOK), F16)
    hfm_lo = sbuf("hfm_lo", (128, 8 * GTOK), F16)
    vbuf = sbuf("vbuf", (128, 3072), F16)
    qfm_hi = sbuf("qfm_hi", (128, TC), F16)
    qfm_lo = sbuf("qfm_lo", (128, TC), F16)
    qtm16 = sbuf("qtm16", (128, DH), F16)
    ktm16 = sbuf("ktm16", (128, DH), F16)
    pt16 = sbuf("pt16", (128, 3 * QB), F16)
    qtm_lo = None
    rkT_hi = sbuf("rkT_hi", (50, NSLOT), F16)
    rkT_lo = sbuf("rkT_lo", (48, NSLOT), F16)
    w1T16 = sbuf("w1T16", (128, 8 * 256), F16)
    wq16 = sbuf("wq16", (128, 8 * 2 * DH), F16)
    wk16 = sbuf("wk16", (128, 8 * 2 * DH), F16)
    wrqT16 = sbuf("wrqT16", (128, 8 * 96), F16)

    w2T16 = sbuf("w2T16", (128, 2), F16)
    id16 = sbuf("id16s", (128, 128), F16)
    ones16 = sbuf("ones16", (128, 2), F16)
    ones16r = sbuf("ones16r", (2, 128), F16)
    bk16 = sbuf("bk16", (2, DH), F16)
    k0hl = sbuf("k0hl", (128, 4), F16)
    k0aug = sbuf("k0aug", (4, 1), F16)
    b1c = sbuf("b1c", (128, 2), F32)
    b2v = sbuf("b2v", (128, 1), F32)
    gw1v = sbuf("gw1v", (128, 1), F32)
    gbv = sbuf("gbv", (128, 1), F32)
    sA = sbuf("sA", (128, 64), F32)
    sM = sbuf("sM", (128, 64), F32)
    one1 = sbuf("one1", (1, 16), F32)
    lv16 = sbuf("lv16", (128, 16), F16)
    invb = sbuf("invb", (128, 1), F32)
    top8 = sbuf("top8", (128, 8), F32)
    idx8 = sbuf("idx8", (128, 8), U32)
    w8 = sbuf("w8", (128, 16), F32)
    zsb = sbuf("zsb", (128, 16), F32)
    psk = sbuf("psk", (128, NT), F32)
    psk16 = sbuf("psk16", (128, NT), F16)
    ssink = sbuf("ssink", (128, NT), F32)
    mrow = sbuf("mrow", (128, NT), F32)

    h1pre = arenaW[:, 0:4096]
    rq_hi = arenaW[0:50, 4096:6144]
    rq_lo = arenaW[50:98, 6144:8192]
    rq_lo = arenaW[0:48, 6144:8192]
    qp16 = vbuf
    kfm_hi = stage32.bitcast(F16)[:, 0:8192]
    kfm_lo = stage32.bitcast(F16)[:, 8192:16384]
    kfs_hi = stage32.bitcast(F16)[:, 8192:8192 + 2048]
    kfs_lo = stage32.bitcast(F16)[:, 10240:10240 + 2048]
    kfs_ra = stage32.bitcast(F16)[0:1, 12288:12288 + 2048]
    kfs_rb = stage32.bitcast(F16)[64:65, 12288:12288 + 2048]
    fwd16 = scrF.bitcast(BF16)
    k129 = scrG.bitcast(F16)[0:4, 0:8192]
    wv16 = arenaW
    scrG16 = scrG.bitcast(F16)
    qaug = scrH_hi[0:4, 0:TC]
    v0r16 = scrG16[64:65, 6144:6144 + D]
    pskT = scrG16[64:65, 4096:6144]
    bv16 = scrG16[64:66, 2048:2048 + D]
    ones64 = scrG16[64:66, 3072:3200]
    stg32r = scrG16[32:33, 0:8192]
    stg64r = scrG16[64:65, 0:2048]
    tmp32 = scrF[:, 0:D]
    qtm_lo = pt16[:, 0:DH]
    ktm_lo = pt16[:, 256:256 + DH]
    rt16 = pt16[:, 512:608]
    bq16 = pt16[0:2, 608:608 + DH]

    sem_names = ["ld", "te", "sc", "ve", "gp", "cc", "vs", "st", "aux", "kv",
                 "vs1", "vs2"]
    sems = {s: nc.alloc_semaphore(s) for s in sem_names}

    x_in_t = x_in.rearrange("(t p) d -> t p d", p=128)
    x_out_t = x_out.rearrange("(t p) d -> t p d", p=128)

    def program(p: Pass):
        emit_prep(p)
        emit_ffn(p, 0)
        if stage in ("attn", "full"):
            emit_attn(p)
        if stage == "full":
            emit_ffn(p, 1, skip_stats=True)
        emit_out(p)

    def hilo_(p, dst_hi, dst_lo, src32):
        p.wait("scalar", "sc")
        p.act(dst_hi, src32, AF.Copy, sem="sc")
        p.wait("vector", "sc")
        p.v("tensor_tensor", dst_lo, src32, dst_hi, ALU.subtract, sem="ve")

    def emit_prep(p: Pass):
        for i in range(4):
            p.dma("gpsimd", aux_bf.ap()[i * 1024:(i + 1) * 1024, :],
                  aux_in[i * 1024:(i + 1) * 1024, :], sem="aux")
        for t in range(NT):
            p.dma("sync", x_sb[:, t * D:(t + 1) * D], x_in_t[t], sem="ld")
        p.dma("sync", id16[:, :], id16_in, sem="ld")
        p.dma("sync", b2v[:, :], b2v_in, sem="ld")
        p.dma("sync", gw1v[:, :], gw1v_in, sem="ld")
        p.dma("sync", gbv[:, :], gbv_in, sem="ld")
        p.dma("sync", b1c[:, :], b1p_in, sem="ld")
        p.dma("sync", stage32[:, 0:D], qproj_in, sem="ld")
        p.dma("sync", stage32[:, D:D + 48], routerT_in, sem="ld")
        p.dma("sync", stage32[:, D + 48:D + 48 + 2048]
              .rearrange("p (c d) -> p c d", c=8),
              w1T_in.rearrange("(c p) d -> p c d", p=128), sem="ld")
        p.dma("sync", stage32[:, 3120:3120 + NSLOT], auxkT_in, sem="ld")
        p.dma("sync", stage32[:, 7216:7218], w2p_in, sem="ld")
        p.dma("sync", stage32[:, 7218:7250], maskp_in, sem="ld")
        p.wait("scalar", "ld")
        p.wait("vector", "ld")
        p.v("memset", ones16[:, :], 1.0)
        p.v("memset", ones16r[:, :], 1.0)
        p.v("memset", rq_hi[0:50, :], 1.0, sem="ve")
        hilo_(p, qp16[:, 0:D], qp16[:, D:2 * D], stage32[:, 0:D])
        hilo_(p, rt16[:, 0:48], rt16[:, 48:96], stage32[:, D:D + 48])
        for ch in range(8):
            p.act(w1T16[:, ch * 256:(ch + 1) * 256],
                  stage32[:, D + 48 + ch * 256:D + 48 + (ch + 1) * 256],
                  AF.Copy, sem="sc")
        p.act(w2T16[:, 0:2], stage32[:, 7216:7218], AF.Copy, sem="sc")
        mh16 = scrF.bitcast(F16)[:, 0:32]
        ml16 = scrF.bitcast(F16)[:, 32:64]
        hilo_(p, mh16, ml16, stage32[:, 7218:7250])
        p.wait("sync", "ve")
        p.wait("sync", "sc")
        p.dma("sync", rkT_hi[48:49, :], mh16, sem="ld")
        p.dma("sync", rkT_hi[49:50, :], ml16, sem="ld")
        auxk_hi = scrG.bitcast(F16)[:, 0:NSLOT]
        auxk_lo = scrG.bitcast(F16)[:, NSLOT:2 * NSLOT]
        asrc = stage32[:, 3120:3120 + NSLOT]
        for q in range(4):
            sl = slice(q * 1024, (q + 1) * 1024)
            hilo_(p, auxk_hi[:, sl], auxk_lo[:, sl], asrc[:, sl])
        p.wait("tensor", "sc")
        p.wait("tensor", "ve")
        p.wait("tensor", "ld")
        p.wait("scalar", "ve")
        for n in range(8):
            ps = psA[0:48, (n % 4) * 512:(n % 4) * 512 + 512]
            rh = auxk_hi[:, n * 512:(n + 1) * 512]
            rl = auxk_lo[:, n * 512:(n + 1) * 512]
            p.mm(ps, rt16[:, 0:48], rh, start=True, stop=False)
            p.mm(ps, rt16[:, 0:48], rl, start=False, stop=False)
            p.mm(ps, rt16[:, 48:96], rh, start=False, stop=True, sem="te")
            p.wait("scalar", "te")
            p.act(stage32[0:48, n * 512:(n + 1) * 512], ps, AF.Copy,
                  scale=RSQ48, sem="sc")
            p.wait("tensor", "sc")
        for n in range(8):
            sl = slice(n * 512, (n + 1) * 512)
            hilo_(p, rkT_hi[0:48, sl], rkT_lo[0:48, sl], stage32[0:48, sl])
        for n in range(2):
            ps = psB[0:48, n * 512:(n + 1) * 512]
            qh = qp16[:, n * 512:(n + 1) * 512]
            ql = qp16[:, D + n * 512:D + (n + 1) * 512]
            p.mm(ps, rt16[:, 0:48], qh, start=True, stop=False)
            p.mm(ps, rt16[:, 0:48], ql, start=False, stop=False)
            p.mm(ps, rt16[:, 48:96], qh, start=False, stop=True, sem="te")
        p.wait("scalar", "te")
        p.wait("scalar", "ve")
        for n in range(2):
            p.act(stage32[0:48, 4096 + n * 512:4096 + (n + 1) * 512],
                  psB[0:48, n * 512:(n + 1) * 512], AF.Copy, sem="sc")
        wrq32 = stage32[0:48, 4096:4096 + 1024]
        wrq_h = scrH_hi[0:48, 0:1024]
        wrq_l = scrH_lo[0:48, 0:1024]
        hilo_(p, wrq_h, wrq_l, wrq32)
        p.wait("tensor", "sc")
        p.wait("tensor", "ve")
        psA16 = psA.bitcast(F16)
        for ch in range(8):
            p.tr(psA16[0:128, 0:48], wrq_h[:, ch * 128:(ch + 1) * 128],
                 id16[0:48, 0:48])
            p.tr(psA16[0:128, 1024:1072], wrq_l[:, ch * 128:(ch + 1) * 128],
                 id16[0:48, 0:48], sem="te")
            p.wait("scalar", "te")
            p.act(wrqT16[:, ch * 96:ch * 96 + 48], psA16[0:128, 0:48], AF.Copy,
                  sem="sc")
            p.act(wrqT16[:, ch * 96 + 48:ch * 96 + 96], psA16[0:128, 1024:1072],
                  AF.Copy, sem="sc")
            p.wait("tensor", "sc")
        p.wait("gpsimd", "aux")

    def emit_ffn(p: Pass, call, skip_stats=False):
        if not skip_stats:
            p.wait("scalar", "ve")
            p.wait("vector", "sc")
            for t in range(NT):
                xt = x_sb[:, t * D:(t + 1) * D]
                p.v("reduce_sum", out=sA[:, t:t + 1], in_=xt, axis=AX, sem="ve")
                p.wait("scalar", "sc")
                p.act(tmp32[:, :], xt, AF.Square, accum_out=sA[:, 16 + t:17 + t],
                      sem="sc")
        p.wait("vector", "sc")
        p.wait("scalar", "ve")
        p.v("tensor_scalar", sA[:, 32:48], sA[:, 0:16], 1.0 / D, None, ALU.mult)
        p.v("tensor_scalar", sA[:, 48:64], sA[:, 16:32], 1.0 / D, None, ALU.mult)
        p.v("tensor_tensor", tmp32[:, 0:16], sA[:, 32:48], sA[:, 32:48], ALU.mult)
        p.v("tensor_sub", sA[:, 48:64], sA[:, 48:64], tmp32[:, 0:16])
        p.v("tensor_scalar", tmp32[:, 0:16], sA[:, 48:64], EPS_LN, None, ALU.add,
            sem="ve")
        p.wait("scalar", "ve")
        p.act(tmp32[:, 16:32], tmp32[:, 0:16], AF.Ln, sem="sc")
        p.wait("scalar", "sc")
        p.act(sM[:, 0:16], tmp32[:, 16:32], AF.Exp, scale=-0.5, sem="sc")
        p.wait("vector", "sc")
        p.v("tensor_tensor", tmp32[:, 48:64], sA[:, 32:48], sM[:, 0:16], ALU.mult)
        p.v("tensor_scalar", sM[:, 16:32], tmp32[:, 48:64], -1.0, None, ALU.mult)
        p.v("reciprocal", out=tmp32[:, 16:32], in_=tmp32[:, 0:16])
        p.v("tensor_tensor", tmp32[:, 32:48], sA[:, 48:64], tmp32[:, 16:32],
            ALU.mult, sem="ve")
        p.wait("scalar", "ve")
        p.act(sM[:, 32:48], tmp32[:, 32:48], AF.Ln, bias=1.0, sem="sc")
        p.wait("vector", "sc")
        p.v("tensor_copy", out=lv16[:, :], in_=sM[:, 32:48], sem="ve")
        p.wait("tensor", "ve")
        p.mm(psB[0:1, 0:16], ones16[:, 0:1], lv16[:, :], sem="te")
        p.wait("vector", "te")
        p.v("reduce_sum", out=one1[0:1, 1:2], in_=psB[0:1, 0:16], axis=AX,
            sem="ve")
        p.wait("sync", "ve")
        p.dma("sync", ar_in[call].ap(), one1[0:1, 1:2], sem="st")
        p.wait("gpsimd", "st")
        p.coll("AllReduce", ALU.add, [[0, 1, 2, 3, 4, 5, 6, 7]],
               [ar_in[call].ap().opt()], [ar_out[call].ap().opt()], sem="cc")
        p.wait("vector", "st")
        if call == 1:
            p.v("memset", rq_hi[0:50, :], 1.0, sem="ve")
        ln_ve = {}
        tmpab = [tmp32, scrF[:, 1024:2048]]

        def emit_ln_ffn(t):
            ti = t % GT
            xt = x_sb[:, t * D:(t + 1) * D]
            tmp = tmpab[t % 2]
            if t < 2:
                p.wait("scalar", "ve")
            else:
                p.wait("scalar", "ve", val=ln_ve[t - 2])
            p.act(tmp[:, :], xt, AF.Copy, bias=sM[:, 16 + t:17 + t],
                  scale=sM[:, 0 + t:1 + t], sem="sc")
            p.act(scrH_hi[:, ti * D:(ti + 1) * D], tmp[:, :], AF.Copy,
                  sem="sc")
            p.wait("vector", "sc")
            p.v("tensor_tensor", scrH_lo[:, ti * D:(ti + 1) * D], tmp[:, :],
                scrH_hi[:, ti * D:(ti + 1) * D], ALU.subtract, sem="ve")
            p.v("scalar_tensor_tensor", xt, tmp[:, :], 0.5, xt,
                ALU.mult, ALU.add, sem="ve")
            ln_ve[t] = p.c["ve"]

        emit_ln_ffn(0)
        emit_ln_ffn(1)
        for g in range(NG):
            p.wait("tensor", "sc")
            p.wait("tensor", "ve", val=ln_ve[g * GT + 1])
            psA16_ = psA.bitcast(F16)
            for hl, (hsrc, dst) in enumerate(((scrH_hi, hfm_hi), (scrH_lo, hfm_lo))):
                for ch in range(8):
                    for ti in range(GT):
                        p.tr(psA16_[:, hl * 2048 + ch * 256 + ti * 128:
                                    hl * 2048 + ch * 256 + (ti + 1) * 128],
                             hsrc[:, ti * D + ch * 128:ti * D + (ch + 1) * 128],
                             id16[:, :], sem="te")
                p.wait("scalar", "te")
                p.act(dst[:, 0:1024], psA16_[:, hl * 2048:hl * 2048 + 1024],
                      AF.Copy, sem="sc")
                p.act(dst[:, 1024:2048],
                      psA16_[:, hl * 2048 + 1024:hl * 2048 + 2048],
                      AF.Copy, sem="sc")
            evac_sc = p.c["sc"]
            if g + 1 < NG:
                emit_ln_ffn((g + 1) * GT)
                emit_ln_ffn((g + 1) * GT + 1)
            p.wait("tensor", "sc", val=evac_sc)
            for m in range(2):
                ps = psB[:, m * 512:m * 512 + GTOK]
                for ch in range(8):
                    p.mm(ps, w1T16[:, ch * 256 + m * 128:ch * 256 + m * 128 + 128],
                         hfm_hi[:, ch * GTOK:(ch + 1) * GTOK],
                         start=(ch == 0), stop=(ch == 7), sem="te")
                p.wait("scalar", "te")
                p.act(h1pre[:, m * 2048 + g * GTOK:m * 2048 + (g + 1) * GTOK], ps,
                      AF.Gelu, bias=b1c[:, m:m + 1], sem="sc")
            p.wait("tensor", "sc")
            for ti in range(GT):
                t = g * GT + ti
                ps = psB[:, 1536 + t:1536 + t + 1]
                for m in range(2):
                    p.mm(ps, h1pre[:, m * 2048 + g * GTOK + ti * 128:
                                   m * 2048 + g * GTOK + (ti + 1) * 128],
                         w2T16[:, m:m + 1],
                         start=(t == 0 and m == 0), stop=(t == NT - 1 and m == 1),
                         sem="te")
            ps = psA[0:48, 0:GTOK]
            for ch in range(8):
                wh = wrqT16[:, ch * 96:ch * 96 + 48]
                wl = wrqT16[:, ch * 96 + 48:ch * 96 + 96]
                hh = hfm_hi[:, ch * GTOK:(ch + 1) * GTOK]
                hl_ = hfm_lo[:, ch * GTOK:(ch + 1) * GTOK]
                p.mm(ps, wh, hh, start=(ch == 0), stop=False)
                p.mm(ps, wh, hl_, start=False, stop=False)
                p.mm(ps, wl, hh, start=False, stop=(ch == 7), sem="te")
            p.wait("scalar", "te")
            p.act(rq_hi[0:48, g * GTOK:(g + 1) * GTOK], ps, AF.Copy, sem="sc")
            p.wait("vector", "sc")
            p.v("tensor_tensor", rq_lo[0:48, g * GTOK:(g + 1) * GTOK], ps,
                rq_hi[0:48, g * GTOK:(g + 1) * GTOK], ALU.subtract, sem="ve")
            p.wait("tensor", "ve")
        p.wait("scalar", "te")
        p.act(tmp32[:, 0:16], psB[:, 1536:1552], AF.Copy, bias=b2v[:, 0:1],
              sem="sc")
        p.wait("scalar", "sc")
        p.act(tmp32[:, 16:32], tmp32[:, 0:16], AF.Exp, scale=-1.0, sem="sc")
        p.wait("vector", "sc")
        p.v("tensor_scalar", tmp32[:, 16:32], tmp32[:, 16:32], 1.0, None, ALU.add)
        p.v("reciprocal", out=sM[:, 48:64], in_=tmp32[:, 16:32], sem="ve")
        p.wait("sync", "cc")
        p.dma("sync", one1[0:1, 2:3], ar_out[call].ap(), sem="ld")
        p.wait("sync", "ld")
        p.wait("vector", "ld")
        p.v("tensor_scalar", one1[0:1, 3:4], one1[0:1, 2:3], 1.0 / (B * S), 1e-6,
            ALU.mult, op1=ALU.add)
        p.v("reciprocal", out=one1[0:1, 4:5], in_=one1[0:1, 3:4])
        p.v("tensor_scalar", one1[0:1, 5:6], one1[0:1, 4:5], 0.5, None, ALU.mult)
        p.v("tensor_copy", out=one1.bitcast(F16)[0:1, 12:13], in_=one1[0:1, 5:6],
            sem="ve")
        p.wait("tensor", "ve")
        p.mm(psB[:, 1024:1025], ones16r[0:1, :], one1.bitcast(F16)[0:1, 12:13],
             sem="te")
        p.wait("scalar", "te")
        p.act(invb[:, :], psB[:, 1024:1025], AF.Copy, sem="sc")
        p.wait("vector", "sc")
        p.v("tensor_scalar", sM[:, 32:48], sM[:, 32:48], invb[:, 0:1], None,
            ALU.mult)
        p.v("scalar_tensor_tensor", sM[:, 48:64], sM[:, 48:64], 2.5,
            sM[:, 32:48], ALU.mult, ALU.add)
        p.v("tensor_scalar", sM[:, 48:64], sM[:, 48:64], 5.0, 0.0, ALU.min,
            op1=ALU.max, sem="ve")
        p.wait("scalar", "ve")
        p.act(tmp32[:, 16:32], sM[:, 48:64], AF.Copy, scale=gw1v[:, 0:1],
              bias=gbv[:, 0:1], sem="sc")
        p.wait("scalar", "sc")
        p.act(tmp32[:, 32:48], tmp32[:, 16:32], AF.Exp, scale=-1.0, sem="sc")
        p.wait("vector", "sc")
        p.v("tensor_scalar", tmp32[:, 32:48], tmp32[:, 32:48], 1.0, None, ALU.add)
        p.v("reciprocal", out=sM[:, 48:64], in_=tmp32[:, 32:48], sem="ve")
        a4_ve = p.c["ve"]
        idx_ve = {}
        agg_ve = {}
        w8b = [w8[:, 0:8], w8[:, 8:16]]

        def gslot(t, k):
            if t % 2 == 0:
                return scrG16[:, k * D:(k + 1) * D]
            half = [scrH_hi, scrH_lo, hfm_hi, hfm_lo][k // 2]
            return half[:, (k % 2) * D:(k % 2) * D + D]

        def emit_scores_topk(t):
            g, ti = t // GT, t % GT
            half = psA if t % 2 == 0 else psB
            rq_s = slice(g * GTOK + ti * 128, g * GTOK + (ti + 1) * 128)
            for n in range(8):
                ps = half[:, (n % 4) * 512:(n % 4) * 512 + 512]
                p.mm(ps, rq_hi[0:50, rq_s], rkT_hi[0:50, n * 512:(n + 1) * 512],
                     start=True, stop=False)
                p.mm(ps, rq_hi[0:48, rq_s], rkT_lo[0:48, n * 512:(n + 1) * 512],
                     start=False, stop=False)
                p.mm(ps, rq_lo[0:48, rq_s], rkT_hi[0:48, n * 512:(n + 1) * 512],
                     start=False, stop=True, sem="te")
                if n % 4 == 3:
                    p.wait("scalar", "te")
                    p.wait("scalar", "ve", val=max(idx_ve.get(t - 1, 0), a4_ve))
                    p.act(scrF[:, (n // 4) * 2048:(n // 4) * 2048 + 2048],
                          half[:, 0:2048], AF.Copy, sem="sc")
                    p.wait("tensor", "sc")
            p.wait("vector", "sc")
            p.wait("vector", "gp", val=gp_done.get(t - 1, 0))
            p.v("max", out=top8[:, :], in_=scrF[:, :])
            p.v("tensor_scalar", w8b[t % 2], top8[:, :], top8[:, 0:1], None,
                ALU.subtract)
            p.v("max_index", out=idx8[:, :], in_max=top8[:, :],
                in_values=scrF[:, :], sem="ve")
            idx_ve[t] = p.c["ve"]
            p.wait("gpsimd", "ve", val=idx_ve[t])
            if t >= 2:
                p.wait("gpsimd", "ve", val=agg_ve.get(t - 2, 0))
            for k in range(8):
                p.gather(gslot(t, k), aux_bf.ap(), idx8[:, k:k + 1], sem="gp")
            p.wait("scalar", "ve")
            p.act(w8b[t % 2], w8b[t % 2], AF.Exp, sem="sc")
            p.wait("vector", "sc")
            p.v("reduce_sum", out=zsb[:, 8 + (t % 2):9 + (t % 2)], in_=w8b[t % 2],
                axis=AX)
            p.v("reciprocal", out=zsb[:, 10 + (t % 2):11 + (t % 2)],
                in_=zsb[:, 8 + (t % 2):9 + (t % 2)])
            p.v("tensor_scalar", zsb[:, 12 + (t % 2):13 + (t % 2)],
                zsb[:, 10 + (t % 2):11 + (t % 2)],
                sM[:, 48 + t:49 + t], 0.5, ALU.mult, op1=ALU.mult)
            p.v("tensor_scalar", w8b[t % 2], w8b[t % 2],
                zsb[:, 12 + (t % 2):13 + (t % 2)], None, ALU.mult, sem="ve")

        acc16 = vbuf[:, 0:1024]
        def emit_agg(t):
            p.wait("vector", "gp", val=gp_done[t])
            xt = x_sb[:, t * D:(t + 1) * D]
            p.v("tensor_scalar", acc16, gslot(t, 0), w8b[t % 2][:, 0:1], None,
                ALU.mult)
            for k in range(1, 8):
                p.v("scalar_tensor_tensor", acc16, gslot(t, k),
                    w8b[t % 2][:, k:k + 1], acc16, ALU.mult, ALU.add)
            p.v("scalar_tensor_tensor", xt, acc16, 1.0, xt, ALU.mult, ALU.add,
                sem="ve")
            agg_ve[t] = p.c["ve"]
            if call == 0:
                p.v("reduce_sum", out=sA[:, t:t + 1], in_=xt, axis=AX, sem="ve")

        gp_done = {}
        for t in range(NT):
            emit_scores_topk(t)
            gp_done[t] = p.c["gp"]
            if t >= 1:
                emit_agg(t - 1)
        emit_agg(NT - 1)
        if call == 0:
            p.wait("scalar", "ve")
            for t in range(NT):
                p.act(qfm_hi.bitcast(F32)[:, 0:1024],
                      x_sb[:, t * D:(t + 1) * D], AF.Square,
                      accum_out=sA[:, 16 + t:17 + t], sem="sc")

    def emit_attn(p: Pass):
        p.dma("sync", stage32.rearrange("p (c d) -> p c d", c=8),
              wvT_in.rearrange("(c p) d -> p c d", p=128), sem="ld")
        p.wait("scalar", "ld")
        p.wait("scalar", "te")
        p.wait("scalar", "ve")
        p.wait("vector", "sc")
        for ch in range(8):
            p.act(wv16[:, ch * D:(ch + 1) * D], stage32[:, ch * D:(ch + 1) * D],
                  AF.Copy)
        p.wait("sync", "sc")
        p.dma("sync", stage32[:, 0:8 * DH].rearrange("p (c d) -> p c d", c=8),
              wqT_in.rearrange("(c p) d -> p c d", p=128), sem="ld")
        p.dma("sync", stage32[:, 8 * DH:16 * DH].rearrange("p (c d) -> p c d", c=8),
              wkT_in.rearrange("(c p) d -> p c d", p=128), sem="ld")
        p.dma("sync", stage32[:, 4096:4098], bqp_in, sem="ld")
        p.dma("sync", stage32[:, 4098:4100], bkp_in, sem="ld")
        p.dma("sync", stage32[:, 4100:4102], k0p_in, sem="ld")
        p.dma("sync", stage32[:, 4102:4110], v0p_in, sem="ld")
        p.dma("sync", stage32[:, 4110:4118], bvp_in, sem="ld")
        p.wait("scalar", "ld")
        p.wait("vector", "ld")
        for ch in range(8):
            hilo_(p, wq16[:, ch * 258:ch * 258 + DH],
                  wq16[:, ch * 258 + DH:ch * 258 + 258],
                  stage32[:, ch * DH:(ch + 1) * DH])
            hilo_(p, wk16[:, ch * 258:ch * 258 + DH],
                  wk16[:, ch * 258 + DH:ch * 258 + 258],
                  stage32[:, 8 * DH + ch * DH:8 * DH + (ch + 1) * DH])
        bq_c = scrG16[:, 7680:7684]
        bk_c = scrG16[:, 7684:7688]
        v0_c = scrG16[:, 7688:7704]
        bv_c = scrG16[:, 7704:7720]
        hilo_(p, bq_c[:, 0:2], bq_c[:, 2:4], stage32[:, 4096:4098])
        hilo_(p, bk_c[:, 0:2], bk_c[:, 2:4], stage32[:, 4098:4100])
        hilo_(p, k0hl[:, 0:2], k0hl[:, 2:4], stage32[:, 4100:4102])
        hilo_(p, v0_c[:, 0:8], v0_c[:, 8:16], stage32[:, 4102:4110])
        hilo_(p, bv_c[:, 0:8], bv_c[:, 8:16], stage32[:, 4110:4118])
        p.wait("sync", "ve")
        p.wait("sync", "sc")
        for r, cc in ((0, 0), (1, 2)):
            p.dma("sync", bq16[r:r + 1, 0:128], bq_c[:, cc:cc + 1], sem="ld")
            p.dma("sync", bq16[r:r + 1, 128:129], bq_c[0:1, cc + 1:cc + 2], sem="ld")
            p.dma("sync", bk16[r:r + 1, 0:128], bk_c[:, cc:cc + 1], sem="ld")
            p.dma("sync", bk16[r:r + 1, 128:129], bk_c[0:1, cc + 1:cc + 2], sem="ld")
        p.dma("sync", v0r16[0:1, :], v0_c[:, 0:8], sem="ld")
        p.dma("sync", bv16[0:1, :], bv_c[:, 0:8], sem="ld")
        p.dma("sync", bv16[1:2, :], bv_c[:, 8:16], sem="ld")
        p.dma("sync", k0aug[0:1, :], k0hl[0:1, 1:2], sem="ld")
        p.dma("sync", k0aug[1:2, :], k0hl[0:1, 3:4], sem="ld")
        p.dma("sync", k0aug[2:3, :], k0hl[0:1, 1:2], sem="ld")
        p.wait("sync", "ld")
        p.wait("tensor", "ld")
        p.v("memset", ones64, 1.0)
        p.wait("vector", "sc")
        p.v("tensor_scalar", sA[:, 32:48], sA[:, 0:16], 1.0 / D, None, ALU.mult)
        p.v("tensor_scalar", sA[:, 48:64], sA[:, 16:32], 1.0 / D, None, ALU.mult)
        p.v("tensor_tensor", tmp32[:, 0:16], sA[:, 32:48], sA[:, 32:48], ALU.mult)
        p.v("tensor_sub", sA[:, 48:64], sA[:, 48:64], tmp32[:, 0:16])
        p.v("tensor_scalar", tmp32[:, 0:16], sA[:, 48:64], EPS_LN, None, ALU.add,
            sem="ve")
        p.wait("scalar", "ve")
        p.act(tmp32[:, 16:32], tmp32[:, 0:16], AF.Ln)
        p.act(sM[:, 0:16], tmp32[:, 16:32], AF.Exp, scale=-0.5, sem="sc")
        p.wait("vector", "sc")
        p.v("tensor_tensor", tmp32[:, 48:64], sA[:, 32:48], sM[:, 0:16], ALU.mult)
        p.v("tensor_scalar", sM[:, 16:32], tmp32[:, 48:64], -1.0, None, ALU.mult,
            sem="ve")
        cc_kb = None
        ln2_ve = {}
        tmpab2 = [tmp32, scrF[:, 1024:2048]]

        def emit_ln_attn(t):
            ti = t % GT
            xt = x_sb[:, t * D:(t + 1) * D]
            tmp = tmpab2[t % 2]
            if t < 2:
                p.wait("scalar", "ve")
            else:
                p.wait("scalar", "ve", val=ln2_ve[t - 2])
            p.act(tmp[:, :], xt, AF.Copy, bias=sM[:, 16 + t:17 + t],
                  scale=sM[:, 0 + t:1 + t], sem="sc")
            p.act(scrH_hi[:, ti * D:(ti + 1) * D], tmp[:, :], AF.Copy,
                  sem="sc")
            p.wait("vector", "sc")
            p.v("tensor_tensor", scrH_lo[:, ti * D:(ti + 1) * D], tmp[:, :],
                scrH_hi[:, ti * D:(ti + 1) * D], ALU.subtract)
            p.v("tensor_add", xt, xt, tmp[:, :], sem="ve")
            ln2_ve[t] = p.c["ve"]

        emit_ln_attn(0)
        emit_ln_attn(1)
        for g in range(NG):
            p.wait("tensor", "sc")
            p.wait("tensor", "ve", val=ln2_ve[g * GT + 1])
            psA16_ = psA.bitcast(F16)
            for hl, (hsrc, dst) in enumerate(((scrH_hi, hfm_hi), (scrH_lo, hfm_lo))):
                for ch in range(8):
                    for ti in range(GT):
                        p.tr(psA16_[:, hl * 2048 + ch * 256 + ti * 128:
                                    hl * 2048 + ch * 256 + (ti + 1) * 128],
                             hsrc[:, ti * D + ch * 128:ti * D + (ch + 1) * 128],
                             id16[:, :], sem="te")
                p.wait("scalar", "te")
                p.act(dst[:, 0:1024], psA16_[:, hl * 2048:hl * 2048 + 1024],
                      AF.Copy, sem="sc")
                p.act(dst[:, 1024:2048],
                      psA16_[:, hl * 2048 + 1024:hl * 2048 + 2048],
                      AF.Copy, sem="sc")
            evac_sc = p.c["sc"]
            if g + 1 < NG:
                emit_ln_attn((g + 1) * GT)
                emit_ln_attn((g + 1) * GT + 1)
            p.wait("tensor", "sc", val=evac_sc)
            for ti in range(GT):
                t = g * GT + ti
                tsl = slice(ti * 128, (ti + 1) * 128)
                psq = psB[:, 0:DH]
                for ch in range(8):
                    hh = hfm_hi[:, ch * GTOK + ti * 128:ch * GTOK + (ti + 1) * 128]
                    hl_ = hfm_lo[:, ch * GTOK + ti * 128:ch * GTOK + (ti + 1) * 128]
                    wh = wq16[:, ch * 258:ch * 258 + DH]
                    wl = wq16[:, ch * 258 + DH:ch * 258 + 258]
                    p.mm(psq, hh, wh, start=(ch == 0), stop=False)
                    p.mm(psq, hl_, wh, start=False, stop=False)
                    p.mm(psq, hh, wl, start=False, stop=False)
                p.mm(psq, ones16r[0:2, 0:128], bq16[0:2, :], start=False,
                     stop=True, sem="te")
                p.wait("scalar", "te")
                p.act(qtm16[:, 0:DH], psq, AF.Copy, sem="sc")
                p.wait("vector", "sc")
                p.v("tensor_tensor", qtm_lo, psq, qtm16[:, 0:DH],
                    ALU.subtract, sem="ve")
                psk_ = psB[:, 512:512 + DH]
                for ch in range(8):
                    hh = hfm_hi[:, ch * GTOK + ti * 128:ch * GTOK + (ti + 1) * 128]
                    hl_ = hfm_lo[:, ch * GTOK + ti * 128:ch * GTOK + (ti + 1) * 128]
                    wh = wk16[:, ch * 258:ch * 258 + DH]
                    wl = wk16[:, ch * 258 + DH:ch * 258 + 258]
                    p.mm(psk_, hh, wh, start=(ch == 0), stop=False)
                    p.mm(psk_, hl_, wh, start=False, stop=False)
                    p.mm(psk_, hh, wl, start=False, stop=False)
                p.mm(psk_, ones16r[0:2, 0:128], bk16[0:2, :], start=False,
                     stop=True, sem="te")
                p.wait("scalar", "te")
                p.wait("scalar", "kv")
                p.act(ktm16[:, 0:DH], psk_, AF.Copy)
                p.wait("scalar", "sc")
                p.wait("vector", "sc")
                p.v("tensor_tensor", ktm_lo, psk_, ktm16[:, 0:DH],
                    ALU.subtract, sem="ve")
                for half in range(2):
                    psv = psA[:, half * 512:(half + 1) * 512]
                    for ch in range(8):
                        hh = hfm_hi[:, ch * GTOK + ti * 128:ch * GTOK + (ti + 1) * 128]
                        wv_ = wv16[:, ch * D + half * 512:ch * D + (half + 1) * 512]
                        p.mm(psv, hh, wv_, start=(ch == 0), stop=False)
                    p.mm(psv, ones64, bv16[0:2, half * 512:(half + 1) * 512],
                         start=False, stop=True, sem="te")
                p.wait("scalar", "te")
                p.wait("scalar", "kv")
                p.act(vbuf[:, 0:512], psA[:, 0:512], AF.Copy)
                p.wait("scalar", "sc")
                p.act(vbuf[:, 512:1024], psA[:, 512:1024], AF.Copy, sem="sc")
                p.wait("sync", "sc")
                p.dma("sync", kvv_inh[t // 8].ap()[(t % 8) * 128:(t % 8 + 1) * 128, :],
                      vbuf[:, 0:1024], sem="kv")
                p.wait("tensor", "ve")
                p.tr(psA16_[:, 3072:3072 + 128], qtm16[:, 0:128], id16[:, :])
                p.tr(psA16_[:, 3072 + 128:3072 + 256], qtm_lo[:, 0:128],
                     id16[:, :])
                p.tr(psA16_[0:1, 3072 + 256:3072 + 384], qtm16[:, 128:129],
                     id16[:, :])
                p.tr(psA16_[0:1, 3072 + 384:3072 + 512], qtm_lo[:, 128:129],
                     id16[:, :], sem="te")
                p.tr(psA16_[:, 3584:3584 + 128], ktm16[:, 0:128], id16[:, :])
                p.tr(psA16_[:, 3584 + 128:3584 + 256], ktm_lo[:, 0:128],
                     id16[:, :])
                p.tr(psA16_[0:1, 3584 + 256:3584 + 384], ktm16[:, 128:129],
                     id16[:, :])
                p.tr(psA16_[0:1, 3584 + 384:3584 + 512], ktm_lo[:, 128:129],
                     id16[:, :], sem="te")
                p.wait("scalar", "te")
                p.act(qfm_hi[:, t * 128:(t + 1) * 128], psA16_[:, 3072:3200],
                      AF.Copy)
                p.act(qfm_lo[:, t * 128:(t + 1) * 128], psA16_[:, 3200:3328],
                      AF.Copy)
                p.act(stg32r[0:1, t * 128:(t + 1) * 128],
                      psA16_[0:1, 3328:3456], AF.Copy)
                p.act(stg64r[0:1, t * 128:(t + 1) * 128],
                      psA16_[0:1, 3456:3584], AF.Copy)
                p.act(kfs_hi[:, t * 128:(t + 1) * 128], psA16_[:, 3584:3712],
                      AF.Copy)
                p.act(kfs_lo[:, t * 128:(t + 1) * 128], psA16_[:, 3712:3840],
                      AF.Copy)
                p.act(kfs_ra[0:1, t * 128:(t + 1) * 128],
                      psA16_[0:1, 3840:3968], AF.Copy)
                p.act(kfs_rb[0:1, t * 128:(t + 1) * 128],
                      psA16_[0:1, 3968:4096], AF.Copy, sem="sc")
                p.wait("tensor", "sc")
                if t in (7, 15):
                    hh = t // 8
                    cs = slice(hh * 1024, hh * 1024 + 1024)
                    p.wait("sync", "sc")
                    p.dma("sync", kvk_inh[hh].ap()[0:128, :], kfs_hi[:, cs],
                          sem="kv")
                    p.dma("sync", kvk_inh[hh].ap()[128:129, :], kfs_ra[0:1, cs],
                          sem="kv")
                    p.dma("sync", kvk_inh[hh].ap()[129:257, :], kfs_lo[:, cs],
                          sem="kv")
                    p.dma("sync", kvk_inh[hh].ap()[257:258, :], kfs_rb[0:1, cs],
                          sem="kv")
                    p.wait("gpsimd", "kv")
                    p.coll("AllGather", ALU.bypass, [[0, 1, 2, 3], [4, 5, 6, 7]],
                           [kvk_inh[hh].ap().opt()], [kvk_outh[hh].ap().opt()],
                           sem="cc")
                    p.coll("AllGather", ALU.bypass, [[0, 1, 2, 3], [4, 5, 6, 7]],
                           [kvv_inh[hh].ap().opt()], [kvv_outh[hh].ap().opt()],
                           sem="cc")
                    if hh == 1:
                        cc_kb = p.c["cc"] - 1
        p.wait("sync", "sc")
        p.dma("sync", qaug[0:1, :], stg32r[0:1, 0:TC], sem="ld")
        p.dma("sync", qaug[1:2, :], stg32r[0:1, 0:TC], sem="ld")
        p.dma("sync", qaug[2:3, :], stg64r[0:1, 0:TC], sem="ld")
        p.wait("sync", "ld")
        p.wait("sync", "cc", val=cc_kb)
        for hh in range(2):
            for r in range(4):
                ko = kvk_outh[hh].ap()
                c0 = r * 2048 + hh * 1024
                p.dma("sync", kfm_hi[:, c0:c0 + 1024],
                      ko[r * 2 * DH:r * 2 * DH + 128, :], sem="ld")
                p.dma("sync", kfm_lo[:, c0:c0 + 1024],
                      ko[r * 2 * DH + 129:r * 2 * DH + 257, :], sem="ld")
                p.dma("sync", scrG16[0:1, c0:c0 + 1024],
                      ko[r * 2 * DH + 128:r * 2 * DH + 129, :], sem="ld")
                p.dma("sync", stg32r[0:1, c0:c0 + 1024],
                      ko[r * 2 * DH + 257:r * 2 * DH + 258, :], sem="ld")
        p.wait("sync", "ld")
        p.wait("vector", "ld")
        p.wait("vector", "sc")
        p.v("memset", stg64r[0:1, 0:2048], 1.0, sem="ve")
        p.wait("sync", "sc")
        p.wait("sync", "ve")
        p.dma("sync", scrG16[1:2, 0:8192], stg32r[0:1, 0:8192], sem="ld")
        p.dma("sync", scrG16[2:3, 0:8192], scrG16[0:1, 0:8192], sem="ld")
        for i4 in range(4):
            p.dma("sync", scrG16[3:4, i4 * 2048:(i4 + 1) * 2048],
                  stg64r[0:1, 0:2048], sem="ld")
        p.wait("tensor", "ld")
        for t in range(NT):
            tsl = slice(t * 128, (t + 1) * 128)
            ps = psB[:, 1536 + t:1537 + t]
            p.mm(ps, qfm_hi[:, tsl], k0hl[:, 0:1], start=(t == 0), stop=False)
            p.mm(ps, qfm_hi[:, tsl], k0hl[:, 2:3], start=False, stop=False)
            p.mm(ps, qfm_lo[:, tsl], k0hl[:, 0:1], start=False, stop=False)
            p.mm(ps, qaug[0:3, tsl], k0aug[0:3, :], start=False,
                 stop=(t == NT - 1), sem="te")
        p.wait("scalar", "te")
        p.act(ssink[:, :], psB[:, 1536:1552], AF.Copy, sem="sc")
        p.wait("tensor", "sc")
        fwd_sc = {}
        red_ve = {}
        for t in range(NT):
            tsl = slice(t * 128, (t + 1) * 128)
            for n in range(16):
                ps = psA[:, (n % 4) * 512:(n % 4) * 512 + 512]
                key = (t * 16 + n) - 4
                if key >= 0:
                    p.wait("tensor", "sc", val=fwd_sc[key])
                p.mm(ps, qfm_hi[:, tsl], kfm_hi[:, n * 512:(n + 1) * 512],
                     start=True, stop=False)
                p.mm(ps, qaug[0:1, tsl], scrG16[0:1, n * 512:(n + 1) * 512],
                     start=False, stop=True, sem="te")
                p.wait("scalar", "te")
                p.wait("scalar", "ve", val=red_ve.get((t - 1, n // 4), 0))
                p.act(fwd16[:, n * 512:(n + 1) * 512], ps, AF.Copy, sem="sc")
                fwd_sc[t * 16 + n] = p.c["sc"]
                if n % 4 == 3:
                    p.wait("vector", "sc")
                    p.v("reduce_max", out=zsb[:, 12 + (n // 4):13 + (n // 4)],
                        in_=fwd16[:, (n // 4) * 2048:(n // 4) * 2048 + 2048],
                        axis=AX, sem="ve")
                    red_ve[(t, n // 4)] = p.c["ve"]
            p.v("reduce_max", out=mrow[:, t:t + 1], in_=zsb[:, 12:16], axis=AX,
                sem="ve")
        p.wait("vector", "sc")
        p.v("tensor_max", mrow[:, :], mrow[:, :], ssink[:, :])
        p.v("tensor_scalar", mrow[:, :], mrow[:, :], 0.5, None, ALU.add)
        p.v("tensor_sub", psk[:, :], ssink[:, :], mrow[:, :], sem="ve")
        p.wait("scalar", "ve")
        p.act(psk[:, :], psk[:, :], AF.Exp, sem="sc")
        p.wait("vector", "sc")
        p.v("tensor_copy", out=psk16[:, :], in_=psk[:, :])
        p.v("tensor_scalar", lv16[:, :], mrow[:, :], -1.0, None, ALU.mult,
            sem="ve")
        p.wait("tensor", "ve")
        for t in range(NT):
            p.tr(psA16_[0:1, 0:128], psk16[:, t:t + 1], id16[:, :])
            p.tr(psA16_[0:1, 1024:1152], lv16[:, t:t + 1], id16[:, :], sem="te")
            p.wait("scalar", "te")
            p.act(pskT[0:1, t * 128:(t + 1) * 128], psA16_[0:1, 0:128], AF.Copy)
            p.wait("scalar", "sc")
            p.act(stg32r[0:1, t * 128:(t + 1) * 128],
                  psA16_[0:1, 1024:1152], AF.Copy, sem="sc")
            p.wait("tensor", "sc")
        p.wait("sync", "sc")
        p.dma("sync", qaug[3:4, 0:TC], stg32r[0:1, 0:TC], sem="ld")
        p.wait("sync", "ld")
        p.wait("tensor", "ld")
        p.wait("sync", "cc")
        NVT = NQB * NKT
        v_done = {}
        vsems = ["vs", "vs1", "vs2"]

        def issue_v(g):
            kt = g % NKT
            r, o = kt // 16, kt % 16
            hh = 0 if o < 8 else 1
            row = r * 1024 + (o - hh * 8) * 128
            if g >= 3:
                p.wait("sync", "te", val=pv_te[g - 3])
            p.dma("sync", vbuf[:, (g % 3) * D:(g % 3) * D + D],
                  kvv_outh[hh].ap()[row:row + 128, :], sem=vsems[g % 3])
            v_done[g] = p.c[vsems[g % 3]]

        exp_sc = {}
        issue_v(0)
        issue_v(1)

        def emit_scores(qb, kt):
            qsl_ = slice(qb * QB, (qb + 1) * QB)
            par = kt % 2
            sbk = psB[:, par * 512:par * 512 + QB]
            if kt >= 2:
                p.wait("tensor", "sc", val=exp_sc[kt - 2])
            p.mm(sbk, kfm_hi[:, kt * 128:(kt + 1) * 128], qfm_hi[:, qsl_],
                 start=True, stop=False)
            p.mm(sbk, kfm_hi[:, kt * 128:(kt + 1) * 128], qfm_lo[:, qsl_],
                 start=False, stop=False)
            p.mm(sbk, kfm_lo[:, kt * 128:(kt + 1) * 128], qfm_hi[:, qsl_],
                 start=False, stop=False)
            p.mm(sbk, scrG16[0:4, kt * 128:(kt + 1) * 128], qaug[0:4, qsl_],
                 start=False, stop=True, sem="te")
            p.wait("scalar", "te")
            p.act(pt16[:, (kt % 3) * QB:(kt % 3 + 1) * QB], sbk, AF.Exp,
                  sem="sc")
            exp_sc[kt] = p.c["sc"]

        def emit_pv(qb, kt):
            g = qb * NKT + kt
            par = kt % 3
            p.wait("tensor", "sc", val=exp_sc[kt])
            p.wait("tensor", vsems[g % 3], val=v_done[g])
            for qt in range(2):
                ptsl = pt16[:, par * QB + qt * 128:par * QB + (qt + 1) * 128]
                vsl = vbuf[:, (g % 3) * D:(g % 3) * D + D]
                p.mm(psA[:, qt * 1024:qt * 1024 + 512], ptsl, vsl[:, 0:512],
                     start=(kt == 0), stop=False)
                p.mm(psA[:, qt * 1024 + 512:qt * 1024 + 1024], ptsl,
                     vsl[:, 512:1024], start=(kt == 0), stop=False)
                p.mm(psB[:, 1024 + qt:1025 + qt], ptsl, ones16[:, 0:1],
                     start=(kt == 0 and qt == 0),
                     stop=(kt == NKT - 1 and qt == 1), sem="te")
            pv_te[g] = p.c["te"]
            if g + 3 < NVT:
                issue_v(g + 3)

        pv_te = {}
        issue_v(2)
        for qb in range(NQB):
            exp_sc.clear()
            emit_scores(qb, 0)
            emit_scores(qb, 1)
            for kt in range(NKT):
                if kt + 2 < NKT:
                    emit_scores(qb, kt + 2)
                emit_pv(qb, kt)
            for qt in range(2):
                t = qb * 2 + qt
                tsl2 = slice(t * 128, (t + 1) * 128)
                p.mm(psA[:, qt * 1024:qt * 1024 + 512], pskT[0:1, tsl2],
                     v0r16[0:1, 0:512], start=False, stop=True)
                p.mm(psA[:, qt * 1024 + 512:qt * 1024 + 1024],
                     pskT[0:1, tsl2], v0r16[0:1, 512:1024],
                     start=False, stop=True, sem="te")
            p.wait("vector", "te")
            p.v("tensor_copy", out=zsb[:, 0:2], in_=psB[:, 1024:1026])
            p.v("tensor_add", zsb[:, 2:4], zsb[:, 0:2], psk[:, 2 * qb:2 * qb + 2])
            p.v("reciprocal", out=zsb[:, 4:6], in_=zsb[:, 2:4])
            for qt in range(2):
                t = qb * 2 + qt
                xt = x_sb[:, t * D:(t + 1) * D]
                for half in range(2):
                    p.v("scalar_tensor_tensor", xt[:, half * 512:(half + 1) * 512],
                        psA[:, qt * 1024 + half * 512:qt * 1024 + (half + 1) * 512],
                        zsb[:, 4 + qt:5 + qt], xt[:, half * 512:(half + 1) * 512],
                        ALU.mult, ALU.add, sem="ve")
                p.v("reduce_sum", out=sA[:, t:t + 1], in_=xt, axis=AX, sem="ve")
            p.wait("tensor", "ve")
        p.wait("scalar", "ve")
        for t in range(NT):
            p.act(scrH_lo.bitcast(F32)[:, 0:1024], x_sb[:, t * D:(t + 1) * D],
                  AF.Square, accum_out=sA[:, 16 + t:17 + t], sem="sc")

    def emit_out(p: Pass):
        p.wait("sync", "ve")
        for t in range(NT):
            p.dma("sync", x_out_t[t], x_sb[:, t * D:(t + 1) * D], sem="st")
        p.wait("sync", "st")

    psA = nc.alloc_psum_tensor("psA", [128, 2048], F32).ap()
    psB = nc.alloc_psum_tensor("psB", [128, 2048], F32).ap()

    counts = []
    with nc.Block() as block:
        for eng_name in ENGS:
            c = {s: 0 for s in sem_names}
            counts.append(c)

            def make(f_name, f_c):
                def _(eng):
                    program(Pass(f_name, eng, sems, f_c))
                return _
            getattr(block, eng_name)(make(eng_name, c))
    for c in counts[1:]:
        assert c == counts[0], f"counter mismatch {c} vs {counts[0]}"
    return nc


def host_inputs(I, core):
    b, j = core // 4, core % 4
    t0 = j * TC
    f32 = np.float32

    def T(a):
        return np.ascontiguousarray(np.asarray(a, f32).T)

    def pad129(a):
        out = np.zeros((128, 2), f32)
        a = np.asarray(a, f32).reshape(-1)
        out[:, 0] = a[0:128]
        out[0, 1] = a[128]
        return out

    return {
        "x_in": np.ascontiguousarray(np.asarray(I["L0_attn_x"], f32)[b, t0:t0 + TC]),
        "wqT": T(I["L0_attn_w_q"]),
        "wkT": T(I["L0_attn_w_k"]),
        "wvT": T(I["L0_attn_w_v"]),
        "bqp": pad129(I["L0_attn_b_q"]),
        "bkp": pad129(I["L0_attn_b_k"]),
        "bvp": np.asarray(I["L0_attn_b_v"], f32).reshape(128, 8),
        "k0p": pad129(I["L0_attn_k_0"]),
        "v0p": np.asarray(I["L0_attn_v_0"], f32).reshape(128, 8),
        "qproj": np.asarray(I["L0_ffn_q_proj_w"], f32),
        "routerT": T(I["L0_ffn_router_w"]),
        "auxkT": T(I["L0_ffn_aux_keys"]),
        "maskp": np.asarray(I["L0_ffn_reliability_mask"], f32).reshape(128, 32),
        "w1T": T(I["L0_ffn_unc_w1"]),
        "b1p": np.ascontiguousarray(
            np.asarray(I["L0_ffn_unc_b1"], f32).reshape(2, 128).T),
        "w2p": np.ascontiguousarray(
            np.asarray(I["L0_ffn_unc_w2"], f32).reshape(2, 128).T),
        "b2v_in": np.broadcast_to(
            np.asarray(I["L0_ffn_unc_b2"], f32).reshape(1, 1), (128, 1)).copy(),
        "gw1v_in": np.full((128, 1), np.float32(I["L0_ffn_gate_w1"]), f32),
        "gbv_in": np.full((128, 1), np.float32(I["L0_ffn_gate_bias"]), f32),
        "aux_vals": np.asarray(I["L0_ffn_aux_values"], f32),
        "id16": np.eye(128, dtype=np.float16),
    }


def kernel(**inputs):
    nc = build_nc("full")
    in_maps = [host_inputs(inputs, c) for c in range(8)]
    res = run_bass_kernel_spmd(nc, in_maps, core_ids=list(range(8)))
    out = np.empty((B, S, D), np.float32)
    for c in range(8):
        b, j = c // 4, c % 4
        out[b, j * TC:(j + 1) * TC] = res.results[c]["x_out"]
    return out


# version 11 (final)
# speedup vs baseline: 2.3090x; 1.0892x over previous
import numpy as np
import concourse.bass as bass
import concourse.mybir as mybir
from concourse.bass import IndirectOffsetOnAxis
from concourse.bass_utils import run_bass_kernel_spmd

F32 = mybir.dt.float32
F16 = mybir.dt.float16
BF16 = mybir.dt.bfloat16
U32 = mybir.dt.uint32
AF = mybir.ActivationFunctionType
ALU = mybir.AluOpType
AX = mybir.AxisListType.X

B, S, D = 2, 8192, 1024
TC = 2048
NT = 16
GT = 2
GTOK = GT * 128
NG = NT // GT
NKT = 64
NSLOT = 4096
DH = 129
QB = 256
NQB = TC // QB
EPS_LN = 1e-5
RSQ48 = float(1.0 / np.sqrt(48.0))

ENGS = ("sync", "tensor", "scalar", "vector", "gpsimd")


class Pass:
    def __init__(self, eng_name, eng, sems, c):
        self.e = eng_name
        self.eng = eng
        self.sems = sems
        self.c = c

    def bump(self, inst, sem, n):
        if sem is not None:
            self.c[sem] += n
            if inst is not None:
                inst.then_inc(self.sems[sem], n)
        return inst

    def wait(self, eng_name, sem, val=None):
        if self.e == eng_name:
            self.eng.wait_ge(self.sems[sem], self.c[sem] if val is None else val)

    def mm(self, out, lhsT, rhs, start=True, stop=True, sem=None, n=1):
        i = self.eng.matmul(out, lhsT, rhs, start=start, stop=stop) \
            if self.e == "tensor" else None
        return self.bump(i, sem, n)

    def tr(self, out, in_, ident, sem=None, n=1):
        i = self.eng.transpose(out, in_, ident) if self.e == "tensor" else None
        return self.bump(i, sem, n)

    def act(self, out, in_, func, bias=0.0, scale=1.0, accum_out=None,
            sem=None, n=1):
        i = None
        self.wait("scalar", "sc")
        if self.e == "scalar":
            if func == AF.Copy and not (isinstance(bias, float)
                                        and isinstance(scale, float)):
                func = AF.Identity
            kw = {"accum_out": accum_out} if accum_out is not None else {}
            i = self.eng.activation(out, in_, func, bias=bias, scale=scale, **kw)
        return self.bump(i, "sc", 1)

    def v(self, meth, *args, sem=None, n=1, **kw):
        self.wait("vector", "ve")
        i = getattr(self.eng, meth)(*args, **kw) if self.e == "vector" else None
        return self.bump(i, "ve", 1)

    def dma(self, queue, out, in_, sem=None, transpose=False, slow=False):
        i = None
        if self.e == queue:
            if transpose:
                i = self.eng.dma_start_transpose(out=out, in_=in_)
            elif slow:
                i = self.eng.dma_start(out=out, in_=in_,
                                       allow_slow_non_contiguous=True)
            else:
                i = self.eng.dma_start(out=out, in_=in_)
        return self.bump(i, sem, 16 if sem else 0)

    def gather(self, out, table_ap, idx_ap, sem=None):
        i = None
        if self.e == "gpsimd":
            i = self.eng.indirect_dma_start(
                out=out, out_offset=None, in_=table_ap,
                in_offset=IndirectOffsetOnAxis(idx_ap, 0))
        return self.bump(i, sem, 16 if sem else 0)

    def coll(self, kind, op, groups, ins, outs, sem=None):
        i = None
        if self.e == "gpsimd":
            i = self.eng.collective_compute(kind, op, replica_groups=groups,
                                            ins=ins, outs=outs)
        return self.bump(i, sem, 1 if sem else 0)


def build_nc(stage="full", core0_groups=None):
    nc = bass.Bass("TRN2", target_bir_lowering=False)

    def din(name, shape, dt=F32):
        return nc.dram_tensor(name, list(shape), dt, kind="ExternalInput").ap()

    x_in = din("x_in", (TC, D))
    wqT_in = din("wqT", (D, DH))
    wkT_in = din("wkT", (D, DH))
    wvT_in = din("wvT", (D, D))
    bqp_in = din("bqp", (128, 2))
    bkp_in = din("bkp", (128, 2))
    bvp_in = din("bvp", (128, 8))
    k0p_in = din("k0p", (128, 2))
    v0p_in = din("v0p", (128, 8))
    qproj_in = din("qproj", (128, D))
    routerT_in = din("routerT", (128, 48))
    auxkT_in = din("auxkT", (128, NSLOT))
    maskp_in = din("maskp", (128, 32))
    w1T_in = din("w1T", (D, 256))
    b1p_in = din("b1p", (128, 2))
    w2p_in = din("w2p", (128, 2))
    b2v_in = din("b2v_in", (128, 1))
    gw1v_in = din("gw1v_in", (128, 1))
    gbv_in = din("gbv_in", (128, 1))
    aux_in = din("aux_vals", (NSLOT, D))
    id16_in = din("id16", (128, 128), F16)

    x_out = nc.dram_tensor("x_out", [TC, D], F32, kind="ExternalOutput").ap()

    aux_bf = nc.dram_tensor("aux_bf", [NSLOT, D], F16)
    kvv_inh = [nc.dram_tensor(f"kvv_in{i}", [TC // 2, D], F16) for i in range(2)]
    kvv_outh = [nc.dram_tensor(f"kvv_out{i}", [S // 2, D], F16) for i in range(2)]
    kvk_inh = [nc.dram_tensor(f"kvk_in{i}", [2 * DH, TC // 2], F16)
               for i in range(2)]
    kvk_outh = [nc.dram_tensor(f"kvk_out{i}", [8 * DH, S // 8], F16)
                for i in range(2)]
    ar_in = [nc.dram_tensor(f"ar_in{i}", [1, 1], F32) for i in range(2)]
    ar_out = [nc.dram_tensor(f"ar_out{i}", [1, 1], F32) for i in range(2)]

    A = {}

    def sbuf(name, shape, dt):
        A[name] = nc.alloc_sbuf_tensor(name, list(shape), dt).ap()
        return A[name]

    x_sb = sbuf("x_sb", (128, NT * D), F32)
    stage32 = sbuf("stage32", (128, 8192), F32)
    scrF = sbuf("scrF", (128, NSLOT), F32)
    scrG = sbuf("scrG", (128, 8192), BF16)
    arenaW = sbuf("arenaW", (128, 8192), F16)
    scrH_hi = sbuf("scrH_hi", (128, GT * D), F16)
    scrH_lo = sbuf("scrH_lo", (128, GT * D), F16)
    hfm_hi = sbuf("hfm_hi", (128, 8 * GTOK), F16)
    hfm_lo = sbuf("hfm_lo", (128, 8 * GTOK), F16)
    vbuf = sbuf("vbuf", (128, 3072), F16)
    qfm_hi = sbuf("qfm_hi", (128, TC), F16)
    qfm_lo = sbuf("qfm_lo", (128, TC), F16)
    qtm16 = sbuf("qtm16", (128, DH), F16)
    ktm16 = sbuf("ktm16", (128, DH), F16)
    pt16 = sbuf("pt16", (128, 3 * QB), F16)
    qtm_lo = None
    rkT_hi = sbuf("rkT_hi", (50, NSLOT), F16)
    rkT_lo = sbuf("rkT_lo", (48, NSLOT), F16)
    w1T16 = sbuf("w1T16", (128, 8 * 256), F16)
    wq16 = sbuf("wq16", (128, 8 * 2 * DH), F16)
    wk16 = sbuf("wk16", (128, 8 * 2 * DH), F16)
    wrqT16 = sbuf("wrqT16", (128, 8 * 96), F16)

    w2T16 = sbuf("w2T16", (128, 2), F16)
    id16 = sbuf("id16s", (128, 128), F16)
    ones16 = sbuf("ones16", (128, 2), F16)
    ones16r = sbuf("ones16r", (2, 128), F16)
    bk16 = sbuf("bk16", (2, DH), F16)
    k0hl = sbuf("k0hl", (128, 4), F16)
    k0aug = sbuf("k0aug", (4, 1), F16)
    b1c = sbuf("b1c", (128, 2), F32)
    b2v = sbuf("b2v", (128, 1), F32)
    gw1v = sbuf("gw1v", (128, 1), F32)
    gbv = sbuf("gbv", (128, 1), F32)
    sA = sbuf("sA", (128, 64), F32)
    sM = sbuf("sM", (128, 64), F32)
    one1 = sbuf("one1", (1, 16), F32)
    lv16 = sbuf("lv16", (128, 16), F16)
    invb = sbuf("invb", (128, 1), F32)
    top8 = sbuf("top8", (128, 8), F32)
    idx8 = sbuf("idx8", (128, 8), U32)
    w8 = sbuf("w8", (128, 16), F32)
    zsb = sbuf("zsb", (128, 16), F32)
    psk = sbuf("psk", (128, NT), F32)
    psk16 = sbuf("psk16", (128, NT), F16)
    ssink = sbuf("ssink", (128, NT), F32)
    mrow = sbuf("mrow", (128, NT), F32)

    h1pre = arenaW[:, 0:4096]
    rq_hi = arenaW[0:50, 4096:6144]
    rq_lo = arenaW[50:98, 6144:8192]
    rq_lo = arenaW[0:48, 6144:8192]
    qp16 = vbuf
    kfm_hi = stage32.bitcast(F16)[:, 0:8192]
    kfm_lo = stage32.bitcast(F16)[:, 8192:16384]
    kfs_hi = stage32.bitcast(F16)[:, 8192:8192 + 2048]
    kfs_lo = stage32.bitcast(F16)[:, 10240:10240 + 2048]
    kfs_ra = stage32.bitcast(F16)[0:1, 12288:12288 + 2048]
    kfs_rb = stage32.bitcast(F16)[64:65, 12288:12288 + 2048]
    fwd16 = scrF.bitcast(BF16)
    k129 = scrG.bitcast(F16)[0:4, 0:8192]
    wv16 = arenaW
    scrG16 = scrG.bitcast(F16)
    qaug = scrH_hi[0:4, 0:TC]
    v0r16 = scrG16[64:65, 6144:6144 + D]
    pskT = scrG16[64:65, 4096:6144]
    bv16 = scrG16[64:66, 2048:2048 + D]
    ones64 = scrG16[64:66, 3072:3200]
    stg32r = scrG16[32:33, 0:8192]
    stg64r = scrG16[64:65, 0:2048]
    tmp32 = scrF[:, 0:D]
    qtm_lo = pt16[:, 0:DH]
    ktm_lo = pt16[:, 256:256 + DH]
    rt16 = pt16[:, 512:608]
    bq16 = pt16[0:2, 608:608 + DH]

    sem_names = ["ld", "te", "sc", "ve", "gp", "cc", "vs", "st", "aux", "kv",
                 "vs1", "vs2"]
    sems = {s: nc.alloc_semaphore(s) for s in sem_names}

    x_in_t = x_in.rearrange("(t p) d -> t p d", p=128)
    x_out_t = x_out.rearrange("(t p) d -> t p d", p=128)

    def program(p: Pass):
        emit_prep(p)
        emit_ffn(p, 0, bcb=at0_prefetch)
        if stage in ("attn", "full"):
            emit_attn(p)
        if stage == "full":
            emit_ffn(p, 1, skip_stats=True)
        emit_out(p)

    def hilo_(p, dst_hi, dst_lo, src32):
        p.wait("scalar", "sc")
        p.act(dst_hi, src32, AF.Copy, sem="sc")
        p.wait("vector", "sc")
        p.v("tensor_tensor", dst_lo, src32, dst_hi, ALU.subtract, sem="ve")

    def hi_(p, dst_hi, src32):
        p.wait("scalar", "sc")
        p.act(dst_hi, src32, AF.Copy, sem="sc")

    def emit_prep(p: Pass):
        for i in range(4):
            p.dma("gpsimd", aux_bf.ap()[i * 1024:(i + 1) * 1024, :],
                  aux_in[i * 1024:(i + 1) * 1024, :], sem="aux")
        for t in range(NT):
            p.dma("sync", x_sb[:, t * D:(t + 1) * D], x_in_t[t], sem="ld")
        p.dma("sync", id16[:, :], id16_in, sem="ld")
        p.dma("sync", b2v[:, :], b2v_in, sem="ld")
        p.dma("sync", gw1v[:, :], gw1v_in, sem="ld")
        p.dma("sync", gbv[:, :], gbv_in, sem="ld")
        p.dma("sync", b1c[:, :], b1p_in, sem="ld")
        p.dma("sync", stage32[:, 0:D], qproj_in, sem="ld")
        p.dma("sync", stage32[:, D:D + 48], routerT_in, sem="ld")
        p.dma("sync", stage32[:, D + 48:D + 48 + 2048]
              .rearrange("p (c d) -> p c d", c=8),
              w1T_in.rearrange("(c p) d -> p c d", p=128), sem="ld")
        p.dma("sync", stage32[:, 3120:3120 + NSLOT], auxkT_in, sem="ld")
        p.dma("sync", stage32[:, 7216:7218], w2p_in, sem="ld")
        p.dma("sync", stage32[:, 7218:7250], maskp_in, sem="ld")
        p.wait("scalar", "ld")
        p.wait("vector", "ld")
        p.v("memset", ones16[:, :], 1.0)
        p.v("memset", ones16r[:, :], 1.0)
        p.v("memset", rq_hi[0:50, :], 1.0, sem="ve")
        hi_(p, qp16[:, 0:D], stage32[:, 0:D])
        hi_(p, rt16[:, 0:48], stage32[:, D:D + 48])
        for ch in range(8):
            p.act(w1T16[:, ch * 256:(ch + 1) * 256],
                  stage32[:, D + 48 + ch * 256:D + 48 + (ch + 1) * 256],
                  AF.Copy, sem="sc")
        p.act(w2T16[:, 0:2], stage32[:, 7216:7218], AF.Copy, sem="sc")
        mh16 = scrF.bitcast(F16)[:, 0:32]
        ml16 = scrF.bitcast(F16)[:, 32:64]
        hilo_(p, mh16, ml16, stage32[:, 7218:7250])
        p.wait("sync", "ve")
        p.wait("sync", "sc")
        p.dma("sync", rkT_hi[48:49, :], mh16, sem="ld")
        p.dma("sync", rkT_hi[49:50, :], ml16, sem="ld")
        auxk_hi = scrG.bitcast(F16)[:, 0:NSLOT]
        asrc = stage32[:, 3120:3120 + NSLOT]
        for q in range(4):
            sl = slice(q * 1024, (q + 1) * 1024)
            hi_(p, auxk_hi[:, sl], asrc[:, sl])
        p.wait("tensor", "sc")
        p.wait("tensor", "ve")
        p.wait("tensor", "ld")
        p.wait("scalar", "ve")
        for n in range(8):
            ps = psA[0:48, (n % 4) * 512:(n % 4) * 512 + 512]
            rh = auxk_hi[:, n * 512:(n + 1) * 512]
            p.mm(ps, rt16[:, 0:48], rh, start=True, stop=True, sem="te")
            p.wait("scalar", "te")
            p.act(stage32[0:48, n * 512:(n + 1) * 512], ps, AF.Copy,
                  scale=RSQ48, sem="sc")
            p.wait("tensor", "sc")
        for n in range(8):
            sl = slice(n * 512, (n + 1) * 512)
            hi_(p, rkT_hi[0:48, sl], stage32[0:48, sl])
        for n in range(2):
            ps = psB[0:48, n * 512:(n + 1) * 512]
            qh = qp16[:, n * 512:(n + 1) * 512]
            p.mm(ps, rt16[:, 0:48], qh, start=True, stop=True, sem="te")
        p.wait("scalar", "te")
        p.wait("scalar", "ve")
        for n in range(2):
            p.act(stage32[0:48, 4096 + n * 512:4096 + (n + 1) * 512],
                  psB[0:48, n * 512:(n + 1) * 512], AF.Copy, sem="sc")
        wrq32 = stage32[0:48, 4096:4096 + 1024]
        wrq_h = scrH_hi[0:48, 0:1024]
        hi_(p, wrq_h, wrq32)
        p.wait("tensor", "sc")
        p.wait("tensor", "ve")
        psA16 = psA.bitcast(F16)
        for ch in range(8):
            p.tr(psA16[0:128, 0:48], wrq_h[:, ch * 128:(ch + 1) * 128],
                 id16[0:48, 0:48], sem="te")
            p.wait("scalar", "te")
            p.act(wrqT16[:, ch * 96:ch * 96 + 48], psA16[0:128, 0:48], AF.Copy,
                  sem="sc")
            p.wait("tensor", "sc")
        p.wait("gpsimd", "aux")

    def emit_ffn(p: Pass, call, skip_stats=False, bcb=None):
        if not skip_stats:
            p.wait("scalar", "ve")
            p.wait("vector", "sc")
            for t in range(NT):
                xt = x_sb[:, t * D:(t + 1) * D]
                p.v("reduce_sum", out=sA[:, t:t + 1], in_=xt, axis=AX, sem="ve")
                p.wait("scalar", "sc")
                p.act(tmp32[:, :], xt, AF.Square, accum_out=sA[:, 16 + t:17 + t],
                      sem="sc")
        p.wait("vector", "sc")
        p.wait("scalar", "ve")
        p.v("tensor_scalar", sA[:, 32:48], sA[:, 0:16], 1.0 / D, None, ALU.mult)
        p.v("tensor_scalar", sA[:, 48:64], sA[:, 16:32], 1.0 / D, None, ALU.mult)
        p.v("tensor_tensor", tmp32[:, 0:16], sA[:, 32:48], sA[:, 32:48], ALU.mult)
        p.v("tensor_sub", sA[:, 48:64], sA[:, 48:64], tmp32[:, 0:16])
        p.v("tensor_scalar", tmp32[:, 0:16], sA[:, 48:64], EPS_LN, None, ALU.add,
            sem="ve")
        p.wait("scalar", "ve")
        p.act(tmp32[:, 16:32], tmp32[:, 0:16], AF.Ln, sem="sc")
        p.wait("scalar", "sc")
        p.act(sM[:, 0:16], tmp32[:, 16:32], AF.Exp, scale=-0.5, sem="sc")
        p.wait("vector", "sc")
        p.v("tensor_tensor", tmp32[:, 48:64], sA[:, 32:48], sM[:, 0:16], ALU.mult)
        p.v("tensor_scalar", sM[:, 16:32], tmp32[:, 48:64], -1.0, None, ALU.mult)
        p.v("reciprocal", out=tmp32[:, 16:32], in_=tmp32[:, 0:16])
        p.v("tensor_tensor", tmp32[:, 32:48], sA[:, 48:64], tmp32[:, 16:32],
            ALU.mult, sem="ve")
        p.wait("scalar", "ve")
        p.act(sM[:, 32:48], tmp32[:, 32:48], AF.Ln, bias=1.0, sem="sc")
        p.wait("vector", "sc")
        p.v("tensor_copy", out=lv16[:, :], in_=sM[:, 32:48], sem="ve")
        p.wait("tensor", "ve")
        p.mm(psB[0:1, 0:16], ones16[:, 0:1], lv16[:, :], sem="te")
        p.wait("vector", "te")
        p.v("reduce_sum", out=one1[0:1, 1:2], in_=psB[0:1, 0:16], axis=AX,
            sem="ve")
        p.wait("sync", "ve")
        p.dma("sync", ar_in[call].ap(), one1[0:1, 1:2], sem="st")
        p.wait("gpsimd", "st")
        p.coll("AllReduce", ALU.add, [[0, 1, 2, 3, 4, 5, 6, 7]],
               [ar_in[call].ap().opt()], [ar_out[call].ap().opt()], sem="cc")
        p.wait("vector", "st")
        if call == 1:
            p.v("memset", rq_hi[0:50, :], 1.0, sem="ve")
        ln_ve = {}
        tmpab = [tmp32, scrF[:, 1024:2048]]

        def emit_ln_ffn(t):
            ti = t % GT
            xt = x_sb[:, t * D:(t + 1) * D]
            tmp = tmpab[t % 2]
            if t < 2:
                p.wait("scalar", "ve")
            else:
                p.wait("scalar", "ve", val=ln_ve[t - 2])
            p.act(tmp[:, :], xt, AF.Copy, bias=sM[:, 16 + t:17 + t],
                  scale=sM[:, 0 + t:1 + t], sem="sc")
            p.act(scrH_hi[:, ti * D:(ti + 1) * D], tmp[:, :], AF.Copy,
                  sem="sc")
            p.wait("vector", "sc")
            p.v("scalar_tensor_tensor", xt, tmp[:, :], 0.5, xt,
                ALU.mult, ALU.add, sem="ve")
            ln_ve[t] = p.c["ve"]

        emit_ln_ffn(0)
        emit_ln_ffn(1)
        for g in range(NG):
            p.wait("tensor", "sc")
            p.wait("tensor", "ve", val=ln_ve[g * GT + 1])
            psA16_ = psA.bitcast(F16)
            for ch in range(8):
                for ti in range(GT):
                    p.tr(psA16_[:, ch * 256 + ti * 128:
                                ch * 256 + (ti + 1) * 128],
                         scrH_hi[:, ti * D + ch * 128:ti * D + (ch + 1) * 128],
                         id16[:, :], sem="te")
            p.wait("scalar", "te")
            p.act(hfm_hi[:, 0:1024], psA16_[:, 0:1024], AF.Copy, sem="sc")
            p.act(hfm_hi[:, 1024:2048], psA16_[:, 1024:2048],
                  AF.Copy, sem="sc")
            evac_sc = p.c["sc"]
            if g + 1 < NG:
                emit_ln_ffn((g + 1) * GT)
                emit_ln_ffn((g + 1) * GT + 1)
            p.wait("tensor", "sc", val=evac_sc)
            for m in range(2):
                ps = psB[:, m * 512:m * 512 + GTOK]
                for ch in range(8):
                    p.mm(ps, w1T16[:, ch * 256 + m * 128:ch * 256 + m * 128 + 128],
                         hfm_hi[:, ch * GTOK:(ch + 1) * GTOK],
                         start=(ch == 0), stop=(ch == 7), sem="te")
                p.wait("scalar", "te")
                p.act(h1pre[:, m * 2048 + g * GTOK:m * 2048 + (g + 1) * GTOK], ps,
                      AF.Gelu, bias=b1c[:, m:m + 1], sem="sc")
            p.wait("tensor", "sc")
            for ti in range(GT):
                t = g * GT + ti
                ps = psB[:, 1536 + t:1536 + t + 1]
                for m in range(2):
                    p.mm(ps, h1pre[:, m * 2048 + g * GTOK + ti * 128:
                                   m * 2048 + g * GTOK + (ti + 1) * 128],
                         w2T16[:, m:m + 1],
                         start=(t == 0 and m == 0), stop=(t == NT - 1 and m == 1),
                         sem="te")
            ps = psA[0:48, 0:GTOK]
            for ch in range(8):
                wh = wrqT16[:, ch * 96:ch * 96 + 48]
                hh = hfm_hi[:, ch * GTOK:(ch + 1) * GTOK]
                p.mm(ps, wh, hh, start=(ch == 0), stop=(ch == 7), sem="te")
            p.wait("scalar", "te")
            p.act(rq_hi[0:48, g * GTOK:(g + 1) * GTOK], ps, AF.Copy, sem="sc")
        p.wait("scalar", "te")
        p.act(tmp32[:, 0:16], psB[:, 1536:1552], AF.Copy, bias=b2v[:, 0:1],
              sem="sc")
        p.wait("scalar", "sc")
        p.act(tmp32[:, 16:32], tmp32[:, 0:16], AF.Exp, scale=-1.0, sem="sc")
        p.wait("vector", "sc")
        p.v("tensor_scalar", tmp32[:, 16:32], tmp32[:, 16:32], 1.0, None, ALU.add)
        p.v("reciprocal", out=sM[:, 48:64], in_=tmp32[:, 16:32], sem="ve")
        p.wait("sync", "cc")
        p.dma("sync", one1[0:1, 2:3], ar_out[call].ap(), sem="ld")
        p.wait("sync", "ld")
        p.wait("vector", "ld")
        p.v("tensor_scalar", one1[0:1, 3:4], one1[0:1, 2:3], 1.0 / (B * S), 1e-6,
            ALU.mult, op1=ALU.add)
        p.v("reciprocal", out=one1[0:1, 4:5], in_=one1[0:1, 3:4])
        p.v("tensor_scalar", one1[0:1, 5:6], one1[0:1, 4:5], 0.5, None, ALU.mult)
        p.v("tensor_copy", out=one1.bitcast(F16)[0:1, 12:13], in_=one1[0:1, 5:6],
            sem="ve")
        p.wait("tensor", "ve")
        p.mm(psB[:, 1024:1025], ones16r[0:1, :], one1.bitcast(F16)[0:1, 12:13],
             sem="te")
        p.wait("scalar", "te")
        p.act(invb[:, :], psB[:, 1024:1025], AF.Copy, sem="sc")
        p.wait("vector", "sc")
        p.v("tensor_scalar", sM[:, 32:48], sM[:, 32:48], invb[:, 0:1], None,
            ALU.mult)
        p.v("scalar_tensor_tensor", sM[:, 48:64], sM[:, 48:64], 2.5,
            sM[:, 32:48], ALU.mult, ALU.add)
        p.v("tensor_scalar", sM[:, 48:64], sM[:, 48:64], 5.0, 0.0, ALU.min,
            op1=ALU.max, sem="ve")
        p.wait("scalar", "ve")
        p.act(tmp32[:, 16:32], sM[:, 48:64], AF.Copy, scale=gw1v[:, 0:1],
              bias=gbv[:, 0:1], sem="sc")
        p.wait("scalar", "sc")
        p.act(tmp32[:, 32:48], tmp32[:, 16:32], AF.Exp, scale=-1.0, sem="sc")
        p.wait("vector", "sc")
        p.v("tensor_scalar", tmp32[:, 32:48], tmp32[:, 32:48], 1.0, None, ALU.add)
        p.v("reciprocal", out=sM[:, 48:64], in_=tmp32[:, 32:48], sem="ve")
        a4_ve = p.c["ve"]
        idx_ve = {}
        agg_ve = {}
        w8b = [w8[:, 0:8], w8[:, 8:16]]

        def gslot(t, k):
            if t % 2 == 0:
                return scrG16[:, k * D:(k + 1) * D]
            half = [scrH_hi, scrH_lo, hfm_hi, hfm_lo][k // 2]
            return half[:, (k % 2) * D:(k % 2) * D + D]

        def emit_scores_topk(t):
            g, ti = t // GT, t % GT
            half = psA if t % 2 == 0 else psB
            rq_s = slice(g * GTOK + ti * 128, g * GTOK + (ti + 1) * 128)
            for n in range(8):
                ps = half[:, (n % 4) * 512:(n % 4) * 512 + 512]
                p.mm(ps, rq_hi[0:50, rq_s], rkT_hi[0:50, n * 512:(n + 1) * 512],
                     start=True, stop=True, sem="te")
                if n % 4 == 3:
                    p.wait("scalar", "te")
                    p.wait("scalar", "ve", val=max(idx_ve.get(t - 1, 0), a4_ve))
                    p.act(scrF[:, (n // 4) * 2048:(n // 4) * 2048 + 2048],
                          half[:, 0:2048], AF.Copy, sem="sc")
                    p.wait("tensor", "sc")
            p.wait("vector", "sc")
            p.wait("vector", "gp", val=gp_done.get(t - 1, 0))
            p.v("max", out=top8[:, :], in_=scrF[:, :])
            p.v("tensor_scalar", w8b[t % 2], top8[:, :], top8[:, 0:1], None,
                ALU.subtract)
            p.v("max_index", out=idx8[:, :], in_max=top8[:, :],
                in_values=scrF[:, :], sem="ve")
            idx_ve[t] = p.c["ve"]
            p.wait("gpsimd", "ve", val=idx_ve[t])
            if t >= 2:
                p.wait("gpsimd", "ve", val=agg_ve.get(t - 2, 0))
            for k in range(8):
                p.gather(gslot(t, k), aux_bf.ap(), idx8[:, k:k + 1], sem="gp")
            p.wait("scalar", "ve")
            p.act(w8b[t % 2], w8b[t % 2], AF.Exp, sem="sc")
            p.wait("vector", "sc")
            p.v("reduce_sum", out=zsb[:, 8 + (t % 2):9 + (t % 2)], in_=w8b[t % 2],
                axis=AX)
            p.v("reciprocal", out=zsb[:, 10 + (t % 2):11 + (t % 2)],
                in_=zsb[:, 8 + (t % 2):9 + (t % 2)])
            p.v("tensor_scalar", zsb[:, 12 + (t % 2):13 + (t % 2)],
                zsb[:, 10 + (t % 2):11 + (t % 2)],
                sM[:, 48 + t:49 + t], 0.5, ALU.mult, op1=ALU.mult)
            p.v("tensor_scalar", w8b[t % 2], w8b[t % 2],
                zsb[:, 12 + (t % 2):13 + (t % 2)], None, ALU.mult, sem="ve")

        acc16 = vbuf[:, 0:1024]
        def emit_agg(t):
            p.wait("vector", "gp", val=gp_done[t])
            xt = x_sb[:, t * D:(t + 1) * D]
            p.v("tensor_scalar", acc16, gslot(t, 0), w8b[t % 2][:, 0:1], None,
                ALU.mult)
            for k in range(1, 8):
                p.v("scalar_tensor_tensor", acc16, gslot(t, k),
                    w8b[t % 2][:, k:k + 1], acc16, ALU.mult, ALU.add)
            p.v("scalar_tensor_tensor", xt, acc16, 1.0, xt, ALU.mult, ALU.add,
                sem="ve")
            agg_ve[t] = p.c["ve"]
            if call == 0:
                p.v("reduce_sum", out=sA[:, t:t + 1], in_=xt, axis=AX, sem="ve")

        gp_done = {}
        for t in range(NT):
            emit_scores_topk(t)
            gp_done[t] = p.c["gp"]
            if t >= 1:
                emit_agg(t - 1)
            if bcb is not None:
                bcb(p, t)
        emit_agg(NT - 1)
        if call == 0:
            p.wait("scalar", "ve")
            for t in range(NT):
                p.act(qfm_hi.bitcast(F32)[:, 0:1024],
                      x_sb[:, t * D:(t + 1) * D], AF.Square,
                      accum_out=sA[:, 16 + t:17 + t], sem="sc")

    at0 = {}

    def at0_prefetch(p: Pass, t):
        bq_c = vbuf[:, 2048:2052]
        bk_c = vbuf[:, 2052:2056]
        v0_c = vbuf[:, 2056:2072]
        bv_c = vbuf[:, 2072:2088]
        at0["v0_c"], at0["bv_c"] = v0_c, bv_c
        if t == 0:
            p.dma("sync", stage32[:, 0:4096].rearrange("p (c d) -> p c d", c=4),
                  wvT_in[0:512, :].rearrange("(c p) d -> p c d", p=128),
                  sem="ld")
            p.dma("sync", stage32[:, 4096:4096 + 8 * DH]
                  .rearrange("p (c d) -> p c d", c=8),
                  wqT_in.rearrange("(c p) d -> p c d", p=128), sem="ld")
            p.dma("sync", stage32[:, 5128:5128 + 8 * DH]
                  .rearrange("p (c d) -> p c d", c=8),
                  wkT_in.rearrange("(c p) d -> p c d", p=128), sem="ld")
            p.dma("sync", stage32[:, 6160:6162], bqp_in, sem="ld")
            p.dma("sync", stage32[:, 6162:6164], bkp_in, sem="ld")
            p.dma("sync", stage32[:, 6164:6166], k0p_in, sem="ld")
            p.dma("sync", stage32[:, 6166:6174], v0p_in, sem="ld")
            p.dma("sync", stage32[:, 6174:6182], bvp_in, sem="ld")
            p.wait("sync", "ld")
            at0["ld1"] = p.c["ld"]
        elif t == 2:
            p.wait("scalar", "ld", val=at0["ld1"])
            p.wait("scalar", "te")
            for ch in range(4):
                p.act(wv16[:, ch * D:(ch + 1) * D],
                      stage32[:, ch * D:(ch + 1) * D], AF.Copy, sem="sc")
            at0["cp03"] = p.c["sc"]
        elif t == 3:
            p.wait("vector", "ld", val=at0["ld1"])
            for ch in range(8):
                hilo_(p, wq16[:, ch * 258:ch * 258 + DH],
                      wq16[:, ch * 258 + DH:ch * 258 + 258],
                      stage32[:, 4096 + ch * DH:4096 + (ch + 1) * DH])
                hilo_(p, wk16[:, ch * 258:ch * 258 + DH],
                      wk16[:, ch * 258 + DH:ch * 258 + 258],
                      stage32[:, 5128 + ch * DH:5128 + (ch + 1) * DH])
        elif t == 4:
            hilo_(p, bq_c[:, 0:2], bq_c[:, 2:4], stage32[:, 6160:6162])
            hilo_(p, bk_c[:, 0:2], bk_c[:, 2:4], stage32[:, 6162:6164])
            hilo_(p, k0hl[:, 0:2], k0hl[:, 2:4], stage32[:, 6164:6166])
            hilo_(p, v0_c[:, 0:8], v0_c[:, 8:16], stage32[:, 6166:6174])
            hilo_(p, bv_c[:, 0:8], bv_c[:, 8:16], stage32[:, 6174:6182])
            at0["sv_sc"] = p.c["sc"]
            at0["sv_ve"] = p.c["ve"]
        elif t == 5:
            p.wait("sync", "sc", val=at0["sv_sc"])
            p.wait("sync", "ve", val=at0["sv_ve"])
            for r, cc2 in ((0, 0), (1, 2)):
                p.dma("sync", bq16[r:r + 1, 0:128], bq_c[:, cc2:cc2 + 1],
                      sem="ld")
                p.dma("sync", bq16[r:r + 1, 128:129],
                      bq_c[0:1, cc2 + 1:cc2 + 2], sem="ld")
                p.dma("sync", bk16[r:r + 1, 0:128], bk_c[:, cc2:cc2 + 1],
                      sem="ld")
                p.dma("sync", bk16[r:r + 1, 128:129],
                      bk_c[0:1, cc2 + 1:cc2 + 2], sem="ld")
            p.dma("sync", k0aug[0:1, :], k0hl[0:1, 1:2], sem="ld")
            p.dma("sync", k0aug[1:2, :], k0hl[0:1, 3:4], sem="ld")
            p.dma("sync", k0aug[2:3, :], k0hl[0:1, 1:2], sem="ld")
            p.wait("sync", "ld")
        elif t == 6:
            p.wait("sync", "sc", val=at0["cp03"])
            p.dma("sync", stage32[:, 0:4096].rearrange("p (c d) -> p c d", c=4),
                  wvT_in[512:1024, :].rearrange("(c p) d -> p c d", p=128),
                  sem="ld")
            p.wait("sync", "ld")
            at0["ld3"] = p.c["ld"]

    def emit_attn(p: Pass):
        p.wait("scalar", "ld", val=at0["ld3"])
        p.wait("scalar", "te")
        for ch in range(4, 8):
            p.act(wv16[:, ch * D:(ch + 1) * D],
                  stage32[:, (ch - 4) * D:(ch - 3) * D], AF.Copy, sem="sc")
        p.wait("sync", "ve")
        p.wait("sync", "gp")
        p.dma("sync", v0r16[0:1, :], at0["v0_c"][:, 0:8], sem="ld")
        p.dma("sync", bv16[0:1, :], at0["bv_c"][:, 0:8], sem="ld")
        p.dma("sync", bv16[1:2, :], at0["bv_c"][:, 8:16], sem="ld")
        p.wait("sync", "ld")
        p.wait("tensor", "ld")
        p.v("memset", ones64, 1.0)
        p.wait("vector", "sc")
        p.v("tensor_scalar", sA[:, 32:48], sA[:, 0:16], 1.0 / D, None, ALU.mult)
        p.v("tensor_scalar", sA[:, 48:64], sA[:, 16:32], 1.0 / D, None, ALU.mult)
        p.v("tensor_tensor", tmp32[:, 0:16], sA[:, 32:48], sA[:, 32:48], ALU.mult)
        p.v("tensor_sub", sA[:, 48:64], sA[:, 48:64], tmp32[:, 0:16])
        p.v("tensor_scalar", tmp32[:, 0:16], sA[:, 48:64], EPS_LN, None, ALU.add,
            sem="ve")
        p.wait("scalar", "ve")
        p.act(tmp32[:, 16:32], tmp32[:, 0:16], AF.Ln)
        p.act(sM[:, 0:16], tmp32[:, 16:32], AF.Exp, scale=-0.5, sem="sc")
        p.wait("vector", "sc")
        p.v("tensor_tensor", tmp32[:, 48:64], sA[:, 32:48], sM[:, 0:16], ALU.mult)
        p.v("tensor_scalar", sM[:, 16:32], tmp32[:, 48:64], -1.0, None, ALU.mult,
            sem="ve")
        cc_k1 = cc_k0 = None
        gp_h1 = None
        ld_qaug = {}
        ln2_ve = {}
        ln2_seq = []
        tmpab2 = [tmp32, scrF[:, 1024:2048]]

        def emit_ln_attn(t):
            ti = t % GT
            xt = x_sb[:, t * D:(t + 1) * D]
            tmp = tmpab2[t % 2]
            if len(ln2_seq) < 2:
                p.wait("scalar", "ve")
            else:
                p.wait("scalar", "ve", val=ln2_ve[ln2_seq[-2]])
            p.act(tmp[:, :], xt, AF.Copy, bias=sM[:, 16 + t:17 + t],
                  scale=sM[:, 0 + t:1 + t], sem="sc")
            p.act(scrH_hi[:, ti * D:(ti + 1) * D], tmp[:, :], AF.Copy,
                  sem="sc")
            p.wait("vector", "sc")
            p.v("tensor_tensor", scrH_lo[:, ti * D:(ti + 1) * D], tmp[:, :],
                scrH_hi[:, ti * D:(ti + 1) * D], ALU.subtract)
            p.v("tensor_add", xt, xt, tmp[:, :], sem="ve")
            ln2_ve[t] = p.c["ve"]
            ln2_seq.append(t)

        gorder = [4, 5, 6, 7, 0, 1, 2, 3]
        emit_ln_attn(gorder[0] * GT)
        emit_ln_attn(gorder[0] * GT + 1)
        for gi in range(NG):
            g = gorder[gi]
            p.wait("tensor", "sc")
            p.wait("tensor", "ve", val=ln2_ve[g * GT + 1])
            psA16_ = psA.bitcast(F16)
            for hl, (hsrc, dst) in enumerate(((scrH_hi, hfm_hi), (scrH_lo, hfm_lo))):
                for ch in range(8):
                    for ti in range(GT):
                        p.tr(psA16_[:, hl * 2048 + ch * 256 + ti * 128:
                                    hl * 2048 + ch * 256 + (ti + 1) * 128],
                             hsrc[:, ti * D + ch * 128:ti * D + (ch + 1) * 128],
                             id16[:, :], sem="te")
                p.wait("scalar", "te")
                p.act(dst[:, 0:1024], psA16_[:, hl * 2048:hl * 2048 + 1024],
                      AF.Copy, sem="sc")
                p.act(dst[:, 1024:2048],
                      psA16_[:, hl * 2048 + 1024:hl * 2048 + 2048],
                      AF.Copy, sem="sc")
            evac_sc = p.c["sc"]
            if gi + 1 < NG:
                emit_ln_attn(gorder[gi + 1] * GT)
                emit_ln_attn(gorder[gi + 1] * GT + 1)
            p.wait("tensor", "sc", val=evac_sc)
            for ti in range(GT):
                t = g * GT + ti
                tsl = slice(ti * 128, (ti + 1) * 128)
                psq = psB[:, 0:DH]
                for ch in range(8):
                    hh = hfm_hi[:, ch * GTOK + ti * 128:ch * GTOK + (ti + 1) * 128]
                    hl_ = hfm_lo[:, ch * GTOK + ti * 128:ch * GTOK + (ti + 1) * 128]
                    wh = wq16[:, ch * 258:ch * 258 + DH]
                    wl = wq16[:, ch * 258 + DH:ch * 258 + 258]
                    p.mm(psq, hh, wh, start=(ch == 0), stop=False)
                    p.mm(psq, hl_, wh, start=False, stop=False)
                    p.mm(psq, hh, wl, start=False, stop=False)
                p.mm(psq, ones16r[0:2, 0:128], bq16[0:2, :], start=False,
                     stop=True, sem="te")
                p.wait("scalar", "te")
                p.act(qtm16[:, 0:DH], psq, AF.Copy, sem="sc")
                p.wait("vector", "sc")
                p.v("tensor_tensor", qtm_lo, psq, qtm16[:, 0:DH],
                    ALU.subtract, sem="ve")
                psk_ = psB[:, 512:512 + DH]
                for ch in range(8):
                    hh = hfm_hi[:, ch * GTOK + ti * 128:ch * GTOK + (ti + 1) * 128]
                    hl_ = hfm_lo[:, ch * GTOK + ti * 128:ch * GTOK + (ti + 1) * 128]
                    wh = wk16[:, ch * 258:ch * 258 + DH]
                    wl = wk16[:, ch * 258 + DH:ch * 258 + 258]
                    p.mm(psk_, hh, wh, start=(ch == 0), stop=False)
                    p.mm(psk_, hl_, wh, start=False, stop=False)
                    p.mm(psk_, hh, wl, start=False, stop=False)
                p.mm(psk_, ones16r[0:2, 0:128], bk16[0:2, :], start=False,
                     stop=True, sem="te")
                p.wait("scalar", "te")
                p.wait("scalar", "kv")
                p.act(ktm16[:, 0:DH], psk_, AF.Copy)
                p.wait("scalar", "sc")
                p.wait("vector", "sc")
                p.v("tensor_tensor", ktm_lo, psk_, ktm16[:, 0:DH],
                    ALU.subtract, sem="ve")
                for half in range(2):
                    psv = psA[:, half * 512:(half + 1) * 512]
                    for ch in range(8):
                        hh = hfm_hi[:, ch * GTOK + ti * 128:ch * GTOK + (ti + 1) * 128]
                        wv_ = wv16[:, ch * D + half * 512:ch * D + (half + 1) * 512]
                        p.mm(psv, hh, wv_, start=(ch == 0), stop=False)
                    p.mm(psv, ones64, bv16[0:2, half * 512:(half + 1) * 512],
                         start=False, stop=True, sem="te")
                p.wait("scalar", "te")
                p.wait("scalar", "kv")
                p.act(vbuf[:, 0:512], psA[:, 0:512], AF.Copy)
                p.wait("scalar", "sc")
                p.act(vbuf[:, 512:1024], psA[:, 512:1024], AF.Copy, sem="sc")
                p.wait("sync", "sc")
                p.dma("sync", kvv_inh[t // 8].ap()[(t % 8) * 128:(t % 8 + 1) * 128, :],
                      vbuf[:, 0:1024], sem="kv")
                p.wait("tensor", "ve")
                p.tr(psA16_[:, 3072:3072 + 128], qtm16[:, 0:128], id16[:, :])
                p.tr(psA16_[:, 3072 + 128:3072 + 256], qtm_lo[:, 0:128],
                     id16[:, :])
                p.tr(psA16_[0:1, 3072 + 256:3072 + 384], qtm16[:, 128:129],
                     id16[:, :])
                p.tr(psA16_[0:1, 3072 + 384:3072 + 512], qtm_lo[:, 128:129],
                     id16[:, :], sem="te")
                p.tr(psA16_[:, 3584:3584 + 128], ktm16[:, 0:128], id16[:, :])
                p.tr(psA16_[:, 3584 + 128:3584 + 256], ktm_lo[:, 0:128],
                     id16[:, :])
                p.tr(psA16_[0:1, 3584 + 256:3584 + 384], ktm16[:, 128:129],
                     id16[:, :])
                p.tr(psA16_[0:1, 3584 + 384:3584 + 512], ktm_lo[:, 128:129],
                     id16[:, :], sem="te")
                p.wait("scalar", "te")
                p.act(qfm_hi[:, t * 128:(t + 1) * 128], psA16_[:, 3072:3200],
                      AF.Copy)
                p.act(qfm_lo[:, t * 128:(t + 1) * 128], psA16_[:, 3200:3328],
                      AF.Copy)
                p.act(stg32r[0:1, t * 128:(t + 1) * 128],
                      psA16_[0:1, 3328:3456], AF.Copy)
                p.act(stg64r[0:1, t * 128:(t + 1) * 128],
                      psA16_[0:1, 3456:3584], AF.Copy)
                p.act(kfs_hi[:, t * 128:(t + 1) * 128], psA16_[:, 3584:3712],
                      AF.Copy)
                p.act(kfs_lo[:, t * 128:(t + 1) * 128], psA16_[:, 3712:3840],
                      AF.Copy)
                p.act(kfs_ra[0:1, t * 128:(t + 1) * 128],
                      psA16_[0:1, 3840:3968], AF.Copy)
                p.act(kfs_rb[0:1, t * 128:(t + 1) * 128],
                      psA16_[0:1, 3968:4096], AF.Copy, sem="sc")
                p.wait("tensor", "sc")
                if t in (7, 15):
                    hh = t // 8
                    cs = slice(hh * 1024, hh * 1024 + 1024)
                    p.wait("sync", "sc")
                    p.dma("sync", kvk_inh[hh].ap()[0:128, :], kfs_hi[:, cs],
                          sem="kv")
                    p.dma("sync", kvk_inh[hh].ap()[128:129, :], kfs_ra[0:1, cs],
                          sem="kv")
                    p.dma("sync", kvk_inh[hh].ap()[129:257, :], kfs_lo[:, cs],
                          sem="kv")
                    p.dma("sync", kvk_inh[hh].ap()[257:258, :], kfs_rb[0:1, cs],
                          sem="kv")
                    p.wait("gpsimd", "kv")
                    p.coll("AllGather", ALU.bypass, [[0, 1, 2, 3], [4, 5, 6, 7]],
                           [kvk_inh[hh].ap().opt()], [kvk_outh[hh].ap().opt()],
                           sem="cc")
                    p.coll("AllGather", ALU.bypass, [[0, 1, 2, 3], [4, 5, 6, 7]],
                           [kvv_inh[hh].ap().opt()], [kvv_outh[hh].ap().opt()],
                           sem="cc")
                    if hh == 1:
                        cc_k1 = p.c["cc"] - 1
                    else:
                        cc_k0 = p.c["cc"] - 1
            if gi == 3:
                p.wait("gpsimd", "cc", val=cc_k1)
                p.wait("gpsimd", "gp")
                p.wait("gpsimd", "ve")
                ko = kvk_outh[1].ap()
                for r in range(4):
                    c0 = r * 2048 + 1024
                    p.dma("gpsimd", kfm_hi[:, c0:c0 + 1024],
                          ko[r * 2 * DH:r * 2 * DH + 128, :], sem="gp")
                    p.dma("gpsimd", kfm_lo[:, c0:c0 + 1024],
                          ko[r * 2 * DH + 129:r * 2 * DH + 257, :], sem="gp")
                    p.dma("gpsimd", scrG16[0:1, c0:c0 + 1024],
                          ko[r * 2 * DH + 128:r * 2 * DH + 129, :], sem="gp")
                    p.dma("gpsimd", scrG16[1:2, c0:c0 + 1024],
                          ko[r * 2 * DH + 257:r * 2 * DH + 258, :], sem="gp")
                p.wait("gpsimd", "gp")
                gp_h1 = p.c["gp"]
        p.wait("sync", "sc")
        p.dma("sync", qaug[0:1, :], stg32r[0:1, 0:TC], sem="ld")
        p.dma("sync", qaug[1:2, :], stg32r[0:1, 0:TC], sem="ld")
        p.dma("sync", qaug[2:3, :], stg64r[0:1, 0:TC], sem="ld")
        p.wait("sync", "ld")
        ld_qaug[0] = p.c["ld"]
        p.wait("sync", "cc", val=cc_k0)
        ko = kvk_outh[0].ap()
        for r in range(4):
            c0 = r * 2048
            p.dma("sync", kfm_hi[:, c0:c0 + 1024],
                  ko[r * 2 * DH:r * 2 * DH + 128, :], sem="ld")
            p.dma("sync", kfm_lo[:, c0:c0 + 1024],
                  ko[r * 2 * DH + 129:r * 2 * DH + 257, :], sem="ld")
            p.dma("sync", scrG16[0:1, c0:c0 + 1024],
                  ko[r * 2 * DH + 128:r * 2 * DH + 129, :], sem="ld")
            p.dma("sync", stg32r[0:1, c0:c0 + 1024],
                  ko[r * 2 * DH + 257:r * 2 * DH + 258, :], sem="ld")
        p.wait("sync", "ld")
        p.wait("vector", "ld", val=ld_qaug[0])
        p.wait("vector", "sc")
        p.v("memset", stg64r[0:1, 0:2048], 1.0, sem="ve")
        p.wait("sync", "sc")
        p.wait("sync", "ve")
        p.wait("sync", "gp", val=gp_h1)
        for r in range(4):
            c0 = r * 2048
            p.dma("sync", scrG16[1:2, c0:c0 + 1024], stg32r[0:1, c0:c0 + 1024],
                  sem="ld")
        p.dma("sync", scrG16[2:3, 0:8192], scrG16[0:1, 0:8192], sem="ld")
        for i4 in range(4):
            p.dma("sync", scrG16[3:4, i4 * 2048:(i4 + 1) * 2048],
                  stg64r[0:1, 0:2048], sem="ld")
        p.wait("tensor", "ld", val=ld_qaug[0])
        for t in range(NT):
            tsl = slice(t * 128, (t + 1) * 128)
            ps = psB[:, 1536 + t:1537 + t]
            p.mm(ps, qfm_hi[:, tsl], k0hl[:, 0:1], start=(t == 0), stop=False)
            p.mm(ps, qfm_hi[:, tsl], k0hl[:, 2:3], start=False, stop=False)
            p.mm(ps, qfm_lo[:, tsl], k0hl[:, 0:1], start=False, stop=False)
            p.mm(ps, qaug[0:3, tsl], k0aug[0:3, :], start=False,
                 stop=(t == NT - 1), sem="te")
        p.wait("scalar", "te")
        p.act(ssink[:, :], psB[:, 1536:1552], AF.Copy, sem="sc")
        p.wait("tensor", "sc")
        p.wait("tensor", "gp", val=gp_h1)
        fwd_sc = {}
        red_ve = {}
        H1 = [2, 3, 6, 7, 10, 11, 14, 15]
        H0 = [0, 1, 4, 5, 8, 9, 12, 13]
        seq = 0
        for pa, half in enumerate((H1, H0)):
            if pa == 1:
                p.wait("tensor", "ld")
            for t in range(NT):
                tsl = slice(t * 128, (t + 1) * 128)
                for jj, n in enumerate(half):
                    j = pa * 8 + jj
                    ps = psA[:, (seq % 4) * 512:(seq % 4) * 512 + 512]
                    if seq >= 4:
                        p.wait("tensor", "sc", val=fwd_sc[seq - 4])
                    p.mm(ps, qfm_hi[:, tsl], kfm_hi[:, n * 512:(n + 1) * 512],
                         start=True, stop=False)
                    p.mm(ps, qaug[0:1, tsl], scrG16[0:1, n * 512:(n + 1) * 512],
                         start=False, stop=True, sem="te")
                    p.wait("scalar", "te")
                    p.wait("scalar", "ve", val=red_ve.get((pa, t - 1, jj // 4), 0))
                    p.act(fwd16[:, j * 512:(j + 1) * 512], ps, AF.Copy, sem="sc")
                    fwd_sc[seq] = p.c["sc"]
                    seq += 1
                    if jj % 4 == 3:
                        gidx = jj // 4
                        if pa == 0:
                            rout = sA[:, 32 + 16 * gidx + t:33 + 16 * gidx + t]
                        else:
                            rout = zsb[:, 12 + gidx:13 + gidx]
                        p.wait("vector", "sc")
                        p.v("reduce_max", out=rout,
                            in_=fwd16[:, (pa * 2 + gidx) * 2048:
                                      (pa * 2 + gidx) * 2048 + 2048],
                            axis=AX, sem="ve")
                        red_ve[(pa, t, gidx)] = p.c["ve"]
                if pa == 1:
                    p.v("tensor_max", zsb[:, 14:15], zsb[:, 12:13],
                        zsb[:, 13:14])
                    p.v("tensor_max", zsb[:, 15:16], sA[:, 32 + t:33 + t],
                        sA[:, 48 + t:49 + t])
                    p.v("tensor_max", mrow[:, t:t + 1], zsb[:, 14:15],
                        zsb[:, 15:16], sem="ve")
        p.wait("vector", "sc")
        p.v("tensor_max", mrow[:, :], mrow[:, :], ssink[:, :])
        p.v("tensor_scalar", mrow[:, :], mrow[:, :], 0.5, None, ALU.add)
        p.v("tensor_sub", psk[:, :], ssink[:, :], mrow[:, :], sem="ve")
        p.wait("scalar", "ve")
        p.act(psk[:, :], psk[:, :], AF.Exp, sem="sc")
        p.wait("vector", "sc")
        p.v("tensor_copy", out=psk16[:, :], in_=psk[:, :])
        p.v("tensor_scalar", lv16[:, :], mrow[:, :], -1.0, None, ALU.mult,
            sem="ve")
        p.wait("tensor", "ve")
        for t in range(NT):
            p.tr(psA16_[0:1, 0:128], psk16[:, t:t + 1], id16[:, :])
            p.tr(psA16_[0:1, 1024:1152], lv16[:, t:t + 1], id16[:, :], sem="te")
            p.wait("scalar", "te")
            p.act(pskT[0:1, t * 128:(t + 1) * 128], psA16_[0:1, 0:128], AF.Copy)
            p.wait("scalar", "sc")
            p.act(stg32r[0:1, t * 128:(t + 1) * 128],
                  psA16_[0:1, 1024:1152], AF.Copy, sem="sc")
            p.wait("tensor", "sc")
        p.wait("sync", "sc")
        p.dma("sync", qaug[3:4, 0:TC], stg32r[0:1, 0:TC], sem="ld")
        p.wait("sync", "ld")
        p.wait("tensor", "ld")
        p.wait("sync", "cc")
        NVT = NQB * NKT
        v_done = {}
        vsems = ["vs", "vs1", "vs2"]

        def issue_v(g):
            kt = g % NKT
            r, o = kt // 16, kt % 16
            hh = 0 if o < 8 else 1
            row = r * 1024 + (o - hh * 8) * 128
            if g >= 3:
                p.wait("sync", "te", val=pv_te[g - 3])
            p.dma("sync", vbuf[:, (g % 3) * D:(g % 3) * D + D],
                  kvv_outh[hh].ap()[row:row + 128, :], sem=vsems[g % 3])
            v_done[g] = p.c[vsems[g % 3]]

        exp_sc = {}
        issue_v(0)
        issue_v(1)

        def emit_scores(qb, kt):
            qsl_ = slice(qb * QB, (qb + 1) * QB)
            par = kt % 2
            sbk = psB[:, par * 512:par * 512 + QB]
            if kt >= 2:
                p.wait("tensor", "sc", val=exp_sc[kt - 2])
            p.mm(sbk, kfm_hi[:, kt * 128:(kt + 1) * 128], qfm_hi[:, qsl_],
                 start=True, stop=False)
            p.mm(sbk, kfm_hi[:, kt * 128:(kt + 1) * 128], qfm_lo[:, qsl_],
                 start=False, stop=False)
            p.mm(sbk, kfm_lo[:, kt * 128:(kt + 1) * 128], qfm_hi[:, qsl_],
                 start=False, stop=False)
            p.mm(sbk, scrG16[0:4, kt * 128:(kt + 1) * 128], qaug[0:4, qsl_],
                 start=False, stop=True, sem="te")
            p.wait("scalar", "te")
            p.act(pt16[:, (kt % 3) * QB:(kt % 3 + 1) * QB], sbk, AF.Exp,
                  sem="sc")
            exp_sc[kt] = p.c["sc"]

        def emit_pv(qb, kt):
            g = qb * NKT + kt
            par = kt % 3
            p.wait("tensor", "sc", val=exp_sc[kt])
            p.wait("tensor", vsems[g % 3], val=v_done[g])
            for qt in range(2):
                ptsl = pt16[:, par * QB + qt * 128:par * QB + (qt + 1) * 128]
                vsl = vbuf[:, (g % 3) * D:(g % 3) * D + D]
                p.mm(psA[:, qt * 1024:qt * 1024 + 512], ptsl, vsl[:, 0:512],
                     start=(kt == 0), stop=False)
                p.mm(psA[:, qt * 1024 + 512:qt * 1024 + 1024], ptsl,
                     vsl[:, 512:1024], start=(kt == 0), stop=False)
                p.mm(psB[:, 1024 + qt:1025 + qt], ptsl, ones16[:, 0:1],
                     start=(kt == 0 and qt == 0),
                     stop=(kt == NKT - 1 and qt == 1), sem="te")
            pv_te[g] = p.c["te"]
            if g + 3 < NVT:
                issue_v(g + 3)

        pv_te = {}
        issue_v(2)
        for qb in range(NQB):
            exp_sc.clear()
            emit_scores(qb, 0)
            emit_scores(qb, 1)
            for kt in range(NKT):
                if kt + 2 < NKT:
                    emit_scores(qb, kt + 2)
                emit_pv(qb, kt)
            for qt in range(2):
                t = qb * 2 + qt
                tsl2 = slice(t * 128, (t + 1) * 128)
                p.mm(psA[:, qt * 1024:qt * 1024 + 512], pskT[0:1, tsl2],
                     v0r16[0:1, 0:512], start=False, stop=True)
                p.mm(psA[:, qt * 1024 + 512:qt * 1024 + 1024],
                     pskT[0:1, tsl2], v0r16[0:1, 512:1024],
                     start=False, stop=True, sem="te")
            p.wait("vector", "te")
            p.v("tensor_copy", out=zsb[:, 0:2], in_=psB[:, 1024:1026])
            p.v("tensor_add", zsb[:, 2:4], zsb[:, 0:2], psk[:, 2 * qb:2 * qb + 2])
            p.v("reciprocal", out=zsb[:, 4:6], in_=zsb[:, 2:4])
            for qt in range(2):
                t = qb * 2 + qt
                xt = x_sb[:, t * D:(t + 1) * D]
                for half in range(2):
                    p.v("scalar_tensor_tensor", xt[:, half * 512:(half + 1) * 512],
                        psA[:, qt * 1024 + half * 512:qt * 1024 + (half + 1) * 512],
                        zsb[:, 4 + qt:5 + qt], xt[:, half * 512:(half + 1) * 512],
                        ALU.mult, ALU.add, sem="ve")
                p.v("reduce_sum", out=sA[:, t:t + 1], in_=xt, axis=AX, sem="ve")
            p.wait("tensor", "ve")
        p.wait("scalar", "ve")
        for t in range(NT):
            p.act(scrH_lo.bitcast(F32)[:, 0:1024], x_sb[:, t * D:(t + 1) * D],
                  AF.Square, accum_out=sA[:, 16 + t:17 + t], sem="sc")

    def emit_out(p: Pass):
        p.wait("sync", "ve")
        for t in range(NT):
            p.dma("sync", x_out_t[t], x_sb[:, t * D:(t + 1) * D], sem="st")
        p.wait("sync", "st")

    psA = nc.alloc_psum_tensor("psA", [128, 2048], F32).ap()
    psB = nc.alloc_psum_tensor("psB", [128, 2048], F32).ap()

    counts = []
    with nc.Block() as block:
        for eng_name in ENGS:
            c = {s: 0 for s in sem_names}
            counts.append(c)

            def make(f_name, f_c):
                def _(eng):
                    program(Pass(f_name, eng, sems, f_c))
                return _
            getattr(block, eng_name)(make(eng_name, c))
    for c in counts[1:]:
        assert c == counts[0], f"counter mismatch {c} vs {counts[0]}"
    return nc


def host_inputs(I, core):
    b, j = core // 4, core % 4
    t0 = j * TC
    f32 = np.float32

    def T(a):
        return np.ascontiguousarray(np.asarray(a, f32).T)

    def pad129(a):
        out = np.zeros((128, 2), f32)
        a = np.asarray(a, f32).reshape(-1)
        out[:, 0] = a[0:128]
        out[0, 1] = a[128]
        return out

    return {
        "x_in": np.ascontiguousarray(np.asarray(I["L0_attn_x"], f32)[b, t0:t0 + TC]),
        "wqT": T(I["L0_attn_w_q"]),
        "wkT": T(I["L0_attn_w_k"]),
        "wvT": T(I["L0_attn_w_v"]),
        "bqp": pad129(I["L0_attn_b_q"]),
        "bkp": pad129(I["L0_attn_b_k"]),
        "bvp": np.asarray(I["L0_attn_b_v"], f32).reshape(128, 8),
        "k0p": pad129(I["L0_attn_k_0"]),
        "v0p": np.asarray(I["L0_attn_v_0"], f32).reshape(128, 8),
        "qproj": np.asarray(I["L0_ffn_q_proj_w"], f32),
        "routerT": T(I["L0_ffn_router_w"]),
        "auxkT": T(I["L0_ffn_aux_keys"]),
        "maskp": np.asarray(I["L0_ffn_reliability_mask"], f32).reshape(128, 32),
        "w1T": T(I["L0_ffn_unc_w1"]),
        "b1p": np.ascontiguousarray(
            np.asarray(I["L0_ffn_unc_b1"], f32).reshape(2, 128).T),
        "w2p": np.ascontiguousarray(
            np.asarray(I["L0_ffn_unc_w2"], f32).reshape(2, 128).T),
        "b2v_in": np.broadcast_to(
            np.asarray(I["L0_ffn_unc_b2"], f32).reshape(1, 1), (128, 1)).copy(),
        "gw1v_in": np.full((128, 1), np.float32(I["L0_ffn_gate_w1"]), f32),
        "gbv_in": np.full((128, 1), np.float32(I["L0_ffn_gate_bias"]), f32),
        "aux_vals": np.asarray(I["L0_ffn_aux_values"], f32),
        "id16": np.eye(128, dtype=np.float16),
    }


def kernel(**inputs):
    nc = build_nc("full")
    in_maps = [host_inputs(inputs, c) for c in range(8)]
    res = run_bass_kernel_spmd(nc, in_maps, core_ids=list(range(8)))
    out = np.empty((B, S, D), np.float32)
    for c in range(8):
        b, j = c // 4, c % 4
        out[b, j * TC:(j + 1) * TC] = res.results[c]["x_out"]
    return out
